# version 01
# baseline (speedup 1.0000x reference)
import numpy as np
import ml_dtypes
import concourse.bass as bass
import concourse.mybir as mybir
from concourse.bass_utils import run_bass_kernel_spmd

F32 = mybir.dt.float32
BF16 = mybir.dt.bfloat16
AF = mybir.ActivationFunctionType
ALU = mybir.AluOpType
BF = ml_dtypes.bfloat16

NCORE = 8
B, S, D = 4, 4096, 1024
NT = 2048
NCH = 8
YW = 2056
HL, DH, WIN, NW = 8, 64, 512, 4
HID, HIDP, NJ = 2730, 2816, 22
EXP_BIAS = -40.0
NEG = -30000.0
PAIRS = [[0, 1], [2, 3], [4, 5], [6, 7]]
PERM_WO = np.array([(c + 4 * (r // 64)) * 64 + (r % 64)
                    for c in range(4) for r in range(128)])
PERM_WOUT = np.array([(c + 8 * (r // 64)) * 64 + (r % 64)
                      for c in range(8) for r in range(128)])
PERM_ROT = np.array([(d // 64) * 64 + ((d % 64) + 32) % 64 for d in range(128)])

A_X = 0
A_Y = 66048
A_YB = 0
A_WT = 33024
A_WV = 37120
A_VT = 41216
A_QB = 131840
A_KB = 148224
A_TAB = 168704
A_RAW = 187136
A_RT = 192256
A_EXP = 0
A_OFM = 12288
A_RCP = 28672
A_RBC = 32768
A_WOS = 168704
A_HB = 0
A_AGB = 131840
A_W1T = 154368
A_W2T = 162560
A_LT = 166656
A_XLB = 66048
A_XLO = 98816
A_ODN = 131840
A_QD = 164608
A_KD = 168704
A_VD = 176896
A_EXD = 185216
A_RCD = 191360
A_RBD = 195456
A_WOUT = 66048
ARENA_B = 200448


def _rot_tables(t0):
    half_idx = np.arange(0, DH, 2, dtype=np.float64)
    inv_freq = 1.0 / 10000.0 ** (half_idx / DH)
    tpos = np.arange(t0 - 512, t0 + 2048, dtype=np.float64)
    fr = tpos[:, None] * inv_freq[None, :]
    freqs = np.concatenate([fr, fr], axis=1)
    base = (half_idx + 0.4 * DH) / (1.4 * DH)
    power = (tpos - S // 2) / float(WIN // 2)
    sc = base[None, :] ** power[:, None]
    scale = np.concatenate([sc, sc], axis=1)
    cos, sin = np.cos(freqs), np.sin(freqs)
    out = []
    for arr in (cos * scale, sin * scale, cos / scale, sin / scale):
        a = arr.T.astype(np.float32)
        out.append(np.ascontiguousarray(np.concatenate([a, a], axis=0)))
    return out


def _pack_fm(xs):
    T = xs.shape[0]
    return np.ascontiguousarray(
        xs.T.reshape(NCH, 128, T).transpose(1, 0, 2).reshape(128, NCH * T))


class Ctr:
    def __init__(self, nc):
        self.sem, self.cnt = {}, {}
        for n in ("g", "y", "t", "s", "v", "c", "p0", "p1", "q0", "q1", "r0", "r1"):
            self.sem[n] = nc.alloc_semaphore("sem_" + n)
            self.cnt[n] = 0

    def inc(self, inst, name, amt=1):
        inst.then_inc(self.sem[name], amt)
        self.cnt[name] += amt
        return inst

    def wait(self, eng, name, val=None):
        vv = self.cnt[name] if val is None else val
        if vv > 0:
            eng.wait_ge(self.sem[name], vv)


def build_program(stage=99, dbg=()):
    nc = bass.Bass("TRN2", target_bir_lowering=False, debug=False,
                   detect_race_conditions=False)

    def inp(name, shape, dt=F32):
        return nc.dram_tensor(name, list(shape), dt, kind="ExternalInput").ap()

    xin = inp("xin", [128, NCH * NT])
    wqkv = inp("wqkv", [2, D, 3 * 512])
    wo_p = inp("wo_p", [2, 512, D])
    w1a = inp("w1a", [2, D, HIDP])
    w1g = inp("w1g", [2, D, HIDP])
    w2p = inp("w2p", [2, HIDP, D])
    woutT = inp("woutT", [D, D])
    wq_t = inp("wq_t", [DH, DH])
    wk_t = inp("wk_t", [DH, DH])
    wv_t = inp("wv_t", [DH, DH])
    b_out_r = inp("b_out_r", [128, 8])
    ln_g_r = inp("ln_g_r", [2, 128, 8])
    ln_b_r = inp("ln_b_r", [2, 128, 8])
    b2_r = inp("b2_r", [2, 128, 8])
    b1a_r = inp("b1a_r", [2, 128, NJ])
    b1g_r = inp("b1g_r", [2, 128, NJ])
    qs_r = inp("qs_r", [2, 128, 1])
    qsp_r = inp("qsp_r", [2, 128, 1])
    ks_r = inp("ks_r", [2, 128, 1])
    ksp_r = inp("ksp_r", [2, 128, 1])
    tab_cos = inp("tab_cos", [128, 2560])
    tab_sin = inp("tab_sin", [128, 2560])
    tab_icos = inp("tab_icos", [128, 2560])
    tab_isin = inp("tab_isin", [128, 2560])
    hz = inp("hz", [128, 1])
    maskA = inp("maskA", [128, 1])

    xout = nc.dram_tensor("xout", [128, NCH * NT], F32, kind="ExternalOutput").ap()
    dbg_outs = {}
    for name, shape, dt in dbg:
        dbg_outs[name] = nc.dram_tensor("dbg_" + name, list(shape), dt,
                                        kind="ExternalOutput").ap()

    ident_d = nc.inline_tensor(np.eye(128, dtype=BF), name="c_ident")
    onesb_d = nc.inline_tensor(np.ones((128, 1), dtype=BF), name="c_onesb")
    onesrow_d = nc.inline_tensor(np.ones((1, 128), dtype=np.float32), name="c_onesrow")
    ones64_d = nc.inline_tensor(np.ones((128, 64), dtype=np.float32), name="c_ones64")
    ind2np = np.zeros((2, 128), dtype=np.float32)
    ind2np[0, :64] = 1.0
    ind2np[1, 64:] = 1.0
    ind2b_d = nc.inline_tensor(np.ascontiguousarray(ind2np.T).astype(BF), name="c_ind2b")
    ind2f_d = nc.inline_tensor(ind2np, name="c_ind2f")
    R64 = np.zeros((64, 64), dtype=np.float64)
    R64[np.arange(32), np.arange(32) + 32] = -1.0
    R64[np.arange(32, 64), np.arange(32)] = 1.0
    R128 = np.zeros((128, 128), dtype=np.float64)
    R128[0:64, 0:64] = R64
    R128[64:128, 64:128] = R64
    rotT_d = nc.inline_tensor(np.ascontiguousarray(R128.T).astype(BF), name="c_rotT")
    mk = np.zeros((128, 4, 512), dtype=np.float64)
    kk = np.arange(128)[:, None]
    qq = np.arange(512)[None, :]
    for jb in range(4):
        mk[:, jb, :] = np.where(qq < 128 * jb + kk, NEG, 0.0)
    masks_d = nc.inline_tensor(
        np.ascontiguousarray(mk.reshape(128, 2048)).astype(BF), name="c_masks")
    mlb = np.zeros((128, 4, 512), dtype=np.float64)
    for i_ in range(4):
        mlb[:, i_, :] = np.where(qq > 128 * i_ + kk, NEG, 0.0)
    masks_lb_d = nc.inline_tensor(
        np.ascontiguousarray(mlb.reshape(128, 2048)).astype(BF), name="c_masks_lb")
    cp = np.zeros((128, 8), dtype=np.float32)
    cp[:, 0] = 1e-5
    cp[:, 1] = EXP_BIAS
    cpool_d = nc.inline_tensor(cp, name="c_cpool")

    wc_qkv = nc.dram_tensor("wc_qkv", [2, D, 3 * 512], BF16, kind="Internal")
    wc_wo = nc.dram_tensor("wc_wo", [2, 512, D], BF16, kind="Internal")
    wc_w1a = nc.dram_tensor("wc_w1a", [2, D, HIDP], BF16, kind="Internal")
    wc_w1g = nc.dram_tensor("wc_w1g", [2, D, HIDP], BF16, kind="Internal")
    wc_w2 = nc.dram_tensor("wc_w2", [2, HIDP, D], BF16, kind="Internal")
    wc_woutT = nc.dram_tensor("wc_woutT", [D, D], BF16, kind="Internal")
    x_spill = nc.dram_tensor("x_spill", [128, NCH * NT], F32, kind="Internal")
    ccT_in = nc.dram_tensor("ccT_in", [128, 8], F32, kind="Internal")
    ccT_out = nc.dram_tensor("ccT_out", [2, 128, 8], F32, kind="Internal")
    VW = 4 * HL * 65
    ccA_in = nc.dram_tensor("ccA_in", [128, 2048 + VW], BF16, kind="Internal")
    ccA_out = nc.dram_tensor("ccA_out", [2, 128, 2048 + VW], BF16, kind="Internal")
    ccD_in = nc.dram_tensor("ccD_in", [128, NCH * NT], BF16, kind="Internal")
    ccD_out = nc.dram_tensor("ccD_out", [2, 128, NCH * NT], BF16, kind="Internal")

    ident = nc.alloc_sbuf_tensor("ident", [128, 128], BF16)
    onesb = nc.alloc_sbuf_tensor("onesb", [128, 1], BF16)
    onesrowf = nc.alloc_sbuf_tensor("onesrowf", [1, 128], F32)
    ones64f = nc.alloc_sbuf_tensor("ones64f", [128, 64], F32)
    ind2b = nc.alloc_sbuf_tensor("ind2b", [128, 2], BF16)
    ind2f = nc.alloc_sbuf_tensor("ind2f", [2, 128], F32)
    rotT = nc.alloc_sbuf_tensor("rotT", [128, 128], BF16)
    masksb = nc.alloc_sbuf_tensor("masksb", [128, 2048], BF16)
    maskslb = nc.alloc_sbuf_tensor("maskslb", [128, 2048], BF16)
    cpool = nc.alloc_sbuf_tensor("cpool", [128, 8], F32)
    blkq = nc.alloc_sbuf_tensor("blkq", [128, 128], BF16)
    blkk = nc.alloc_sbuf_tensor("blkk", [128, 128], BF16)
    blkv = nc.alloc_sbuf_tensor("blkv", [128, 128], BF16)
    bout_c = nc.alloc_sbuf_tensor("bout_c", [128, 8], F32)
    lng_c = nc.alloc_sbuf_tensor("lng_c", [128, 16], F32)
    lnb_c = nc.alloc_sbuf_tensor("lnb_c", [128, 16], F32)
    b2_c = nc.alloc_sbuf_tensor("b2_c", [128, 16], F32)
    b1a_c = nc.alloc_sbuf_tensor("b1a_c", [128, 2 * NJ], F32)
    b1g_c = nc.alloc_sbuf_tensor("b1g_c", [128, 2 * NJ], F32)
    qsc = nc.alloc_sbuf_tensor("qsc", [128, 8], F32)
    hzc = nc.alloc_sbuf_tensor("hzc", [128, 1], F32)
    mac = nc.alloc_sbuf_tensor("mac", [128, 1], F32)
    macE = nc.alloc_sbuf_tensor("macE", [128, 1], F32)

    arena = nc.alloc_sbuf_tensor("arena", [128, ARENA_B], mybir.dt.uint8)
    ab = nc.lookup_mloc(arena).addr
    rem = nc.sbuf_bytes_remaining
    assert rem >= 0, rem
    _n = [0]

    def place(off, shape, dtype):
        per = int(np.prod(shape[1:])) * mybir.dt.size(dtype)
        assert off % 32 == 0 and off + per <= ARENA_B, (off, per)
        _n[0] += 1
        h = nc.alloc_sbuf_tensor_at(f"ar{_n[0]}", list(shape), dtype,
                                    offset=ab + off)
        if not hasattr(nc, "_arena_map"):
            nc._arena_map = {}
        nc._arena_map.setdefault(off, []).append(h.name)
        return h

    X = place(A_X, [128, NCH, NT], F32)
    Y = place(A_Y, [128, NCH, YW], F32)
    Yb = place(A_YB, [128, NCH, YW], BF16)
    wtq = [place(A_WT + i * 2048, [128, 8, 128], BF16) for i in range(2)]
    wv_sb = place(A_WV, [128, 8, 256], BF16)
    v_tm = place(A_VT, [128, 20, HL, 65], BF16)
    Qb = place(A_QB, [128, 4, 2048], BF16)
    Kb = place(A_KB, [128, 4, 2560], BF16)
    tqc = place(A_TAB, [128, 2048], BF16)
    tqs = place(A_TAB + 4096, [128, 2048], BF16)
    tkc = place(A_TAB + 8192, [128, 2560], BF16)
    tks = place(A_TAB + 13312, [128, 2560], BF16)
    rawt = place(A_RAW, [128, 2560], BF16)
    rt_q2 = place(A_RT, [128, 512], BF16)
    rt_rb = place(A_RT + 1024, [128, 512], F32)
    rt_qn = place(A_RT + 3072, [128, 512], F32)
    rt_qnb = place(A_RT + 5120, [128, 512], BF16)
    rt_t2 = place(A_RT + 6144, [128, 512], F32)
    rt_ss = place(A_RAW, [2, 512], F32)
    expb = [place(A_EXP + i * 3072, [128, 1536], BF16) for i in range(4)]
    o_fm = place(A_OFM, [128, 4, 2048], BF16)
    recip_sb = place(A_RCP, [128, 1024], F32)
    recbc = place(A_RBC, [128, 1024], F32)
    wos = place(A_WOS, [128, 4, 1024], BF16)
    Hb = place(A_HB, [128, NCH, NT], BF16)
    AGb = place(A_AGB, [128, NJ, 512], BF16)
    wt1a = [place(A_W1T + i * 2048, [128, 8, 128], BF16) for i in range(2)]
    wt1g = [place(A_W1T + 4096 + i * 2048, [128, 8, 128], BF16) for i in range(2)]
    wt2 = [place(A_W2T + i * 2048, [128, 1024], BF16) for i in range(2)]
    tXb = place(A_LT, [128, NT], BF16)
    tSQb = place(A_LT + 4096, [128, NT], BF16)
    tnm = place(A_LT + 8192, [128, NT], F32)
    trs = place(A_LT + 16384, [128, NT], F32)
    r_m = place(A_LT + 24576, [1, 512], F32)
    r_q = place(A_LT + 26624, [1, 512], F32)
    r_v = place(A_LT + 28672, [1, 512], F32)
    r_s = place(A_LT + 30720, [1, 512], F32)
    XLb = place(A_XLB, [128, NCH, NT], BF16)
    XLo = place(A_XLO, [128, NCH, NT], BF16)
    o_dn = place(A_ODN, [128, NCH, NT], BF16)
    Qd = place(A_QD, [128, 2048], BF16)
    Kd = place(A_KD, [128, 4096], BF16)
    Vd = place(A_VD, [128, 32, 130], BF16)
    expd = [place(A_EXD + i * 3072, [128, 1536], BF16) for i in range(2)]
    recip_d = place(A_RCD, [128, 1024], F32)
    recbc_d = place(A_RBD, [128, 1024], F32)
    wouts = place(A_WOUT, [128, 8, 1024], BF16)

    P_A = nc.alloc_psum_tensor("P_A", [128, 2048], F32)
    P_B = nc.alloc_psum_tensor("P_B", [128, 1536], F32)
    P_C = nc.alloc_psum_tensor("P_C", [128, 512], F32)

    C = Ctr(nc)
    g, t, s, v, yq = nc.gpsimd, nc.tensor, nc.scalar, nc.vector, nc.sync

    def gdma(out, in_, slow=False, ch="g"):
        return C.inc(g.dma_start(out=out, in_=in_,
                                 allow_slow_non_contiguous=slow), ch, 16)

    def ydma(out, in_, slow=False, ch="y"):
        return C.inc(yq.dma_start(out=out, in_=in_,
                                  allow_slow_non_contiguous=slow), ch, 16)

    def barrier(*engs):
        for e in engs:
            for nm in ("g", "y", "t", "s", "v", "c", "p0", "p1", "q0", "q1", "r0", "r1"):
                C.wait(e, nm)

    EPS1 = cpool[0:1, 0:1]
    EBIA = cpool[0:128, 1:2]

    def finish_y():
        barrier(t, s, v, g, yq)
        ydma(xout.rearrange("p (c t) -> p c t", t=NT), Y[:, :, 1:1 + NT])
        barrier(g, t, s, v, yq)
        return nc, dbg_outs

    def finish(ap=None, cast=False):
        barrier(t, s, v, g, yq)
        if ap is None:
            ap = X[:, :, :].rearrange("p c t -> p (c t)")
        if cast:
            gdma(xout, ap)
        else:
            ydma(xout, ap)
        barrier(g, t, s, v, yq)
        return nc, dbg_outs

    pid = g.partition_id()
    oth_reg = g.alloc_register("oth")
    g.reg_alu(oth_reg, pid, 2, ALU.mod)
    g.reg_alu(oth_reg, oth_reg, 1, ALU.bitwise_xor)
    oth_sv = g.snap(oth_reg, min_val=0, max_val=1)

    gdma(X[:, :, :], xin.rearrange("p (c t) -> p c t", t=NT))
    for dst, src in [(ident, ident_d), (onesb, onesb_d), (onesrowf, onesrow_d),
                     (ones64f, ones64_d), (ind2b, ind2b_d), (ind2f, ind2f_d),
                     (rotT, rotT_d), (masksb, masks_d), (maskslb, masks_lb_d), (cpool, cpool_d)]:
        gdma(dst[:, :], src.ap())
    gdma(bout_c[:, :], b_out_r)
    gdma(hzc[:, :], hz)
    gdma(mac[:, :], maskA)
    for l in range(2):
        gdma(lng_c[:, l * 8:(l + 1) * 8], ln_g_r[l])
        gdma(lnb_c[:, l * 8:(l + 1) * 8], ln_b_r[l])
        gdma(b2_c[:, l * 8:(l + 1) * 8], b2_r[l])
        gdma(b1a_c[:, l * NJ:(l + 1) * NJ], b1a_r[l])
        gdma(b1g_c[:, l * NJ:(l + 1) * NJ], b1g_r[l])
        gdma(qsc[:, l:l + 1], qs_r[l])
        gdma(qsc[:, 2 + l:3 + l], qsp_r[l])
        gdma(qsc[:, 4 + l:5 + l], ks_r[l])
        gdma(qsc[:, 6 + l:7 + l], ksp_r[l])
    C.inc(v.memset(blkq[:, :], 0.0), "v")
    C.inc(v.memset(blkk[:, :], 0.0), "v")
    C.inc(v.memset(blkv[:, :], 0.0), "v")
    C.wait(g, "v")
    for blk, w in [(blkq, wq_t), (blkk, wk_t), (blkv, wv_t)]:
        gdma(blk[0:64, 0:64], w)
        gdma(blk[64:128, 64:128], w)
    C.wait(v, "g")
    C.inc(v.tensor_scalar_add(macE[:, :], mac[:, :], EXP_BIAS), "v")
    stg = [place(A_QB, [128, HIDP], BF16), place(A_QB + 5664, [128, HIDP], BF16)]
    wjobs = []
    for l in range(2):
        wjobs += [(wqkv[l], wc_qkv.ap()[l], D, 1536),
                  (wo_p[l], wc_wo.ap()[l], 512, D),
                  (w1a[l], wc_w1a.ap()[l], D, HIDP),
                  (w1g[l], wc_w1g.ap()[l], D, HIDP),
                  (w2p[l], wc_w2.ap()[l], HIDP, D)]
    wjobs.append((woutT, wc_woutT.ap(), D, D))
    ci = 0
    for src, dst, rows, cols in wjobs:
        for r0 in range(0, rows, 128):
            bi = ci % 2
            pch, qch = ("p0", "q0") if bi == 0 else ("p1", "q1")
            if ci >= 2:
                C.wait(g, qch)
            gdma(stg[bi][:, 0:cols], src[r0:r0 + 128, :], ch=pch)
            C.wait(yq, pch)
            ydma(dst[r0:r0 + 128, :], stg[bi][:, 0:cols], ch=qch)
            ci += 1
    barrier(t, s, v, g, yq)

    def ln_full(src_fn, mode, l=None):
        for c in range(NCH):
            C.inc(s.copy(tXb[:, :], src_fn(c)), "s")
            C.inc(s.activation(tSQb[:, :], src_fn(c), AF.Square), "s")
            C.wait(t, "s")
            for nt in range(4):
                t.matmul(P_A[0:1, nt * 512:(nt + 1) * 512], onesb[:, :],
                         tXb[:, nt * 512:(nt + 1) * 512],
                         start=(c == 0), stop=(c == NCH - 1))
                dst = (P_B[0:1, nt * 512:(nt + 1) * 512] if nt < 3
                       else P_C[0:1, 0:512])
                mm = t.matmul(dst, onesb[:, :], tSQb[:, nt * 512:(nt + 1) * 512],
                              start=(c == 0), stop=(c == NCH - 1))
            C.inc(mm, "t")
            C.wait(s, "t")
        for nt in range(4):
            mean_ap = P_A[0:1, nt * 512:(nt + 1) * 512]
            sq_ap = (P_B[0:1, nt * 512:(nt + 1) * 512] if nt < 3 else P_C[0:1, 0:512])
            C.wait(v, "t")
            C.inc(v.tensor_scalar(r_m[:, :], mean_ap, -1.0 / 1024.0, None, ALU.mult), "v")
            C.inc(v.tensor_scalar(r_q[:, :], sq_ap, 1.0 / 1024.0, None, ALU.mult), "v")
            C.inc(v.tensor_tensor(r_v[:, :], r_m[:, :], r_m[:, :], ALU.mult), "v")
            C.inc(v.tensor_sub(r_q[:, :], r_q[:, :], r_v[:, :]), "v")
            C.wait(s, "v")
            C.inc(s.activation(r_s[:, :], r_q[:, :], AF.Sqrt, bias=EPS1), "s")
            C.wait(v, "s")
            C.inc(v.reciprocal(r_s[:, :], r_s[:, :]), "v")
            C.wait(t, "v")
            C.inc(t.matmul(P_A[:, nt * 512:(nt + 1) * 512], onesrowf[0:1, :],
                           r_m[0:1, :], start=True, stop=True), "t")
            C.wait(v, "t")
            C.inc(v.tensor_copy(tnm[:, nt * 512:(nt + 1) * 512],
                                P_A[:, nt * 512:(nt + 1) * 512]), "v")
            C.wait(t, "v")
            C.inc(t.matmul(P_A[:, nt * 512:(nt + 1) * 512], onesrowf[0:1, :],
                           r_s[0:1, :], start=True, stop=True), "t")
            C.wait(v, "t")
            C.inc(v.tensor_copy(trs[:, nt * 512:(nt + 1) * 512],
                                P_A[:, nt * 512:(nt + 1) * 512]), "v")
        for c in range(NCH):
            C.inc(v.tensor_add(P_A[:, :], src_fn(c), tnm[:, :]), "v")
            if mode == "y":
                C.inc(v.tensor_mul(Y[:, c, 1:1 + NT], P_A[:, :], trs[:, :]), "v")
            elif mode == "xlb":
                C.inc(v.tensor_mul(XLb[:, c, :], P_A[:, :], trs[:, :]), "v")
            else:
                C.inc(v.tensor_mul(P_A[:, :], P_A[:, :], trs[:, :]), "v")
                C.inc(v.tensor_scalar(Hb[:, c, :], P_A[:, :],
                                      lng_c[0:128, l * 8 + c:l * 8 + c + 1],
                                      lnb_c[0:128, l * 8 + c:l * 8 + c + 1],
                                      ALU.mult, ALU.add), "v")

    def cc_tok():
        C.wait(g, "v")
        C.wait(g, "t")
        C.wait(g, "s")
        gdma(ccT_in.ap(), Y[:, :, 2048:2049].rearrange("p c o -> p (c o)"), slow=True)
        C.wait(g, "g")
        C.inc(g.collective_compute("AllGather", ALU.bypass, replica_groups=PAIRS,
                                   ins=[ccT_in.ap().opt()],
                                   outs=[ccT_out.ap().opt()]), "c")
        C.wait(g, "c")
        gdma(Y[:, :, 0:1].rearrange("p c o -> p (c o)"),
             ccT_out.ap()[bass.ds(oth_sv, 1), :, :].rearrange("o p c -> p (o c)"),
             slow=True)
        C.wait(v, "g")
        C.inc(v.tensor_scalar(Y[:, :, 0:1], Y[:, :, 0:1], hzc[0:128, 0:1], None,
                              ALU.mult), "v")

    def attn_norm(low, oA_ap, oB_ap, sumA_ap, sumB_ap, dstA, dstB, rcp, rbc):
        rr = 64 if low else 0
        C.wait(s, "t")
        C.inc(s.activation(rcp[rr:rr + 1, 512:1024], sumA_ap, AF.Ln), "s")
        C.inc(s.activation(rcp[rr:rr + 1, 0:512], rcp[rr:rr + 1, 512:1024],
                           AF.Exp, scale=-1.0), "s")
        C.inc(s.activation(rbc[rr:rr + 1, 512:1024], sumB_ap, AF.Ln), "s")
        C.inc(s.activation(rbc[rr:rr + 1, 0:512], rbc[rr:rr + 1, 512:1024],
                           AF.Exp, scale=-1.0), "s")
        C.wait(t, "s")
        if low:
            t.matmul(P_B[0:64, 0:512], ones64f[64:65, :], rcp[64:65, 0:512],
                     start=True, stop=True)
            mm2 = t.matmul(P_B[0:64, 512:1024], ones64f[64:65, :],
                           rbc[64:65, 0:512], start=True, stop=True)
        else:
            t.matmul(P_B[64:128, 0:512], ones64f[0:1, :], rcp[0:1, 0:512],
                     start=True, stop=True, tile_position=(0, 64))
            mm2 = t.matmul(P_B[64:128, 512:1024], ones64f[0:1, :],
                           rbc[0:1, 0:512], start=True, stop=True,
                           tile_position=(0, 64))
        C.inc(mm2, "t")
        C.wait(v, "t")
        if low:
            C.inc(v.tensor_copy(rcp[0:64, 0:512], P_B[0:64, 0:512]), "v")
            C.inc(v.tensor_copy(rbc[0:64, 0:512], P_B[0:64, 512:1024]), "v")
            C.inc(v.tensor_mul(dstA, oA_ap, rcp[0:64, 0:512]), "v")
            C.inc(v.tensor_mul(dstB, oB_ap, rbc[0:64, 0:512]), "v")
        else:
            C.inc(v.tensor_copy(rcp[64:128, 0:512], P_B[64:128, 0:512]), "v")
            C.inc(v.tensor_copy(rbc[64:128, 0:512], P_B[64:128, 512:1024]), "v")
            C.inc(v.tensor_mul(dstA, oA_ap, rcp[64:128, 0:512]), "v")
            C.inc(v.tensor_mul(dstB, oB_ap, rbc[64:128, 0:512]), "v")

    LAYER_STOP = {}

    def ffn_layer(l, first_of_call):
        barrier(s)
        if first_of_call:
            C.wait(v, "y")
            C.inc(v.memset(v_tm[:, :, :, 64:65], 1.0), "v")
        for c in range(NCH):
            C.inc(s.copy(Yb[:, c, 0:2049], Y[:, c, 0:2049]), "s")
        for rawsrc, dstt, qcol, ncols in (
                (tab_cos, tqc, 0 + l, 2048), (tab_sin, tqs, 2 + l, 2048),
                (tab_icos, tkc, 4 + l, 2560), (tab_isin, tks, 6 + l, 2560)):
            C.wait(g, "v")
            gdma(rawt[:, :], rawsrc, ch="p0")
            C.wait(v, "p0")
            C.inc(v.tensor_scalar(dstt[:, 0:ncols], rawt[:, 2560 - ncols:2560],
                                  qsc[0:128, qcol:qcol + 1], None, ALU.mult), "v")
        if stage == 87:
            barrier(g, t, s, v, yq)
            gdma(xout[:, 0:2048], tqc[:, :])
            gdma(xout[:, 2048:4096], tqs[:, :])
            gdma(xout[:, 4096:6656], tkc[:, :])
            gdma(xout[:, 6656:9216], tks[:, :])
            barrier(g, t, s, v, yq)
            return
        if stage == 8:
            return
        barrier(t, g)
        for half in range(2):
            C.wait(g, "t")
            gdma(wv_sb[:, :, :],
                 wc_qkv.ap()[l].rearrange("(kc p) o -> p kc o", p=128)
                 [:, :, 1024 + half * 256:1024 + (half + 1) * 256], ch="p1")
            C.wait(t, "p1")
            lastv = {0: 0, 1: 0}
            for tt in range(16):
                ps = P_C[:, 0:512] if (tt % 2 == 0) else P_B[:, 0:512]
                psv = ps[0:128, 0:256]
                if tt >= 2:
                    C.wait(t, "v", lastv[tt % 2])
                for kc in range(NCH):
                    sh = 1 if kc < 4 else 0
                    mm = t.matmul(psv, Yb[:, kc, sh + tt * 128:sh + (tt + 1) * 128],
                                  wv_sb[:, kc, :], start=(kc == 0), stop=(kc == NCH - 1))
                C.inc(mm, "t")
                C.wait(v, "t")
                C.inc(v.tensor_copy(
                    v_tm[:, 4 + tt, half * 4:(half + 1) * 4, 0:64],
                    psv.rearrange("p (h e) -> p h e", e=64)), "v")
                lastv[tt % 2] = C.cnt["v"]
        if stage == 9:
            return
        import os as _os
        K_SUB = int(_os.environ.get("K_SUB", "99"))
        K_OC = int(_os.environ.get("K_OC", "8"))
        barrier(t, g)
        gdma(wtq[0][:, :, :],
             wc_qkv.ap()[l].rearrange("(kc p) o -> p kc o", p=128)[:, :, 0:128],
             ch="p0")
        pa_free = C.cnt["v"]
        tmile = {0: 0, 1: 0}
        for oc in range(K_OC):
            C.wait(t, "p0" if oc % 2 == 0 else "p1")
            C.wait(t, "v", pa_free)
            for nt in range(4):
                for kc in range(NCH):
                    sh = 1 if kc < 4 else 0
                    mm = t.matmul(P_A[:, nt * 512:(nt + 1) * 512],
                                  wtq[oc % 2][:, kc, :],
                                  Yb[:, kc, sh + nt * 512:sh + (nt + 1) * 512],
                                  start=(kc == 0), stop=(kc == NCH - 1))
            C.inc(mm, "t")
            tmile[oc % 2] = C.cnt["t"]
            if oc + 1 < K_OC:
                C.wait(g, "t", tmile[(oc + 1) % 2])
                gdma(wtq[(oc + 1) % 2][:, :, :],
                     wc_qkv.ap()[l].rearrange("(kc p) o -> p kc o", p=128)
                     [:, :, (oc + 1) * 128:(oc + 2) * 128],
                     ch="p0" if (oc + 1) % 2 == 0 else "p1")
            isq = oc < 4
            tc_, ts_ = (tqc, tqs) if isq else (tkc, tks)
            tofs = 0 if isq else 512
            if K_SUB == 0:
                C.wait(v, "t")
                for nt in range(4):
                    dst = (Qb[:, oc, nt * 512:(nt + 1) * 512] if isq else
                           Kb[:, oc - 4, 512 + nt * 512:512 + (nt + 1) * 512])
                    C.inc(v.tensor_copy(dst, P_A[:, nt * 512:(nt + 1) * 512]), "v")
                pa_free = C.cnt["v"]
                continue
            K_STEP = int(_os.environ.get("K_STEP", "99"))
            for nt in range(4):
                C.wait(s, "t")
                C.inc(s.activation(rt_q2[:, :], P_A[:, nt * 512:(nt + 1) * 512],
                                   AF.Square), "s")
                if K_STEP > 1:
                    C.wait(t, "s")
                    C.wait(t, "v")
                    C.inc(t.matmul(P_C[0:2, 0:512], ind2b[:, :], rt_q2[:, :],
                                   start=True, stop=True), "t")
                if K_STEP > 2:
                    C.wait(s, "t")
                    C.inc(s.activation(rt_ss[:, :], P_C[0:2, 0:512], AF.Sqrt), "s")
                if K_STEP > 3:
                    C.wait(v, "s")
                    C.inc(v.tensor_scalar_max(rt_ss[:, :], rt_ss[:, :], 1e-12), "v")
                    C.inc(v.reciprocal(rt_ss[:, :], rt_ss[:, :]), "v")
                if K_STEP > 4:
                    C.wait(t, "v")
                    C.inc(t.matmul(P_C[:, 0:512], ind2f[:, :], rt_ss[0:2, :],
                                   start=True, stop=True), "t")
                if K_STEP > 5:
                    C.wait(v, "t")
                    C.inc(v.tensor_copy(rt_rb[:, :], P_C[:, 0:512]), "v")
                    C.inc(v.tensor_mul(rt_qn[:, :], P_A[:, nt * 512:(nt + 1) * 512],
                                       rt_rb[:, :]), "v")
                if K_STEP > 6:
                    C.wait(s, "v")
                    C.inc(s.copy(rt_qnb[:, :], rt_qn[:, :]), "s")
                if K_STEP > 7:
                    C.wait(t, "s")
                    C.inc(t.matmul(P_C[:, 0:512], rotT[:, :], rt_qnb[:, :],
                                   start=True, stop=True), "t")
                if K_STEP > 8:
                    C.wait(v, "t")
                    tci = tc_[:, tofs + nt * 512:tofs + (nt + 1) * 512]
                    tsi = ts_[:, tofs + nt * 512:tofs + (nt + 1) * 512]
                    C.inc(v.tensor_mul(rt_rb[:, :], rt_qn[:, :], tci), "v")
                if K_STEP > 9:
                    C.inc(v.tensor_copy(rt_t2[:, :], P_C[:, 0:512]), "v")
                    C.inc(v.tensor_mul(rt_qn[:, :], rt_t2[:, :], tsi), "v")
                if K_STEP > 10:
                    if K_SUB >= 2:
                        dst = (Qb[:, oc, nt * 512:(nt + 1) * 512] if isq else
                               Kb[:, oc - 4, 512 + nt * 512:512 + (nt + 1) * 512])
                        C.inc(v.tensor_add(dst, rt_rb[:, :], rt_qn[:, :]), "v")
            pa_free = C.cnt["v"]
        LAYER_STOP["qkv"] = True
        if stage == 10:
            return
        C.wait(g, "v")
        gdma(ccA_in.ap()[:, 0:2048].rearrange("p (c w) -> p c w", w=512),
             Kb[:, :, 2048:2560])
        gdma(ccA_in.ap()[:, 2048:2048 + VW],
             v_tm[:, 16:20, :, :].rearrange("p c h e -> p (c h e)"))
        C.wait(g, "g")
        C.inc(g.collective_compute("AllGather", ALU.bypass, replica_groups=PAIRS,
                                   ins=[ccA_in.ap().opt()],
                                   outs=[ccA_out.ap().opt()]), "c")
        C.wait(g, "c")
        oslc = ccA_out.ap()[bass.ds(oth_sv, 1), :, :]
        gdma(Kb[:, :, 0:512],
             oslc[:, :, 0:2048].rearrange("o p (c w) -> p (o c) w", w=512))
        gdma(v_tm[:, 0:4, :, :].rearrange("p c h e -> p (c h e)"),
             oslc[:, :, 2048:2048 + VW].rearrange("o p e -> p (o e)"))
        C.wait(v, "g")
        C.inc(v.tensor_scalar(Kb[:, :, 0:512], Kb[:, :, 0:512],
                              hzc[0:128, 0:1], None, ALU.mult), "v")
        C.inc(v.tensor_scalar(v_tm[:, 0:4, :, :], v_tm[:, 0:4, :, :],
                              hzc[0:128, 0:1], None, ALU.mult), "v")
        barrier(t, s, v)
        for hp in range(4):
            low = hp < 2
            cA, cB = 2 * hp, 2 * hp + 1
            if not low:
                cA, cB = 2 * hp - 4, 2 * hp - 3
            rA = slice(0, 64) if low else slice(64, 128)
            for w in range(NW):
                groups = [(0, [0, 1, 2]), (1, [3, 4, 5]), (2, [6, 7])]
                expA = [expb[0], expb[1]]
                expB = [expb[2], expb[3]]
                C.wait(t, "v")
                C.wait(t, "s")
                for gi, chunks in groups:
                    for i in chunks:
                        kcol = 512 * w + 128 * i
                        sA = P_A[:, (i - chunks[0]) * 512 + 0:
                                 (i - chunks[0]) * 512 + 512]
                        sB = P_B[:, (i - chunks[0]) * 512:
                                 (i - chunks[0]) * 512 + 512]
                        mtile = (maskslb[:, i * 512:(i + 1) * 512] if i < 4 else
                                 masksb[:, (i - 4) * 512:(i - 3) * 512])
                        t.matmul(sA, Kb[0:64, hp, kcol:kcol + 128],
                                 Qb[0:64, hp, w * 512:(w + 1) * 512],
                                 start=True, stop=False, tile_position=(0, 0))
                        t.matmul(sB, Kb[64:128, hp, kcol:kcol + 128],
                                 Qb[64:128, hp, w * 512:(w + 1) * 512],
                                 start=True, stop=False,
                                 tile_position=(64, 0))
                        t.matmul(sA, ident[:, :], mtile, start=False, stop=True)
                        mm = t.matmul(sB, ident[:, :], mtile, start=False, stop=True)
                    C.inc(mm, "t")
                    C.wait(s, "t")
                    ncl = len(chunks) * 512
                    for head, (S_, eb) in enumerate(((P_A, expA), (P_B, expB))):
                        ebuf = eb[gi % 2]
                        if w == 0 and chunks[0] < 4:
                            n_lb = sum(1 for i in chunks if i < 4) * 512
                            C.inc(s.activation(ebuf[:, 0:n_lb], S_[:, 0:n_lb],
                                               AF.Exp, bias=macE[0:128, 0:1],
                                               scale=8.0), "s")
                            if n_lb < ncl:
                                C.inc(s.activation(ebuf[:, n_lb:ncl],
                                                   S_[:, n_lb:ncl], AF.Exp,
                                                   bias=EBIA, scale=8.0), "s")
                        else:
                            C.inc(s.activation(ebuf[:, 0:ncl], S_[:, 0:ncl],
                                               AF.Exp, bias=EBIA, scale=8.0), "s")
                    C.wait(t, "s")
                    for i in chunks:
                        tc = 4 * w + i
                        co = (i - chunks[0]) * 512
                        st_, sp_ = (i == 0), (i == 7)
                        hA, hB = 2 * hp, 2 * hp + 1
                        if low:
                            t.matmul(P_A[0:65, 1536:2048],
                                     v_tm[:, tc, hA, 0:65],
                                     expA[gi % 2][:, co:co + 512],
                                     start=st_, stop=sp_)
                            mm = t.matmul(P_C[0:65, 0:512],
                                          v_tm[:, tc, hB, 0:65],
                                          expB[gi % 2][:, co:co + 512],
                                          start=st_, stop=sp_)
                        else:
                            t.matmul(P_A[64:128, 1536:2048],
                                     v_tm[:, tc, hA, 0:64],
                                     expA[gi % 2][:, co:co + 512],
                                     start=st_, stop=sp_, tile_position=(0, 64))
                            t.matmul(P_A[0:1, 1536:2048], onesb[:, :],
                                     expA[gi % 2][:, co:co + 512],
                                     start=st_, stop=sp_)
                            t.matmul(P_C[64:128, 0:512],
                                     v_tm[:, tc, hB, 0:64],
                                     expB[gi % 2][:, co:co + 512],
                                     start=st_, stop=sp_, tile_position=(0, 64))
                            mm = t.matmul(P_C[0:1, 0:512], onesb[:, :],
                                          expB[gi % 2][:, co:co + 512],
                                          start=st_, stop=sp_)
                    C.inc(mm, "t")
                qcols = slice(w * 512, (w + 1) * 512)
                if low:
                    attn_norm(True, P_A[0:64, 1536:2048], P_C[0:64, 0:512],
                              P_A[64:65, 1536:2048], P_C[64:65, 0:512],
                              o_fm[0:64, cA, qcols], o_fm[0:64, cB, qcols],
                              recip_sb, recbc)
                else:
                    attn_norm(False, P_A[64:128, 1536:2048], P_C[64:128, 0:512],
                              P_A[0:1, 1536:2048], P_C[0:1, 0:512],
                              o_fm[64:128, cA, qcols], o_fm[64:128, cB, qcols],
                              recip_sb, recbc)
        if stage == 11:
            return
        barrier(g)
        gdma(wos[:, :, :], wc_wo.ap()[l].rearrange("(kc p) o -> p kc o", p=128))
        barrier(t)
        for oc in range(8):
            if oc >= 1:
                C.wait(t, "v")
            for nt in range(4):
                for kc in range(4):
                    mm = t.matmul(P_A[:, nt * 512:(nt + 1) * 512],
                                  wos[:, kc, oc * 128:(oc + 1) * 128],
                                  o_fm[:, kc, nt * 512:(nt + 1) * 512],
                                  start=(kc == 0), stop=(kc == 3))
            C.inc(mm, "t")
            C.wait(v, "t")
            C.inc(v.tensor_add(Y[:, oc, 1:1 + NT], Y[:, oc, 1:1 + NT], P_A[:, :]), "v")
        if stage == 12:
            return
        cc_tok()
        barrier(t, s, v)

        def ysrc(c):
            return Y[:, c, 1:1 + NT] if c < 4 else Y[:, c, 0:NT]
        ln_full(ysrc, "h", l=l)
        if stage == 13:
            return
        barrier(t, g, s, v)
        rearr1a = wc_w1a.ap()[l].rearrange("(kc p) o -> p kc o", p=128)
        rearr1g = wc_w1g.ap()[l].rearrange("(kc p) o -> p kc o", p=128)
        rearr2 = wc_w2.ap()[l].rearrange("(kc p) o -> p kc o", p=128)
        for tt in range(4):
            tcols = slice(tt * 512, (tt + 1) * 512)
            C.wait(g, "t")
            gdma(wt1a[0][:, :, :], rearr1a[:, :, 0:128], ch="p0")
            gdma(wt1g[0][:, :, :], rearr1g[:, :, 0:128], ch="p0")
            w1mile = {0: 0, 1: 0}
            for j in range(NJ):
                bi = j % 2
                C.wait(t, "p0" if bi == 0 else "p1")
                if j >= 2:
                    C.wait(t, "v")
                    C.wait(t, "s")
                pj = (j % 2) * 1024
                for kc in range(NCH):
                    t.matmul(P_A[:, pj:pj + 512], wt1a[bi][:, kc, :],
                             Hb[:, kc, tcols], start=(kc == 0), stop=(kc == 7))
                for kc in range(NCH):
                    mm = t.matmul(P_A[:, pj + 512:pj + 1024], wt1g[bi][:, kc, :],
                                  Hb[:, kc, tcols], start=(kc == 0), stop=(kc == 7))
                C.inc(mm, "t")
                w1mile[bi] = C.cnt["t"]
                if j + 1 < NJ:
                    C.wait(g, "t", w1mile[(j + 1) % 2])
                    pch = "p0" if (j + 1) % 2 == 0 else "p1"
                    gdma(wt1a[(j + 1) % 2][:, :, :],
                         rearr1a[:, :, (j + 1) * 128:(j + 2) * 128], ch=pch)
                    gdma(wt1g[(j + 1) % 2][:, :, :],
                         rearr1g[:, :, (j + 1) * 128:(j + 2) * 128], ch=pch)
                C.wait(s, "t")
                C.inc(s.activation(tnm[:, 0:512], P_A[:, pj + 512:pj + 1024],
                                   AF.Gelu, bias=b1g_c[0:128, l * NJ + j:l * NJ + j + 1]),
                      "s")
                C.wait(v, "s")
                C.inc(v.scalar_tensor_tensor(
                    AGb[:, j, :], P_A[:, pj:pj + 512],
                    b1a_c[0:128, l * NJ + j:l * NJ + j + 1], tnm[:, 0:512],
                    ALU.add, ALU.mult), "v")
            barrier(t, g)
            gdma(wt2[0][:, :], rearr2[:, 0, :], ch="r0")
            slots = [P_A[:, i * 512:(i + 1) * 512] for i in range(4)] + \
                    [P_B[:, i * 512:(i + 1) * 512] for i in range(3)] + [P_C[:, 0:512]]
            w2mile = {0: 0, 1: 0}
            for kc in range(NJ):
                C.wait(t, "r0" if kc % 2 == 0 else "r1")
                for oc in range(8):
                    mm = t.matmul(slots[oc], wt2[kc % 2][:, oc * 128:(oc + 1) * 128],
                                  AGb[:, kc, :], start=(kc == 0), stop=(kc == NJ - 1))
                C.inc(mm, "t")
                w2mile[kc % 2] = C.cnt["t"]
                if kc + 1 < NJ:
                    C.wait(g, "t", w2mile[(kc + 1) % 2])
                    gdma(wt2[(kc + 1) % 2][:, :], rearr2[:, kc + 1, :],
                         ch="r0" if (kc + 1) % 2 == 0 else "r1")
            C.wait(v, "t")
            for oc in range(8):
                C.inc(v.scalar_tensor_tensor(
                    Y[:, oc, 1 + tt * 512:1 + (tt + 1) * 512], slots[oc],
                    b2_c[0:128, l * 8 + oc:l * 8 + oc + 1],
                    Y[:, oc, 1 + tt * 512:1 + (tt + 1) * 512],
                    ALU.add, ALU.add), "v")
            barrier(t)
        if first_of_call:
            cc_tok()

    def dense_phase():
        def xsrc(c):
            return X[:, c, :]
        ln_full(xsrc, "xlb")
        barrier(g)
        gdma(ccD_in.ap(), XLb[:, :, :].rearrange("p c t -> p (c t)"))
        C.wait(g, "g")
        C.inc(g.collective_compute("AllGather", ALU.bypass, replica_groups=PAIRS,
                                   ins=[ccD_in.ap().opt()],
                                   outs=[ccD_out.ap().opt()]), "c")
        C.wait(g, "c")
        gdma(XLo[:, :, :].rearrange("p c t -> p (c t)"),
             ccD_out.ap()[bass.ds(oth_sv, 1), :, :].rearrange("o p e -> p (o e)"))
        C.inc(v.memset(Vd[:, :, 64:65], 1.0), "v")
        C.inc(v.memset(Vd[:, :, 129:130], 1.0), "v")
        barrier(t, s, v)
        import os as _os
        K_NP = int(_os.environ.get("K_NP", "8"))
        for c in range(K_NP):
            low = c < 4
            cA, cB = (2 * c, 2 * c + 1) if low else (2 * c - 8, 2 * c - 7)
            C.wait(t, "v")
            C.wait(t, "s")
            for nt in range(4):
                C.inc(t.matmul(P_A[:, nt * 512:(nt + 1) * 512], blkq[:, :],
                               XLb[:, c, nt * 512:(nt + 1) * 512],
                               start=True, stop=True), "t")
            C.wait(s, "t")
            C.inc(s.copy(Qd[:, :], P_A[:, :]), "s")
            C.wait(t, "s")
            for half, src in ((0, XLb), (1, XLo)):
                for nt in range(4):
                    C.inc(t.matmul(P_A[:, nt * 512:(nt + 1) * 512], blkk[:, :],
                                   src[:, c, nt * 512:(nt + 1) * 512],
                                   start=True, stop=True), "t")
                C.wait(s, "t")
                C.inc(s.copy(Kd[:, half * 2048:(half + 1) * 2048], P_A[:, :]), "s")
                C.wait(t, "s")
                lastv = {0: 0, 1: 0}
                for ttl in range(16):
                    ps = P_C[:, 0:512] if (ttl % 2 == 0) else P_B[:, 0:512]
                    psv = ps[0:128, 0:128]
                    if ttl >= 2:
                        C.wait(t, "v", lastv[ttl % 2])
                    C.inc(t.matmul(psv, src[:, c, ttl * 128:(ttl + 1) * 128],
                                   blkv[:, :], start=True, stop=True), "t")
                    C.wait(v, "t")
                    C.inc(v.tensor_copy(
                        Vd[:, half * 16 + ttl, 0:130].rearrange(
                            "p (h e) -> p h e", e=65)[:, :, 0:64],
                        psv.rearrange("p (h e) -> p h e", e=64)), "v")
                    lastv[ttl % 2] = C.cnt["v"]
            barrier(t, s, v)
            groups = [(gi, list(range(3 * gi, min(3 * gi + 3, 32))))
                      for gi in range(11)]
            for qt in range(4):
                qcols = slice(qt * 512, (qt + 1) * 512)
                C.wait(t, "v")
                C.wait(t, "s")
                for gi, chunks in groups:
                    for i in chunks:
                        co = (i - chunks[0]) * 512
                        sA = P_A[:, co:co + 512]
                        sB = P_B[:, co:co + 512]
                        t.matmul(sA, Kd[0:64, i * 128:(i + 1) * 128],
                                 Qd[0:64, qcols], start=True, stop=True,
                                 tile_position=(0, 0))
                        mm = t.matmul(sB, Kd[64:128, i * 128:(i + 1) * 128],
                                      Qd[64:128, qcols], start=True, stop=True,
                                      tile_position=(64, 0))
                    C.inc(mm, "t")
                    C.wait(s, "t")
                    ncl = len(chunks) * 512
                    C.inc(s.activation(expd[0][:, 0:ncl], P_A[:, 0:ncl], AF.Exp,
                                       bias=EBIA, scale=1.0 / 32.0), "s")
                    C.inc(s.activation(expd[1][:, 0:ncl], P_B[:, 0:ncl], AF.Exp,
                                       bias=EBIA, scale=1.0 / 32.0), "s")
                    C.wait(t, "s")
                    for i in chunks:
                        co = (i - chunks[0]) * 512
                        st_, sp_ = (i == 0), (i == 31)
                        vA = Vd[:, i, 0:65]
                        vB = Vd[:, i, 65:130]
                        if low:
                            t.matmul(P_A[0:65, 1536:2048], vA,
                                     expd[0][:, co:co + 512], start=st_, stop=sp_)
                            mm = t.matmul(P_C[0:65, 0:512], vB,
                                          expd[1][:, co:co + 512], start=st_, stop=sp_)
                        else:
                            t.matmul(P_A[64:128, 1536:2048], vA[0:128, 0:64],
                                     expd[0][:, co:co + 512], start=st_, stop=sp_,
                                     tile_position=(0, 64))
                            t.matmul(P_A[0:1, 1536:2048], onesb[:, :],
                                     expd[0][:, co:co + 512], start=st_, stop=sp_)
                            t.matmul(P_C[64:128, 0:512], vB[0:128, 0:64],
                                     expd[1][:, co:co + 512], start=st_, stop=sp_,
                                     tile_position=(0, 64))
                            mm = t.matmul(P_C[0:1, 0:512], onesb[:, :],
                                          expd[1][:, co:co + 512], start=st_, stop=sp_)
                    C.inc(mm, "t")
                if low:
                    attn_norm(True, P_A[0:64, 1536:2048], P_C[0:64, 0:512],
                              P_A[64:65, 1536:2048], P_C[64:65, 0:512],
                              o_dn[0:64, cA, qcols], o_dn[0:64, cB, qcols],
                              recip_d, recbc_d)
                else:
                    attn_norm(False, P_A[64:128, 1536:2048], P_C[64:128, 0:512],
                              P_A[0:1, 1536:2048], P_C[0:1, 0:512],
                              o_dn[64:128, cA, qcols], o_dn[64:128, cB, qcols],
                              recip_d, recbc_d)
        barrier(g, t, s, v)
        gdma(wouts[:, :, :], wc_woutT.ap().rearrange("(kc p) o -> p kc o", p=128))
        barrier(t)
        for oc in range(8):
            if oc >= 1:
                C.wait(t, "v")
            for nt in range(4):
                for kc in range(8):
                    mm = t.matmul(P_A[:, nt * 512:(nt + 1) * 512],
                                  wouts[:, kc, oc * 128:(oc + 1) * 128],
                                  o_dn[:, kc, nt * 512:(nt + 1) * 512],
                                  start=(kc == 0), stop=(kc == 7))
            C.inc(mm, "t")
            C.wait(v, "t")
            C.inc(v.scalar_tensor_tensor(
                X[:, oc, :], P_A[:, :], bout_c[0:128, oc:oc + 1], X[:, oc, :],
                ALU.add, ALU.add), "v")

    if stage <= 1:
        return finish()
    if stage == 85:
        barrier(t, s, v, g, yq)
        gdma(xout[:, 0:1536], wc_qkv.ap()[0][0:128, :])
        gdma(xout[:, 2048:2048 + 1536], wc_qkv.ap()[0][512:640, :])
        gdma(xout[:, 4096:4096 + 2816], wc_w1a.ap()[0][0:128, :])
        gdma(xout[:, 8192:8192 + 1024], wc_w2.ap()[0][0:128, :])
        barrier(g, t, s, v, yq)
        return nc, dbg_outs
    for call in range(2):
        def xsrc(c):
            return X[:, c, :]
        ln_full(xsrc, "y")
        if stage <= 2:
            return finish_y()
        cc_tok()
        if stage <= 3:
            return finish_y()
        barrier(yq)
        ydma(x_spill.ap(), X[:, :, :].rearrange("p c t -> p (c t)"))
        barrier(t, s, v, g)
        for li, l in enumerate((0, 1)):
            ffn_layer(l, first_of_call=(li == 0))
            if stage == 87:
                return nc, dbg_outs
            if stage in (8, 9, 10, 11, 12, 13):
                barrier(t, s, v, g, yq)
                if stage == 10:
                    import os as _os
                    if _os.environ.get("K_NODUMP"):
                        gdma(xout.rearrange("p (c t) -> p c t", t=NT),
                             Y[:, :, 1:1 + NT])
                    else:
                        gdma(xout[:, 0:8192].rearrange("p (c t) -> p c t", t=2048),
                             Qb[:, :, :])
                        for cc_ in range(4):
                            gdma(xout[:, 8192 + cc_ * 2048:8192 + (cc_ + 1) * 2048],
                                 Kb[:, cc_, 512:2560])
                elif stage == 11:
                    gdma(xout[:, 0:8192].rearrange("p (c t) -> p c t", t=2048),
                         o_fm[:, :, :])
                elif stage in (8, 9, 12):
                    gdma(xout.rearrange("p (c t) -> p c t", t=NT), Y[:, :, 1:1 + NT])
                elif stage == 13:
                    gdma(xout, Hb[:, :, :].rearrange("p c t -> p (c t)"))
                barrier(g, t, s, v, yq)
                return nc, dbg_outs
            if stage <= 4 and li == 0:
                barrier(t, s, v, g, yq)
                gdma(xout.rearrange("p (c t) -> p c t", t=NT), Y[:, :, 1:1 + NT])
                barrier(g, t, s, v, yq)
                return nc, dbg_outs
            if stage <= 5 and li == 1:
                barrier(t, s, v, g, yq)
                gdma(xout.rearrange("p (c t) -> p c t", t=NT), Y[:, :, 1:1 + NT])
                barrier(g, t, s, v, yq)
                return nc, dbg_outs
        barrier(g, yq)
        gdma(X[:, :, :], x_spill.ap().rearrange("p (c t) -> p c t", t=NT))
        C.wait(v, "g")
        for c in range(NCH):
            C.inc(v.scalar_tensor_tensor(
                X[:, c, :], Y[:, c, 1:1 + NT], 0.5, X[:, c, :],
                ALU.mult, ALU.add), "v")
        if stage <= 6:
            return finish()
        if call == 0:
            barrier(t, s, v, g, yq)
            dense_phase()
            if stage <= 7:
                return finish()
        barrier(t, s, v, g, yq)

    return finish()


_CACHE = {}


def _host_inputs(inputs):
    x = np.asarray(inputs["L0_attn_x"])
    Wq = np.asarray(inputs["L0_attn_Wq"])
    Wk = np.asarray(inputs["L0_attn_Wk"])
    Wv = np.asarray(inputs["L0_attn_Wv"])
    W_out = np.asarray(inputs["L0_attn_W_out"])
    b_out = np.asarray(inputs["L0_attn_b_out"])
    wqkv = np.asarray(inputs["L0_ffn_wqkv"])
    q_scale = np.asarray(inputs["L0_ffn_q_scale"])
    k_scale = np.asarray(inputs["L0_ffn_k_scale"])
    wo = np.asarray(inputs["L0_ffn_wo"])
    ln_g = np.asarray(inputs["L0_ffn_ln_g"])
    ln_b = np.asarray(inputs["L0_ffn_ln_b"])
    w1 = np.asarray(inputs["L0_ffn_w1"])
    b1 = np.asarray(inputs["L0_ffn_b1"])
    w2 = np.asarray(inputs["L0_ffn_w2"])
    b2 = np.asarray(inputs["L0_ffn_b2"])

    f32 = np.float32
    com = {}
    com["wqkv"] = np.ascontiguousarray(wqkv, dtype=f32)
    com["wo_p"] = np.ascontiguousarray(wo[:, PERM_WO, :], dtype=f32)
    w1a = np.zeros((2, D, HIDP), f32)
    w1g = np.zeros((2, D, HIDP), f32)
    w1a[:, :, :HID] = w1[:, :, :HID]
    w1g[:, :, :HID] = w1[:, :, HID:]
    com["w1a"], com["w1g"] = w1a, w1g
    w2pp = np.zeros((2, HIDP, D), f32)
    w2pp[:, :HID, :] = w2
    com["w2p"] = w2pp
    com["woutT"] = np.ascontiguousarray(W_out.T[PERM_WOUT, :], dtype=f32)
    com["wq_t"] = np.ascontiguousarray(Wq.T, dtype=f32)
    com["wk_t"] = np.ascontiguousarray(Wk.T, dtype=f32)
    com["wv_t"] = np.ascontiguousarray(Wv.T, dtype=f32)
    com["b_out_r"] = np.ascontiguousarray(b_out.reshape(8, 128).T, dtype=f32)
    com["ln_g_r"] = np.ascontiguousarray(ln_g.reshape(2, 8, 128).swapaxes(1, 2), dtype=f32)
    com["ln_b_r"] = np.ascontiguousarray(ln_b.reshape(2, 8, 128).swapaxes(1, 2), dtype=f32)
    com["b2_r"] = np.ascontiguousarray(b2.reshape(2, 8, 128).swapaxes(1, 2), dtype=f32)
    b1a = np.zeros((2, 128, NJ), f32)
    b1g = np.zeros((2, 128, NJ), f32)
    for l in range(2):
        pa = np.zeros(HIDP, f32)
        pg = np.zeros(HIDP, f32)
        pa[:HID] = b1[l, :HID]
        pg[:HID] = b1[l, HID:]
        b1a[l] = pa.reshape(NJ, 128).T
        b1g[l] = pg.reshape(NJ, 128).T
    com["b1a_r"], com["b1g_r"] = b1a, b1g
    perm64 = (np.arange(64) + 32) % 64
    com["qs_r"] = np.ascontiguousarray(
        np.tile(q_scale, (1, 2)).reshape(2, 128, 1), dtype=f32)
    com["qsp_r"] = np.ascontiguousarray(
        np.tile(q_scale[:, perm64], (1, 2)).reshape(2, 128, 1), dtype=f32)
    com["ks_r"] = np.ascontiguousarray(
        np.tile(k_scale, (1, 2)).reshape(2, 128, 1), dtype=f32)
    com["ksp_r"] = np.ascontiguousarray(
        np.tile(k_scale[:, perm64], (1, 2)).reshape(2, 128, 1), dtype=f32)

    in_maps = []
    for c in range(NCORE):
        b, h = c // 2, c % 2
        t0 = h * NT
        m = dict(com)
        m["xin"] = _pack_fm(np.asarray(x[b, t0:t0 + NT, :], dtype=f32))
        tc_, ts_, tic, tis = _rot_tables(t0)
        m["tab_cos"], m["tab_sin"] = tc_, ts_
        m["tab_icos"], m["tab_isin"] = tic, tis
        m["hz"] = np.full((128, 1), 0.0 if h == 0 else 1.0, f32)
        m["maskA"] = np.full((128, 1), NEG if h == 0 else 0.0, f32)
        in_maps.append(m)
    return in_maps


def kernel(**inputs):
    import os
    stg = int(os.environ.get("K_STAGE", "99"))
    key = ("prog", stg)
    if key not in _CACHE:
        _CACHE[key] = build_program(stage=stg)[0]
    nc = _CACHE[key]
    in_maps = _host_inputs(inputs)
    res = run_bass_kernel_spmd(nc, in_maps, core_ids=list(range(NCORE)))
    out = np.empty((B, S, D), dtype=np.float32)
    for c in range(NCORE):
        b, h = c // 2, c % 2
        fm = res.results[c]["xout"]
        out[b, h * NT:(h + 1) * NT, :] = (
            fm.reshape(128, NCH, NT).transpose(1, 0, 2).reshape(D, NT).T)
    return out


# version 02
# speedup vs baseline: 1.0350x; 1.0350x over previous
import numpy as np
import ml_dtypes
import concourse.bass as bass
import concourse.mybir as mybir
from concourse.bass_utils import run_bass_kernel_spmd

F32 = mybir.dt.float32
BF16 = mybir.dt.bfloat16
AF = mybir.ActivationFunctionType
ALU = mybir.AluOpType
BF = ml_dtypes.bfloat16

NCORE = 8
B, S, D = 4, 4096, 1024
NT = 2048
NCH = 8
YW = 2056
HL, DH, WIN, NW = 8, 64, 512, 4
HID, HIDP, NJ = 2730, 2816, 22
EXP_BIAS = -40.0
NEG = -30000.0
PAIRS = [[0, 1], [2, 3], [4, 5], [6, 7]]
PERM_WO = np.array([(c + 4 * (r // 64)) * 64 + (r % 64)
                    for c in range(4) for r in range(128)])
PERM_WOUT = np.array([(c + 8 * (r // 64)) * 64 + (r % 64)
                      for c in range(8) for r in range(128)])
PERM_ROT = np.array([(d // 64) * 64 + ((d % 64) + 32) % 64 for d in range(128)])

A_X = 0
A_Y = 66048
A_YB = 0
A_WT = 33024
A_WV = 37120
A_VT = 41216
A_QB = 131840
A_KB = 148224
A_TAB = 168704
A_RAW = 187136
A_RT = 192256
A_EXP = 0
A_OFM = 12288
A_RCP = 28672
A_RBC = 32768
A_WOS = 168704
A_HB = 0
A_AGB = 131840
A_W1T = 154368
A_W2T = 162560
A_LT = 166656
A_XLB = 66048
A_XLO = 98816
A_ODN = 131840
A_QD = 164608
A_KD = 168704
A_VD = 176896
A_EXD = 185216
A_RCD = 191360
A_RBD = 195456
A_WOUT = 66048
ARENA_B = 200448


def _rot_tables(t0):
    half_idx = np.arange(0, DH, 2, dtype=np.float64)
    inv_freq = 1.0 / 10000.0 ** (half_idx / DH)
    tpos = np.arange(t0 - 512, t0 + 2048, dtype=np.float64)
    fr = tpos[:, None] * inv_freq[None, :]
    freqs = np.concatenate([fr, fr], axis=1)
    base = (half_idx + 0.4 * DH) / (1.4 * DH)
    power = (tpos - S // 2) / float(WIN // 2)
    sc = base[None, :] ** power[:, None]
    scale = np.concatenate([sc, sc], axis=1)
    cos, sin = np.cos(freqs), np.sin(freqs)
    out = []
    for arr in (cos * scale, sin * scale, cos / scale, sin / scale):
        a = arr.T.astype(np.float32)
        out.append(np.ascontiguousarray(np.concatenate([a, a], axis=0)))
    return out


def _pack_fm(xs):
    T = xs.shape[0]
    return np.ascontiguousarray(
        xs.T.reshape(NCH, 128, T).transpose(1, 0, 2).reshape(128, NCH * T))


class Ctr:
    def __init__(self, nc):
        self.sem, self.cnt = {}, {}
        for n in ("g", "y", "t", "s", "v", "c", "p0", "p1", "q0", "q1", "r0", "r1"):
            self.sem[n] = nc.alloc_semaphore("sem_" + n)
            self.cnt[n] = 0

    def inc(self, inst, name, amt=1):
        inst.then_inc(self.sem[name], amt)
        self.cnt[name] += amt
        return inst

    def wait(self, eng, name, val=None):
        vv = self.cnt[name] if val is None else val
        if vv > 0:
            eng.wait_ge(self.sem[name], vv)


def build_program(stage=99, dbg=()):
    nc = bass.Bass("TRN2", target_bir_lowering=False, debug=False,
                   detect_race_conditions=False)

    def inp(name, shape, dt=F32):
        return nc.dram_tensor(name, list(shape), dt, kind="ExternalInput").ap()

    xin = inp("xin", [128, NCH * NT])
    wqkv = inp("wqkv", [2, D, 3 * 512])
    wo_p = inp("wo_p", [2, 512, D])
    w1a = inp("w1a", [2, D, HIDP])
    w1g = inp("w1g", [2, D, HIDP])
    w2p = inp("w2p", [2, HIDP, D])
    woutT = inp("woutT", [D, D])
    wq_t = inp("wq_t", [DH, DH])
    wk_t = inp("wk_t", [DH, DH])
    wv_t = inp("wv_t", [DH, DH])
    b_out_r = inp("b_out_r", [128, 8])
    ln_g_r = inp("ln_g_r", [2, 128, 8])
    ln_b_r = inp("ln_b_r", [2, 128, 8])
    b2_r = inp("b2_r", [2, 128, 8])
    b1a_r = inp("b1a_r", [2, 128, NJ])
    b1g_r = inp("b1g_r", [2, 128, NJ])
    qs_r = inp("qs_r", [2, 128, 1])
    qsp_r = inp("qsp_r", [2, 128, 1])
    ks_r = inp("ks_r", [2, 128, 1])
    ksp_r = inp("ksp_r", [2, 128, 1])
    tab_cos = inp("tab_cos", [128, 2560])
    tab_sin = inp("tab_sin", [128, 2560])
    tab_icos = inp("tab_icos", [128, 2560])
    tab_isin = inp("tab_isin", [128, 2560])
    hz = inp("hz", [128, 1])
    maskA = inp("maskA", [128, 1])

    xout = nc.dram_tensor("xout", [128, NCH * NT], F32, kind="ExternalOutput").ap()
    dbg_outs = {}
    for name, shape, dt in dbg:
        dbg_outs[name] = nc.dram_tensor("dbg_" + name, list(shape), dt,
                                        kind="ExternalOutput").ap()

    ident_d = nc.inline_tensor(np.eye(128, dtype=BF), name="c_ident")
    onesb_d = nc.inline_tensor(np.ones((128, 1), dtype=BF), name="c_onesb")
    onesrow_d = nc.inline_tensor(np.ones((1, 128), dtype=np.float32), name="c_onesrow")
    ones64_d = nc.inline_tensor(np.ones((128, 64), dtype=np.float32), name="c_ones64")
    ind2np = np.zeros((2, 128), dtype=np.float32)
    ind2np[0, :64] = 1.0
    ind2np[1, 64:] = 1.0
    ind2b_d = nc.inline_tensor(np.ascontiguousarray(ind2np.T).astype(BF), name="c_ind2b")
    ind2f_d = nc.inline_tensor(ind2np, name="c_ind2f")
    R64 = np.zeros((64, 64), dtype=np.float64)
    R64[np.arange(32), np.arange(32) + 32] = -1.0
    R64[np.arange(32, 64), np.arange(32)] = 1.0
    R128 = np.zeros((128, 128), dtype=np.float64)
    R128[0:64, 0:64] = R64
    R128[64:128, 64:128] = R64
    rotT_d = nc.inline_tensor(np.ascontiguousarray(R128.T).astype(BF), name="c_rotT")
    mk = np.zeros((128, 4, 512), dtype=np.float64)
    kk = np.arange(128)[:, None]
    qq = np.arange(512)[None, :]
    for jb in range(4):
        mk[:, jb, :] = np.where(qq < 128 * jb + kk, NEG, 0.0)
    masks_d = nc.inline_tensor(
        np.ascontiguousarray(mk.reshape(128, 2048)).astype(BF), name="c_masks")
    mlb = np.zeros((128, 4, 512), dtype=np.float64)
    for i_ in range(4):
        mlb[:, i_, :] = np.where(qq > 128 * i_ + kk, NEG, 0.0)
    masks_lb_d = nc.inline_tensor(
        np.ascontiguousarray(mlb.reshape(128, 2048)).astype(BF), name="c_masks_lb")
    cp = np.zeros((128, 8), dtype=np.float32)
    cp[:, 0] = 1e-5
    cp[:, 1] = EXP_BIAS
    cp[:, 2] = 1e-12
    cpool_d = nc.inline_tensor(cp, name="c_cpool")

    wc_qkv = nc.dram_tensor("wc_qkv", [2, D, 3 * 512], BF16, kind="Internal")
    wc_wo = nc.dram_tensor("wc_wo", [2, 512, D], BF16, kind="Internal")
    wc_w1a = nc.dram_tensor("wc_w1a", [2, D, HIDP], BF16, kind="Internal")
    wc_w1g = nc.dram_tensor("wc_w1g", [2, D, HIDP], BF16, kind="Internal")
    wc_w2 = nc.dram_tensor("wc_w2", [2, HIDP, D], BF16, kind="Internal")
    wc_woutT = nc.dram_tensor("wc_woutT", [D, D], BF16, kind="Internal")
    x_spill = nc.dram_tensor("x_spill", [128, NCH * NT], F32, kind="Internal")
    ccT_in = nc.dram_tensor("ccT_in", [128, 8], F32, kind="Internal")
    ccT_out = nc.dram_tensor("ccT_out", [2, 128, 8], F32, kind="Internal")
    VW = 4 * HL * 65
    ccA_in = nc.dram_tensor("ccA_in", [128, 2048 + VW], BF16, kind="Internal")
    ccA_out = nc.dram_tensor("ccA_out", [2, 128, 2048 + VW], BF16, kind="Internal")
    ccD_in = nc.dram_tensor("ccD_in", [128, NCH * NT], BF16, kind="Internal")
    ccD_out = nc.dram_tensor("ccD_out", [2, 128, NCH * NT], BF16, kind="Internal")

    ident = nc.alloc_sbuf_tensor("ident", [128, 128], BF16)
    onesb = nc.alloc_sbuf_tensor("onesb", [128, 1], BF16)
    onesrowf = nc.alloc_sbuf_tensor("onesrowf", [1, 128], F32)
    ones64f = nc.alloc_sbuf_tensor("ones64f", [128, 64], F32)
    ind2b = nc.alloc_sbuf_tensor("ind2b", [128, 2], BF16)
    ind2f = nc.alloc_sbuf_tensor("ind2f", [2, 128], F32)
    rotT = nc.alloc_sbuf_tensor("rotT", [128, 128], BF16)
    masksb = nc.alloc_sbuf_tensor("masksb", [128, 2048], BF16)
    maskslb = nc.alloc_sbuf_tensor("maskslb", [128, 2048], BF16)
    cpool = nc.alloc_sbuf_tensor("cpool", [128, 8], F32)
    blkq = nc.alloc_sbuf_tensor("blkq", [128, 128], BF16)
    blkk = nc.alloc_sbuf_tensor("blkk", [128, 128], BF16)
    blkv = nc.alloc_sbuf_tensor("blkv", [128, 128], BF16)
    bout_c = nc.alloc_sbuf_tensor("bout_c", [128, 8], F32)
    lng_c = nc.alloc_sbuf_tensor("lng_c", [128, 16], F32)
    lnb_c = nc.alloc_sbuf_tensor("lnb_c", [128, 16], F32)
    b2_c = nc.alloc_sbuf_tensor("b2_c", [128, 16], F32)
    b1a_c = nc.alloc_sbuf_tensor("b1a_c", [128, 2 * NJ], F32)
    b1g_c = nc.alloc_sbuf_tensor("b1g_c", [128, 2 * NJ], F32)
    qsc = nc.alloc_sbuf_tensor("qsc", [128, 8], F32)
    hzc = nc.alloc_sbuf_tensor("hzc", [128, 1], F32)
    mac = nc.alloc_sbuf_tensor("mac", [128, 1], F32)
    macE = nc.alloc_sbuf_tensor("macE", [128, 1], F32)

    arena = nc.alloc_sbuf_tensor("arena", [128, ARENA_B], mybir.dt.uint8)
    ab = nc.lookup_mloc(arena).addr
    rem = nc.sbuf_bytes_remaining
    assert rem >= 0, rem
    _n = [0]

    def place(off, shape, dtype):
        per = int(np.prod(shape[1:])) * mybir.dt.size(dtype)
        assert off % 32 == 0 and off + per <= ARENA_B, (off, per)
        _n[0] += 1
        h = nc.alloc_sbuf_tensor_at(f"ar{_n[0]}", list(shape), dtype,
                                    offset=ab + off)
        if not hasattr(nc, "_arena_map"):
            nc._arena_map = {}
        nc._arena_map.setdefault(off, []).append(h.name)
        return h

    X = place(A_X, [128, NCH, NT], F32)
    Y = place(A_Y, [128, NCH, YW], F32)
    Yb = place(A_YB, [128, NCH, YW], BF16)
    wtq = [place(A_WT + i * 2048, [128, 8, 128], BF16) for i in range(2)]
    wv_sb = place(A_WV, [128, 8, 256], BF16)
    v_tm = place(A_VT, [128, 20, HL, 65], BF16)
    Qb = place(A_QB, [128, 4, 2048], BF16)
    Kb = place(A_KB, [128, 4, 2560], BF16)
    tqc = place(A_TAB, [128, 2048], BF16)
    tqs = place(A_TAB + 4096, [128, 2048], BF16)
    tkc = place(A_TAB + 8192, [128, 2560], BF16)
    tks = place(A_TAB + 13312, [128, 2560], BF16)
    rawt = place(A_RAW, [128, 2560], BF16)
    rt_q2 = place(A_RT, [128, 512], BF16)
    rt_rb = place(A_RT + 1024, [128, 512], F32)
    rt_qn = place(A_RT + 3072, [128, 512], F32)
    rt_qnb = place(A_RT + 5120, [128, 512], BF16)
    rt_t2 = place(A_RT + 6144, [128, 512], F32)
    rt_ss = place(A_RAW, [2, 512], F32)
    expb = [place(A_EXP + i * 3072, [128, 1536], BF16) for i in range(4)]
    o_fm = place(A_OFM, [128, 4, 2048], BF16)
    recip_sb = place(A_RCP, [128, 1024], F32)
    recbc = place(A_RBC, [128, 1024], F32)
    wos = place(A_WOS, [128, 4, 1024], BF16)
    Hb = place(A_HB, [128, NCH, NT], BF16)
    AGb = place(A_AGB, [128, NJ, 512], BF16)
    wt1a = [place(A_W1T + i * 2048, [128, 8, 128], BF16) for i in range(2)]
    wt1g = [place(A_W1T + 4096 + i * 2048, [128, 8, 128], BF16) for i in range(2)]
    wt2 = [place(A_W2T + i * 2048, [128, 1024], BF16) for i in range(2)]
    tXb = place(A_LT, [128, NT], BF16)
    tSQb = place(A_LT + 4096, [128, NT], BF16)
    tnm = place(A_LT + 8192, [128, NT], F32)
    trs = place(A_LT + 16384, [128, NT], F32)
    r_m = place(A_LT + 24576, [1, 512], F32)
    r_q = place(A_LT + 26624, [1, 512], F32)
    r_v = place(A_LT + 28672, [1, 512], F32)
    r_s = place(A_LT + 30720, [1, 512], F32)
    XLb = place(A_XLB, [128, NCH, NT], BF16)
    XLo = place(A_XLO, [128, NCH, NT], BF16)
    o_dn = place(A_ODN, [128, NCH, NT], BF16)
    Qd = place(A_QD, [128, 2048], BF16)
    Kd = place(A_KD, [128, 4096], BF16)
    Vd = place(A_VD, [128, 32, 130], BF16)
    expd = [place(A_EXD + i * 3072, [128, 1536], BF16) for i in range(2)]
    recip_d = place(A_RCD, [128, 1024], F32)
    recbc_d = place(A_RBD, [128, 1024], F32)
    wouts = place(A_WOUT, [128, 8, 1024], BF16)

    P_A = nc.alloc_psum_tensor("P_A", [128, 2048], F32)
    P_B = nc.alloc_psum_tensor("P_B", [128, 1536], F32)
    P_C = nc.alloc_psum_tensor("P_C", [128, 512], F32)

    C = Ctr(nc)
    g, t, s, v, yq = nc.gpsimd, nc.tensor, nc.scalar, nc.vector, nc.sync

    def gdma(out, in_, slow=False, ch="g"):
        return C.inc(g.dma_start(out=out, in_=in_,
                                 allow_slow_non_contiguous=slow), ch, 16)

    def ydma(out, in_, slow=False, ch="y"):
        return C.inc(yq.dma_start(out=out, in_=in_,
                                  allow_slow_non_contiguous=slow), ch, 16)

    def barrier(*engs):
        for e in engs:
            for nm in ("g", "y", "t", "s", "v", "c", "p0", "p1", "q0", "q1", "r0", "r1"):
                C.wait(e, nm)

    EPS1 = cpool[0:1, 0:1]
    EBIA = cpool[0:128, 1:2]
    EPS12 = cpool[0:2, 2:3]

    def finish_y():
        barrier(t, s, v, g, yq)
        ydma(xout.rearrange("p (c t) -> p c t", t=NT), Y[:, :, 1:1 + NT])
        barrier(g, t, s, v, yq)
        return nc, dbg_outs

    def finish(ap=None, cast=False):
        barrier(t, s, v, g, yq)
        if ap is None:
            ap = X[:, :, :].rearrange("p c t -> p (c t)")
        if cast:
            gdma(xout, ap)
        else:
            ydma(xout, ap)
        barrier(g, t, s, v, yq)
        return nc, dbg_outs

    pid = g.partition_id()
    oth_reg = g.alloc_register("oth")
    g.reg_alu(oth_reg, pid, 2, ALU.mod)
    g.reg_alu(oth_reg, oth_reg, 1, ALU.bitwise_xor)
    oth_sv = g.snap(oth_reg, min_val=0, max_val=1)

    gdma(X[:, :, :], xin.rearrange("p (c t) -> p c t", t=NT))
    for dst, src in [(ident, ident_d), (onesb, onesb_d), (onesrowf, onesrow_d),
                     (ones64f, ones64_d), (ind2b, ind2b_d), (ind2f, ind2f_d),
                     (rotT, rotT_d), (masksb, masks_d), (maskslb, masks_lb_d), (cpool, cpool_d)]:
        gdma(dst[:, :], src.ap())
    gdma(bout_c[:, :], b_out_r)
    gdma(hzc[:, :], hz)
    gdma(mac[:, :], maskA)
    for l in range(2):
        gdma(lng_c[:, l * 8:(l + 1) * 8], ln_g_r[l])
        gdma(lnb_c[:, l * 8:(l + 1) * 8], ln_b_r[l])
        gdma(b2_c[:, l * 8:(l + 1) * 8], b2_r[l])
        gdma(b1a_c[:, l * NJ:(l + 1) * NJ], b1a_r[l])
        gdma(b1g_c[:, l * NJ:(l + 1) * NJ], b1g_r[l])
        gdma(qsc[:, l:l + 1], qs_r[l])
        gdma(qsc[:, 2 + l:3 + l], qsp_r[l])
        gdma(qsc[:, 4 + l:5 + l], ks_r[l])
        gdma(qsc[:, 6 + l:7 + l], ksp_r[l])
    C.inc(v.memset(blkq[:, :], 0.0), "v")
    C.inc(v.memset(blkk[:, :], 0.0), "v")
    C.inc(v.memset(blkv[:, :], 0.0), "v")
    C.wait(g, "v")
    for blk, w in [(blkq, wq_t), (blkk, wk_t), (blkv, wv_t)]:
        gdma(blk[0:64, 0:64], w)
        gdma(blk[64:128, 64:128], w)
    C.wait(v, "g")
    C.inc(v.tensor_scalar_add(macE[:, :], mac[:, :], EXP_BIAS), "v")
    stg = [place(A_QB, [128, HIDP], BF16), place(A_QB + 5664, [128, HIDP], BF16)]
    wjobs = []
    for l in range(2):
        wjobs += [(wqkv[l], wc_qkv.ap()[l], D, 1536),
                  (wo_p[l], wc_wo.ap()[l], 512, D),
                  (w1a[l], wc_w1a.ap()[l], D, HIDP),
                  (w1g[l], wc_w1g.ap()[l], D, HIDP),
                  (w2p[l], wc_w2.ap()[l], HIDP, D)]
    wjobs.append((woutT, wc_woutT.ap(), D, D))
    ci = 0
    for src, dst, rows, cols in wjobs:
        for r0 in range(0, rows, 128):
            bi = ci % 2
            pch, qch = ("p0", "q0") if bi == 0 else ("p1", "q1")
            if ci >= 2:
                C.wait(g, qch)
            gdma(stg[bi][:, 0:cols], src[r0:r0 + 128, :], ch=pch)
            C.wait(yq, pch)
            ydma(dst[r0:r0 + 128, :], stg[bi][:, 0:cols], ch=qch)
            ci += 1
    barrier(t, s, v, g, yq)

    def ln_full(src_fn, mode, l=None):
        for c in range(NCH):
            C.inc(s.copy(tXb[:, :], src_fn(c)), "s")
            C.inc(s.activation(tSQb[:, :], src_fn(c), AF.Square), "s")
            C.wait(t, "s")
            for nt in range(4):
                t.matmul(P_A[0:1, nt * 512:(nt + 1) * 512], onesb[:, :],
                         tXb[:, nt * 512:(nt + 1) * 512],
                         start=(c == 0), stop=(c == NCH - 1))
                dst = (P_B[0:1, nt * 512:(nt + 1) * 512] if nt < 3
                       else P_C[0:1, 0:512])
                mm = t.matmul(dst, onesb[:, :], tSQb[:, nt * 512:(nt + 1) * 512],
                              start=(c == 0), stop=(c == NCH - 1))
            C.inc(mm, "t")
            C.wait(s, "t")
        for nt in range(4):
            mean_ap = P_A[0:1, nt * 512:(nt + 1) * 512]
            sq_ap = (P_B[0:1, nt * 512:(nt + 1) * 512] if nt < 3 else P_C[0:1, 0:512])
            C.wait(v, "t")
            C.inc(v.tensor_scalar(r_m[:, :], mean_ap, -1.0 / 1024.0, None, ALU.mult), "v")
            C.inc(v.tensor_scalar(r_q[:, :], sq_ap, 1.0 / 1024.0, None, ALU.mult), "v")
            C.inc(v.tensor_tensor(r_v[:, :], r_m[:, :], r_m[:, :], ALU.mult), "v")
            C.inc(v.tensor_sub(r_q[:, :], r_q[:, :], r_v[:, :]), "v")
            C.wait(s, "v")
            C.inc(s.activation(r_s[:, :], r_q[:, :], AF.Ln, bias=EPS1), "s")
            C.inc(s.activation(r_s[:, :], r_s[:, :], AF.Exp, scale=-0.5), "s")
            C.wait(t, "s")
            C.inc(t.matmul(P_A[:, nt * 512:(nt + 1) * 512], onesrowf[0:1, :],
                           r_m[0:1, :], start=True, stop=True), "t")
            C.wait(v, "t")
            C.inc(v.tensor_copy(tnm[:, nt * 512:(nt + 1) * 512],
                                P_A[:, nt * 512:(nt + 1) * 512]), "v")
            C.wait(t, "v")
            C.inc(t.matmul(P_A[:, nt * 512:(nt + 1) * 512], onesrowf[0:1, :],
                           r_s[0:1, :], start=True, stop=True), "t")
            C.wait(v, "t")
            C.inc(v.tensor_copy(trs[:, nt * 512:(nt + 1) * 512],
                                P_A[:, nt * 512:(nt + 1) * 512]), "v")
        for c in range(NCH):
            C.inc(v.tensor_add(P_A[:, :], src_fn(c), tnm[:, :]), "v")
            if mode == "y":
                C.inc(v.tensor_mul(Y[:, c, 1:1 + NT], P_A[:, :], trs[:, :]), "v")
            elif mode == "xlb":
                C.inc(v.tensor_mul(XLb[:, c, :], P_A[:, :], trs[:, :]), "v")
            else:
                C.inc(v.tensor_mul(P_A[:, :], P_A[:, :], trs[:, :]), "v")
                C.inc(v.tensor_scalar(Hb[:, c, :], P_A[:, :],
                                      lng_c[0:128, l * 8 + c:l * 8 + c + 1],
                                      lnb_c[0:128, l * 8 + c:l * 8 + c + 1],
                                      ALU.mult, ALU.add), "v")

    def cc_tok():
        C.wait(g, "v")
        C.wait(g, "t")
        C.wait(g, "s")
        gdma(ccT_in.ap(), Y[:, :, 2048:2049].rearrange("p c o -> p (c o)"), slow=True)
        C.wait(g, "g")
        C.inc(g.collective_compute("AllGather", ALU.bypass, replica_groups=PAIRS,
                                   ins=[ccT_in.ap().opt()],
                                   outs=[ccT_out.ap().opt()]), "c")
        C.wait(g, "c")
        gdma(Y[:, :, 0:1].rearrange("p c o -> p (c o)"),
             ccT_out.ap()[bass.ds(oth_sv, 1), :, :].rearrange("o p c -> p (o c)"),
             slow=True)
        C.wait(v, "g")
        C.inc(v.tensor_scalar(Y[:, :, 0:1], Y[:, :, 0:1], hzc[0:128, 0:1], None,
                              ALU.mult), "v")

    def attn_norm(low, oA_ap, oB_ap, sumA_ap, sumB_ap, dstA, dstB, rcp, rbc):
        rr = 64 if low else 0
        C.wait(s, "t")
        C.inc(s.activation(rcp[rr:rr + 1, 512:1024], sumA_ap, AF.Ln), "s")
        C.inc(s.activation(rcp[rr:rr + 1, 0:512], rcp[rr:rr + 1, 512:1024],
                           AF.Exp, scale=-1.0), "s")
        C.inc(s.activation(rbc[rr:rr + 1, 512:1024], sumB_ap, AF.Ln), "s")
        C.inc(s.activation(rbc[rr:rr + 1, 0:512], rbc[rr:rr + 1, 512:1024],
                           AF.Exp, scale=-1.0), "s")
        C.wait(t, "s")
        if low:
            t.matmul(P_B[0:64, 0:512], ones64f[64:65, :], rcp[64:65, 0:512],
                     start=True, stop=True)
            mm2 = t.matmul(P_B[0:64, 512:1024], ones64f[64:65, :],
                           rbc[64:65, 0:512], start=True, stop=True)
        else:
            t.matmul(P_B[64:128, 0:512], ones64f[0:1, :], rcp[0:1, 0:512],
                     start=True, stop=True, tile_position=(0, 64))
            mm2 = t.matmul(P_B[64:128, 512:1024], ones64f[0:1, :],
                           rbc[0:1, 0:512], start=True, stop=True,
                           tile_position=(0, 64))
        C.inc(mm2, "t")
        C.wait(v, "t")
        if low:
            C.inc(v.tensor_copy(rcp[0:64, 0:512], P_B[0:64, 0:512]), "v")
            C.inc(v.tensor_copy(rbc[0:64, 0:512], P_B[0:64, 512:1024]), "v")
            C.inc(v.tensor_mul(dstA, oA_ap, rcp[0:64, 0:512]), "v")
            C.inc(v.tensor_mul(dstB, oB_ap, rbc[0:64, 0:512]), "v")
        else:
            C.inc(v.tensor_copy(rcp[64:128, 0:512], P_B[64:128, 0:512]), "v")
            C.inc(v.tensor_copy(rbc[64:128, 0:512], P_B[64:128, 512:1024]), "v")
            C.inc(v.tensor_mul(dstA, oA_ap, rcp[64:128, 0:512]), "v")
            C.inc(v.tensor_mul(dstB, oB_ap, rbc[64:128, 0:512]), "v")

    LAYER_STOP = {}

    def ffn_layer(l, first_of_call):
        barrier(s)
        if first_of_call:
            C.wait(v, "y")
            C.inc(v.memset(v_tm[:, :, :, 64:65], 1.0), "v")
        for c in range(NCH):
            C.inc(s.copy(Yb[:, c, 0:2049], Y[:, c, 0:2049]), "s")
        for rawsrc, dstt, qcol, ncols in (
                (tab_cos, tqc, 0 + l, 2048), (tab_sin, tqs, 2 + l, 2048),
                (tab_icos, tkc, 4 + l, 2560), (tab_isin, tks, 6 + l, 2560)):
            C.wait(g, "v")
            gdma(rawt[:, :], rawsrc, ch="p0")
            C.wait(v, "p0")
            C.inc(v.tensor_scalar(dstt[:, 0:ncols], rawt[:, 2560 - ncols:2560],
                                  qsc[0:128, qcol:qcol + 1], None, ALU.mult), "v")
        if stage == 87:
            barrier(g, t, s, v, yq)
            gdma(xout[:, 0:2048], tqc[:, :])
            gdma(xout[:, 2048:4096], tqs[:, :])
            gdma(xout[:, 4096:6656], tkc[:, :])
            gdma(xout[:, 6656:9216], tks[:, :])
            barrier(g, t, s, v, yq)
            return
        if stage == 8:
            return
        barrier(t, g)
        for half in range(2):
            C.wait(g, "t")
            gdma(wv_sb[:, :, :],
                 wc_qkv.ap()[l].rearrange("(kc p) o -> p kc o", p=128)
                 [:, :, 1024 + half * 256:1024 + (half + 1) * 256], ch="p1")
            C.wait(t, "p1")
            lastv = {0: 0, 1: 0}
            for tt in range(16):
                ps = P_C[:, 0:512] if (tt % 2 == 0) else P_B[:, 0:512]
                psv = ps[0:128, 0:256]
                if tt >= 2:
                    C.wait(t, "v", lastv[tt % 2])
                for kc in range(NCH):
                    sh = 1 if kc < 4 else 0
                    mm = t.matmul(psv, Yb[:, kc, sh + tt * 128:sh + (tt + 1) * 128],
                                  wv_sb[:, kc, :], start=(kc == 0), stop=(kc == NCH - 1))
                C.inc(mm, "t")
                C.wait(v, "t")
                C.inc(v.tensor_copy(
                    v_tm[:, 4 + tt, half * 4:(half + 1) * 4, 0:64],
                    psv.rearrange("p (h e) -> p h e", e=64)), "v")
                lastv[tt % 2] = C.cnt["v"]
        if stage == 9:
            return
        import os as _os
        K_SUB = int(_os.environ.get("K_SUB", "99"))
        K_OC = int(_os.environ.get("K_OC", "8"))
        barrier(t, g)
        gdma(wtq[0][:, :, :],
             wc_qkv.ap()[l].rearrange("(kc p) o -> p kc o", p=128)[:, :, 0:128],
             ch="p0")
        pa_free = C.cnt["v"]
        tmile = {0: 0, 1: 0}
        for oc in range(K_OC):
            C.wait(t, "p0" if oc % 2 == 0 else "p1")
            C.wait(t, "v", pa_free)
            for nt in range(4):
                for kc in range(NCH):
                    sh = 1 if kc < 4 else 0
                    mm = t.matmul(P_A[:, nt * 512:(nt + 1) * 512],
                                  wtq[oc % 2][:, kc, :],
                                  Yb[:, kc, sh + nt * 512:sh + (nt + 1) * 512],
                                  start=(kc == 0), stop=(kc == NCH - 1))
            C.inc(mm, "t")
            tmile[oc % 2] = C.cnt["t"]
            if oc + 1 < K_OC:
                C.wait(g, "t", tmile[(oc + 1) % 2])
                gdma(wtq[(oc + 1) % 2][:, :, :],
                     wc_qkv.ap()[l].rearrange("(kc p) o -> p kc o", p=128)
                     [:, :, (oc + 1) * 128:(oc + 2) * 128],
                     ch="p0" if (oc + 1) % 2 == 0 else "p1")
            isq = oc < 4
            tc_, ts_ = (tqc, tqs) if isq else (tkc, tks)
            tofs = 0 if isq else 512
            if K_SUB == 0:
                C.wait(v, "t")
                for nt in range(4):
                    dst = (Qb[:, oc, nt * 512:(nt + 1) * 512] if isq else
                           Kb[:, oc - 4, 512 + nt * 512:512 + (nt + 1) * 512])
                    C.inc(v.tensor_copy(dst, P_A[:, nt * 512:(nt + 1) * 512]), "v")
                pa_free = C.cnt["v"]
                continue
            K_STEP = int(_os.environ.get("K_STEP", "99"))
            for nt in range(4):
                C.wait(s, "t")
                C.inc(s.activation(rt_q2[:, :], P_A[:, nt * 512:(nt + 1) * 512],
                                   AF.Square), "s")
                if K_STEP > 1:
                    C.wait(t, "s")
                    C.wait(t, "v")
                    C.inc(t.matmul(P_C[0:2, 0:512], ind2b[:, :], rt_q2[:, :],
                                   start=True, stop=True), "t")
                if K_STEP > 2:
                    C.wait(s, "t")
                    C.inc(s.activation(rt_ss[:, :], P_C[0:2, 0:512], AF.Ln,
                                       bias=EPS12), "s")
                    C.inc(s.activation(rt_ss[:, :], rt_ss[:, :], AF.Exp,
                                       scale=-0.5), "s")
                if K_STEP > 4:
                    C.wait(t, "s")
                    C.inc(t.matmul(P_C[:, 0:512], ind2f[:, :], rt_ss[0:2, :],
                                   start=True, stop=True), "t")
                if K_STEP > 5:
                    C.wait(v, "t")
                    C.inc(v.tensor_copy(rt_rb[:, :], P_C[:, 0:512]), "v")
                    C.inc(v.tensor_mul(rt_qn[:, :], P_A[:, nt * 512:(nt + 1) * 512],
                                       rt_rb[:, :]), "v")
                if K_STEP > 6:
                    C.wait(s, "v")
                    C.inc(s.copy(rt_qnb[:, :], rt_qn[:, :]), "s")
                if K_STEP > 7:
                    C.wait(t, "s")
                    C.inc(t.matmul(P_C[:, 0:512], rotT[:, :], rt_qnb[:, :],
                                   start=True, stop=True), "t")
                if K_STEP > 8:
                    C.wait(v, "t")
                    tci = tc_[:, tofs + nt * 512:tofs + (nt + 1) * 512]
                    tsi = ts_[:, tofs + nt * 512:tofs + (nt + 1) * 512]
                    C.inc(v.tensor_mul(rt_rb[:, :], rt_qn[:, :], tci), "v")
                if K_STEP > 9:
                    C.inc(v.tensor_copy(rt_t2[:, :], P_C[:, 0:512]), "v")
                    C.inc(v.tensor_mul(rt_qn[:, :], rt_t2[:, :], tsi), "v")
                if K_STEP > 10:
                    if K_SUB >= 2:
                        dst = (Qb[:, oc, nt * 512:(nt + 1) * 512] if isq else
                               Kb[:, oc - 4, 512 + nt * 512:512 + (nt + 1) * 512])
                        C.inc(v.tensor_add(dst, rt_rb[:, :], rt_qn[:, :]), "v")
            pa_free = C.cnt["v"]
        LAYER_STOP["qkv"] = True
        if stage == 10:
            return
        C.wait(g, "v")
        gdma(ccA_in.ap()[:, 0:2048].rearrange("p (c w) -> p c w", w=512),
             Kb[:, :, 2048:2560])
        gdma(ccA_in.ap()[:, 2048:2048 + VW],
             v_tm[:, 16:20, :, :].rearrange("p c h e -> p (c h e)"))
        C.wait(g, "g")
        C.inc(g.collective_compute("AllGather", ALU.bypass, replica_groups=PAIRS,
                                   ins=[ccA_in.ap().opt()],
                                   outs=[ccA_out.ap().opt()]), "c")
        C.wait(g, "c")
        oslc = ccA_out.ap()[bass.ds(oth_sv, 1), :, :]
        gdma(Kb[:, :, 0:512],
             oslc[:, :, 0:2048].rearrange("o p (c w) -> p (o c) w", w=512))
        gdma(v_tm[:, 0:4, :, :].rearrange("p c h e -> p (c h e)"),
             oslc[:, :, 2048:2048 + VW].rearrange("o p e -> p (o e)"))
        C.wait(v, "g")
        C.inc(v.tensor_scalar(Kb[:, :, 0:512], Kb[:, :, 0:512],
                              hzc[0:128, 0:1], None, ALU.mult), "v")
        C.inc(v.tensor_scalar(v_tm[:, 0:4, :, :], v_tm[:, 0:4, :, :],
                              hzc[0:128, 0:1], None, ALU.mult), "v")
        barrier(t, s, v)
        for hp in range(4):
            low = hp < 2
            cA, cB = 2 * hp, 2 * hp + 1
            if not low:
                cA, cB = 2 * hp - 4, 2 * hp - 3
            rA = slice(0, 64) if low else slice(64, 128)
            for w in range(NW):
                groups = [(0, [0, 1, 2]), (1, [3, 4, 5]), (2, [6, 7])]
                expA = [expb[0], expb[1]]
                expB = [expb[2], expb[3]]
                C.wait(t, "v")
                C.wait(t, "s")
                for gi, chunks in groups:
                    for i in chunks:
                        kcol = 512 * w + 128 * i
                        sA = P_A[:, (i - chunks[0]) * 512 + 0:
                                 (i - chunks[0]) * 512 + 512]
                        sB = P_B[:, (i - chunks[0]) * 512:
                                 (i - chunks[0]) * 512 + 512]
                        mtile = (maskslb[:, i * 512:(i + 1) * 512] if i < 4 else
                                 masksb[:, (i - 4) * 512:(i - 3) * 512])
                        t.matmul(sA, Kb[0:64, hp, kcol:kcol + 128],
                                 Qb[0:64, hp, w * 512:(w + 1) * 512],
                                 start=True, stop=False, tile_position=(0, 0))
                        t.matmul(sB, Kb[64:128, hp, kcol:kcol + 128],
                                 Qb[64:128, hp, w * 512:(w + 1) * 512],
                                 start=True, stop=False,
                                 tile_position=(64, 0))
                        t.matmul(sA, ident[:, :], mtile, start=False, stop=True)
                        mm = t.matmul(sB, ident[:, :], mtile, start=False, stop=True)
                    C.inc(mm, "t")
                    C.wait(s, "t")
                    ncl = len(chunks) * 512
                    for head, (S_, eb) in enumerate(((P_A, expA), (P_B, expB))):
                        ebuf = eb[gi % 2]
                        if w == 0 and chunks[0] < 4:
                            n_lb = sum(1 for i in chunks if i < 4) * 512
                            C.inc(s.activation(ebuf[:, 0:n_lb], S_[:, 0:n_lb],
                                               AF.Exp, bias=macE[0:128, 0:1],
                                               scale=8.0), "s")
                            if n_lb < ncl:
                                C.inc(s.activation(ebuf[:, n_lb:ncl],
                                                   S_[:, n_lb:ncl], AF.Exp,
                                                   bias=EBIA, scale=8.0), "s")
                        else:
                            C.inc(s.activation(ebuf[:, 0:ncl], S_[:, 0:ncl],
                                               AF.Exp, bias=EBIA, scale=8.0), "s")
                    C.wait(t, "s")
                    for i in chunks:
                        tc = 4 * w + i
                        co = (i - chunks[0]) * 512
                        st_, sp_ = (i == 0), (i == 7)
                        hA, hB = 2 * hp, 2 * hp + 1
                        if low:
                            t.matmul(P_A[0:65, 1536:2048],
                                     v_tm[:, tc, hA, 0:65],
                                     expA[gi % 2][:, co:co + 512],
                                     start=st_, stop=sp_)
                            mm = t.matmul(P_C[0:65, 0:512],
                                          v_tm[:, tc, hB, 0:65],
                                          expB[gi % 2][:, co:co + 512],
                                          start=st_, stop=sp_)
                        else:
                            t.matmul(P_A[64:128, 1536:2048],
                                     v_tm[:, tc, hA, 0:64],
                                     expA[gi % 2][:, co:co + 512],
                                     start=st_, stop=sp_, tile_position=(0, 64))
                            t.matmul(P_A[0:1, 1536:2048], onesb[:, :],
                                     expA[gi % 2][:, co:co + 512],
                                     start=st_, stop=sp_)
                            t.matmul(P_C[64:128, 0:512],
                                     v_tm[:, tc, hB, 0:64],
                                     expB[gi % 2][:, co:co + 512],
                                     start=st_, stop=sp_, tile_position=(0, 64))
                            mm = t.matmul(P_C[0:1, 0:512], onesb[:, :],
                                          expB[gi % 2][:, co:co + 512],
                                          start=st_, stop=sp_)
                    C.inc(mm, "t")
                qcols = slice(w * 512, (w + 1) * 512)
                if low:
                    attn_norm(True, P_A[0:64, 1536:2048], P_C[0:64, 0:512],
                              P_A[64:65, 1536:2048], P_C[64:65, 0:512],
                              o_fm[0:64, cA, qcols], o_fm[0:64, cB, qcols],
                              recip_sb, recbc)
                else:
                    attn_norm(False, P_A[64:128, 1536:2048], P_C[64:128, 0:512],
                              P_A[0:1, 1536:2048], P_C[0:1, 0:512],
                              o_fm[64:128, cA, qcols], o_fm[64:128, cB, qcols],
                              recip_sb, recbc)
        if stage == 11:
            return
        barrier(g)
        rearr_wo = wc_wo.ap()[l].rearrange("(kc p) o -> p kc o", p=128)
        C.wait(yq, "t")
        gdma(wos[:, 0:2, :], rearr_wo[:, 0:2, :])
        ydma(wos[:, 2:4, :], rearr_wo[:, 2:4, :], ch="q0")
        barrier(t)
        for oc in range(8):
            if oc >= 1:
                C.wait(t, "v")
            for nt in range(4):
                for kc in range(4):
                    mm = t.matmul(P_A[:, nt * 512:(nt + 1) * 512],
                                  wos[:, kc, oc * 128:(oc + 1) * 128],
                                  o_fm[:, kc, nt * 512:(nt + 1) * 512],
                                  start=(kc == 0), stop=(kc == 3))
            C.inc(mm, "t")
            C.wait(v, "t")
            C.inc(v.tensor_add(Y[:, oc, 1:1 + NT], Y[:, oc, 1:1 + NT], P_A[:, :]), "v")
        if stage == 12:
            return
        cc_tok()
        barrier(t, s, v)

        def ysrc(c):
            return Y[:, c, 1:1 + NT] if c < 4 else Y[:, c, 0:NT]
        ln_full(ysrc, "h", l=l)
        if stage == 13:
            return
        barrier(t, g, s, v)
        rearr1a = wc_w1a.ap()[l].rearrange("(kc p) o -> p kc o", p=128)
        rearr1g = wc_w1g.ap()[l].rearrange("(kc p) o -> p kc o", p=128)
        rearr2 = wc_w2.ap()[l].rearrange("(kc p) o -> p kc o", p=128)
        for tt in range(4):
            tcols = slice(tt * 512, (tt + 1) * 512)
            C.wait(g, "t")
            C.wait(yq, "t")
            gdma(wt1a[0][:, :, :], rearr1a[:, :, 0:128], ch="p0")
            ydma(wt1g[0][:, :, :], rearr1g[:, :, 0:128], ch="q0")
            w1mile = {0: 0, 1: 0}
            for j in range(NJ):
                bi = j % 2
                C.wait(t, "p0" if bi == 0 else "p1")
                C.wait(t, "q0" if bi == 0 else "q1")
                if j >= 2:
                    C.wait(t, "v")
                    C.wait(t, "s")
                pj = (j % 2) * 1024
                for kc in range(NCH):
                    t.matmul(P_A[:, pj:pj + 512], wt1a[bi][:, kc, :],
                             Hb[:, kc, tcols], start=(kc == 0), stop=(kc == 7))
                for kc in range(NCH):
                    mm = t.matmul(P_A[:, pj + 512:pj + 1024], wt1g[bi][:, kc, :],
                                  Hb[:, kc, tcols], start=(kc == 0), stop=(kc == 7))
                C.inc(mm, "t")
                w1mile[bi] = C.cnt["t"]
                if j + 1 < NJ:
                    C.wait(g, "t", w1mile[(j + 1) % 2])
                    C.wait(yq, "t", w1mile[(j + 1) % 2])
                    pch = "p0" if (j + 1) % 2 == 0 else "p1"
                    qch = "q0" if (j + 1) % 2 == 0 else "q1"
                    gdma(wt1a[(j + 1) % 2][:, :, :],
                         rearr1a[:, :, (j + 1) * 128:(j + 2) * 128], ch=pch)
                    ydma(wt1g[(j + 1) % 2][:, :, :],
                         rearr1g[:, :, (j + 1) * 128:(j + 2) * 128], ch=qch)
                C.wait(s, "t")
                C.inc(s.activation(tnm[:, 0:512], P_A[:, pj + 512:pj + 1024],
                                   AF.Gelu, bias=b1g_c[0:128, l * NJ + j:l * NJ + j + 1]),
                      "s")
                C.wait(v, "s")
                C.inc(v.scalar_tensor_tensor(
                    AGb[:, j, :], P_A[:, pj:pj + 512],
                    b1a_c[0:128, l * NJ + j:l * NJ + j + 1], tnm[:, 0:512],
                    ALU.add, ALU.mult), "v")
            barrier(t, g)
            C.wait(yq, "t")
            gdma(wt2[0][:, 0:512], rearr2[:, 0, 0:512], ch="r0")
            ydma(wt2[0][:, 512:1024], rearr2[:, 0, 512:1024], ch="q0")
            slots = [P_A[:, i * 512:(i + 1) * 512] for i in range(4)] + \
                    [P_B[:, i * 512:(i + 1) * 512] for i in range(3)] + [P_C[:, 0:512]]
            w2mile = {0: 0, 1: 0}
            for kc in range(NJ):
                C.wait(t, "r0" if kc % 2 == 0 else "r1")
                C.wait(t, "q0" if kc % 2 == 0 else "q1")
                for oc in range(8):
                    mm = t.matmul(slots[oc], wt2[kc % 2][:, oc * 128:(oc + 1) * 128],
                                  AGb[:, kc, :], start=(kc == 0), stop=(kc == NJ - 1))
                C.inc(mm, "t")
                w2mile[kc % 2] = C.cnt["t"]
                if kc + 1 < NJ:
                    C.wait(g, "t", w2mile[(kc + 1) % 2])
                    C.wait(yq, "t", w2mile[(kc + 1) % 2])
                    gdma(wt2[(kc + 1) % 2][:, 0:512], rearr2[:, kc + 1, 0:512],
                         ch="r0" if (kc + 1) % 2 == 0 else "r1")
                    ydma(wt2[(kc + 1) % 2][:, 512:1024],
                         rearr2[:, kc + 1, 512:1024],
                         ch="q0" if (kc + 1) % 2 == 0 else "q1")
            C.wait(v, "t")
            for oc in range(8):
                C.inc(v.scalar_tensor_tensor(
                    Y[:, oc, 1 + tt * 512:1 + (tt + 1) * 512], slots[oc],
                    b2_c[0:128, l * 8 + oc:l * 8 + oc + 1],
                    Y[:, oc, 1 + tt * 512:1 + (tt + 1) * 512],
                    ALU.add, ALU.add), "v")
            barrier(t)
        if first_of_call:
            cc_tok()

    def dense_phase():
        def xsrc(c):
            return X[:, c, :]
        ln_full(xsrc, "xlb")
        barrier(g)
        gdma(ccD_in.ap(), XLb[:, :, :].rearrange("p c t -> p (c t)"))
        C.wait(g, "g")
        C.inc(g.collective_compute("AllGather", ALU.bypass, replica_groups=PAIRS,
                                   ins=[ccD_in.ap().opt()],
                                   outs=[ccD_out.ap().opt()]), "c")
        C.wait(g, "c")
        gdma(XLo[:, :, :].rearrange("p c t -> p (c t)"),
             ccD_out.ap()[bass.ds(oth_sv, 1), :, :].rearrange("o p e -> p (o e)"))
        C.inc(v.memset(Vd[:, :, 64:65], 1.0), "v")
        C.inc(v.memset(Vd[:, :, 129:130], 1.0), "v")
        barrier(t, s, v)
        import os as _os
        K_NP = int(_os.environ.get("K_NP", "8"))
        for c in range(K_NP):
            low = c < 4
            cA, cB = (2 * c, 2 * c + 1) if low else (2 * c - 8, 2 * c - 7)
            C.wait(t, "v")
            C.wait(t, "s")
            for nt in range(4):
                C.inc(t.matmul(P_A[:, nt * 512:(nt + 1) * 512], blkq[:, :],
                               XLb[:, c, nt * 512:(nt + 1) * 512],
                               start=True, stop=True), "t")
            C.wait(s, "t")
            C.inc(s.copy(Qd[:, :], P_A[:, :]), "s")
            C.wait(t, "s")
            for half, src in ((0, XLb), (1, XLo)):
                for nt in range(4):
                    C.inc(t.matmul(P_A[:, nt * 512:(nt + 1) * 512], blkk[:, :],
                                   src[:, c, nt * 512:(nt + 1) * 512],
                                   start=True, stop=True), "t")
                C.wait(s, "t")
                C.inc(s.copy(Kd[:, half * 2048:(half + 1) * 2048], P_A[:, :]), "s")
                C.wait(t, "s")
                lastv = {0: 0, 1: 0}
                for ttl in range(16):
                    ps = P_C[:, 0:512] if (ttl % 2 == 0) else P_B[:, 0:512]
                    psv = ps[0:128, 0:128]
                    if ttl >= 2:
                        C.wait(t, "v", lastv[ttl % 2])
                    C.inc(t.matmul(psv, src[:, c, ttl * 128:(ttl + 1) * 128],
                                   blkv[:, :], start=True, stop=True), "t")
                    C.wait(v, "t")
                    C.inc(v.tensor_copy(
                        Vd[:, half * 16 + ttl, 0:130].rearrange(
                            "p (h e) -> p h e", e=65)[:, :, 0:64],
                        psv.rearrange("p (h e) -> p h e", e=64)), "v")
                    lastv[ttl % 2] = C.cnt["v"]
            barrier(t, s, v)
            groups = [(gi, list(range(3 * gi, min(3 * gi + 3, 32))))
                      for gi in range(11)]
            for qt in range(4):
                qcols = slice(qt * 512, (qt + 1) * 512)
                C.wait(t, "v")
                C.wait(t, "s")
                for gi, chunks in groups:
                    for i in chunks:
                        co = (i - chunks[0]) * 512
                        sA = P_A[:, co:co + 512]
                        sB = P_B[:, co:co + 512]
                        t.matmul(sA, Kd[0:64, i * 128:(i + 1) * 128],
                                 Qd[0:64, qcols], start=True, stop=True,
                                 tile_position=(0, 0))
                        mm = t.matmul(sB, Kd[64:128, i * 128:(i + 1) * 128],
                                      Qd[64:128, qcols], start=True, stop=True,
                                      tile_position=(64, 0))
                    C.inc(mm, "t")
                    C.wait(s, "t")
                    ncl = len(chunks) * 512
                    C.inc(s.activation(expd[0][:, 0:ncl], P_A[:, 0:ncl], AF.Exp,
                                       bias=EBIA, scale=1.0 / 32.0), "s")
                    C.inc(s.activation(expd[1][:, 0:ncl], P_B[:, 0:ncl], AF.Exp,
                                       bias=EBIA, scale=1.0 / 32.0), "s")
                    C.wait(t, "s")
                    for i in chunks:
                        co = (i - chunks[0]) * 512
                        st_, sp_ = (i == 0), (i == 31)
                        vA = Vd[:, i, 0:65]
                        vB = Vd[:, i, 65:130]
                        if low:
                            t.matmul(P_A[0:65, 1536:2048], vA,
                                     expd[0][:, co:co + 512], start=st_, stop=sp_)
                            mm = t.matmul(P_C[0:65, 0:512], vB,
                                          expd[1][:, co:co + 512], start=st_, stop=sp_)
                        else:
                            t.matmul(P_A[64:128, 1536:2048], vA[0:128, 0:64],
                                     expd[0][:, co:co + 512], start=st_, stop=sp_,
                                     tile_position=(0, 64))
                            t.matmul(P_A[0:1, 1536:2048], onesb[:, :],
                                     expd[0][:, co:co + 512], start=st_, stop=sp_)
                            t.matmul(P_C[64:128, 0:512], vB[0:128, 0:64],
                                     expd[1][:, co:co + 512], start=st_, stop=sp_,
                                     tile_position=(0, 64))
                            mm = t.matmul(P_C[0:1, 0:512], onesb[:, :],
                                          expd[1][:, co:co + 512], start=st_, stop=sp_)
                    C.inc(mm, "t")
                if low:
                    attn_norm(True, P_A[0:64, 1536:2048], P_C[0:64, 0:512],
                              P_A[64:65, 1536:2048], P_C[64:65, 0:512],
                              o_dn[0:64, cA, qcols], o_dn[0:64, cB, qcols],
                              recip_d, recbc_d)
                else:
                    attn_norm(False, P_A[64:128, 1536:2048], P_C[64:128, 0:512],
                              P_A[0:1, 1536:2048], P_C[0:1, 0:512],
                              o_dn[64:128, cA, qcols], o_dn[64:128, cB, qcols],
                              recip_d, recbc_d)
        barrier(g, t, s, v, yq)
        rearr_wout = wc_woutT.ap().rearrange("(kc p) o -> p kc o", p=128)
        gdma(wouts[:, 0:4, :], rearr_wout[:, 0:4, :])
        ydma(wouts[:, 4:8, :], rearr_wout[:, 4:8, :], ch="q0")
        barrier(t)
        for oc in range(8):
            if oc >= 1:
                C.wait(t, "v")
            for nt in range(4):
                for kc in range(8):
                    mm = t.matmul(P_A[:, nt * 512:(nt + 1) * 512],
                                  wouts[:, kc, oc * 128:(oc + 1) * 128],
                                  o_dn[:, kc, nt * 512:(nt + 1) * 512],
                                  start=(kc == 0), stop=(kc == 7))
            C.inc(mm, "t")
            C.wait(v, "t")
            C.inc(v.scalar_tensor_tensor(
                X[:, oc, :], P_A[:, :], bout_c[0:128, oc:oc + 1], X[:, oc, :],
                ALU.add, ALU.add), "v")

    if stage <= 1:
        return finish()
    if stage == 85:
        barrier(t, s, v, g, yq)
        gdma(xout[:, 0:1536], wc_qkv.ap()[0][0:128, :])
        gdma(xout[:, 2048:2048 + 1536], wc_qkv.ap()[0][512:640, :])
        gdma(xout[:, 4096:4096 + 2816], wc_w1a.ap()[0][0:128, :])
        gdma(xout[:, 8192:8192 + 1024], wc_w2.ap()[0][0:128, :])
        barrier(g, t, s, v, yq)
        return nc, dbg_outs
    for call in range(2):
        def xsrc(c):
            return X[:, c, :]
        ln_full(xsrc, "y")
        if stage <= 2:
            return finish_y()
        cc_tok()
        if stage <= 3:
            return finish_y()
        barrier(yq)
        ydma(x_spill.ap(), X[:, :, :].rearrange("p c t -> p (c t)"))
        barrier(t, s, v, g)
        for li, l in enumerate((0, 1)):
            ffn_layer(l, first_of_call=(li == 0))
            if stage == 87:
                return nc, dbg_outs
            if stage in (8, 9, 10, 11, 12, 13):
                barrier(t, s, v, g, yq)
                if stage == 10:
                    import os as _os
                    if _os.environ.get("K_NODUMP"):
                        gdma(xout.rearrange("p (c t) -> p c t", t=NT),
                             Y[:, :, 1:1 + NT])
                    else:
                        gdma(xout[:, 0:8192].rearrange("p (c t) -> p c t", t=2048),
                             Qb[:, :, :])
                        for cc_ in range(4):
                            gdma(xout[:, 8192 + cc_ * 2048:8192 + (cc_ + 1) * 2048],
                                 Kb[:, cc_, 512:2560])
                elif stage == 11:
                    gdma(xout[:, 0:8192].rearrange("p (c t) -> p c t", t=2048),
                         o_fm[:, :, :])
                elif stage in (8, 9, 12):
                    gdma(xout.rearrange("p (c t) -> p c t", t=NT), Y[:, :, 1:1 + NT])
                elif stage == 13:
                    gdma(xout, Hb[:, :, :].rearrange("p c t -> p (c t)"))
                barrier(g, t, s, v, yq)
                return nc, dbg_outs
            if stage <= 4 and li == 0:
                barrier(t, s, v, g, yq)
                gdma(xout.rearrange("p (c t) -> p c t", t=NT), Y[:, :, 1:1 + NT])
                barrier(g, t, s, v, yq)
                return nc, dbg_outs
            if stage <= 5 and li == 1:
                barrier(t, s, v, g, yq)
                gdma(xout.rearrange("p (c t) -> p c t", t=NT), Y[:, :, 1:1 + NT])
                barrier(g, t, s, v, yq)
                return nc, dbg_outs
        barrier(g, yq)
        gdma(X[:, :, :], x_spill.ap().rearrange("p (c t) -> p c t", t=NT))
        C.wait(v, "g")
        for c in range(NCH):
            C.inc(v.scalar_tensor_tensor(
                X[:, c, :], Y[:, c, 1:1 + NT], 0.5, X[:, c, :],
                ALU.mult, ALU.add), "v")
        if stage <= 6:
            return finish()
        if call == 0:
            barrier(t, s, v, g, yq)
            dense_phase()
            if stage <= 7:
                return finish()
        barrier(t, s, v, g, yq)

    return finish()


_CACHE = {}


def _host_inputs(inputs):
    x = np.asarray(inputs["L0_attn_x"])
    Wq = np.asarray(inputs["L0_attn_Wq"])
    Wk = np.asarray(inputs["L0_attn_Wk"])
    Wv = np.asarray(inputs["L0_attn_Wv"])
    W_out = np.asarray(inputs["L0_attn_W_out"])
    b_out = np.asarray(inputs["L0_attn_b_out"])
    wqkv = np.asarray(inputs["L0_ffn_wqkv"])
    q_scale = np.asarray(inputs["L0_ffn_q_scale"])
    k_scale = np.asarray(inputs["L0_ffn_k_scale"])
    wo = np.asarray(inputs["L0_ffn_wo"])
    ln_g = np.asarray(inputs["L0_ffn_ln_g"])
    ln_b = np.asarray(inputs["L0_ffn_ln_b"])
    w1 = np.asarray(inputs["L0_ffn_w1"])
    b1 = np.asarray(inputs["L0_ffn_b1"])
    w2 = np.asarray(inputs["L0_ffn_w2"])
    b2 = np.asarray(inputs["L0_ffn_b2"])

    f32 = np.float32
    com = {}
    com["wqkv"] = np.ascontiguousarray(wqkv, dtype=f32)
    com["wo_p"] = np.ascontiguousarray(wo[:, PERM_WO, :], dtype=f32)
    w1a = np.zeros((2, D, HIDP), f32)
    w1g = np.zeros((2, D, HIDP), f32)
    w1a[:, :, :HID] = w1[:, :, :HID]
    w1g[:, :, :HID] = w1[:, :, HID:]
    com["w1a"], com["w1g"] = w1a, w1g
    w2pp = np.zeros((2, HIDP, D), f32)
    w2pp[:, :HID, :] = w2
    com["w2p"] = w2pp
    com["woutT"] = np.ascontiguousarray(W_out.T[PERM_WOUT, :], dtype=f32)
    com["wq_t"] = np.ascontiguousarray(Wq.T, dtype=f32)
    com["wk_t"] = np.ascontiguousarray(Wk.T, dtype=f32)
    com["wv_t"] = np.ascontiguousarray(Wv.T, dtype=f32)
    com["b_out_r"] = np.ascontiguousarray(b_out.reshape(8, 128).T, dtype=f32)
    com["ln_g_r"] = np.ascontiguousarray(ln_g.reshape(2, 8, 128).swapaxes(1, 2), dtype=f32)
    com["ln_b_r"] = np.ascontiguousarray(ln_b.reshape(2, 8, 128).swapaxes(1, 2), dtype=f32)
    com["b2_r"] = np.ascontiguousarray(b2.reshape(2, 8, 128).swapaxes(1, 2), dtype=f32)
    b1a = np.zeros((2, 128, NJ), f32)
    b1g = np.zeros((2, 128, NJ), f32)
    for l in range(2):
        pa = np.zeros(HIDP, f32)
        pg = np.zeros(HIDP, f32)
        pa[:HID] = b1[l, :HID]
        pg[:HID] = b1[l, HID:]
        b1a[l] = pa.reshape(NJ, 128).T
        b1g[l] = pg.reshape(NJ, 128).T
    com["b1a_r"], com["b1g_r"] = b1a, b1g
    perm64 = (np.arange(64) + 32) % 64
    com["qs_r"] = np.ascontiguousarray(
        np.tile(q_scale, (1, 2)).reshape(2, 128, 1), dtype=f32)
    com["qsp_r"] = np.ascontiguousarray(
        np.tile(q_scale[:, perm64], (1, 2)).reshape(2, 128, 1), dtype=f32)
    com["ks_r"] = np.ascontiguousarray(
        np.tile(k_scale, (1, 2)).reshape(2, 128, 1), dtype=f32)
    com["ksp_r"] = np.ascontiguousarray(
        np.tile(k_scale[:, perm64], (1, 2)).reshape(2, 128, 1), dtype=f32)

    in_maps = []
    for c in range(NCORE):
        b, h = c // 2, c % 2
        t0 = h * NT
        m = dict(com)
        m["xin"] = _pack_fm(np.asarray(x[b, t0:t0 + NT, :], dtype=f32))
        tc_, ts_, tic, tis = _rot_tables(t0)
        m["tab_cos"], m["tab_sin"] = tc_, ts_
        m["tab_icos"], m["tab_isin"] = tic, tis
        m["hz"] = np.full((128, 1), 0.0 if h == 0 else 1.0, f32)
        m["maskA"] = np.full((128, 1), NEG if h == 0 else 0.0, f32)
        in_maps.append(m)
    return in_maps


def kernel(**inputs):
    import os
    stg = int(os.environ.get("K_STAGE", "99"))
    key = ("prog", stg)
    if key not in _CACHE:
        _CACHE[key] = build_program(stage=stg)[0]
    nc = _CACHE[key]
    in_maps = _host_inputs(inputs)
    res = run_bass_kernel_spmd(nc, in_maps, core_ids=list(range(NCORE)))
    out = np.empty((B, S, D), dtype=np.float32)
    for c in range(NCORE):
        b, h = c // 2, c % 2
        fm = res.results[c]["xout"]
        out[b, h * NT:(h + 1) * NT, :] = (
            fm.reshape(128, NCH, NT).transpose(1, 0, 2).reshape(D, NT).T)
    return out


# version 03
# speedup vs baseline: 1.1679x; 1.1284x over previous
import numpy as np
import ml_dtypes
import concourse.bass as bass
import concourse.mybir as mybir
from concourse.bass_utils import run_bass_kernel_spmd

F32 = mybir.dt.float32
BF16 = mybir.dt.bfloat16
AF = mybir.ActivationFunctionType
ALU = mybir.AluOpType
BF = ml_dtypes.bfloat16

NCORE = 8
B, S, D = 4, 4096, 1024
NT = 2048
NCH = 8
YW = 2056
HL, DH, WIN, NW = 8, 64, 512, 4
HID, HIDP, NJ = 2730, 2816, 22
EXP_BIAS = -40.0
NEG = -30000.0
PAIRS = [[0, 1], [2, 3], [4, 5], [6, 7]]
PERM_WO = np.array([(c + 4 * (r // 64)) * 64 + (r % 64)
                    for c in range(4) for r in range(128)])
PERM_WOUT = np.array([(c + 8 * (r // 64)) * 64 + (r % 64)
                      for c in range(8) for r in range(128)])
PERM_ROT = np.array([(d // 64) * 64 + ((d % 64) + 32) % 64 for d in range(128)])

A_X = 0
A_Y = 66048
A_YB = 0
A_WT = 33024
A_WV = 37120
A_VT = 41216
A_QB = 131840
A_KB = 148224
A_TAB = 168704
A_RAW = 187136
A_RT = 192256
A_EXP = 0
A_OFM = 12288
A_RCP = 28672
A_RBC = 32768
A_WOS = 168704
A_HB = 0
A_AGB = 131840
A_W1T = 154368
A_W2T = 162560
A_LT = 166656
A_XLB = 66048
A_XLO = 98816
A_ODN = 131840
A_QD = 164608
A_KD = 168704
A_VD = 176896
A_EXD = 185216
A_RCD = 191360
A_RBD = 195456
A_WOUT = 66048
ARENA_B = 200448


def _rot_tables(t0):
    half_idx = np.arange(0, DH, 2, dtype=np.float64)
    inv_freq = 1.0 / 10000.0 ** (half_idx / DH)
    tpos = np.arange(t0 - 512, t0 + 2048, dtype=np.float64)
    fr = tpos[:, None] * inv_freq[None, :]
    freqs = np.concatenate([fr, fr], axis=1)
    base = (half_idx + 0.4 * DH) / (1.4 * DH)
    power = (tpos - S // 2) / float(WIN // 2)
    sc = base[None, :] ** power[:, None]
    scale = np.concatenate([sc, sc], axis=1)
    cos, sin = np.cos(freqs), np.sin(freqs)
    out = []
    for arr in (cos * scale, sin * scale, cos / scale, sin / scale):
        a = arr.T.astype(np.float32)
        out.append(np.ascontiguousarray(np.concatenate([a, a], axis=0)))
    return out


def _pack_fm(xs):
    T = xs.shape[0]
    return np.ascontiguousarray(
        xs.T.reshape(NCH, 128, T).transpose(1, 0, 2).reshape(128, NCH * T))


class Ctr:
    def __init__(self, nc):
        self.sem, self.cnt = {}, {}
        for n in ("g", "y", "t", "s", "v", "c", "p0", "p1", "q0", "q1", "r0", "r1"):
            self.sem[n] = nc.alloc_semaphore("sem_" + n)
            self.cnt[n] = 0

    def inc(self, inst, name, amt=1):
        inst.then_inc(self.sem[name], amt)
        self.cnt[name] += amt
        return inst

    def wait(self, eng, name, val=None):
        vv = self.cnt[name] if val is None else val
        if vv > 0:
            eng.wait_ge(self.sem[name], vv)


def build_program(stage=99, dbg=()):
    nc = bass.Bass("TRN2", target_bir_lowering=False, debug=False,
                   detect_race_conditions=False)

    def inp(name, shape, dt=F32):
        return nc.dram_tensor(name, list(shape), dt, kind="ExternalInput").ap()

    xin = inp("xin", [128, NCH * NT])
    wqkv = inp("wqkv", [2, D, 3 * 512])
    wo_p = inp("wo_p", [2, 512, D])
    w1a = inp("w1a", [2, D, HIDP])
    w1g = inp("w1g", [2, D, HIDP])
    w2p = inp("w2p", [2, HIDP, D])
    woutT = inp("woutT", [D, D])
    wq_t = inp("wq_t", [DH, DH])
    wk_t = inp("wk_t", [DH, DH])
    wv_t = inp("wv_t", [DH, DH])
    b_out_r = inp("b_out_r", [128, 8])
    ln_g_r = inp("ln_g_r", [2, 128, 8])
    ln_b_r = inp("ln_b_r", [2, 128, 8])
    b2_r = inp("b2_r", [2, 128, 8])
    b1a_r = inp("b1a_r", [2, 128, NJ])
    b1g_r = inp("b1g_r", [2, 128, NJ])
    qs_r = inp("qs_r", [2, 128, 1])
    qsp_r = inp("qsp_r", [2, 128, 1])
    ks_r = inp("ks_r", [2, 128, 1])
    ksp_r = inp("ksp_r", [2, 128, 1])
    tab_cos = inp("tab_cos", [128, 2560])
    tab_sin = inp("tab_sin", [128, 2560])
    tab_icos = inp("tab_icos", [128, 2560])
    tab_isin = inp("tab_isin", [128, 2560])
    hz = inp("hz", [128, 1])
    maskA = inp("maskA", [128, 1])

    xout = nc.dram_tensor("xout", [128, NCH * NT], F32, kind="ExternalOutput").ap()
    dbg_outs = {}
    for name, shape, dt in dbg:
        dbg_outs[name] = nc.dram_tensor("dbg_" + name, list(shape), dt,
                                        kind="ExternalOutput").ap()

    ident_d = nc.inline_tensor(np.eye(128, dtype=BF), name="c_ident")
    onesb_d = nc.inline_tensor(np.ones((128, 1), dtype=BF), name="c_onesb")
    onesrow_d = nc.inline_tensor(np.ones((1, 128), dtype=np.float32), name="c_onesrow")
    ones64_d = nc.inline_tensor(np.ones((128, 64), dtype=np.float32), name="c_ones64")
    ind2np = np.zeros((2, 128), dtype=np.float32)
    ind2np[0, :64] = 1.0
    ind2np[1, 64:] = 1.0
    ind2b_d = nc.inline_tensor(np.ascontiguousarray(ind2np.T).astype(BF), name="c_ind2b")
    ind2f_d = nc.inline_tensor(ind2np, name="c_ind2f")
    R64 = np.zeros((64, 64), dtype=np.float64)
    R64[np.arange(32), np.arange(32) + 32] = -1.0
    R64[np.arange(32, 64), np.arange(32)] = 1.0
    R128 = np.zeros((128, 128), dtype=np.float64)
    R128[0:64, 0:64] = R64
    R128[64:128, 64:128] = R64
    rotT_d = nc.inline_tensor(np.ascontiguousarray(R128.T).astype(BF), name="c_rotT")
    mk = np.zeros((128, 4, 512), dtype=np.float64)
    kk = np.arange(128)[:, None]
    qq = np.arange(512)[None, :]
    for jb in range(4):
        mk[:, jb, :] = np.where(qq < 128 * jb + kk, NEG, 0.0)
    masks_d = nc.inline_tensor(
        np.ascontiguousarray(mk.reshape(128, 2048)).astype(BF), name="c_masks")
    mlb = np.zeros((128, 4, 512), dtype=np.float64)
    for i_ in range(4):
        mlb[:, i_, :] = np.where(qq > 128 * i_ + kk, NEG, 0.0)
    masks_lb_d = nc.inline_tensor(
        np.ascontiguousarray(mlb.reshape(128, 2048)).astype(BF), name="c_masks_lb")
    cp = np.zeros((128, 8), dtype=np.float32)
    cp[:, 0] = 1e-5
    cp[:, 1] = EXP_BIAS
    cp[:, 2] = 1e-12
    cpool_d = nc.inline_tensor(cp, name="c_cpool")

    wc_qkv = nc.dram_tensor("wc_qkv", [2, D, 3 * 512], BF16, kind="Internal")
    wc_wo = nc.dram_tensor("wc_wo", [2, 512, D], BF16, kind="Internal")
    wc_w1a = nc.dram_tensor("wc_w1a", [2, 128, NJ, 8, 128], BF16, kind="Internal")
    wc_w1g = nc.dram_tensor("wc_w1g", [2, 128, NJ, 8, 128], BF16, kind="Internal")
    wc_w2 = nc.dram_tensor("wc_w2", [2, HIDP, D], BF16, kind="Internal")
    wc_woutT = nc.dram_tensor("wc_woutT", [D, D], BF16, kind="Internal")
    x_spill = nc.dram_tensor("x_spill", [128, NCH * NT], F32, kind="Internal")
    ccT_in = nc.dram_tensor("ccT_in", [128, 8], F32, kind="Internal")
    ccT_out = nc.dram_tensor("ccT_out", [2, 128, 8], F32, kind="Internal")
    VW = 4 * HL * 65
    ccA_in = nc.dram_tensor("ccA_in", [128, 2048 + VW], BF16, kind="Internal")
    ccA_out = nc.dram_tensor("ccA_out", [2, 128, 2048 + VW], BF16, kind="Internal")
    ccD_in = nc.dram_tensor("ccD_in", [128, NCH * NT], BF16, kind="Internal")
    ccD_out = nc.dram_tensor("ccD_out", [2, 128, NCH * NT], BF16, kind="Internal")

    ident = nc.alloc_sbuf_tensor("ident", [128, 128], BF16)
    onesb = nc.alloc_sbuf_tensor("onesb", [128, 1], BF16)
    onesrowf = nc.alloc_sbuf_tensor("onesrowf", [1, 128], F32)
    ones64f = nc.alloc_sbuf_tensor("ones64f", [128, 64], F32)
    ind2b = nc.alloc_sbuf_tensor("ind2b", [128, 2], BF16)
    ind2f = nc.alloc_sbuf_tensor("ind2f", [2, 128], F32)
    rotT = nc.alloc_sbuf_tensor("rotT", [128, 128], BF16)
    masksb = nc.alloc_sbuf_tensor("masksb", [128, 2048], BF16)
    maskslb = nc.alloc_sbuf_tensor("maskslb", [128, 2048], BF16)
    cpool = nc.alloc_sbuf_tensor("cpool", [128, 8], F32)
    blkq = nc.alloc_sbuf_tensor("blkq", [128, 128], BF16)
    blkk = nc.alloc_sbuf_tensor("blkk", [128, 128], BF16)
    blkv = nc.alloc_sbuf_tensor("blkv", [128, 128], BF16)
    bout_c = nc.alloc_sbuf_tensor("bout_c", [128, 8], F32)
    lng_c = nc.alloc_sbuf_tensor("lng_c", [128, 16], F32)
    lnb_c = nc.alloc_sbuf_tensor("lnb_c", [128, 16], F32)
    b2_c = nc.alloc_sbuf_tensor("b2_c", [128, 16], F32)
    b1a_c = nc.alloc_sbuf_tensor("b1a_c", [128, 2 * NJ], F32)
    b1g_c = nc.alloc_sbuf_tensor("b1g_c", [128, 2 * NJ], F32)
    qsc = nc.alloc_sbuf_tensor("qsc", [128, 8], F32)
    hzc = nc.alloc_sbuf_tensor("hzc", [128, 1], F32)
    mac = nc.alloc_sbuf_tensor("mac", [128, 1], F32)
    macE = nc.alloc_sbuf_tensor("macE", [128, 1], F32)

    arena = nc.alloc_sbuf_tensor("arena", [128, ARENA_B], mybir.dt.uint8)
    ab = nc.lookup_mloc(arena).addr
    rem = nc.sbuf_bytes_remaining
    assert rem >= 0, rem
    _n = [0]

    def place(off, shape, dtype):
        per = int(np.prod(shape[1:])) * mybir.dt.size(dtype)
        assert off % 32 == 0 and off + per <= ARENA_B, (off, per)
        _n[0] += 1
        h = nc.alloc_sbuf_tensor_at(f"ar{_n[0]}", list(shape), dtype,
                                    offset=ab + off)
        if not hasattr(nc, "_arena_map"):
            nc._arena_map = {}
        nc._arena_map.setdefault(off, []).append(h.name)
        return h

    X = place(A_X, [128, NCH, NT], F32)
    Y = place(A_Y, [128, NCH, YW], F32)
    Yb = place(A_YB, [128, NCH, YW], BF16)
    wtq = [place(A_WT + i * 2048, [128, 8, 128], BF16) for i in range(2)]
    wv_sb = place(A_WV, [128, 8, 256], BF16)
    v_tm = place(A_VT, [128, 20, HL, 65], BF16)
    Qb = place(A_QB, [128, 4, 2048], BF16)
    Kb = place(A_KB, [128, 4, 2560], BF16)
    tqc = place(A_TAB, [128, 2048], BF16)
    tqs = place(A_TAB + 4096, [128, 2048], BF16)
    tkc = place(A_TAB + 8192, [128, 2560], BF16)
    tks = place(A_TAB + 13312, [128, 2560], BF16)
    rawt = place(A_RAW, [128, 2560], BF16)
    rt_q2 = place(A_RT, [128, 512], BF16)
    rt_rb = place(A_RT + 1024, [128, 512], F32)
    rt_qn = place(A_RT + 3072, [128, 512], F32)
    rt_qnb = place(A_RT + 5120, [128, 512], BF16)
    rt_t2 = place(A_RT + 6144, [128, 512], F32)
    rt_ss = place(A_RAW, [2, 512], F32)
    expb = [place(A_EXP + i * 3072, [128, 1536], BF16) for i in range(4)]
    o_fm = place(A_OFM, [128, 4, 2048], BF16)
    recip_sb = place(A_RCP, [128, 1024], F32)
    recbc = place(A_RBC, [128, 1024], F32)
    wos = place(A_WOS, [128, 4, 1024], BF16)
    Hb = place(A_HB, [128, NCH, NT], BF16)
    AGb = place(A_AGB, [128, NJ, 512], BF16)
    wt1a = [place(A_W1T + i * 2048, [128, 8, 128], BF16) for i in range(2)]
    wt1g = [place(A_W1T + 4096 + i * 2048, [128, 8, 128], BF16) for i in range(2)]
    wt2 = [place(A_W2T + i * 2048, [128, 1024], BF16) for i in range(2)]
    tXb = place(A_LT, [128, NT], BF16)
    tSQb = place(A_LT + 4096, [128, NT], BF16)
    tnm = place(A_LT + 8192, [128, NT], F32)
    trs = place(A_LT + 16384, [128, NT], F32)
    r_m = place(A_LT + 24576, [1, 512], F32)
    r_q = place(A_LT + 26624, [1, 512], F32)
    r_v = place(A_LT + 28672, [1, 512], F32)
    r_s = place(A_LT + 30720, [1, 512], F32)
    XLb = place(A_XLB, [128, NCH, NT], BF16)
    XLo = place(A_XLO, [128, NCH, NT], BF16)
    o_dn = place(A_ODN, [128, NCH, NT], BF16)
    Qd = place(A_QD, [128, 2048], BF16)
    Kd = place(A_KD, [128, 4096], BF16)
    Vd = place(A_VD, [128, 32, 130], BF16)
    expd = [place(A_EXD + i * 3072, [128, 1536], BF16) for i in range(2)]
    recip_d = place(A_RCD, [128, 1024], F32)
    recbc_d = place(A_RBD, [128, 1024], F32)
    wouts = place(A_WOUT, [128, 8, 1024], BF16)

    P_A = nc.alloc_psum_tensor("P_A", [128, 2048], F32)
    P_B = nc.alloc_psum_tensor("P_B", [128, 1536], F32)
    P_C = nc.alloc_psum_tensor("P_C", [128, 512], F32)

    C = Ctr(nc)
    g, t, s, v, yq = nc.gpsimd, nc.tensor, nc.scalar, nc.vector, nc.sync

    def gdma(out, in_, slow=False, ch="g"):
        return C.inc(g.dma_start(out=out, in_=in_,
                                 allow_slow_non_contiguous=slow), ch, 16)

    def ydma(out, in_, slow=False, ch="y"):
        return C.inc(yq.dma_start(out=out, in_=in_,
                                  allow_slow_non_contiguous=slow), ch, 16)

    def barrier(*engs):
        for e in engs:
            for nm in ("g", "y", "t", "s", "v", "c", "p0", "p1", "q0", "q1", "r0", "r1"):
                C.wait(e, nm)

    EPS1 = cpool[0:1, 0:1]
    EBIA = cpool[0:128, 1:2]
    EPS12 = cpool[0:2, 2:3]

    def finish_y():
        barrier(t, s, v, g, yq)
        ydma(xout.rearrange("p (c t) -> p c t", t=NT), Y[:, :, 1:1 + NT])
        barrier(g, t, s, v, yq)
        return nc, dbg_outs

    def finish(ap=None, cast=False):
        barrier(t, s, v, g, yq)
        if ap is None:
            ap = X[:, :, :].rearrange("p c t -> p (c t)")
        if cast:
            gdma(xout, ap)
        else:
            ydma(xout, ap)
        barrier(g, t, s, v, yq)
        return nc, dbg_outs

    pid = g.partition_id()
    oth_reg = g.alloc_register("oth")
    g.reg_alu(oth_reg, pid, 2, ALU.mod)
    g.reg_alu(oth_reg, oth_reg, 1, ALU.bitwise_xor)
    oth_sv = g.snap(oth_reg, min_val=0, max_val=1)

    gdma(X[:, :, :], xin.rearrange("p (c t) -> p c t", t=NT))
    for dst, src in [(ident, ident_d), (onesb, onesb_d), (onesrowf, onesrow_d),
                     (ones64f, ones64_d), (ind2b, ind2b_d), (ind2f, ind2f_d),
                     (rotT, rotT_d), (masksb, masks_d), (maskslb, masks_lb_d), (cpool, cpool_d)]:
        gdma(dst[:, :], src.ap())
    gdma(bout_c[:, :], b_out_r)
    gdma(hzc[:, :], hz)
    gdma(mac[:, :], maskA)
    for l in range(2):
        gdma(lng_c[:, l * 8:(l + 1) * 8], ln_g_r[l])
        gdma(lnb_c[:, l * 8:(l + 1) * 8], ln_b_r[l])
        gdma(b2_c[:, l * 8:(l + 1) * 8], b2_r[l])
        gdma(b1a_c[:, l * NJ:(l + 1) * NJ], b1a_r[l])
        gdma(b1g_c[:, l * NJ:(l + 1) * NJ], b1g_r[l])
        gdma(qsc[:, l:l + 1], qs_r[l])
        gdma(qsc[:, 2 + l:3 + l], qsp_r[l])
        gdma(qsc[:, 4 + l:5 + l], ks_r[l])
        gdma(qsc[:, 6 + l:7 + l], ksp_r[l])
    C.inc(v.memset(blkq[:, :], 0.0), "v")
    C.inc(v.memset(blkk[:, :], 0.0), "v")
    C.inc(v.memset(blkv[:, :], 0.0), "v")
    C.wait(g, "v")
    for blk, w in [(blkq, wq_t), (blkk, wk_t), (blkv, wv_t)]:
        gdma(blk[0:64, 0:64], w)
        gdma(blk[64:128, 64:128], w)
    C.wait(v, "g")
    C.inc(v.tensor_scalar_add(macE[:, :], mac[:, :], EXP_BIAS), "v")
    stg = [place(A_QB, [128, HIDP], BF16), place(A_QB + 5664, [128, HIDP], BF16)]
    wjobs = []
    for l in range(2):
        wjobs += [(wqkv[l], wc_qkv.ap()[l], D, 1536, False),
                  (wo_p[l], wc_wo.ap()[l], 512, D, False),
                  (w1a[l], wc_w1a.ap()[l], D, HIDP, True),
                  (w1g[l], wc_w1g.ap()[l], D, HIDP, True),
                  (w2p[l], wc_w2.ap()[l], HIDP, D, False)]
    wjobs.append((woutT, wc_woutT.ap(), D, D, False))
    ci = 0
    for src, dst, rows, cols, tiled in wjobs:
        for r0 in range(0, rows, 128):
            bi = ci % 2
            pch, qch = ("p0", "q0") if bi == 0 else ("p1", "q1")
            if ci >= 2:
                C.wait(g, qch)
            gdma(stg[bi][:, 0:cols], src[r0:r0 + 128, :], ch=pch)
            C.wait(yq, pch)
            if tiled:
                ydma(dst[:, :, r0 // 128, :],
                     stg[bi][:, 0:cols].rearrange("p (j e) -> p j e", e=128),
                     ch=qch)
            else:
                ydma(dst[r0:r0 + 128, :], stg[bi][:, 0:cols], ch=qch)
            ci += 1
    barrier(t, s, v, g, yq)

    def ln_full(src_fn, mode, l=None):
        for c in range(NCH):
            C.inc(s.copy(tXb[:, :], src_fn(c)), "s")
            C.inc(s.activation(tSQb[:, :], src_fn(c), AF.Square), "s")
            C.wait(t, "s")
            for nt in range(4):
                t.matmul(P_A[0:1, nt * 512:(nt + 1) * 512], onesb[:, :],
                         tXb[:, nt * 512:(nt + 1) * 512],
                         start=(c == 0), stop=(c == NCH - 1))
                dst = (P_B[0:1, nt * 512:(nt + 1) * 512] if nt < 3
                       else P_C[0:1, 0:512])
                mm = t.matmul(dst, onesb[:, :], tSQb[:, nt * 512:(nt + 1) * 512],
                              start=(c == 0), stop=(c == NCH - 1))
            C.inc(mm, "t")
            C.wait(s, "t")
        for nt in range(4):
            mean_ap = P_A[0:1, nt * 512:(nt + 1) * 512]
            sq_ap = (P_B[0:1, nt * 512:(nt + 1) * 512] if nt < 3 else P_C[0:1, 0:512])
            C.wait(v, "t")
            C.inc(v.tensor_scalar(r_m[:, :], mean_ap, -1.0 / 1024.0, None, ALU.mult), "v")
            C.inc(v.tensor_scalar(r_q[:, :], sq_ap, 1.0 / 1024.0, None, ALU.mult), "v")
            C.inc(v.tensor_tensor(r_v[:, :], r_m[:, :], r_m[:, :], ALU.mult), "v")
            C.inc(v.tensor_sub(r_q[:, :], r_q[:, :], r_v[:, :]), "v")
            C.wait(s, "v")
            C.inc(s.activation(r_s[:, :], r_q[:, :], AF.Ln, bias=EPS1), "s")
            C.inc(s.activation(r_s[:, :], r_s[:, :], AF.Exp, scale=-0.5), "s")
            C.wait(t, "s")
            C.inc(t.matmul(P_A[:, nt * 512:(nt + 1) * 512], onesrowf[0:1, :],
                           r_m[0:1, :], start=True, stop=True), "t")
            C.wait(v, "t")
            C.inc(v.tensor_copy(tnm[:, nt * 512:(nt + 1) * 512],
                                P_A[:, nt * 512:(nt + 1) * 512]), "v")
            C.wait(t, "v")
            C.inc(t.matmul(P_A[:, nt * 512:(nt + 1) * 512], onesrowf[0:1, :],
                           r_s[0:1, :], start=True, stop=True), "t")
            C.wait(v, "t")
            C.inc(v.tensor_copy(trs[:, nt * 512:(nt + 1) * 512],
                                P_A[:, nt * 512:(nt + 1) * 512]), "v")
        for c in range(NCH):
            C.inc(v.tensor_add(P_A[:, :], src_fn(c), tnm[:, :]), "v")
            if mode == "y":
                C.inc(v.tensor_mul(Y[:, c, 1:1 + NT], P_A[:, :], trs[:, :]), "v")
            elif mode == "xlb":
                C.inc(v.tensor_mul(XLb[:, c, :], P_A[:, :], trs[:, :]), "v")
            else:
                C.inc(v.tensor_mul(P_A[:, :], P_A[:, :], trs[:, :]), "v")
                C.inc(v.tensor_scalar(Hb[:, c, :], P_A[:, :],
                                      lng_c[0:128, l * 8 + c:l * 8 + c + 1],
                                      lnb_c[0:128, l * 8 + c:l * 8 + c + 1],
                                      ALU.mult, ALU.add), "v")

    def cc_tok():
        C.wait(g, "v")
        C.wait(g, "t")
        C.wait(g, "s")
        gdma(ccT_in.ap(), Y[:, :, 2048:2049].rearrange("p c o -> p (c o)"), slow=True)
        C.wait(g, "g")
        C.inc(g.collective_compute("AllGather", ALU.bypass, replica_groups=PAIRS,
                                   ins=[ccT_in.ap().opt()],
                                   outs=[ccT_out.ap().opt()]), "c")
        C.wait(g, "c")
        gdma(Y[:, :, 0:1].rearrange("p c o -> p (c o)"),
             ccT_out.ap()[bass.ds(oth_sv, 1), :, :].rearrange("o p c -> p (o c)"),
             slow=True)
        C.wait(v, "g")
        C.inc(v.tensor_scalar(Y[:, :, 0:1], Y[:, :, 0:1], hzc[0:128, 0:1], None,
                              ALU.mult), "v")

    def attn_norm(low, oA_ap, oB_ap, sumA_ap, sumB_ap, dstA, dstB, rcp, rbc):
        rr = 64 if low else 0
        C.wait(s, "t")
        C.inc(s.activation(rcp[rr:rr + 1, 512:1024], sumA_ap, AF.Ln), "s")
        C.inc(s.activation(rcp[rr:rr + 1, 0:512], rcp[rr:rr + 1, 512:1024],
                           AF.Exp, scale=-1.0), "s")
        C.inc(s.activation(rbc[rr:rr + 1, 512:1024], sumB_ap, AF.Ln), "s")
        C.inc(s.activation(rbc[rr:rr + 1, 0:512], rbc[rr:rr + 1, 512:1024],
                           AF.Exp, scale=-1.0), "s")
        C.wait(t, "s")
        if low:
            t.matmul(P_B[0:64, 0:512], ones64f[64:65, :], rcp[64:65, 0:512],
                     start=True, stop=True)
            mm2 = t.matmul(P_B[0:64, 512:1024], ones64f[64:65, :],
                           rbc[64:65, 0:512], start=True, stop=True)
        else:
            t.matmul(P_B[64:128, 0:512], ones64f[0:1, :], rcp[0:1, 0:512],
                     start=True, stop=True, tile_position=(0, 64))
            mm2 = t.matmul(P_B[64:128, 512:1024], ones64f[0:1, :],
                           rbc[0:1, 0:512], start=True, stop=True,
                           tile_position=(0, 64))
        C.inc(mm2, "t")
        C.wait(v, "t")
        if low:
            C.inc(v.tensor_copy(rcp[0:64, 0:512], P_B[0:64, 0:512]), "v")
            C.inc(v.tensor_copy(rbc[0:64, 0:512], P_B[0:64, 512:1024]), "v")
            C.inc(v.tensor_mul(dstA, oA_ap, rcp[0:64, 0:512]), "v")
            C.inc(v.tensor_mul(dstB, oB_ap, rbc[0:64, 0:512]), "v")
        else:
            C.inc(v.tensor_copy(rcp[64:128, 0:512], P_B[64:128, 0:512]), "v")
            C.inc(v.tensor_copy(rbc[64:128, 0:512], P_B[64:128, 512:1024]), "v")
            C.inc(v.tensor_mul(dstA, oA_ap, rcp[64:128, 0:512]), "v")
            C.inc(v.tensor_mul(dstB, oB_ap, rbc[64:128, 0:512]), "v")

    LAYER_STOP = {}

    def ffn_layer(l, first_of_call):
        barrier(s)
        if first_of_call:
            C.wait(v, "y")
            C.inc(v.memset(v_tm[:, :, :, 64:65], 1.0), "v")
        for c in range(NCH):
            C.inc(s.copy(Yb[:, c, 0:2049], Y[:, c, 0:2049]), "s")
        for rawsrc, dstt, qcol, ncols in (
                (tab_cos, tqc, 0 + l, 2048), (tab_sin, tqs, 2 + l, 2048),
                (tab_icos, tkc, 4 + l, 2560), (tab_isin, tks, 6 + l, 2560)):
            C.wait(g, "v")
            gdma(rawt[:, :], rawsrc, ch="p0")
            C.wait(v, "p0")
            C.inc(v.tensor_scalar(dstt[:, 0:ncols], rawt[:, 2560 - ncols:2560],
                                  qsc[0:128, qcol:qcol + 1], None, ALU.mult), "v")
        if stage == 87:
            barrier(g, t, s, v, yq)
            gdma(xout[:, 0:2048], tqc[:, :])
            gdma(xout[:, 2048:4096], tqs[:, :])
            gdma(xout[:, 4096:6656], tkc[:, :])
            gdma(xout[:, 6656:9216], tks[:, :])
            barrier(g, t, s, v, yq)
            return
        if stage == 8:
            return
        barrier(t, g)
        for half in range(2):
            C.wait(g, "t")
            gdma(wv_sb[:, :, :],
                 wc_qkv.ap()[l].rearrange("(kc p) o -> p kc o", p=128)
                 [:, :, 1024 + half * 256:1024 + (half + 1) * 256], ch="p1")
            C.wait(t, "p1")
            lastv = {0: 0, 1: 0}
            for tt in range(16):
                ps = P_C[:, 0:512] if (tt % 2 == 0) else P_B[:, 0:512]
                psv = ps[0:128, 0:256]
                if tt >= 2:
                    C.wait(t, "v", lastv[tt % 2])
                for kc in range(NCH):
                    sh = 1 if kc < 4 else 0
                    mm = t.matmul(psv, Yb[:, kc, sh + tt * 128:sh + (tt + 1) * 128],
                                  wv_sb[:, kc, :], start=(kc == 0), stop=(kc == NCH - 1))
                C.inc(mm, "t")
                C.wait(v, "t")
                C.inc(v.tensor_copy(
                    v_tm[:, 4 + tt, half * 4:(half + 1) * 4, 0:64],
                    psv.rearrange("p (h e) -> p h e", e=64)), "v")
                lastv[tt % 2] = C.cnt["v"]
        if stage == 9:
            return
        import os as _os
        K_SUB = int(_os.environ.get("K_SUB", "99"))
        K_OC = int(_os.environ.get("K_OC", "8"))
        barrier(t, g)
        gdma(wtq[0][:, :, :],
             wc_qkv.ap()[l].rearrange("(kc p) o -> p kc o", p=128)[:, :, 0:128],
             ch="p0")
        pa_free = C.cnt["v"]
        tmile = {0: 0, 1: 0}
        for oc in range(K_OC):
            C.wait(t, "p0" if oc % 2 == 0 else "p1")
            C.wait(t, "v", pa_free)
            for nt in range(4):
                for kc in range(NCH):
                    sh = 1 if kc < 4 else 0
                    mm = t.matmul(P_A[:, nt * 512:(nt + 1) * 512],
                                  wtq[oc % 2][:, kc, :],
                                  Yb[:, kc, sh + nt * 512:sh + (nt + 1) * 512],
                                  start=(kc == 0), stop=(kc == NCH - 1))
            C.inc(mm, "t")
            tmile[oc % 2] = C.cnt["t"]
            if oc + 1 < K_OC:
                C.wait(g, "t", tmile[(oc + 1) % 2])
                gdma(wtq[(oc + 1) % 2][:, :, :],
                     wc_qkv.ap()[l].rearrange("(kc p) o -> p kc o", p=128)
                     [:, :, (oc + 1) * 128:(oc + 2) * 128],
                     ch="p0" if (oc + 1) % 2 == 0 else "p1")
            isq = oc < 4
            tc_, ts_ = (tqc, tqs) if isq else (tkc, tks)
            tofs = 0 if isq else 512
            if K_SUB == 0:
                C.wait(v, "t")
                for nt in range(4):
                    dst = (Qb[:, oc, nt * 512:(nt + 1) * 512] if isq else
                           Kb[:, oc - 4, 512 + nt * 512:512 + (nt + 1) * 512])
                    C.inc(v.tensor_copy(dst, P_A[:, nt * 512:(nt + 1) * 512]), "v")
                pa_free = C.cnt["v"]
                continue
            K_STEP = int(_os.environ.get("K_STEP", "99"))
            for nt in range(4):
                C.wait(s, "t")
                C.inc(s.activation(rt_q2[:, :], P_A[:, nt * 512:(nt + 1) * 512],
                                   AF.Square), "s")
                if K_STEP > 1:
                    C.wait(t, "s")
                    C.wait(t, "v")
                    C.inc(t.matmul(P_C[0:2, 0:512], ind2b[:, :], rt_q2[:, :],
                                   start=True, stop=True), "t")
                if K_STEP > 2:
                    C.wait(s, "t")
                    C.inc(s.activation(rt_ss[:, :], P_C[0:2, 0:512], AF.Ln,
                                       bias=EPS12), "s")
                    C.inc(s.activation(rt_ss[:, :], rt_ss[:, :], AF.Exp,
                                       scale=-0.5), "s")
                if K_STEP > 4:
                    C.wait(t, "s")
                    C.inc(t.matmul(P_C[:, 0:512], ind2f[:, :], rt_ss[0:2, :],
                                   start=True, stop=True), "t")
                if K_STEP > 5:
                    C.wait(v, "t")
                    C.inc(v.tensor_copy(rt_rb[:, :], P_C[:, 0:512]), "v")
                    C.inc(v.tensor_mul(rt_qn[:, :], P_A[:, nt * 512:(nt + 1) * 512],
                                       rt_rb[:, :]), "v")
                if K_STEP > 6:
                    C.wait(s, "v")
                    C.inc(s.copy(rt_qnb[:, :], rt_qn[:, :]), "s")
                if K_STEP > 7:
                    C.wait(t, "s")
                    C.inc(t.matmul(P_C[:, 0:512], rotT[:, :], rt_qnb[:, :],
                                   start=True, stop=True), "t")
                if K_STEP > 8:
                    C.wait(v, "t")
                    tci = tc_[:, tofs + nt * 512:tofs + (nt + 1) * 512]
                    tsi = ts_[:, tofs + nt * 512:tofs + (nt + 1) * 512]
                    C.inc(v.tensor_mul(rt_rb[:, :], rt_qn[:, :], tci), "v")
                if K_STEP > 9:
                    C.inc(v.tensor_copy(rt_t2[:, :], P_C[:, 0:512]), "v")
                    C.inc(v.tensor_mul(rt_qn[:, :], rt_t2[:, :], tsi), "v")
                if K_STEP > 10:
                    if K_SUB >= 2:
                        dst = (Qb[:, oc, nt * 512:(nt + 1) * 512] if isq else
                               Kb[:, oc - 4, 512 + nt * 512:512 + (nt + 1) * 512])
                        C.inc(v.tensor_add(dst, rt_rb[:, :], rt_qn[:, :]), "v")
            pa_free = C.cnt["v"]
        LAYER_STOP["qkv"] = True
        if stage == 10:
            return
        C.wait(g, "v")
        gdma(ccA_in.ap()[:, 0:2048].rearrange("p (c w) -> p c w", w=512),
             Kb[:, :, 2048:2560])
        gdma(ccA_in.ap()[:, 2048:2048 + VW],
             v_tm[:, 16:20, :, :].rearrange("p c h e -> p (c h e)"))
        C.wait(g, "g")
        C.inc(g.collective_compute("AllGather", ALU.bypass, replica_groups=PAIRS,
                                   ins=[ccA_in.ap().opt()],
                                   outs=[ccA_out.ap().opt()]), "c")
        C.wait(g, "c")
        oslc = ccA_out.ap()[bass.ds(oth_sv, 1), :, :]
        gdma(Kb[:, :, 0:512],
             oslc[:, :, 0:2048].rearrange("o p (c w) -> p (o c) w", w=512))
        gdma(v_tm[:, 0:4, :, :].rearrange("p c h e -> p (c h e)"),
             oslc[:, :, 2048:2048 + VW].rearrange("o p e -> p (o e)"))
        C.wait(v, "g")
        C.inc(v.tensor_scalar(Kb[:, :, 0:512], Kb[:, :, 0:512],
                              hzc[0:128, 0:1], None, ALU.mult), "v")
        C.inc(v.tensor_scalar(v_tm[:, 0:4, :, :], v_tm[:, 0:4, :, :],
                              hzc[0:128, 0:1], None, ALU.mult), "v")
        barrier(t, s, v)
        for hp in range(4):
            low = hp < 2
            cA, cB = 2 * hp, 2 * hp + 1
            if not low:
                cA, cB = 2 * hp - 4, 2 * hp - 3
            rA = slice(0, 64) if low else slice(64, 128)
            for w in range(NW):
                groups = [(0, [0, 1, 2]), (1, [3, 4, 5]), (2, [6, 7])]
                expA = [expb[0], expb[1]]
                expB = [expb[2], expb[3]]
                C.wait(t, "v")
                C.wait(t, "s")
                for gi, chunks in groups:
                    for i in chunks:
                        kcol = 512 * w + 128 * i
                        sA = P_A[:, (i - chunks[0]) * 512 + 0:
                                 (i - chunks[0]) * 512 + 512]
                        sB = P_B[:, (i - chunks[0]) * 512:
                                 (i - chunks[0]) * 512 + 512]
                        mtile = (maskslb[:, i * 512:(i + 1) * 512] if i < 4 else
                                 masksb[:, (i - 4) * 512:(i - 3) * 512])
                        t.matmul(sA, Kb[0:64, hp, kcol:kcol + 128],
                                 Qb[0:64, hp, w * 512:(w + 1) * 512],
                                 start=True, stop=False, tile_position=(0, 0))
                        t.matmul(sB, Kb[64:128, hp, kcol:kcol + 128],
                                 Qb[64:128, hp, w * 512:(w + 1) * 512],
                                 start=True, stop=False,
                                 tile_position=(64, 0))
                        t.matmul(sA, ident[:, :], mtile, start=False, stop=True)
                        mm = t.matmul(sB, ident[:, :], mtile, start=False, stop=True)
                    C.inc(mm, "t")
                    C.wait(s, "t")
                    ncl = len(chunks) * 512
                    for head, (S_, eb) in enumerate(((P_A, expA), (P_B, expB))):
                        ebuf = eb[gi % 2]
                        if w == 0 and chunks[0] < 4:
                            n_lb = sum(1 for i in chunks if i < 4) * 512
                            C.inc(s.activation(ebuf[:, 0:n_lb], S_[:, 0:n_lb],
                                               AF.Exp, bias=macE[0:128, 0:1],
                                               scale=8.0), "s")
                            if n_lb < ncl:
                                C.inc(s.activation(ebuf[:, n_lb:ncl],
                                                   S_[:, n_lb:ncl], AF.Exp,
                                                   bias=EBIA, scale=8.0), "s")
                        else:
                            C.inc(s.activation(ebuf[:, 0:ncl], S_[:, 0:ncl],
                                               AF.Exp, bias=EBIA, scale=8.0), "s")
                    C.wait(t, "s")
                    for i in chunks:
                        tc = 4 * w + i
                        co = (i - chunks[0]) * 512
                        st_, sp_ = (i == 0), (i == 7)
                        hA, hB = 2 * hp, 2 * hp + 1
                        if low:
                            t.matmul(P_A[0:65, 1536:2048],
                                     v_tm[:, tc, hA, 0:65],
                                     expA[gi % 2][:, co:co + 512],
                                     start=st_, stop=sp_)
                            mm = t.matmul(P_C[0:65, 0:512],
                                          v_tm[:, tc, hB, 0:65],
                                          expB[gi % 2][:, co:co + 512],
                                          start=st_, stop=sp_)
                        else:
                            t.matmul(P_A[64:128, 1536:2048],
                                     v_tm[:, tc, hA, 0:64],
                                     expA[gi % 2][:, co:co + 512],
                                     start=st_, stop=sp_, tile_position=(0, 64))
                            t.matmul(P_A[0:1, 1536:2048], onesb[:, :],
                                     expA[gi % 2][:, co:co + 512],
                                     start=st_, stop=sp_)
                            t.matmul(P_C[64:128, 0:512],
                                     v_tm[:, tc, hB, 0:64],
                                     expB[gi % 2][:, co:co + 512],
                                     start=st_, stop=sp_, tile_position=(0, 64))
                            mm = t.matmul(P_C[0:1, 0:512], onesb[:, :],
                                          expB[gi % 2][:, co:co + 512],
                                          start=st_, stop=sp_)
                    C.inc(mm, "t")
                qcols = slice(w * 512, (w + 1) * 512)
                if low:
                    attn_norm(True, P_A[0:64, 1536:2048], P_C[0:64, 0:512],
                              P_A[64:65, 1536:2048], P_C[64:65, 0:512],
                              o_fm[0:64, cA, qcols], o_fm[0:64, cB, qcols],
                              recip_sb, recbc)
                else:
                    attn_norm(False, P_A[64:128, 1536:2048], P_C[64:128, 0:512],
                              P_A[0:1, 1536:2048], P_C[0:1, 0:512],
                              o_fm[64:128, cA, qcols], o_fm[64:128, cB, qcols],
                              recip_sb, recbc)
        if stage == 11:
            return
        barrier(g)
        rearr_wo = wc_wo.ap()[l].rearrange("(kc p) o -> p kc o", p=128)
        C.wait(yq, "t")
        gdma(wos[:, 0:2, :], rearr_wo[:, 0:2, :])
        ydma(wos[:, 2:4, :], rearr_wo[:, 2:4, :], ch="q0")
        barrier(t)
        for oc in range(8):
            if oc >= 1:
                C.wait(t, "v")
            for nt in range(4):
                for kc in range(4):
                    mm = t.matmul(P_A[:, nt * 512:(nt + 1) * 512],
                                  wos[:, kc, oc * 128:(oc + 1) * 128],
                                  o_fm[:, kc, nt * 512:(nt + 1) * 512],
                                  start=(kc == 0), stop=(kc == 3))
            C.inc(mm, "t")
            C.wait(v, "t")
            C.inc(v.tensor_add(Y[:, oc, 1:1 + NT], Y[:, oc, 1:1 + NT], P_A[:, :]), "v")
        if stage == 12:
            return
        cc_tok()
        barrier(t, s, v)

        def ysrc(c):
            return Y[:, c, 1:1 + NT] if c < 4 else Y[:, c, 0:NT]
        ln_full(ysrc, "h", l=l)
        if stage == 13:
            return
        barrier(t, g, s, v)
        rearr1a = wc_w1a.ap()[l]
        rearr1g = wc_w1g.ap()[l]
        rearr2 = wc_w2.ap()[l].rearrange("(kc p) o -> p kc o", p=128)
        for tt in range(4):
            tcols = slice(tt * 512, (tt + 1) * 512)
            C.wait(g, "t")
            C.wait(yq, "t")
            gdma(wt1a[0][:, :, :], rearr1a[:, 0, :, :], ch="p0")
            ydma(wt1g[0][:, :, :], rearr1g[:, 0, :, :], ch="q0")
            w1mile = {0: 0, 1: 0}
            for j in range(NJ):
                bi = j % 2
                C.wait(t, "p0" if bi == 0 else "p1")
                C.wait(t, "q0" if bi == 0 else "q1")
                if j >= 2:
                    C.wait(t, "v")
                    C.wait(t, "s")
                pj = (j % 2) * 1024
                for kc in range(NCH):
                    t.matmul(P_A[:, pj:pj + 512], wt1a[bi][:, kc, :],
                             Hb[:, kc, tcols], start=(kc == 0), stop=(kc == 7))
                for kc in range(NCH):
                    mm = t.matmul(P_A[:, pj + 512:pj + 1024], wt1g[bi][:, kc, :],
                                  Hb[:, kc, tcols], start=(kc == 0), stop=(kc == 7))
                C.inc(mm, "t")
                w1mile[bi] = C.cnt["t"]
                if j + 1 < NJ:
                    C.wait(g, "t", w1mile[(j + 1) % 2])
                    C.wait(yq, "t", w1mile[(j + 1) % 2])
                    pch = "p0" if (j + 1) % 2 == 0 else "p1"
                    qch = "q0" if (j + 1) % 2 == 0 else "q1"
                    gdma(wt1a[(j + 1) % 2][:, :, :],
                         rearr1a[:, j + 1, :, :], ch=pch)
                    ydma(wt1g[(j + 1) % 2][:, :, :],
                         rearr1g[:, j + 1, :, :], ch=qch)
                C.wait(s, "t")
                C.inc(s.activation(tnm[:, 0:512], P_A[:, pj + 512:pj + 1024],
                                   AF.Gelu, bias=b1g_c[0:128, l * NJ + j:l * NJ + j + 1]),
                      "s")
                C.wait(v, "s")
                C.inc(v.scalar_tensor_tensor(
                    AGb[:, j, :], P_A[:, pj:pj + 512],
                    b1a_c[0:128, l * NJ + j:l * NJ + j + 1], tnm[:, 0:512],
                    ALU.add, ALU.mult), "v")
            barrier(t, g)
            C.wait(yq, "t")
            gdma(wt2[0][:, 0:512], rearr2[:, 0, 0:512], ch="r0")
            ydma(wt2[0][:, 512:1024], rearr2[:, 0, 512:1024], ch="q0")
            slots = [P_A[:, i * 512:(i + 1) * 512] for i in range(4)] + \
                    [P_B[:, i * 512:(i + 1) * 512] for i in range(3)] + [P_C[:, 0:512]]
            w2mile = {0: 0, 1: 0}
            for kc in range(NJ):
                C.wait(t, "r0" if kc % 2 == 0 else "r1")
                C.wait(t, "q0" if kc % 2 == 0 else "q1")
                for oc in range(8):
                    mm = t.matmul(slots[oc], wt2[kc % 2][:, oc * 128:(oc + 1) * 128],
                                  AGb[:, kc, :], start=(kc == 0), stop=(kc == NJ - 1))
                C.inc(mm, "t")
                w2mile[kc % 2] = C.cnt["t"]
                if kc + 1 < NJ:
                    C.wait(g, "t", w2mile[(kc + 1) % 2])
                    C.wait(yq, "t", w2mile[(kc + 1) % 2])
                    gdma(wt2[(kc + 1) % 2][:, 0:512], rearr2[:, kc + 1, 0:512],
                         ch="r0" if (kc + 1) % 2 == 0 else "r1")
                    ydma(wt2[(kc + 1) % 2][:, 512:1024],
                         rearr2[:, kc + 1, 512:1024],
                         ch="q0" if (kc + 1) % 2 == 0 else "q1")
            C.wait(v, "t")
            for oc in range(8):
                C.inc(v.scalar_tensor_tensor(
                    Y[:, oc, 1 + tt * 512:1 + (tt + 1) * 512], slots[oc],
                    b2_c[0:128, l * 8 + oc:l * 8 + oc + 1],
                    Y[:, oc, 1 + tt * 512:1 + (tt + 1) * 512],
                    ALU.add, ALU.add), "v")
            barrier(t)
        if first_of_call:
            cc_tok()

    def dense_phase():
        def xsrc(c):
            return X[:, c, :]
        ln_full(xsrc, "xlb")
        barrier(g)
        gdma(ccD_in.ap(), XLb[:, :, :].rearrange("p c t -> p (c t)"))
        C.wait(g, "g")
        C.inc(g.collective_compute("AllGather", ALU.bypass, replica_groups=PAIRS,
                                   ins=[ccD_in.ap().opt()],
                                   outs=[ccD_out.ap().opt()]), "c")
        C.wait(g, "c")
        gdma(XLo[:, :, :].rearrange("p c t -> p (c t)"),
             ccD_out.ap()[bass.ds(oth_sv, 1), :, :].rearrange("o p e -> p (o e)"))
        C.inc(v.memset(Vd[:, :, 64:65], 1.0), "v")
        C.inc(v.memset(Vd[:, :, 129:130], 1.0), "v")
        barrier(t, s, v)
        import os as _os
        K_NP = int(_os.environ.get("K_NP", "8"))
        for c in range(K_NP):
            low = c < 4
            cA, cB = (2 * c, 2 * c + 1) if low else (2 * c - 8, 2 * c - 7)
            C.wait(t, "v")
            C.wait(t, "s")
            for nt in range(4):
                C.inc(t.matmul(P_A[:, nt * 512:(nt + 1) * 512], blkq[:, :],
                               XLb[:, c, nt * 512:(nt + 1) * 512],
                               start=True, stop=True), "t")
            C.wait(s, "t")
            C.inc(s.copy(Qd[:, :], P_A[:, :]), "s")
            C.wait(t, "s")
            for half, src in ((0, XLb), (1, XLo)):
                for nt in range(4):
                    C.inc(t.matmul(P_A[:, nt * 512:(nt + 1) * 512], blkk[:, :],
                                   src[:, c, nt * 512:(nt + 1) * 512],
                                   start=True, stop=True), "t")
                C.wait(s, "t")
                C.inc(s.copy(Kd[:, half * 2048:(half + 1) * 2048], P_A[:, :]), "s")
                C.wait(t, "s")
                lastv = {0: 0, 1: 0}
                for ttl in range(16):
                    ps = P_C[:, 0:512] if (ttl % 2 == 0) else P_B[:, 0:512]
                    psv = ps[0:128, 0:128]
                    if ttl >= 2:
                        C.wait(t, "v", lastv[ttl % 2])
                    C.inc(t.matmul(psv, src[:, c, ttl * 128:(ttl + 1) * 128],
                                   blkv[:, :], start=True, stop=True), "t")
                    C.wait(v, "t")
                    C.inc(v.tensor_copy(
                        Vd[:, half * 16 + ttl, 0:130].rearrange(
                            "p (h e) -> p h e", e=65)[:, :, 0:64],
                        psv.rearrange("p (h e) -> p h e", e=64)), "v")
                    lastv[ttl % 2] = C.cnt["v"]
            barrier(t, s, v)
            sets = [P_A[:, 0:1024], P_A[:, 1024:2048], P_B[:, 0:1024]]
            for qt in range(4):
                qcols = slice(qt * 512, (qt + 1) * 512)
                C.wait(t, "v")
                C.wait(t, "s")
                expmile, avmile = {}, {}
                for i in range(33):
                    if i < 32:
                        m = i % 3
                        sp_ = sets[m]
                        C.wait(t, "s", expmile.get(i - 3, 0))
                        t.matmul(sp_[0:128, 0:512],
                                 Kd[0:64, i * 128:(i + 1) * 128],
                                 Qd[0:64, qcols], start=True, stop=True,
                                 tile_position=(0, 0))
                        mm = t.matmul(sp_[0:128, 512:1024],
                                      Kd[64:128, i * 128:(i + 1) * 128],
                                      Qd[64:128, qcols], start=True, stop=True,
                                      tile_position=(64, 0))
                        C.inc(mm, "t")
                        qkm = C.cnt["t"]
                    if i >= 1:
                        j = i - 1
                        eb = expd[j % 2]
                        st_, sp2 = (j == 0), (j == 31)
                        C.wait(t, "s", expmile[j])
                        if low:
                            t.matmul(P_B[0:65, 1024:1536], Vd[:, j, 0:65],
                                     eb[:, 0:512], start=st_, stop=sp2)
                            mm = t.matmul(P_C[0:65, 0:512], Vd[:, j, 65:130],
                                          eb[:, 512:1024], start=st_, stop=sp2)
                        else:
                            t.matmul(P_B[64:128, 1024:1536], Vd[:, j, 0:64],
                                     eb[:, 0:512], start=st_, stop=sp2,
                                     tile_position=(0, 64))
                            t.matmul(P_B[0:1, 1024:1536], onesb[:, :],
                                     eb[:, 0:512], start=st_, stop=sp2)
                            t.matmul(P_C[64:128, 0:512], Vd[:, j, 65:129],
                                     eb[:, 512:1024], start=st_, stop=sp2,
                                     tile_position=(0, 64))
                            mm = t.matmul(P_C[0:1, 0:512], onesb[:, :],
                                          eb[:, 512:1024], start=st_, stop=sp2)
                        C.inc(mm, "t")
                        avmile[j] = C.cnt["t"]
                    if i < 32:
                        C.wait(s, "t", qkm)
                        if i >= 2:
                            C.wait(s, "t", avmile[i - 2])
                        C.inc(s.activation(expd[i % 2][:, 0:1024], sets[i % 3],
                                           AF.Exp, bias=EBIA,
                                           scale=1.0 / 32.0), "s")
                        expmile[i] = C.cnt["s"]
                if low:
                    attn_norm(True, P_B[0:64, 1024:1536], P_C[0:64, 0:512],
                              P_B[64:65, 1024:1536], P_C[64:65, 0:512],
                              o_dn[0:64, cA, qcols], o_dn[0:64, cB, qcols],
                              recip_d, recbc_d)
                else:
                    attn_norm(False, P_B[64:128, 1024:1536], P_C[64:128, 0:512],
                              P_B[0:1, 1024:1536], P_C[0:1, 0:512],
                              o_dn[64:128, cA, qcols], o_dn[64:128, cB, qcols],
                              recip_d, recbc_d)
        barrier(g, t, s, v, yq)
        rearr_wout = wc_woutT.ap().rearrange("(kc p) o -> p kc o", p=128)
        gdma(wouts[:, 0:4, :], rearr_wout[:, 0:4, :])
        ydma(wouts[:, 4:8, :], rearr_wout[:, 4:8, :], ch="q0")
        barrier(t)
        for oc in range(8):
            if oc >= 1:
                C.wait(t, "v")
            for nt in range(4):
                for kc in range(8):
                    mm = t.matmul(P_A[:, nt * 512:(nt + 1) * 512],
                                  wouts[:, kc, oc * 128:(oc + 1) * 128],
                                  o_dn[:, kc, nt * 512:(nt + 1) * 512],
                                  start=(kc == 0), stop=(kc == 7))
            C.inc(mm, "t")
            C.wait(v, "t")
            C.inc(v.scalar_tensor_tensor(
                X[:, oc, :], P_A[:, :], bout_c[0:128, oc:oc + 1], X[:, oc, :],
                ALU.add, ALU.add), "v")

    if stage <= 1:
        return finish()
    if stage == 85:
        barrier(t, s, v, g, yq)
        gdma(xout[:, 0:1536], wc_qkv.ap()[0][0:128, :])
        gdma(xout[:, 2048:2048 + 1536], wc_qkv.ap()[0][512:640, :])
        gdma(xout[:, 4096:4096 + 2816], wc_w1a.ap()[0][0:128, :])
        gdma(xout[:, 8192:8192 + 1024], wc_w2.ap()[0][0:128, :])
        barrier(g, t, s, v, yq)
        return nc, dbg_outs
    for call in range(2):
        def xsrc(c):
            return X[:, c, :]
        ln_full(xsrc, "y")
        if stage <= 2:
            return finish_y()
        cc_tok()
        if stage <= 3:
            return finish_y()
        barrier(yq)
        ydma(x_spill.ap(), X[:, :, :].rearrange("p c t -> p (c t)"))
        barrier(t, s, v, g)
        for li, l in enumerate((0, 1)):
            ffn_layer(l, first_of_call=(li == 0))
            if stage == 87:
                return nc, dbg_outs
            if stage in (8, 9, 10, 11, 12, 13):
                barrier(t, s, v, g, yq)
                if stage == 10:
                    import os as _os
                    if _os.environ.get("K_NODUMP"):
                        gdma(xout.rearrange("p (c t) -> p c t", t=NT),
                             Y[:, :, 1:1 + NT])
                    else:
                        gdma(xout[:, 0:8192].rearrange("p (c t) -> p c t", t=2048),
                             Qb[:, :, :])
                        for cc_ in range(4):
                            gdma(xout[:, 8192 + cc_ * 2048:8192 + (cc_ + 1) * 2048],
                                 Kb[:, cc_, 512:2560])
                elif stage == 11:
                    gdma(xout[:, 0:8192].rearrange("p (c t) -> p c t", t=2048),
                         o_fm[:, :, :])
                elif stage in (8, 9, 12):
                    gdma(xout.rearrange("p (c t) -> p c t", t=NT), Y[:, :, 1:1 + NT])
                elif stage == 13:
                    gdma(xout, Hb[:, :, :].rearrange("p c t -> p (c t)"))
                barrier(g, t, s, v, yq)
                return nc, dbg_outs
            if stage <= 4 and li == 0:
                barrier(t, s, v, g, yq)
                gdma(xout.rearrange("p (c t) -> p c t", t=NT), Y[:, :, 1:1 + NT])
                barrier(g, t, s, v, yq)
                return nc, dbg_outs
            if stage <= 5 and li == 1:
                barrier(t, s, v, g, yq)
                gdma(xout.rearrange("p (c t) -> p c t", t=NT), Y[:, :, 1:1 + NT])
                barrier(g, t, s, v, yq)
                return nc, dbg_outs
        barrier(g, yq)
        gdma(X[:, :, :], x_spill.ap().rearrange("p (c t) -> p c t", t=NT))
        C.wait(v, "g")
        for c in range(NCH):
            C.inc(v.scalar_tensor_tensor(
                X[:, c, :], Y[:, c, 1:1 + NT], 0.5, X[:, c, :],
                ALU.mult, ALU.add), "v")
        if stage <= 6:
            return finish()
        if call == 0:
            barrier(t, s, v, g, yq)
            dense_phase()
            if stage <= 7:
                return finish()
        barrier(t, s, v, g, yq)

    return finish()


_CACHE = {}


def _host_inputs(inputs):
    x = np.asarray(inputs["L0_attn_x"])
    Wq = np.asarray(inputs["L0_attn_Wq"])
    Wk = np.asarray(inputs["L0_attn_Wk"])
    Wv = np.asarray(inputs["L0_attn_Wv"])
    W_out = np.asarray(inputs["L0_attn_W_out"])
    b_out = np.asarray(inputs["L0_attn_b_out"])
    wqkv = np.asarray(inputs["L0_ffn_wqkv"])
    q_scale = np.asarray(inputs["L0_ffn_q_scale"])
    k_scale = np.asarray(inputs["L0_ffn_k_scale"])
    wo = np.asarray(inputs["L0_ffn_wo"])
    ln_g = np.asarray(inputs["L0_ffn_ln_g"])
    ln_b = np.asarray(inputs["L0_ffn_ln_b"])
    w1 = np.asarray(inputs["L0_ffn_w1"])
    b1 = np.asarray(inputs["L0_ffn_b1"])
    w2 = np.asarray(inputs["L0_ffn_w2"])
    b2 = np.asarray(inputs["L0_ffn_b2"])

    f32 = np.float32
    com = {}
    com["wqkv"] = np.ascontiguousarray(wqkv, dtype=f32)
    com["wo_p"] = np.ascontiguousarray(wo[:, PERM_WO, :], dtype=f32)
    w1a = np.zeros((2, D, HIDP), f32)
    w1g = np.zeros((2, D, HIDP), f32)
    w1a[:, :, :HID] = w1[:, :, :HID]
    w1g[:, :, :HID] = w1[:, :, HID:]
    com["w1a"], com["w1g"] = w1a, w1g
    w2pp = np.zeros((2, HIDP, D), f32)
    w2pp[:, :HID, :] = w2
    com["w2p"] = w2pp
    com["woutT"] = np.ascontiguousarray(W_out.T[PERM_WOUT, :], dtype=f32)
    com["wq_t"] = np.ascontiguousarray(Wq.T, dtype=f32)
    com["wk_t"] = np.ascontiguousarray(Wk.T, dtype=f32)
    com["wv_t"] = np.ascontiguousarray(Wv.T, dtype=f32)
    com["b_out_r"] = np.ascontiguousarray(b_out.reshape(8, 128).T, dtype=f32)
    com["ln_g_r"] = np.ascontiguousarray(ln_g.reshape(2, 8, 128).swapaxes(1, 2), dtype=f32)
    com["ln_b_r"] = np.ascontiguousarray(ln_b.reshape(2, 8, 128).swapaxes(1, 2), dtype=f32)
    com["b2_r"] = np.ascontiguousarray(b2.reshape(2, 8, 128).swapaxes(1, 2), dtype=f32)
    b1a = np.zeros((2, 128, NJ), f32)
    b1g = np.zeros((2, 128, NJ), f32)
    for l in range(2):
        pa = np.zeros(HIDP, f32)
        pg = np.zeros(HIDP, f32)
        pa[:HID] = b1[l, :HID]
        pg[:HID] = b1[l, HID:]
        b1a[l] = pa.reshape(NJ, 128).T
        b1g[l] = pg.reshape(NJ, 128).T
    com["b1a_r"], com["b1g_r"] = b1a, b1g
    perm64 = (np.arange(64) + 32) % 64
    com["qs_r"] = np.ascontiguousarray(
        np.tile(q_scale, (1, 2)).reshape(2, 128, 1), dtype=f32)
    com["qsp_r"] = np.ascontiguousarray(
        np.tile(q_scale[:, perm64], (1, 2)).reshape(2, 128, 1), dtype=f32)
    com["ks_r"] = np.ascontiguousarray(
        np.tile(k_scale, (1, 2)).reshape(2, 128, 1), dtype=f32)
    com["ksp_r"] = np.ascontiguousarray(
        np.tile(k_scale[:, perm64], (1, 2)).reshape(2, 128, 1), dtype=f32)

    in_maps = []
    for c in range(NCORE):
        b, h = c // 2, c % 2
        t0 = h * NT
        m = dict(com)
        m["xin"] = _pack_fm(np.asarray(x[b, t0:t0 + NT, :], dtype=f32))
        tc_, ts_, tic, tis = _rot_tables(t0)
        m["tab_cos"], m["tab_sin"] = tc_, ts_
        m["tab_icos"], m["tab_isin"] = tic, tis
        m["hz"] = np.full((128, 1), 0.0 if h == 0 else 1.0, f32)
        m["maskA"] = np.full((128, 1), NEG if h == 0 else 0.0, f32)
        in_maps.append(m)
    return in_maps


def kernel(**inputs):
    import os
    stg = int(os.environ.get("K_STAGE", "99"))
    key = ("prog", stg)
    if key not in _CACHE:
        _CACHE[key] = build_program(stage=stg)[0]
    nc = _CACHE[key]
    in_maps = _host_inputs(inputs)
    res = run_bass_kernel_spmd(nc, in_maps, core_ids=list(range(NCORE)))
    out = np.empty((B, S, D), dtype=np.float32)
    for c in range(NCORE):
        b, h = c // 2, c % 2
        fm = res.results[c]["xout"]
        out[b, h * NT:(h + 1) * NT, :] = (
            fm.reshape(128, NCH, NT).transpose(1, 0, 2).reshape(D, NT).T)
    return out


# version 04
# speedup vs baseline: 1.2888x; 1.1035x over previous
import numpy as np
import ml_dtypes
import concourse.bass as bass
import concourse.mybir as mybir
from concourse.bass_utils import run_bass_kernel_spmd

F32 = mybir.dt.float32
BF16 = mybir.dt.bfloat16
AF = mybir.ActivationFunctionType
ALU = mybir.AluOpType
BF = ml_dtypes.bfloat16

NCORE = 8
B, S, D = 4, 4096, 1024
NT = 2048
NCH = 8
YW = 2056
HL, DH, WIN, NW = 8, 64, 512, 4
HID, HIDP, NJ = 2730, 2816, 22
EXP_BIAS = -40.0
NEG = -30000.0
PAIRS = [[0, 1], [2, 3], [4, 5], [6, 7]]
PERM_WO = np.array([(c + 4 * (r // 64)) * 64 + (r % 64)
                    for c in range(4) for r in range(128)])
PERM_WOUT = np.array([(c + 8 * (r // 64)) * 64 + (r % 64)
                      for c in range(8) for r in range(128)])
PERM_ROT = np.array([(d // 64) * 64 + ((d % 64) + 32) % 64 for d in range(128)])

A_X = 0
A_Y = 66048
A_YB = 0
A_WT = 33024
A_WV = 37120
A_VT = 41216
A_QB = 131840
A_KB = 148224
A_TAB = 168704
A_RAW = 187136
A_RT = 192256
A_EXP = 0
A_OFM = 12288
A_RCP = 28672
A_RBC = 32768
A_WOS = 168704
A_HB = 0
A_AGB = 131840
A_W1T = 154368
A_W2T = 162560
A_LT = 166656
A_XLB = 66048
A_XLO = 98816
A_ODN = 131840
A_QD = 164608
A_KD = 168704
A_VD = 176896
A_EXD = 185216
A_RCD = 191360
A_RBD = 195456
A_WOUT = 66048
ARENA_B = 200448


def _rot_tables(t0):
    half_idx = np.arange(0, DH, 2, dtype=np.float64)
    inv_freq = 1.0 / 10000.0 ** (half_idx / DH)
    tpos = np.arange(t0 - 512, t0 + 2048, dtype=np.float64)
    fr = tpos[:, None] * inv_freq[None, :]
    freqs = np.concatenate([fr, fr], axis=1)
    base = (half_idx + 0.4 * DH) / (1.4 * DH)
    power = (tpos - S // 2) / float(WIN // 2)
    sc = base[None, :] ** power[:, None]
    scale = np.concatenate([sc, sc], axis=1)
    cos, sin = np.cos(freqs), np.sin(freqs)
    out = []
    for arr in (cos * scale, sin * scale, cos / scale, sin / scale):
        a = arr.T.astype(np.float32)
        out.append(np.ascontiguousarray(np.concatenate([a, a], axis=0)))
    return out


def _pack_fm(xs):
    T = xs.shape[0]
    return np.ascontiguousarray(
        xs.T.reshape(NCH, 128, T).transpose(1, 0, 2).reshape(128, NCH * T))


class Ctr:
    def __init__(self, nc):
        self.sem, self.cnt = {}, {}
        for n in ("g", "y", "t", "s", "v", "c", "p0", "p1", "q0", "q1", "r0", "r1"):
            self.sem[n] = nc.alloc_semaphore("sem_" + n)
            self.cnt[n] = 0

    def inc(self, inst, name, amt=1):
        inst.then_inc(self.sem[name], amt)
        self.cnt[name] += amt
        return inst

    def wait(self, eng, name, val=None):
        vv = self.cnt[name] if val is None else val
        if vv > 0:
            eng.wait_ge(self.sem[name], vv)


def build_program(stage=99, dbg=()):
    nc = bass.Bass("TRN2", target_bir_lowering=False, debug=False,
                   detect_race_conditions=False)

    def inp(name, shape, dt=F32):
        return nc.dram_tensor(name, list(shape), dt, kind="ExternalInput").ap()

    xin = inp("xin", [128, NCH * NT])
    wqkv = inp("wqkv", [2, D, 3 * 512])
    wo_p = inp("wo_p", [2, 512, D])
    w1a = inp("w1a", [2, D, HIDP])
    w1g = inp("w1g", [2, D, HIDP])
    w2p = inp("w2p", [2, HIDP, D])
    woutT = inp("woutT", [D, D])
    wq_t = inp("wq_t", [DH, DH])
    wk_t = inp("wk_t", [DH, DH])
    wv_t = inp("wv_t", [DH, DH])
    b_out_r = inp("b_out_r", [128, 8])
    ln_g_r = inp("ln_g_r", [2, 128, 8])
    ln_b_r = inp("ln_b_r", [2, 128, 8])
    b2_r = inp("b2_r", [2, 128, 8])
    b1a_r = inp("b1a_r", [2, 128, NJ])
    b1g_r = inp("b1g_r", [2, 128, NJ])
    qs_r = inp("qs_r", [2, 128, 1])
    qsp_r = inp("qsp_r", [2, 128, 1])
    ks_r = inp("ks_r", [2, 128, 1])
    ksp_r = inp("ksp_r", [2, 128, 1])
    tab_cos = inp("tab_cos", [128, 2560])
    tab_sin = inp("tab_sin", [128, 2560])
    tab_icos = inp("tab_icos", [128, 2560])
    tab_isin = inp("tab_isin", [128, 2560])
    hz = inp("hz", [128, 1])
    maskA = inp("maskA", [128, 1])

    xout = nc.dram_tensor("xout", [128, NCH * NT], F32, kind="ExternalOutput").ap()
    dbg_outs = {}
    for name, shape, dt in dbg:
        dbg_outs[name] = nc.dram_tensor("dbg_" + name, list(shape), dt,
                                        kind="ExternalOutput").ap()

    ident_d = nc.inline_tensor(np.eye(128, dtype=BF), name="c_ident")
    onesb_d = nc.inline_tensor(np.ones((128, 1), dtype=BF), name="c_onesb")
    onesrow_d = nc.inline_tensor(np.ones((1, 128), dtype=np.float32), name="c_onesrow")
    ones64_d = nc.inline_tensor(np.ones((128, 64), dtype=np.float32), name="c_ones64")
    ind2np = np.zeros((2, 128), dtype=np.float32)
    ind2np[0, :64] = 1.0
    ind2np[1, 64:] = 1.0
    ind2b_d = nc.inline_tensor(np.ascontiguousarray(ind2np.T).astype(BF), name="c_ind2b")
    ind2f_d = nc.inline_tensor(ind2np, name="c_ind2f")
    R64 = np.zeros((64, 64), dtype=np.float64)
    R64[np.arange(32), np.arange(32) + 32] = -1.0
    R64[np.arange(32, 64), np.arange(32)] = 1.0
    R128 = np.zeros((128, 128), dtype=np.float64)
    R128[0:64, 0:64] = R64
    R128[64:128, 64:128] = R64
    rotT_d = nc.inline_tensor(np.ascontiguousarray(R128.T).astype(BF), name="c_rotT")
    mk = np.zeros((128, 4, 512), dtype=np.float64)
    kk = np.arange(128)[:, None]
    qq = np.arange(512)[None, :]
    for jb in range(4):
        mk[:, jb, :] = np.where(qq < 128 * jb + kk, NEG, 0.0)
    masks_d = nc.inline_tensor(
        np.ascontiguousarray(mk.reshape(128, 2048)).astype(BF), name="c_masks")
    mlb = np.zeros((128, 4, 512), dtype=np.float64)
    for i_ in range(4):
        mlb[:, i_, :] = np.where(qq > 128 * i_ + kk, NEG, 0.0)
    masks_lb_d = nc.inline_tensor(
        np.ascontiguousarray(mlb.reshape(128, 2048)).astype(BF), name="c_masks_lb")
    cp = np.zeros((128, 8), dtype=np.float32)
    cp[:, 0] = 1e-5
    cp[:, 1] = EXP_BIAS
    cp[:, 2] = 1e-12
    cpool_d = nc.inline_tensor(cp, name="c_cpool")

    wc_qkv = nc.dram_tensor("wc_qkv", [2, D, 3 * 512], BF16, kind="Internal")
    wc_wo = nc.dram_tensor("wc_wo", [2, 512, D], BF16, kind="Internal")
    wc_w1a = nc.dram_tensor("wc_w1a", [2, 128, NJ, 8, 128], BF16, kind="Internal")
    wc_w1g = nc.dram_tensor("wc_w1g", [2, 128, NJ, 8, 128], BF16, kind="Internal")
    wc_w2 = nc.dram_tensor("wc_w2", [2, HIDP, D], BF16, kind="Internal")
    wc_woutT = nc.dram_tensor("wc_woutT", [D, D], BF16, kind="Internal")
    x_spill = nc.dram_tensor("x_spill", [128, NCH * NT], F32, kind="Internal")
    ccT_in = nc.dram_tensor("ccT_in", [128, 8], F32, kind="Internal")
    ccT_out = nc.dram_tensor("ccT_out", [2, 128, 8], F32, kind="Internal")
    VW = 4 * HL * 65
    ccA_in = nc.dram_tensor("ccA_in", [128, 2048 + VW], BF16, kind="Internal")
    ccA_out = nc.dram_tensor("ccA_out", [2, 128, 2048 + VW], BF16, kind="Internal")
    ccD_in = nc.dram_tensor("ccD_in", [128, NCH * NT], BF16, kind="Internal")
    ccD_out = nc.dram_tensor("ccD_out", [2, 128, NCH * NT], BF16, kind="Internal")

    ident = nc.alloc_sbuf_tensor("ident", [128, 128], BF16)
    onesb = nc.alloc_sbuf_tensor("onesb", [128, 1], BF16)
    onesrowf = nc.alloc_sbuf_tensor("onesrowf", [1, 128], F32)
    ones64f = nc.alloc_sbuf_tensor("ones64f", [128, 64], F32)
    ind2b = nc.alloc_sbuf_tensor("ind2b", [128, 2], BF16)
    ind2f = nc.alloc_sbuf_tensor("ind2f", [2, 128], F32)
    rotT = nc.alloc_sbuf_tensor("rotT", [128, 128], BF16)
    masksb = nc.alloc_sbuf_tensor("masksb", [128, 2048], BF16)
    maskslb = nc.alloc_sbuf_tensor("maskslb", [128, 2048], BF16)
    cpool = nc.alloc_sbuf_tensor("cpool", [128, 8], F32)
    blkq = nc.alloc_sbuf_tensor("blkq", [128, 128], BF16)
    blkk = nc.alloc_sbuf_tensor("blkk", [128, 128], BF16)
    blkv = nc.alloc_sbuf_tensor("blkv", [128, 128], BF16)
    bout_c = nc.alloc_sbuf_tensor("bout_c", [128, 8], F32)
    lng_c = nc.alloc_sbuf_tensor("lng_c", [128, 16], F32)
    lnb_c = nc.alloc_sbuf_tensor("lnb_c", [128, 16], F32)
    b2_c = nc.alloc_sbuf_tensor("b2_c", [128, 16], F32)
    b1a_c = nc.alloc_sbuf_tensor("b1a_c", [128, 2 * NJ], F32)
    b1g_c = nc.alloc_sbuf_tensor("b1g_c", [128, 2 * NJ], F32)
    qsc = nc.alloc_sbuf_tensor("qsc", [128, 8], F32)
    hzc = nc.alloc_sbuf_tensor("hzc", [128, 1], F32)
    mac = nc.alloc_sbuf_tensor("mac", [128, 1], F32)
    macE = nc.alloc_sbuf_tensor("macE", [128, 1], F32)

    arena = nc.alloc_sbuf_tensor("arena", [128, ARENA_B], mybir.dt.uint8)
    ab = nc.lookup_mloc(arena).addr
    rem = nc.sbuf_bytes_remaining
    assert rem >= 0, rem
    _n = [0]

    def place(off, shape, dtype):
        per = int(np.prod(shape[1:])) * mybir.dt.size(dtype)
        assert off % 32 == 0 and off + per <= ARENA_B, (off, per)
        _n[0] += 1
        h = nc.alloc_sbuf_tensor_at(f"ar{_n[0]}", list(shape), dtype,
                                    offset=ab + off)
        if not hasattr(nc, "_arena_map"):
            nc._arena_map = {}
        nc._arena_map.setdefault(off, []).append(h.name)
        return h

    X = place(A_X, [128, NCH, NT], F32)
    Y = place(A_Y, [128, NCH, YW], F32)
    Yb = place(A_YB, [128, NCH, YW], BF16)
    wtq = [place(A_WT + i * 2048, [128, 8, 128], BF16) for i in range(2)]
    wv_sb = place(A_WV, [128, 8, 256], BF16)
    v_tm = place(A_VT, [128, 20, HL, 65], BF16)
    Qb = place(A_QB, [128, 4, 2048], BF16)
    Kb = place(A_KB, [128, 4, 2560], BF16)
    tqc = place(A_TAB, [128, 2048], BF16)
    tqs = place(A_TAB + 4096, [128, 2048], BF16)
    tkc = place(A_TAB + 8192, [128, 2560], BF16)
    tks = place(A_TAB + 13312, [128, 2560], BF16)
    rawt = place(A_RAW, [128, 2560], BF16)
    rt_q2 = place(A_RT, [128, 512], BF16)
    rt_rb = place(A_RT + 1024, [128, 512], F32)
    rt_qn = place(A_RT + 3072, [128, 512], F32)
    rt_qnb = place(A_RT + 5120, [128, 512], BF16)
    rt_t2 = place(A_RT + 6144, [128, 512], F32)
    rt_ss = place(A_RAW, [2, 512], F32)
    expb = [place(A_EXP + i * 3072, [128, 1536], BF16) for i in range(4)]
    o_fm = place(A_OFM, [128, 4, 2048], BF16)
    recip_sb = place(A_RCP, [128, 1024], F32)
    recbc = place(A_RBC, [128, 1024], F32)
    wos = place(A_WOS, [128, 4, 1024], BF16)
    Hb = place(A_HB, [128, NCH, NT], BF16)
    AGb = place(A_AGB, [128, NJ, 512], BF16)
    wt1a = [place(A_W1T + i * 2048, [128, 8, 128], BF16) for i in range(2)]
    wt1g = [place(A_W1T + 4096 + i * 2048, [128, 8, 128], BF16) for i in range(2)]
    wt2 = [place(A_W2T + i * 2048, [128, 1024], BF16) for i in range(2)]
    tXb = place(A_LT, [128, NT], BF16)
    tSQb = place(A_LT + 4096, [128, NT], BF16)
    tnm = place(A_LT + 8192, [128, NT], F32)
    trs = place(A_LT + 16384, [128, NT], F32)
    r_m = place(A_LT + 24576, [1, 512], F32)
    r_q = place(A_LT + 26624, [1, 512], F32)
    r_v = place(A_LT + 28672, [1, 512], F32)
    r_s = place(A_LT + 30720, [1, 512], F32)
    XLb = place(A_XLB, [128, NCH, NT], BF16)
    XLo = place(A_XLO, [128, NCH, NT], BF16)
    o_dn = place(A_ODN, [128, NCH, NT], BF16)
    Qd = place(A_QD, [128, 2048], BF16)
    Kd = place(A_KD, [128, 4096], BF16)
    Vd = place(A_VD, [128, 32, 130], BF16)
    expd = [place(A_EXD + i * 3072, [128, 1536], BF16) for i in range(2)]
    recip_d = place(A_RCD, [128, 1024], F32)
    recbc_d = place(A_RBD, [128, 1024], F32)
    wouts = place(A_WOUT, [128, 8, 1024], BF16)

    P_A = nc.alloc_psum_tensor("P_A", [128, 2048], F32)
    P_B = nc.alloc_psum_tensor("P_B", [128, 1536], F32)
    P_C = nc.alloc_psum_tensor("P_C", [128, 512], F32)

    C = Ctr(nc)
    g, t, s, v, yq = nc.gpsimd, nc.tensor, nc.scalar, nc.vector, nc.sync

    def gdma(out, in_, slow=False, ch="g"):
        return C.inc(g.dma_start(out=out, in_=in_,
                                 allow_slow_non_contiguous=slow), ch, 16)

    def ydma(out, in_, slow=False, ch="y"):
        return C.inc(yq.dma_start(out=out, in_=in_,
                                  allow_slow_non_contiguous=slow), ch, 16)

    def barrier(*engs):
        for e in engs:
            for nm in ("g", "y", "t", "s", "v", "c", "p0", "p1", "q0", "q1", "r0", "r1"):
                C.wait(e, nm)

    EPS1 = cpool[0:1, 0:1]
    EBIA = cpool[0:128, 1:2]
    EPS12 = cpool[0:2, 2:3]

    def finish_y():
        barrier(t, s, v, g, yq)
        ydma(xout.rearrange("p (c t) -> p c t", t=NT), Y[:, :, 1:1 + NT])
        barrier(g, t, s, v, yq)
        return nc, dbg_outs

    def finish(ap=None, cast=False):
        barrier(t, s, v, g, yq)
        if ap is None:
            ap = X[:, :, :].rearrange("p c t -> p (c t)")
        if cast:
            gdma(xout, ap)
        else:
            ydma(xout, ap)
        barrier(g, t, s, v, yq)
        return nc, dbg_outs

    pid = g.partition_id()
    oth_reg = g.alloc_register("oth")
    g.reg_alu(oth_reg, pid, 2, ALU.mod)
    g.reg_alu(oth_reg, oth_reg, 1, ALU.bitwise_xor)
    oth_sv = g.snap(oth_reg, min_val=0, max_val=1)

    gdma(X[:, :, :], xin.rearrange("p (c t) -> p c t", t=NT))
    for dst, src in [(ident, ident_d), (onesb, onesb_d), (onesrowf, onesrow_d),
                     (ones64f, ones64_d), (ind2b, ind2b_d), (ind2f, ind2f_d),
                     (rotT, rotT_d), (masksb, masks_d), (maskslb, masks_lb_d), (cpool, cpool_d)]:
        gdma(dst[:, :], src.ap())
    gdma(bout_c[:, :], b_out_r)
    gdma(hzc[:, :], hz)
    gdma(mac[:, :], maskA)
    for l in range(2):
        gdma(lng_c[:, l * 8:(l + 1) * 8], ln_g_r[l])
        gdma(lnb_c[:, l * 8:(l + 1) * 8], ln_b_r[l])
        gdma(b2_c[:, l * 8:(l + 1) * 8], b2_r[l])
        gdma(b1a_c[:, l * NJ:(l + 1) * NJ], b1a_r[l])
        gdma(b1g_c[:, l * NJ:(l + 1) * NJ], b1g_r[l])
        gdma(qsc[:, l:l + 1], qs_r[l])
        gdma(qsc[:, 2 + l:3 + l], qsp_r[l])
        gdma(qsc[:, 4 + l:5 + l], ks_r[l])
        gdma(qsc[:, 6 + l:7 + l], ksp_r[l])
    C.inc(v.memset(blkq[:, :], 0.0), "v")
    C.inc(v.memset(blkk[:, :], 0.0), "v")
    C.inc(v.memset(blkv[:, :], 0.0), "v")
    C.wait(g, "v")
    for blk, w in [(blkq, wq_t), (blkk, wk_t), (blkv, wv_t)]:
        gdma(blk[0:64, 0:64], w)
        gdma(blk[64:128, 64:128], w)
    C.wait(v, "g")
    C.inc(v.tensor_scalar_add(macE[:, :], mac[:, :], EXP_BIAS), "v")
    stg = [place(A_QB, [128, HIDP], BF16), place(A_QB + 5664, [128, HIDP], BF16)]
    wjobs = []
    for l in range(2):
        wjobs += [(wqkv[l], wc_qkv.ap()[l], D, 1536, False),
                  (wo_p[l], wc_wo.ap()[l], 512, D, False),
                  (w1a[l], wc_w1a.ap()[l], D, HIDP, True),
                  (w1g[l], wc_w1g.ap()[l], D, HIDP, True),
                  (w2p[l], wc_w2.ap()[l], HIDP, D, False)]
    wjobs.append((woutT, wc_woutT.ap(), D, D, False))
    ci = 0
    for src, dst, rows, cols, tiled in wjobs:
        for r0 in range(0, rows, 128):
            bi = ci % 2
            pch, qch = ("p0", "q0") if bi == 0 else ("p1", "q1")
            if ci >= 2:
                C.wait(g, qch)
            gdma(stg[bi][:, 0:cols], src[r0:r0 + 128, :], ch=pch)
            C.wait(yq, pch)
            if tiled:
                ydma(dst[:, :, r0 // 128, :],
                     stg[bi][:, 0:cols].rearrange("p (j e) -> p j e", e=128),
                     ch=qch)
            else:
                ydma(dst[r0:r0 + 128, :], stg[bi][:, 0:cols], ch=qch)
            ci += 1
    barrier(t, s, v, g, yq)

    def ln_full(src_fn, mode, l=None):
        for c in range(NCH):
            C.inc(s.copy(tXb[:, :], src_fn(c)), "s")
            C.inc(s.activation(tSQb[:, :], src_fn(c), AF.Square), "s")
            C.wait(t, "s")
            for nt in range(4):
                t.matmul(P_A[0:1, nt * 512:(nt + 1) * 512], onesb[:, :],
                         tXb[:, nt * 512:(nt + 1) * 512],
                         start=(c == 0), stop=(c == NCH - 1))
                dst = (P_B[0:1, nt * 512:(nt + 1) * 512] if nt < 3
                       else P_C[0:1, 0:512])
                mm = t.matmul(dst, onesb[:, :], tSQb[:, nt * 512:(nt + 1) * 512],
                              start=(c == 0), stop=(c == NCH - 1))
            C.inc(mm, "t")
            C.wait(s, "t")
        for nt in range(4):
            mean_ap = P_A[0:1, nt * 512:(nt + 1) * 512]
            sq_ap = (P_B[0:1, nt * 512:(nt + 1) * 512] if nt < 3 else P_C[0:1, 0:512])
            C.wait(v, "t")
            C.inc(v.tensor_scalar(r_m[:, :], mean_ap, -1.0 / 1024.0, None, ALU.mult), "v")
            C.inc(v.tensor_scalar(r_q[:, :], sq_ap, 1.0 / 1024.0, None, ALU.mult), "v")
            C.inc(v.tensor_tensor(r_v[:, :], r_m[:, :], r_m[:, :], ALU.mult), "v")
            C.inc(v.tensor_sub(r_q[:, :], r_q[:, :], r_v[:, :]), "v")
            C.wait(s, "v")
            C.inc(s.activation(r_s[:, :], r_q[:, :], AF.Ln, bias=EPS1), "s")
            C.inc(s.activation(r_s[:, :], r_s[:, :], AF.Exp, scale=-0.5), "s")
            C.wait(t, "s")
            C.inc(t.matmul(P_A[:, nt * 512:(nt + 1) * 512], onesrowf[0:1, :],
                           r_m[0:1, :], start=True, stop=True), "t")
            C.wait(v, "t")
            C.inc(v.tensor_copy(tnm[:, nt * 512:(nt + 1) * 512],
                                P_A[:, nt * 512:(nt + 1) * 512]), "v")
            C.wait(t, "v")
            C.inc(t.matmul(P_A[:, nt * 512:(nt + 1) * 512], onesrowf[0:1, :],
                           r_s[0:1, :], start=True, stop=True), "t")
            C.wait(v, "t")
            C.inc(v.tensor_copy(trs[:, nt * 512:(nt + 1) * 512],
                                P_A[:, nt * 512:(nt + 1) * 512]), "v")
        for c in range(NCH):
            C.inc(v.tensor_add(P_A[:, :], src_fn(c), tnm[:, :]), "v")
            if mode == "y":
                C.inc(v.tensor_mul(Y[:, c, 1:1 + NT], P_A[:, :], trs[:, :]), "v")
            elif mode == "xlb":
                C.inc(v.tensor_mul(XLb[:, c, :], P_A[:, :], trs[:, :]), "v")
            else:
                C.inc(v.tensor_mul(P_A[:, :], P_A[:, :], trs[:, :]), "v")
                C.inc(v.tensor_scalar(Hb[:, c, :], P_A[:, :],
                                      lng_c[0:128, l * 8 + c:l * 8 + c + 1],
                                      lnb_c[0:128, l * 8 + c:l * 8 + c + 1],
                                      ALU.mult, ALU.add), "v")

    def cc_tok():
        C.wait(g, "v")
        C.wait(g, "t")
        C.wait(g, "s")
        gdma(ccT_in.ap(), Y[:, :, 2048:2049].rearrange("p c o -> p (c o)"), slow=True)
        C.wait(g, "g")
        C.inc(g.collective_compute("AllGather", ALU.bypass, replica_groups=PAIRS,
                                   ins=[ccT_in.ap().opt()],
                                   outs=[ccT_out.ap().opt()]), "c")
        C.wait(g, "c")
        gdma(Y[:, :, 0:1].rearrange("p c o -> p (c o)"),
             ccT_out.ap()[bass.ds(oth_sv, 1), :, :].rearrange("o p c -> p (o c)"),
             slow=True)
        C.wait(v, "g")
        C.inc(v.tensor_scalar(Y[:, :, 0:1], Y[:, :, 0:1], hzc[0:128, 0:1], None,
                              ALU.mult), "v")

    def attn_norm(low, oA_ap, oB_ap, sumA_ap, sumB_ap, dstA, dstB, rcp, rbc):
        rr = 64 if low else 0
        C.wait(s, "t")
        C.inc(s.activation(rcp[rr:rr + 1, 512:1024], sumA_ap, AF.Ln), "s")
        C.inc(s.activation(rcp[rr:rr + 1, 0:512], rcp[rr:rr + 1, 512:1024],
                           AF.Exp, scale=-1.0), "s")
        C.inc(s.activation(rbc[rr:rr + 1, 512:1024], sumB_ap, AF.Ln), "s")
        C.inc(s.activation(rbc[rr:rr + 1, 0:512], rbc[rr:rr + 1, 512:1024],
                           AF.Exp, scale=-1.0), "s")
        C.wait(t, "s")
        if low:
            t.matmul(P_B[0:64, 0:512], ones64f[64:65, :], rcp[64:65, 0:512],
                     start=True, stop=True)
            mm2 = t.matmul(P_B[0:64, 512:1024], ones64f[64:65, :],
                           rbc[64:65, 0:512], start=True, stop=True)
        else:
            t.matmul(P_B[64:128, 0:512], ones64f[0:1, :], rcp[0:1, 0:512],
                     start=True, stop=True, tile_position=(0, 64))
            mm2 = t.matmul(P_B[64:128, 512:1024], ones64f[0:1, :],
                           rbc[0:1, 0:512], start=True, stop=True,
                           tile_position=(0, 64))
        C.inc(mm2, "t")
        C.wait(v, "t")
        if low:
            C.inc(v.tensor_copy(rcp[0:64, 0:512], P_B[0:64, 0:512]), "v")
            C.inc(v.tensor_copy(rbc[0:64, 0:512], P_B[0:64, 512:1024]), "v")
            C.inc(v.tensor_mul(dstA, oA_ap, rcp[0:64, 0:512]), "v")
            C.inc(v.tensor_mul(dstB, oB_ap, rbc[0:64, 0:512]), "v")
        else:
            C.inc(v.tensor_copy(rcp[64:128, 0:512], P_B[64:128, 0:512]), "v")
            C.inc(v.tensor_copy(rbc[64:128, 0:512], P_B[64:128, 512:1024]), "v")
            C.inc(v.tensor_mul(dstA, oA_ap, rcp[64:128, 0:512]), "v")
            C.inc(v.tensor_mul(dstB, oB_ap, rbc[64:128, 0:512]), "v")

    LAYER_STOP = {}

    def ffn_layer(l, first_of_call):
        barrier(s)
        if first_of_call:
            C.wait(v, "y")
            C.inc(v.memset(v_tm[:, :, :, 64:65], 1.0), "v")
        for c in range(NCH):
            C.inc(s.copy(Yb[:, c, 0:2049], Y[:, c, 0:2049]), "s")
        for rawsrc, dstt, qcol, ncols in (
                (tab_cos, tqc, 0 + l, 2048), (tab_sin, tqs, 2 + l, 2048),
                (tab_icos, tkc, 4 + l, 2560), (tab_isin, tks, 6 + l, 2560)):
            C.wait(g, "v")
            gdma(rawt[:, :], rawsrc, ch="p0")
            C.wait(v, "p0")
            C.inc(v.tensor_scalar(dstt[:, 0:ncols], rawt[:, 2560 - ncols:2560],
                                  qsc[0:128, qcol:qcol + 1], None, ALU.mult), "v")
        if stage == 87:
            barrier(g, t, s, v, yq)
            gdma(xout[:, 0:2048], tqc[:, :])
            gdma(xout[:, 2048:4096], tqs[:, :])
            gdma(xout[:, 4096:6656], tkc[:, :])
            gdma(xout[:, 6656:9216], tks[:, :])
            barrier(g, t, s, v, yq)
            return
        if stage == 8:
            return
        barrier(t, g)
        for half in range(2):
            C.wait(g, "t")
            gdma(wv_sb[:, :, :],
                 wc_qkv.ap()[l].rearrange("(kc p) o -> p kc o", p=128)
                 [:, :, 1024 + half * 256:1024 + (half + 1) * 256], ch="p1")
            C.wait(t, "p1")
            lastv = {0: 0, 1: 0}
            for tt in range(16):
                ps = P_C[:, 0:512] if (tt % 2 == 0) else P_B[:, 0:512]
                psv = ps[0:128, 0:256]
                if tt >= 2:
                    C.wait(t, "v", lastv[tt % 2])
                for kc in range(NCH):
                    sh = 1 if kc < 4 else 0
                    mm = t.matmul(psv, Yb[:, kc, sh + tt * 128:sh + (tt + 1) * 128],
                                  wv_sb[:, kc, :], start=(kc == 0), stop=(kc == NCH - 1))
                C.inc(mm, "t")
                C.wait(v, "t")
                C.inc(v.tensor_copy(
                    v_tm[:, 4 + tt, half * 4:(half + 1) * 4, 0:64],
                    psv.rearrange("p (h e) -> p h e", e=64)), "v")
                lastv[tt % 2] = C.cnt["v"]
        if stage == 9:
            return
        import os as _os
        K_SUB = int(_os.environ.get("K_SUB", "99"))
        K_OC = int(_os.environ.get("K_OC", "8"))
        barrier(t, g)
        gdma(wtq[0][:, :, :],
             wc_qkv.ap()[l].rearrange("(kc p) o -> p kc o", p=128)[:, :, 0:128],
             ch="p0")
        pa_free = C.cnt["v"]
        tmile = {0: 0, 1: 0}
        for oc in range(K_OC):
            C.wait(t, "p0" if oc % 2 == 0 else "p1")
            C.wait(t, "v", pa_free)
            for nt in range(4):
                for kc in range(NCH):
                    sh = 1 if kc < 4 else 0
                    mm = t.matmul(P_A[:, nt * 512:(nt + 1) * 512],
                                  wtq[oc % 2][:, kc, :],
                                  Yb[:, kc, sh + nt * 512:sh + (nt + 1) * 512],
                                  start=(kc == 0), stop=(kc == NCH - 1))
            C.inc(mm, "t")
            tmile[oc % 2] = C.cnt["t"]
            if oc + 1 < K_OC:
                C.wait(g, "t", tmile[(oc + 1) % 2])
                gdma(wtq[(oc + 1) % 2][:, :, :],
                     wc_qkv.ap()[l].rearrange("(kc p) o -> p kc o", p=128)
                     [:, :, (oc + 1) * 128:(oc + 2) * 128],
                     ch="p0" if (oc + 1) % 2 == 0 else "p1")
            isq = oc < 4
            tc_, ts_ = (tqc, tqs) if isq else (tkc, tks)
            tofs = 0 if isq else 512
            if K_SUB == 0:
                C.wait(v, "t")
                for nt in range(4):
                    dst = (Qb[:, oc, nt * 512:(nt + 1) * 512] if isq else
                           Kb[:, oc - 4, 512 + nt * 512:512 + (nt + 1) * 512])
                    C.inc(v.tensor_copy(dst, P_A[:, nt * 512:(nt + 1) * 512]), "v")
                pa_free = C.cnt["v"]
                continue
            K_STEP = int(_os.environ.get("K_STEP", "99"))
            for nt in range(4):
                C.wait(s, "t")
                C.inc(s.activation(rt_q2[:, :], P_A[:, nt * 512:(nt + 1) * 512],
                                   AF.Square), "s")
                if K_STEP > 1:
                    C.wait(t, "s")
                    C.wait(t, "v")
                    C.inc(t.matmul(P_C[0:2, 0:512], ind2b[:, :], rt_q2[:, :],
                                   start=True, stop=True), "t")
                if K_STEP > 2:
                    C.wait(s, "t")
                    C.inc(s.activation(rt_ss[:, :], P_C[0:2, 0:512], AF.Ln,
                                       bias=EPS12), "s")
                    C.inc(s.activation(rt_ss[:, :], rt_ss[:, :], AF.Exp,
                                       scale=-0.5), "s")
                if K_STEP > 4:
                    C.wait(t, "s")
                    C.inc(t.matmul(P_C[:, 0:512], ind2f[:, :], rt_ss[0:2, :],
                                   start=True, stop=True), "t")
                if K_STEP > 5:
                    C.wait(v, "t")
                    C.inc(v.tensor_copy(rt_rb[:, :], P_C[:, 0:512]), "v")
                    C.inc(v.tensor_mul(rt_qn[:, :], P_A[:, nt * 512:(nt + 1) * 512],
                                       rt_rb[:, :]), "v")
                if K_STEP > 6:
                    C.wait(s, "v")
                    C.inc(s.copy(rt_qnb[:, :], rt_qn[:, :]), "s")
                if K_STEP > 7:
                    C.wait(t, "s")
                    C.inc(t.matmul(P_C[:, 0:512], rotT[:, :], rt_qnb[:, :],
                                   start=True, stop=True), "t")
                if K_STEP > 8:
                    C.wait(v, "t")
                    tci = tc_[:, tofs + nt * 512:tofs + (nt + 1) * 512]
                    tsi = ts_[:, tofs + nt * 512:tofs + (nt + 1) * 512]
                    C.inc(v.tensor_mul(rt_rb[:, :], rt_qn[:, :], tci), "v")
                if K_STEP > 9:
                    C.inc(v.tensor_copy(rt_t2[:, :], P_C[:, 0:512]), "v")
                    C.inc(v.tensor_mul(rt_qn[:, :], rt_t2[:, :], tsi), "v")
                if K_STEP > 10:
                    if K_SUB >= 2:
                        dst = (Qb[:, oc, nt * 512:(nt + 1) * 512] if isq else
                               Kb[:, oc - 4, 512 + nt * 512:512 + (nt + 1) * 512])
                        C.inc(v.tensor_add(dst, rt_rb[:, :], rt_qn[:, :]), "v")
            pa_free = C.cnt["v"]
        LAYER_STOP["qkv"] = True
        if stage == 10:
            return
        C.wait(g, "v")
        gdma(ccA_in.ap()[:, 0:2048].rearrange("p (c w) -> p c w", w=512),
             Kb[:, :, 2048:2560])
        gdma(ccA_in.ap()[:, 2048:2048 + VW],
             v_tm[:, 16:20, :, :].rearrange("p c h e -> p (c h e)"))
        C.wait(g, "g")
        C.inc(g.collective_compute("AllGather", ALU.bypass, replica_groups=PAIRS,
                                   ins=[ccA_in.ap().opt()],
                                   outs=[ccA_out.ap().opt()]), "c")
        C.wait(g, "c")
        oslc = ccA_out.ap()[bass.ds(oth_sv, 1), :, :]
        gdma(Kb[:, :, 0:512],
             oslc[:, :, 0:2048].rearrange("o p (c w) -> p (o c) w", w=512))
        gdma(v_tm[:, 0:4, :, :].rearrange("p c h e -> p (c h e)"),
             oslc[:, :, 2048:2048 + VW].rearrange("o p e -> p (o e)"))
        C.wait(v, "g")
        C.inc(v.tensor_scalar(Kb[:, :, 0:512], Kb[:, :, 0:512],
                              hzc[0:128, 0:1], None, ALU.mult), "v")
        C.inc(v.tensor_scalar(v_tm[:, 0:4, :, :], v_tm[:, 0:4, :, :],
                              hzc[0:128, 0:1], None, ALU.mult), "v")
        barrier(t, s, v)
        for hp in range(4):
            low = hp < 2
            cA, cB = 2 * hp, 2 * hp + 1
            if not low:
                cA, cB = 2 * hp - 4, 2 * hp - 3
            rA = slice(0, 64) if low else slice(64, 128)
            sets = [P_A[:, 0:1024], P_A[:, 1024:2048], P_B[:, 0:1024]]
            for w in range(NW):
                C.wait(t, "v")
                C.wait(t, "s")
                expmile, avmile = {}, {}
                for i in range(9):
                    if i < 8:
                        m = i % 3
                        sp_ = sets[m]
                        kcol = 512 * w + 128 * i
                        mtile = (maskslb[:, i * 512:(i + 1) * 512] if i < 4 else
                                 masksb[:, (i - 4) * 512:(i - 3) * 512])
                        C.wait(t, "s", expmile.get(i - 3, 0))
                        t.matmul(sp_[0:128, 0:512], Kb[0:64, hp, kcol:kcol + 128],
                                 Qb[0:64, hp, w * 512:(w + 1) * 512],
                                 start=True, stop=False, tile_position=(0, 0))
                        t.matmul(sp_[0:128, 512:1024],
                                 Kb[64:128, hp, kcol:kcol + 128],
                                 Qb[64:128, hp, w * 512:(w + 1) * 512],
                                 start=True, stop=False, tile_position=(64, 0))
                        t.matmul(sp_[0:128, 0:512], ident[:, :], mtile,
                                 start=False, stop=True)
                        mm = t.matmul(sp_[0:128, 512:1024], ident[:, :], mtile,
                                      start=False, stop=True)
                        C.inc(mm, "t")
                        qkm = C.cnt["t"]
                    if i >= 1:
                        j = i - 1
                        eb = expb[j % 2]
                        tc = 4 * w + j
                        st_, sp2 = (j == 0), (j == 7)
                        hA, hB = 2 * hp, 2 * hp + 1
                        C.wait(t, "s", expmile[j])
                        if low:
                            t.matmul(P_B[0:65, 1024:1536],
                                     v_tm[:, tc, hA, 0:65], eb[:, 0:512],
                                     start=st_, stop=sp2)
                            mm = t.matmul(P_C[0:65, 0:512],
                                          v_tm[:, tc, hB, 0:65],
                                          eb[:, 512:1024], start=st_, stop=sp2)
                        else:
                            t.matmul(P_B[64:128, 1024:1536],
                                     v_tm[:, tc, hA, 0:64], eb[:, 0:512],
                                     start=st_, stop=sp2, tile_position=(0, 64))
                            t.matmul(P_B[0:1, 1024:1536], onesb[:, :],
                                     eb[:, 0:512], start=st_, stop=sp2)
                            t.matmul(P_C[64:128, 0:512],
                                     v_tm[:, tc, hB, 0:64], eb[:, 512:1024],
                                     start=st_, stop=sp2, tile_position=(0, 64))
                            mm = t.matmul(P_C[0:1, 0:512], onesb[:, :],
                                          eb[:, 512:1024], start=st_, stop=sp2)
                        C.inc(mm, "t")
                        avmile[j] = C.cnt["t"]
                    if i < 8:
                        C.wait(s, "t", qkm)
                        if i >= 2:
                            C.wait(s, "t", avmile[i - 2])
                        bia = (macE[0:128, 0:1] if (w == 0 and i < 4) else EBIA)
                        C.inc(s.activation(expb[i % 2][:, 0:1024], sets[i % 3],
                                           AF.Exp, bias=bia, scale=8.0), "s")
                        expmile[i] = C.cnt["s"]
                qcols = slice(w * 512, (w + 1) * 512)
                if low:
                    attn_norm(True, P_B[0:64, 1024:1536], P_C[0:64, 0:512],
                              P_B[64:65, 1024:1536], P_C[64:65, 0:512],
                              o_fm[0:64, cA, qcols], o_fm[0:64, cB, qcols],
                              recip_sb, recbc)
                else:
                    attn_norm(False, P_B[64:128, 1024:1536], P_C[64:128, 0:512],
                              P_B[0:1, 1024:1536], P_C[0:1, 0:512],
                              o_fm[64:128, cA, qcols], o_fm[64:128, cB, qcols],
                              recip_sb, recbc)
        if stage == 11:
            return
        barrier(g)
        rearr_wo = wc_wo.ap()[l].rearrange("(kc p) o -> p kc o", p=128)
        C.wait(yq, "t")
        gdma(wos[:, 0:2, :], rearr_wo[:, 0:2, :])
        ydma(wos[:, 2:4, :], rearr_wo[:, 2:4, :], ch="q0")
        barrier(t)
        for oc in range(8):
            if oc >= 1:
                C.wait(t, "v")
            for nt in range(4):
                for kc in range(4):
                    mm = t.matmul(P_A[:, nt * 512:(nt + 1) * 512],
                                  wos[:, kc, oc * 128:(oc + 1) * 128],
                                  o_fm[:, kc, nt * 512:(nt + 1) * 512],
                                  start=(kc == 0), stop=(kc == 3))
            C.inc(mm, "t")
            C.wait(v, "t")
            C.inc(v.tensor_add(Y[:, oc, 1:1 + NT], Y[:, oc, 1:1 + NT], P_A[:, :]), "v")
        if stage == 12:
            return
        cc_tok()
        barrier(t, s, v)

        def ysrc(c):
            return Y[:, c, 1:1 + NT] if c < 4 else Y[:, c, 0:NT]
        ln_full(ysrc, "h", l=l)
        if stage == 13:
            return
        barrier(t, g, s, v)
        rearr1a = wc_w1a.ap()[l]
        rearr1g = wc_w1g.ap()[l]
        rearr2 = wc_w2.ap()[l].rearrange("(kc p) o -> p kc o", p=128)
        for tt in range(4):
            tcols = slice(tt * 512, (tt + 1) * 512)
            C.wait(g, "t")
            C.wait(yq, "t")
            gdma(wt1a[0][:, :, :], rearr1a[:, 0, :, :], ch="p0")
            ydma(wt1g[0][:, :, :], rearr1g[:, 0, :, :], ch="q0")
            w1mile = {0: 0, 1: 0}
            vmile, smile = {}, {}
            for j in range(NJ):
                bi = j % 2
                C.wait(t, "p0" if bi == 0 else "p1")
                C.wait(t, "q0" if bi == 0 else "q1")
                if j >= 2:
                    C.wait(t, "v", vmile[j - 2])
                    C.wait(t, "s", smile[j - 2])
                pj = (j % 2) * 1024
                for kc in range(NCH):
                    t.matmul(P_A[:, pj:pj + 512], wt1a[bi][:, kc, :],
                             Hb[:, kc, tcols], start=(kc == 0), stop=(kc == 7))
                for kc in range(NCH):
                    mm = t.matmul(P_A[:, pj + 512:pj + 1024], wt1g[bi][:, kc, :],
                                  Hb[:, kc, tcols], start=(kc == 0), stop=(kc == 7))
                C.inc(mm, "t")
                w1mile[bi] = C.cnt["t"]
                if j + 1 < NJ:
                    C.wait(g, "t", w1mile[(j + 1) % 2])
                    C.wait(yq, "t", w1mile[(j + 1) % 2])
                    pch = "p0" if (j + 1) % 2 == 0 else "p1"
                    qch = "q0" if (j + 1) % 2 == 0 else "q1"
                    gdma(wt1a[(j + 1) % 2][:, :, :],
                         rearr1a[:, j + 1, :, :], ch=pch)
                    ydma(wt1g[(j + 1) % 2][:, :, :],
                         rearr1g[:, j + 1, :, :], ch=qch)
                tj = tnm[:, (j % 2) * 512:(j % 2) * 512 + 512]
                C.wait(s, "t")
                C.inc(s.activation(tj, P_A[:, pj + 512:pj + 1024],
                                   AF.Gelu, bias=b1g_c[0:128, l * NJ + j:l * NJ + j + 1]),
                      "s")
                smile[j] = C.cnt["s"]
                C.wait(v, "s")
                C.inc(v.scalar_tensor_tensor(
                    AGb[:, j, :], P_A[:, pj:pj + 512],
                    b1a_c[0:128, l * NJ + j:l * NJ + j + 1], tj,
                    ALU.add, ALU.mult), "v")
                vmile[j] = C.cnt["v"]
            barrier(t, g)
            C.wait(yq, "t")
            gdma(wt2[0][:, 0:512], rearr2[:, 0, 0:512], ch="r0")
            ydma(wt2[0][:, 512:1024], rearr2[:, 0, 512:1024], ch="q0")
            slots = [P_A[:, i * 512:(i + 1) * 512] for i in range(4)] + \
                    [P_B[:, i * 512:(i + 1) * 512] for i in range(3)] + [P_C[:, 0:512]]
            w2mile = {0: 0, 1: 0}
            for kc in range(NJ):
                C.wait(t, "r0" if kc % 2 == 0 else "r1")
                C.wait(t, "q0" if kc % 2 == 0 else "q1")
                for oc in range(8):
                    mm = t.matmul(slots[oc], wt2[kc % 2][:, oc * 128:(oc + 1) * 128],
                                  AGb[:, kc, :], start=(kc == 0), stop=(kc == NJ - 1))
                C.inc(mm, "t")
                w2mile[kc % 2] = C.cnt["t"]
                if kc + 1 < NJ:
                    C.wait(g, "t", w2mile[(kc + 1) % 2])
                    C.wait(yq, "t", w2mile[(kc + 1) % 2])
                    gdma(wt2[(kc + 1) % 2][:, 0:512], rearr2[:, kc + 1, 0:512],
                         ch="r0" if (kc + 1) % 2 == 0 else "r1")
                    ydma(wt2[(kc + 1) % 2][:, 512:1024],
                         rearr2[:, kc + 1, 512:1024],
                         ch="q0" if (kc + 1) % 2 == 0 else "q1")
            C.wait(v, "t")
            for oc in range(8):
                C.inc(v.scalar_tensor_tensor(
                    Y[:, oc, 1 + tt * 512:1 + (tt + 1) * 512], slots[oc],
                    b2_c[0:128, l * 8 + oc:l * 8 + oc + 1],
                    Y[:, oc, 1 + tt * 512:1 + (tt + 1) * 512],
                    ALU.add, ALU.add), "v")
            barrier(t)
        if first_of_call:
            cc_tok()

    def dense_phase():
        def xsrc(c):
            return X[:, c, :]
        ln_full(xsrc, "xlb")
        barrier(g)
        gdma(ccD_in.ap(), XLb[:, :, :].rearrange("p c t -> p (c t)"))
        C.wait(g, "g")
        C.inc(g.collective_compute("AllGather", ALU.bypass, replica_groups=PAIRS,
                                   ins=[ccD_in.ap().opt()],
                                   outs=[ccD_out.ap().opt()]), "c")
        C.wait(g, "c")
        gdma(XLo[:, :, :].rearrange("p c t -> p (c t)"),
             ccD_out.ap()[bass.ds(oth_sv, 1), :, :].rearrange("o p e -> p (o e)"))
        C.inc(v.memset(Vd[:, :, 64:65], 1.0), "v")
        C.inc(v.memset(Vd[:, :, 129:130], 1.0), "v")
        barrier(t, s, v)
        import os as _os
        K_NP = int(_os.environ.get("K_NP", "8"))
        for c in range(K_NP):
            low = c < 4
            cA, cB = (2 * c, 2 * c + 1) if low else (2 * c - 8, 2 * c - 7)
            C.wait(t, "v")
            C.wait(t, "s")
            for nt in range(4):
                C.inc(t.matmul(P_A[:, nt * 512:(nt + 1) * 512], blkq[:, :],
                               XLb[:, c, nt * 512:(nt + 1) * 512],
                               start=True, stop=True), "t")
            C.wait(s, "t")
            C.inc(s.copy(Qd[:, :], P_A[:, :]), "s")
            C.wait(t, "s")
            for half, src in ((0, XLb), (1, XLo)):
                for nt in range(4):
                    C.inc(t.matmul(P_A[:, nt * 512:(nt + 1) * 512], blkk[:, :],
                                   src[:, c, nt * 512:(nt + 1) * 512],
                                   start=True, stop=True), "t")
                C.wait(s, "t")
                C.inc(s.copy(Kd[:, half * 2048:(half + 1) * 2048], P_A[:, :]), "s")
                C.wait(t, "s")
                lastv = {0: 0, 1: 0}
                for ttl in range(16):
                    ps = P_C[:, 0:512] if (ttl % 2 == 0) else P_B[:, 0:512]
                    psv = ps[0:128, 0:128]
                    if ttl >= 2:
                        C.wait(t, "v", lastv[ttl % 2])
                    C.inc(t.matmul(psv, src[:, c, ttl * 128:(ttl + 1) * 128],
                                   blkv[:, :], start=True, stop=True), "t")
                    C.wait(v, "t")
                    C.inc(v.tensor_copy(
                        Vd[:, half * 16 + ttl, 0:130].rearrange(
                            "p (h e) -> p h e", e=65)[:, :, 0:64],
                        psv.rearrange("p (h e) -> p h e", e=64)), "v")
                    lastv[ttl % 2] = C.cnt["v"]
            barrier(t, s, v)
            sets = [P_A[:, 0:1024], P_A[:, 1024:2048], P_B[:, 0:1024]]
            for qt in range(4):
                qcols = slice(qt * 512, (qt + 1) * 512)
                C.wait(t, "v")
                C.wait(t, "s")
                expmile, avmile = {}, {}
                for i in range(33):
                    if i < 32:
                        m = i % 3
                        sp_ = sets[m]
                        C.wait(t, "s", expmile.get(i - 3, 0))
                        t.matmul(sp_[0:128, 0:512],
                                 Kd[0:64, i * 128:(i + 1) * 128],
                                 Qd[0:64, qcols], start=True, stop=True,
                                 tile_position=(0, 0))
                        mm = t.matmul(sp_[0:128, 512:1024],
                                      Kd[64:128, i * 128:(i + 1) * 128],
                                      Qd[64:128, qcols], start=True, stop=True,
                                      tile_position=(64, 0))
                        C.inc(mm, "t")
                        qkm = C.cnt["t"]
                    if i >= 1:
                        j = i - 1
                        eb = expd[j % 2]
                        st_, sp2 = (j == 0), (j == 31)
                        C.wait(t, "s", expmile[j])
                        if low:
                            t.matmul(P_B[0:65, 1024:1536], Vd[:, j, 0:65],
                                     eb[:, 0:512], start=st_, stop=sp2)
                            mm = t.matmul(P_C[0:65, 0:512], Vd[:, j, 65:130],
                                          eb[:, 512:1024], start=st_, stop=sp2)
                        else:
                            t.matmul(P_B[64:128, 1024:1536], Vd[:, j, 0:64],
                                     eb[:, 0:512], start=st_, stop=sp2,
                                     tile_position=(0, 64))
                            t.matmul(P_B[0:1, 1024:1536], onesb[:, :],
                                     eb[:, 0:512], start=st_, stop=sp2)
                            t.matmul(P_C[64:128, 0:512], Vd[:, j, 65:129],
                                     eb[:, 512:1024], start=st_, stop=sp2,
                                     tile_position=(0, 64))
                            mm = t.matmul(P_C[0:1, 0:512], onesb[:, :],
                                          eb[:, 512:1024], start=st_, stop=sp2)
                        C.inc(mm, "t")
                        avmile[j] = C.cnt["t"]
                    if i < 32:
                        C.wait(s, "t", qkm)
                        if i >= 2:
                            C.wait(s, "t", avmile[i - 2])
                        C.inc(s.activation(expd[i % 2][:, 0:1024], sets[i % 3],
                                           AF.Exp, bias=EBIA,
                                           scale=1.0 / 32.0), "s")
                        expmile[i] = C.cnt["s"]
                if low:
                    attn_norm(True, P_B[0:64, 1024:1536], P_C[0:64, 0:512],
                              P_B[64:65, 1024:1536], P_C[64:65, 0:512],
                              o_dn[0:64, cA, qcols], o_dn[0:64, cB, qcols],
                              recip_d, recbc_d)
                else:
                    attn_norm(False, P_B[64:128, 1024:1536], P_C[64:128, 0:512],
                              P_B[0:1, 1024:1536], P_C[0:1, 0:512],
                              o_dn[64:128, cA, qcols], o_dn[64:128, cB, qcols],
                              recip_d, recbc_d)
        barrier(g, t, s, v, yq)
        rearr_wout = wc_woutT.ap().rearrange("(kc p) o -> p kc o", p=128)
        gdma(wouts[:, 0:4, :], rearr_wout[:, 0:4, :])
        ydma(wouts[:, 4:8, :], rearr_wout[:, 4:8, :], ch="q0")
        barrier(t)
        for oc in range(8):
            if oc >= 1:
                C.wait(t, "v")
            for nt in range(4):
                for kc in range(8):
                    mm = t.matmul(P_A[:, nt * 512:(nt + 1) * 512],
                                  wouts[:, kc, oc * 128:(oc + 1) * 128],
                                  o_dn[:, kc, nt * 512:(nt + 1) * 512],
                                  start=(kc == 0), stop=(kc == 7))
            C.inc(mm, "t")
            C.wait(v, "t")
            C.inc(v.scalar_tensor_tensor(
                X[:, oc, :], P_A[:, :], bout_c[0:128, oc:oc + 1], X[:, oc, :],
                ALU.add, ALU.add), "v")

    if stage <= 1:
        return finish()
    if stage == 85:
        barrier(t, s, v, g, yq)
        gdma(xout[:, 0:1536], wc_qkv.ap()[0][0:128, :])
        gdma(xout[:, 2048:2048 + 1536], wc_qkv.ap()[0][512:640, :])
        gdma(xout[:, 4096:4096 + 2816], wc_w1a.ap()[0][0:128, :])
        gdma(xout[:, 8192:8192 + 1024], wc_w2.ap()[0][0:128, :])
        barrier(g, t, s, v, yq)
        return nc, dbg_outs
    for call in range(2):
        def xsrc(c):
            return X[:, c, :]
        ln_full(xsrc, "y")
        if stage <= 2:
            return finish_y()
        cc_tok()
        if stage <= 3:
            return finish_y()
        barrier(yq)
        ydma(x_spill.ap(), X[:, :, :].rearrange("p c t -> p (c t)"))
        barrier(t, s, v, g)
        for li, l in enumerate((0, 1)):
            ffn_layer(l, first_of_call=(li == 0))
            if stage == 87:
                return nc, dbg_outs
            if stage in (8, 9, 10, 11, 12, 13):
                barrier(t, s, v, g, yq)
                if stage == 10:
                    import os as _os
                    if _os.environ.get("K_NODUMP"):
                        gdma(xout.rearrange("p (c t) -> p c t", t=NT),
                             Y[:, :, 1:1 + NT])
                    else:
                        gdma(xout[:, 0:8192].rearrange("p (c t) -> p c t", t=2048),
                             Qb[:, :, :])
                        for cc_ in range(4):
                            gdma(xout[:, 8192 + cc_ * 2048:8192 + (cc_ + 1) * 2048],
                                 Kb[:, cc_, 512:2560])
                elif stage == 11:
                    gdma(xout[:, 0:8192].rearrange("p (c t) -> p c t", t=2048),
                         o_fm[:, :, :])
                elif stage in (8, 9, 12):
                    gdma(xout.rearrange("p (c t) -> p c t", t=NT), Y[:, :, 1:1 + NT])
                elif stage == 13:
                    gdma(xout, Hb[:, :, :].rearrange("p c t -> p (c t)"))
                barrier(g, t, s, v, yq)
                return nc, dbg_outs
            if stage <= 4 and li == 0:
                barrier(t, s, v, g, yq)
                gdma(xout.rearrange("p (c t) -> p c t", t=NT), Y[:, :, 1:1 + NT])
                barrier(g, t, s, v, yq)
                return nc, dbg_outs
            if stage <= 5 and li == 1:
                barrier(t, s, v, g, yq)
                gdma(xout.rearrange("p (c t) -> p c t", t=NT), Y[:, :, 1:1 + NT])
                barrier(g, t, s, v, yq)
                return nc, dbg_outs
        barrier(g, yq)
        gdma(X[:, :, :], x_spill.ap().rearrange("p (c t) -> p c t", t=NT))
        C.wait(v, "g")
        for c in range(NCH):
            C.inc(v.scalar_tensor_tensor(
                X[:, c, :], Y[:, c, 1:1 + NT], 0.5, X[:, c, :],
                ALU.mult, ALU.add), "v")
        if stage <= 6:
            return finish()
        if call == 0:
            barrier(t, s, v, g, yq)
            dense_phase()
            if stage <= 7:
                return finish()
        barrier(t, s, v, g, yq)

    return finish()


_CACHE = {}


def _host_inputs(inputs):
    x = np.asarray(inputs["L0_attn_x"])
    Wq = np.asarray(inputs["L0_attn_Wq"])
    Wk = np.asarray(inputs["L0_attn_Wk"])
    Wv = np.asarray(inputs["L0_attn_Wv"])
    W_out = np.asarray(inputs["L0_attn_W_out"])
    b_out = np.asarray(inputs["L0_attn_b_out"])
    wqkv = np.asarray(inputs["L0_ffn_wqkv"])
    q_scale = np.asarray(inputs["L0_ffn_q_scale"])
    k_scale = np.asarray(inputs["L0_ffn_k_scale"])
    wo = np.asarray(inputs["L0_ffn_wo"])
    ln_g = np.asarray(inputs["L0_ffn_ln_g"])
    ln_b = np.asarray(inputs["L0_ffn_ln_b"])
    w1 = np.asarray(inputs["L0_ffn_w1"])
    b1 = np.asarray(inputs["L0_ffn_b1"])
    w2 = np.asarray(inputs["L0_ffn_w2"])
    b2 = np.asarray(inputs["L0_ffn_b2"])

    f32 = np.float32
    com = {}
    com["wqkv"] = np.ascontiguousarray(wqkv, dtype=f32)
    com["wo_p"] = np.ascontiguousarray(wo[:, PERM_WO, :], dtype=f32)
    w1a = np.zeros((2, D, HIDP), f32)
    w1g = np.zeros((2, D, HIDP), f32)
    w1a[:, :, :HID] = w1[:, :, :HID]
    w1g[:, :, :HID] = w1[:, :, HID:]
    com["w1a"], com["w1g"] = w1a, w1g
    w2pp = np.zeros((2, HIDP, D), f32)
    w2pp[:, :HID, :] = w2
    com["w2p"] = w2pp
    com["woutT"] = np.ascontiguousarray(W_out.T[PERM_WOUT, :], dtype=f32)
    com["wq_t"] = np.ascontiguousarray(Wq.T, dtype=f32)
    com["wk_t"] = np.ascontiguousarray(Wk.T, dtype=f32)
    com["wv_t"] = np.ascontiguousarray(Wv.T, dtype=f32)
    com["b_out_r"] = np.ascontiguousarray(b_out.reshape(8, 128).T, dtype=f32)
    com["ln_g_r"] = np.ascontiguousarray(ln_g.reshape(2, 8, 128).swapaxes(1, 2), dtype=f32)
    com["ln_b_r"] = np.ascontiguousarray(ln_b.reshape(2, 8, 128).swapaxes(1, 2), dtype=f32)
    com["b2_r"] = np.ascontiguousarray(b2.reshape(2, 8, 128).swapaxes(1, 2), dtype=f32)
    b1a = np.zeros((2, 128, NJ), f32)
    b1g = np.zeros((2, 128, NJ), f32)
    for l in range(2):
        pa = np.zeros(HIDP, f32)
        pg = np.zeros(HIDP, f32)
        pa[:HID] = b1[l, :HID]
        pg[:HID] = b1[l, HID:]
        b1a[l] = pa.reshape(NJ, 128).T
        b1g[l] = pg.reshape(NJ, 128).T
    com["b1a_r"], com["b1g_r"] = b1a, b1g
    perm64 = (np.arange(64) + 32) % 64
    com["qs_r"] = np.ascontiguousarray(
        np.tile(q_scale, (1, 2)).reshape(2, 128, 1), dtype=f32)
    com["qsp_r"] = np.ascontiguousarray(
        np.tile(q_scale[:, perm64], (1, 2)).reshape(2, 128, 1), dtype=f32)
    com["ks_r"] = np.ascontiguousarray(
        np.tile(k_scale, (1, 2)).reshape(2, 128, 1), dtype=f32)
    com["ksp_r"] = np.ascontiguousarray(
        np.tile(k_scale[:, perm64], (1, 2)).reshape(2, 128, 1), dtype=f32)

    in_maps = []
    for c in range(NCORE):
        b, h = c // 2, c % 2
        t0 = h * NT
        m = dict(com)
        m["xin"] = _pack_fm(np.asarray(x[b, t0:t0 + NT, :], dtype=f32))
        tc_, ts_, tic, tis = _rot_tables(t0)
        m["tab_cos"], m["tab_sin"] = tc_, ts_
        m["tab_icos"], m["tab_isin"] = tic, tis
        m["hz"] = np.full((128, 1), 0.0 if h == 0 else 1.0, f32)
        m["maskA"] = np.full((128, 1), NEG if h == 0 else 0.0, f32)
        in_maps.append(m)
    return in_maps


def kernel(**inputs):
    import os
    stg = int(os.environ.get("K_STAGE", "99"))
    key = ("prog", stg)
    if key not in _CACHE:
        _CACHE[key] = build_program(stage=stg)[0]
    nc = _CACHE[key]
    in_maps = _host_inputs(inputs)
    res = run_bass_kernel_spmd(nc, in_maps, core_ids=list(range(NCORE)))
    out = np.empty((B, S, D), dtype=np.float32)
    for c in range(NCORE):
        b, h = c // 2, c % 2
        fm = res.results[c]["xout"]
        out[b, h * NT:(h + 1) * NT, :] = (
            fm.reshape(128, NCH, NT).transpose(1, 0, 2).reshape(D, NT).T)
    return out


# version 05
# speedup vs baseline: 1.3596x; 1.0549x over previous
import numpy as np
import ml_dtypes
import concourse.bass as bass
import concourse.mybir as mybir
from concourse.bass_utils import run_bass_kernel_spmd

F32 = mybir.dt.float32
BF16 = mybir.dt.bfloat16
AF = mybir.ActivationFunctionType
ALU = mybir.AluOpType
BF = ml_dtypes.bfloat16

NCORE = 8
B, S, D = 4, 4096, 1024
NT = 2048
NCH = 8
YW = 2056
HL, DH, WIN, NW = 8, 64, 512, 4
HID, HIDP, NJ = 2730, 2816, 22
EXP_BIAS = -40.0
NEG = -30000.0
PAIRS = [[0, 1], [2, 3], [4, 5], [6, 7]]
PERM_WO = np.array([(c + 4 * (r // 64)) * 64 + (r % 64)
                    for c in range(4) for r in range(128)])
PERM_WOUT = np.array([(c + 8 * (r // 64)) * 64 + (r % 64)
                      for c in range(8) for r in range(128)])
PERM_ROT = np.array([(d // 64) * 64 + ((d % 64) + 32) % 64 for d in range(128)])

A_X = 0
A_Y = 66048
A_YB = 0
A_WT = 33024
A_WV = 37120
A_VT = 41216
A_QB = 131840
A_KB = 148224
A_TAB = 168704
A_RAW = 187136
A_RT = 192256
A_EXP = 0
A_OFM = 12288
A_RCP = 28672
A_RBC = 32768
A_WOS = 168704
A_HB = 0
A_AGB = 131840
A_W1T = 154368
A_W2T = 162560
A_LT = 166656
A_XLB = 66048
A_XLO = 98816
A_ODN = 131840
A_QD = 164608
A_KD = 168704
A_VD = 176896
A_EXD = 185216
A_RCD = 191360
A_RBD = 195456
A_WOUT = 66048
ARENA_B = 200448


def _rot_tables(t0):
    half_idx = np.arange(0, DH, 2, dtype=np.float64)
    inv_freq = 1.0 / 10000.0 ** (half_idx / DH)
    tpos = np.arange(t0 - 512, t0 + 2048, dtype=np.float64)
    fr = tpos[:, None] * inv_freq[None, :]
    freqs = np.concatenate([fr, fr], axis=1)
    base = (half_idx + 0.4 * DH) / (1.4 * DH)
    power = (tpos - S // 2) / float(WIN // 2)
    sc = base[None, :] ** power[:, None]
    scale = np.concatenate([sc, sc], axis=1)
    cos, sin = np.cos(freqs), np.sin(freqs)
    out = []
    for arr in (cos * scale, sin * scale, cos / scale, sin / scale):
        a = arr.T.astype(np.float32)
        out.append(np.ascontiguousarray(np.concatenate([a, a], axis=0)))
    return out


def _pack_fm(xs):
    T = xs.shape[0]
    return np.ascontiguousarray(
        xs.T.reshape(NCH, 128, T).transpose(1, 0, 2).reshape(128, NCH * T))


class Ctr:
    def __init__(self, nc):
        self.sem, self.cnt = {}, {}
        for n in ("g", "y", "t", "s", "v", "c", "p0", "p1", "q0", "q1", "r0", "r1"):
            self.sem[n] = nc.alloc_semaphore("sem_" + n)
            self.cnt[n] = 0

    def inc(self, inst, name, amt=1):
        inst.then_inc(self.sem[name], amt)
        self.cnt[name] += amt
        return inst

    def wait(self, eng, name, val=None):
        vv = self.cnt[name] if val is None else val
        if vv > 0:
            eng.wait_ge(self.sem[name], vv)


def build_program(stage=99, dbg=()):
    nc = bass.Bass("TRN2", target_bir_lowering=False, debug=False,
                   detect_race_conditions=False)

    def inp(name, shape, dt=F32):
        return nc.dram_tensor(name, list(shape), dt, kind="ExternalInput").ap()

    xin = inp("xin", [128, NCH * NT])
    wqkv = inp("wqkv", [2, D, 3 * 512])
    wo_p = inp("wo_p", [2, 512, D])
    w1a = inp("w1a", [2, D, HIDP])
    w1g = inp("w1g", [2, D, HIDP])
    w2p = inp("w2p", [2, HIDP, D])
    woutT = inp("woutT", [D, D])
    wq_t = inp("wq_t", [DH, DH])
    wk_t = inp("wk_t", [DH, DH])
    wv_t = inp("wv_t", [DH, DH])
    b_out_r = inp("b_out_r", [128, 8])
    ln_g_r = inp("ln_g_r", [2, 128, 8])
    ln_b_r = inp("ln_b_r", [2, 128, 8])
    b2_r = inp("b2_r", [2, 128, 8])
    b1a_r = inp("b1a_r", [2, 128, NJ])
    b1g_r = inp("b1g_r", [2, 128, NJ])
    qs_r = inp("qs_r", [2, 128, 1])
    qsp_r = inp("qsp_r", [2, 128, 1])
    ks_r = inp("ks_r", [2, 128, 1])
    ksp_r = inp("ksp_r", [2, 128, 1])
    tab_cos = inp("tab_cos", [128, 2560])
    tab_sin = inp("tab_sin", [128, 2560])
    tab_icos = inp("tab_icos", [128, 2560])
    tab_isin = inp("tab_isin", [128, 2560])
    hz = inp("hz", [128, 1])
    maskA = inp("maskA", [128, 1])

    xout = nc.dram_tensor("xout", [128, NCH * NT], F32, kind="ExternalOutput").ap()
    dbg_outs = {}
    for name, shape, dt in dbg:
        dbg_outs[name] = nc.dram_tensor("dbg_" + name, list(shape), dt,
                                        kind="ExternalOutput").ap()

    ident_d = nc.inline_tensor(np.eye(128, dtype=BF), name="c_ident")
    onesb_d = nc.inline_tensor(np.ones((128, 1), dtype=BF), name="c_onesb")
    onesrow_d = nc.inline_tensor(np.ones((1, 128), dtype=np.float32), name="c_onesrow")
    ones64_d = nc.inline_tensor(np.ones((128, 64), dtype=np.float32), name="c_ones64")
    ind2np = np.zeros((2, 128), dtype=np.float32)
    ind2np[0, :64] = 1.0
    ind2np[1, 64:] = 1.0
    ind2b_d = nc.inline_tensor(np.ascontiguousarray(ind2np.T).astype(BF), name="c_ind2b")
    ind2f_d = nc.inline_tensor(ind2np, name="c_ind2f")
    R64 = np.zeros((64, 64), dtype=np.float64)
    R64[np.arange(32), np.arange(32) + 32] = -1.0
    R64[np.arange(32, 64), np.arange(32)] = 1.0
    R128 = np.zeros((128, 128), dtype=np.float64)
    R128[0:64, 0:64] = R64
    R128[64:128, 64:128] = R64
    rotT_d = nc.inline_tensor(np.ascontiguousarray(R128.T).astype(BF), name="c_rotT")
    mk = np.zeros((128, 4, 512), dtype=np.float64)
    kk = np.arange(128)[:, None]
    qq = np.arange(512)[None, :]
    for jb in range(4):
        mk[:, jb, :] = np.where(qq < 128 * jb + kk, NEG, 0.0)
    masks_d = nc.inline_tensor(
        np.ascontiguousarray(mk.reshape(128, 2048)).astype(BF), name="c_masks")
    mlb = np.zeros((128, 4, 512), dtype=np.float64)
    for i_ in range(4):
        mlb[:, i_, :] = np.where(qq > 128 * i_ + kk, NEG, 0.0)
    masks_lb_d = nc.inline_tensor(
        np.ascontiguousarray(mlb.reshape(128, 2048)).astype(BF), name="c_masks_lb")
    cp = np.zeros((128, 8), dtype=np.float32)
    cp[:, 0] = 1e-5
    cp[:, 1] = EXP_BIAS
    cp[:, 2] = 1e-12
    cpool_d = nc.inline_tensor(cp, name="c_cpool")

    wc_qkv = nc.dram_tensor("wc_qkv", [2, D, 3 * 512], BF16, kind="Internal")
    wc_wo = nc.dram_tensor("wc_wo", [2, 512, D], BF16, kind="Internal")
    wc_w1a = nc.dram_tensor("wc_w1a", [2, 128, NJ, 8, 128], BF16, kind="Internal")
    wc_w1g = nc.dram_tensor("wc_w1g", [2, 128, NJ, 8, 128], BF16, kind="Internal")
    wc_w2 = nc.dram_tensor("wc_w2", [2, HIDP, D], BF16, kind="Internal")
    wc_woutT = nc.dram_tensor("wc_woutT", [D, D], BF16, kind="Internal")
    x_spill = nc.dram_tensor("x_spill", [128, NCH * NT], F32, kind="Internal")
    ccT_in = nc.dram_tensor("ccT_in", [128, 8], F32, kind="Internal")
    ccT_out = nc.dram_tensor("ccT_out", [2, 128, 8], F32, kind="Internal")
    VW = 4 * HL * 65
    ccA_in = nc.dram_tensor("ccA_in", [128, 2048 + VW], BF16, kind="Internal")
    ccA_out = nc.dram_tensor("ccA_out", [2, 128, 2048 + VW], BF16, kind="Internal")
    ccD_in = nc.dram_tensor("ccD_in", [128, NCH * NT], BF16, kind="Internal")
    ccD_out = nc.dram_tensor("ccD_out", [2, 128, NCH * NT], BF16, kind="Internal")

    ident = nc.alloc_sbuf_tensor("ident", [128, 128], BF16)
    onesb = nc.alloc_sbuf_tensor("onesb", [128, 1], BF16)
    onesrowf = nc.alloc_sbuf_tensor("onesrowf", [1, 128], F32)
    ones64f = nc.alloc_sbuf_tensor("ones64f", [128, 64], F32)
    ind2b = nc.alloc_sbuf_tensor("ind2b", [128, 2], BF16)
    ind2f = nc.alloc_sbuf_tensor("ind2f", [2, 128], F32)
    rotT = nc.alloc_sbuf_tensor("rotT", [128, 128], BF16)
    masksb = nc.alloc_sbuf_tensor("masksb", [128, 2048], BF16)
    maskslb = nc.alloc_sbuf_tensor("maskslb", [128, 2048], BF16)
    cpool = nc.alloc_sbuf_tensor("cpool", [128, 8], F32)
    blkq = nc.alloc_sbuf_tensor("blkq", [128, 128], BF16)
    blkk = nc.alloc_sbuf_tensor("blkk", [128, 128], BF16)
    blkv = nc.alloc_sbuf_tensor("blkv", [128, 128], BF16)
    bout_c = nc.alloc_sbuf_tensor("bout_c", [128, 8], F32)
    lng_c = nc.alloc_sbuf_tensor("lng_c", [128, 16], F32)
    lnb_c = nc.alloc_sbuf_tensor("lnb_c", [128, 16], F32)
    b2_c = nc.alloc_sbuf_tensor("b2_c", [128, 16], F32)
    b1a_c = nc.alloc_sbuf_tensor("b1a_c", [128, 2 * NJ], F32)
    b1g_c = nc.alloc_sbuf_tensor("b1g_c", [128, 2 * NJ], F32)
    qsc = nc.alloc_sbuf_tensor("qsc", [128, 8], F32)
    hzc = nc.alloc_sbuf_tensor("hzc", [128, 1], F32)
    mac = nc.alloc_sbuf_tensor("mac", [128, 1], F32)
    macE = nc.alloc_sbuf_tensor("macE", [128, 1], F32)

    arena = nc.alloc_sbuf_tensor("arena", [128, ARENA_B], mybir.dt.uint8)
    ab = nc.lookup_mloc(arena).addr
    rem = nc.sbuf_bytes_remaining
    assert rem >= 0, rem
    _n = [0]

    def place(off, shape, dtype):
        per = int(np.prod(shape[1:])) * mybir.dt.size(dtype)
        assert off % 32 == 0 and off + per <= ARENA_B, (off, per)
        _n[0] += 1
        h = nc.alloc_sbuf_tensor_at(f"ar{_n[0]}", list(shape), dtype,
                                    offset=ab + off)
        if not hasattr(nc, "_arena_map"):
            nc._arena_map = {}
        nc._arena_map.setdefault(off, []).append(h.name)
        return h

    X = place(A_X, [128, NCH, NT], F32)
    Y = place(A_Y, [128, NCH, YW], F32)
    Yb = place(A_YB, [128, NCH, YW], BF16)
    wtq = [place(A_WT + i * 2048, [128, 8, 128], BF16) for i in range(2)]
    wv_sb = place(A_WV, [128, 8, 256], BF16)
    v_tm = place(A_VT, [128, 20, HL, 65], BF16)
    Qb = place(A_QB, [128, 4, 2048], BF16)
    Kb = place(A_KB, [128, 4, 2560], BF16)
    tqc = place(A_TAB, [128, 2048], BF16)
    tqs = place(A_TAB + 4096, [128, 2048], BF16)
    tkc = place(A_TAB + 8192, [128, 2560], BF16)
    tks = place(A_TAB + 13312, [128, 2560], BF16)
    rawt = place(A_RAW, [128, 2560], BF16)
    rt_q2 = place(A_RT, [128, 512], BF16)
    rt_rb = place(A_RT + 1024, [128, 512], F32)
    rt_qn = place(A_RT + 3072, [128, 512], F32)
    rt_qnb = place(A_RT + 5120, [128, 512], BF16)
    rt_t2 = place(A_RT + 6144, [128, 512], F32)
    rt_ss = place(A_RAW, [2, 512], F32)
    expb = [place(A_EXP + i * 3072, [128, 1536], BF16) for i in range(4)]
    o_fm = place(A_OFM, [128, 4, 2048], BF16)
    recip_sb = place(A_RCP, [128, 1024], F32)
    recbc = place(A_RBC, [128, 1024], F32)
    wos = place(A_WOS, [128, 4, 1024], BF16)
    Hb = place(A_HB, [128, NCH, NT], BF16)
    AGb = place(A_AGB, [128, NJ, 512], BF16)
    wt1a = [place(A_W1T + i * 2048, [128, 8, 128], BF16) for i in range(3)]
    wt1g = [place(A_WT + i * 2048, [128, 8, 128], BF16) for i in range(3)]
    wt2 = [place(A_W2T, [128, 1024], BF16),
           place(A_W2T + 2048, [128, 1024], BF16),
           place(A_WT + 6144, [128, 1024], BF16)]
    tXb = place(A_LT, [128, NT], BF16)
    tSQb = place(A_LT + 4096, [128, NT], BF16)
    tXb_ = [tXb, place(A_RT, [128, NT], BF16)]
    tSQb_ = [tSQb, place(A_RT + 4096, [128, NT], BF16)]
    tnm = place(A_LT + 8192, [128, NT], F32)
    trs = place(A_LT + 16384, [128, NT], F32)
    r_m = place(A_LT + 24576, [1, 512], F32)
    r_q = place(A_LT + 26624, [1, 512], F32)
    r_v = place(A_LT + 28672, [1, 512], F32)
    r_s = place(A_LT + 30720, [1, 512], F32)
    XLb = place(A_XLB, [128, NCH, NT], BF16)
    XLo = place(A_XLO, [128, NCH, NT], BF16)
    o_dn = place(A_ODN, [128, NCH, NT], BF16)
    Qd = place(A_QD, [128, 2048], BF16)
    Kd = place(A_KD, [128, 4096], BF16)
    Vd = place(A_VD, [128, 32, 130], BF16)
    expd = [place(A_EXD + i * 3072, [128, 1536], BF16) for i in range(2)]
    recip_d = place(A_RCD, [128, 1024], F32)
    recbc_d = place(A_RBD, [128, 1024], F32)
    wouts = place(A_WOUT, [128, 8, 1024], BF16)

    P_A = nc.alloc_psum_tensor("P_A", [128, 2048], F32)
    P_B = nc.alloc_psum_tensor("P_B", [128, 1536], F32)
    P_C = nc.alloc_psum_tensor("P_C", [128, 512], F32)

    C = Ctr(nc)
    g, t, s, v, yq = nc.gpsimd, nc.tensor, nc.scalar, nc.vector, nc.sync

    def gdma(out, in_, slow=False, ch="g"):
        return C.inc(g.dma_start(out=out, in_=in_,
                                 allow_slow_non_contiguous=slow), ch, 16)

    def ydma(out, in_, slow=False, ch="y"):
        return C.inc(yq.dma_start(out=out, in_=in_,
                                  allow_slow_non_contiguous=slow), ch, 16)

    def barrier(*engs):
        for e in engs:
            for nm in ("g", "y", "t", "s", "v", "c", "p0", "p1", "q0", "q1", "r0", "r1"):
                C.wait(e, nm)

    EPS1 = cpool[0:1, 0:1]
    EBIA = cpool[0:128, 1:2]
    EPS12 = cpool[0:2, 2:3]

    def finish_y():
        barrier(t, s, v, g, yq)
        ydma(xout.rearrange("p (c t) -> p c t", t=NT), Y[:, :, 1:1 + NT])
        barrier(g, t, s, v, yq)
        return nc, dbg_outs

    def finish(ap=None, cast=False):
        barrier(t, s, v, g, yq)
        if ap is None:
            ap = X[:, :, :].rearrange("p c t -> p (c t)")
        if cast:
            gdma(xout, ap)
        else:
            ydma(xout, ap)
        barrier(g, t, s, v, yq)
        return nc, dbg_outs

    pid = g.partition_id()
    oth_reg = g.alloc_register("oth")
    g.reg_alu(oth_reg, pid, 2, ALU.mod)
    g.reg_alu(oth_reg, oth_reg, 1, ALU.bitwise_xor)
    oth_sv = g.snap(oth_reg, min_val=0, max_val=1)

    gdma(X[:, :, :], xin.rearrange("p (c t) -> p c t", t=NT))
    for dst, src in [(ident, ident_d), (onesb, onesb_d), (onesrowf, onesrow_d),
                     (ones64f, ones64_d), (ind2b, ind2b_d), (ind2f, ind2f_d),
                     (rotT, rotT_d), (masksb, masks_d), (maskslb, masks_lb_d), (cpool, cpool_d)]:
        gdma(dst[:, :], src.ap())
    gdma(bout_c[:, :], b_out_r)
    gdma(hzc[:, :], hz)
    gdma(mac[:, :], maskA)
    for l in range(2):
        gdma(lng_c[:, l * 8:(l + 1) * 8], ln_g_r[l])
        gdma(lnb_c[:, l * 8:(l + 1) * 8], ln_b_r[l])
        gdma(b2_c[:, l * 8:(l + 1) * 8], b2_r[l])
        gdma(b1a_c[:, l * NJ:(l + 1) * NJ], b1a_r[l])
        gdma(b1g_c[:, l * NJ:(l + 1) * NJ], b1g_r[l])
        gdma(qsc[:, l:l + 1], qs_r[l])
        gdma(qsc[:, 2 + l:3 + l], qsp_r[l])
        gdma(qsc[:, 4 + l:5 + l], ks_r[l])
        gdma(qsc[:, 6 + l:7 + l], ksp_r[l])
    C.inc(v.memset(blkq[:, :], 0.0), "v")
    C.inc(v.memset(blkk[:, :], 0.0), "v")
    C.inc(v.memset(blkv[:, :], 0.0), "v")
    C.wait(g, "v")
    for blk, w in [(blkq, wq_t), (blkk, wk_t), (blkv, wv_t)]:
        gdma(blk[0:64, 0:64], w)
        gdma(blk[64:128, 64:128], w)
    C.wait(v, "g")
    C.inc(v.tensor_scalar_add(macE[:, :], mac[:, :], EXP_BIAS), "v")
    stg = [place(A_QB, [128, HIDP], BF16), place(A_QB + 5664, [128, HIDP], BF16)]
    wjobs = []
    for l in range(2):
        wjobs += [(wqkv[l], wc_qkv.ap()[l], D, 1536, False),
                  (wo_p[l], wc_wo.ap()[l], 512, D, False),
                  (w1a[l], wc_w1a.ap()[l], D, HIDP, True),
                  (w1g[l], wc_w1g.ap()[l], D, HIDP, True),
                  (w2p[l], wc_w2.ap()[l], HIDP, D, False)]
    wjobs.append((woutT, wc_woutT.ap(), D, D, False))
    ci = 0
    for src, dst, rows, cols, tiled in wjobs:
        for r0 in range(0, rows, 128):
            bi = ci % 2
            pch, qch = ("p0", "q0") if bi == 0 else ("p1", "q1")
            if ci >= 2:
                C.wait(g, qch)
            gdma(stg[bi][:, 0:cols], src[r0:r0 + 128, :], ch=pch)
            C.wait(yq, pch)
            if tiled:
                ydma(dst[:, :, r0 // 128, :],
                     stg[bi][:, 0:cols].rearrange("p (j e) -> p j e", e=128),
                     ch=qch)
            else:
                ydma(dst[r0:r0 + 128, :], stg[bi][:, 0:cols], ch=qch)
            ci += 1
    barrier(t, s, v, g, yq)

    def ln_full(src_fn, mode, l=None):
        def lcopies(c):
            b = c % 2
            C.inc(s.copy(tXb_[b][:, :], src_fn(c)), "s")
            C.inc(s.activation(tSQb_[b][:, :], src_fn(c), AF.Square), "s")
            return C.cnt["s"]
        C.wait(s, "v")
        cmile = {0: lcopies(0)}
        mmile = {}
        for c in range(NCH):
            b = c % 2
            C.wait(t, "s", cmile[c])
            for nt in range(4):
                t.matmul(P_A[0:1, nt * 512:(nt + 1) * 512], onesb[:, :],
                         tXb_[b][:, nt * 512:(nt + 1) * 512],
                         start=(c == 0), stop=(c == NCH - 1))
                dst = (P_B[0:1, nt * 512:(nt + 1) * 512] if nt < 3
                       else P_C[0:1, 0:512])
                mm = t.matmul(dst, onesb[:, :],
                              tSQb_[b][:, nt * 512:(nt + 1) * 512],
                              start=(c == 0), stop=(c == NCH - 1))
            C.inc(mm, "t")
            mmile[c] = C.cnt["t"]
            if c + 1 < NCH:
                C.wait(s, "t", mmile.get(c - 1, 0))
                cmile[c + 1] = lcopies(c + 1)
        for nt in range(4):
            mean_ap = P_A[0:1, nt * 512:(nt + 1) * 512]
            sq_ap = (P_B[0:1, nt * 512:(nt + 1) * 512] if nt < 3 else P_C[0:1, 0:512])
            C.wait(v, "t")
            C.inc(v.tensor_scalar(r_m[:, :], mean_ap, -1.0 / 1024.0, None, ALU.mult), "v")
            C.inc(v.tensor_scalar(r_q[:, :], sq_ap, 1.0 / 1024.0, None, ALU.mult), "v")
            C.inc(v.tensor_tensor(r_v[:, :], r_m[:, :], r_m[:, :], ALU.mult), "v")
            C.inc(v.tensor_sub(r_q[:, :], r_q[:, :], r_v[:, :]), "v")
            C.wait(s, "v")
            C.inc(s.activation(r_s[:, :], r_q[:, :], AF.Ln, bias=EPS1), "s")
            C.inc(s.activation(r_s[:, :], r_s[:, :], AF.Exp, scale=-0.5), "s")
            C.wait(t, "s")
            C.inc(t.matmul(P_A[:, nt * 512:(nt + 1) * 512], onesrowf[0:1, :],
                           r_m[0:1, :], start=True, stop=True), "t")
            C.wait(v, "t")
            C.inc(v.tensor_copy(tnm[:, nt * 512:(nt + 1) * 512],
                                P_A[:, nt * 512:(nt + 1) * 512]), "v")
            C.wait(t, "v")
            C.inc(t.matmul(P_A[:, nt * 512:(nt + 1) * 512], onesrowf[0:1, :],
                           r_s[0:1, :], start=True, stop=True), "t")
            C.wait(v, "t")
            C.inc(v.tensor_copy(trs[:, nt * 512:(nt + 1) * 512],
                                P_A[:, nt * 512:(nt + 1) * 512]), "v")
        for c in range(NCH):
            C.inc(v.tensor_add(P_A[:, :], src_fn(c), tnm[:, :]), "v")
            if mode == "y":
                C.inc(v.tensor_mul(Y[:, c, 1:1 + NT], P_A[:, :], trs[:, :]), "v")
            elif mode == "xlb":
                C.inc(v.tensor_mul(XLb[:, c, :], P_A[:, :], trs[:, :]), "v")
            else:
                C.inc(v.tensor_mul(P_A[:, :], P_A[:, :], trs[:, :]), "v")
                C.inc(v.tensor_scalar(Hb[:, c, :], P_A[:, :],
                                      lng_c[0:128, l * 8 + c:l * 8 + c + 1],
                                      lnb_c[0:128, l * 8 + c:l * 8 + c + 1],
                                      ALU.mult, ALU.add), "v")

    def cc_tok():
        C.wait(g, "v")
        C.wait(g, "t")
        C.wait(g, "s")
        gdma(ccT_in.ap(), Y[:, :, 2048:2049].rearrange("p c o -> p (c o)"), slow=True)
        C.wait(g, "g")
        C.inc(g.collective_compute("AllGather", ALU.bypass, replica_groups=PAIRS,
                                   ins=[ccT_in.ap().opt()],
                                   outs=[ccT_out.ap().opt()]), "c")
        C.wait(g, "c")
        gdma(Y[:, :, 0:1].rearrange("p c o -> p (c o)"),
             ccT_out.ap()[bass.ds(oth_sv, 1), :, :].rearrange("o p c -> p (o c)"),
             slow=True)
        C.wait(v, "g")
        C.inc(v.tensor_scalar(Y[:, :, 0:1], Y[:, :, 0:1], hzc[0:128, 0:1], None,
                              ALU.mult), "v")

    def attn_norm(low, oA_ap, oB_ap, sumA_ap, sumB_ap, dstA, dstB, rcp, rbc):
        rr = 64 if low else 0
        C.wait(s, "t")
        C.inc(s.activation(rcp[rr:rr + 1, 512:1024], sumA_ap, AF.Ln), "s")
        C.inc(s.activation(rcp[rr:rr + 1, 0:512], rcp[rr:rr + 1, 512:1024],
                           AF.Exp, scale=-1.0), "s")
        C.inc(s.activation(rbc[rr:rr + 1, 512:1024], sumB_ap, AF.Ln), "s")
        C.inc(s.activation(rbc[rr:rr + 1, 0:512], rbc[rr:rr + 1, 512:1024],
                           AF.Exp, scale=-1.0), "s")
        C.wait(t, "s")
        if low:
            t.matmul(P_B[0:64, 0:512], ones64f[64:65, :], rcp[64:65, 0:512],
                     start=True, stop=True)
            mm2 = t.matmul(P_B[0:64, 512:1024], ones64f[64:65, :],
                           rbc[64:65, 0:512], start=True, stop=True)
        else:
            t.matmul(P_B[64:128, 0:512], ones64f[0:1, :], rcp[0:1, 0:512],
                     start=True, stop=True, tile_position=(0, 64))
            mm2 = t.matmul(P_B[64:128, 512:1024], ones64f[0:1, :],
                           rbc[0:1, 0:512], start=True, stop=True,
                           tile_position=(0, 64))
        C.inc(mm2, "t")
        C.wait(v, "t")
        if low:
            C.inc(v.tensor_copy(rcp[0:64, 0:512], P_B[0:64, 0:512]), "v")
            C.inc(v.tensor_copy(rbc[0:64, 0:512], P_B[0:64, 512:1024]), "v")
            C.inc(v.tensor_mul(dstA, oA_ap, rcp[0:64, 0:512]), "v")
            C.inc(v.tensor_mul(dstB, oB_ap, rbc[0:64, 0:512]), "v")
        else:
            C.inc(v.tensor_copy(rcp[64:128, 0:512], P_B[64:128, 0:512]), "v")
            C.inc(v.tensor_copy(rbc[64:128, 0:512], P_B[64:128, 512:1024]), "v")
            C.inc(v.tensor_mul(dstA, oA_ap, rcp[64:128, 0:512]), "v")
            C.inc(v.tensor_mul(dstB, oB_ap, rbc[64:128, 0:512]), "v")

    LAYER_STOP = {}

    def ffn_layer(l, first_of_call):
        barrier(s)
        if first_of_call:
            C.wait(v, "y")
            C.inc(v.memset(v_tm[:, :, :, 64:65], 1.0), "v")
        for c in range(NCH):
            C.inc(s.copy(Yb[:, c, 0:2049], Y[:, c, 0:2049]), "s")
        for rawsrc, dstt, qcol, ncols in (
                (tab_cos, tqc, 0 + l, 2048), (tab_sin, tqs, 2 + l, 2048),
                (tab_icos, tkc, 4 + l, 2560), (tab_isin, tks, 6 + l, 2560)):
            C.wait(g, "v")
            gdma(rawt[:, :], rawsrc, ch="p0")
            C.wait(v, "p0")
            C.inc(v.tensor_scalar(dstt[:, 0:ncols], rawt[:, 2560 - ncols:2560],
                                  qsc[0:128, qcol:qcol + 1], None, ALU.mult), "v")
        if stage == 87:
            barrier(g, t, s, v, yq)
            gdma(xout[:, 0:2048], tqc[:, :])
            gdma(xout[:, 2048:4096], tqs[:, :])
            gdma(xout[:, 4096:6656], tkc[:, :])
            gdma(xout[:, 6656:9216], tks[:, :])
            barrier(g, t, s, v, yq)
            return
        if stage == 8:
            return
        barrier(t, g)
        for half in range(2):
            C.wait(g, "t")
            gdma(wv_sb[:, :, :],
                 wc_qkv.ap()[l].rearrange("(kc p) o -> p kc o", p=128)
                 [:, :, 1024 + half * 256:1024 + (half + 1) * 256], ch="p1")
            C.wait(t, "p1")
            lastv = {0: 0, 1: 0}
            for tt in range(16):
                ps = P_C[:, 0:512] if (tt % 2 == 0) else P_B[:, 0:512]
                psv = ps[0:128, 0:256]
                if tt >= 2:
                    C.wait(t, "v", lastv[tt % 2])
                for kc in range(NCH):
                    sh = 1 if kc < 4 else 0
                    mm = t.matmul(psv, Yb[:, kc, sh + tt * 128:sh + (tt + 1) * 128],
                                  wv_sb[:, kc, :], start=(kc == 0), stop=(kc == NCH - 1))
                C.inc(mm, "t")
                C.wait(v, "t")
                C.inc(v.tensor_copy(
                    v_tm[:, 4 + tt, half * 4:(half + 1) * 4, 0:64],
                    psv.rearrange("p (h e) -> p h e", e=64)), "v")
                lastv[tt % 2] = C.cnt["v"]
        if stage == 9:
            return
        import os as _os
        K_SUB = int(_os.environ.get("K_SUB", "99"))
        K_OC = int(_os.environ.get("K_OC", "8"))
        barrier(t, g)
        gdma(wtq[0][:, :, :],
             wc_qkv.ap()[l].rearrange("(kc p) o -> p kc o", p=128)[:, :, 0:128],
             ch="p0")
        pa_free = C.cnt["v"]
        tmile = {0: 0, 1: 0}
        for oc in range(K_OC):
            C.wait(t, "p0" if oc % 2 == 0 else "p1")
            C.wait(t, "v", pa_free)
            for nt in range(4):
                for kc in range(NCH):
                    sh = 1 if kc < 4 else 0
                    mm = t.matmul(P_A[:, nt * 512:(nt + 1) * 512],
                                  wtq[oc % 2][:, kc, :],
                                  Yb[:, kc, sh + nt * 512:sh + (nt + 1) * 512],
                                  start=(kc == 0), stop=(kc == NCH - 1))
            C.inc(mm, "t")
            tmile[oc % 2] = C.cnt["t"]
            if oc + 1 < K_OC:
                C.wait(g, "t", tmile[(oc + 1) % 2])
                gdma(wtq[(oc + 1) % 2][:, :, :],
                     wc_qkv.ap()[l].rearrange("(kc p) o -> p kc o", p=128)
                     [:, :, (oc + 1) * 128:(oc + 2) * 128],
                     ch="p0" if (oc + 1) % 2 == 0 else "p1")
            isq = oc < 4
            tc_, ts_ = (tqc, tqs) if isq else (tkc, tks)
            tofs = 0 if isq else 512
            if K_SUB == 0:
                C.wait(v, "t")
                for nt in range(4):
                    dst = (Qb[:, oc, nt * 512:(nt + 1) * 512] if isq else
                           Kb[:, oc - 4, 512 + nt * 512:512 + (nt + 1) * 512])
                    C.inc(v.tensor_copy(dst, P_A[:, nt * 512:(nt + 1) * 512]), "v")
                pa_free = C.cnt["v"]
                continue
            K_STEP = int(_os.environ.get("K_STEP", "99"))
            for nt in range(4):
                C.wait(s, "t")
                C.inc(s.activation(rt_q2[:, :], P_A[:, nt * 512:(nt + 1) * 512],
                                   AF.Square), "s")
                if K_STEP > 1:
                    C.wait(t, "s")
                    C.wait(t, "v")
                    C.inc(t.matmul(P_C[0:2, 0:512], ind2b[:, :], rt_q2[:, :],
                                   start=True, stop=True), "t")
                if K_STEP > 2:
                    C.wait(s, "t")
                    C.inc(s.activation(rt_ss[:, :], P_C[0:2, 0:512], AF.Ln,
                                       bias=EPS12), "s")
                    C.inc(s.activation(rt_ss[:, :], rt_ss[:, :], AF.Exp,
                                       scale=-0.5), "s")
                if K_STEP > 4:
                    C.wait(t, "s")
                    C.inc(t.matmul(P_C[:, 0:512], ind2f[:, :], rt_ss[0:2, :],
                                   start=True, stop=True), "t")
                if K_STEP > 5:
                    C.wait(v, "t")
                    C.inc(v.tensor_copy(rt_rb[:, :], P_C[:, 0:512]), "v")
                    C.inc(v.tensor_mul(rt_qn[:, :], P_A[:, nt * 512:(nt + 1) * 512],
                                       rt_rb[:, :]), "v")
                if K_STEP > 6:
                    C.wait(s, "v")
                    C.inc(s.copy(rt_qnb[:, :], rt_qn[:, :]), "s")
                if K_STEP > 7:
                    C.wait(t, "s")
                    C.inc(t.matmul(P_C[:, 0:512], rotT[:, :], rt_qnb[:, :],
                                   start=True, stop=True), "t")
                if K_STEP > 8:
                    C.wait(v, "t")
                    tci = tc_[:, tofs + nt * 512:tofs + (nt + 1) * 512]
                    tsi = ts_[:, tofs + nt * 512:tofs + (nt + 1) * 512]
                    C.inc(v.tensor_mul(rt_rb[:, :], rt_qn[:, :], tci), "v")
                if K_STEP > 9:
                    C.inc(v.tensor_copy(rt_t2[:, :], P_C[:, 0:512]), "v")
                    C.inc(v.tensor_mul(rt_qn[:, :], rt_t2[:, :], tsi), "v")
                if K_STEP > 10:
                    if K_SUB >= 2:
                        dst = (Qb[:, oc, nt * 512:(nt + 1) * 512] if isq else
                               Kb[:, oc - 4, 512 + nt * 512:512 + (nt + 1) * 512])
                        C.inc(v.tensor_add(dst, rt_rb[:, :], rt_qn[:, :]), "v")
            pa_free = C.cnt["v"]
        LAYER_STOP["qkv"] = True
        if stage == 10:
            return
        C.wait(g, "v")
        gdma(ccA_in.ap()[:, 0:2048].rearrange("p (c w) -> p c w", w=512),
             Kb[:, :, 2048:2560])
        gdma(ccA_in.ap()[:, 2048:2048 + VW],
             v_tm[:, 16:20, :, :].rearrange("p c h e -> p (c h e)"))
        C.wait(g, "g")
        C.inc(g.collective_compute("AllGather", ALU.bypass, replica_groups=PAIRS,
                                   ins=[ccA_in.ap().opt()],
                                   outs=[ccA_out.ap().opt()]), "c")
        C.wait(g, "c")
        oslc = ccA_out.ap()[bass.ds(oth_sv, 1), :, :]
        gdma(Kb[:, :, 0:512],
             oslc[:, :, 0:2048].rearrange("o p (c w) -> p (o c) w", w=512))
        gdma(v_tm[:, 0:4, :, :].rearrange("p c h e -> p (c h e)"),
             oslc[:, :, 2048:2048 + VW].rearrange("o p e -> p (o e)"))
        C.wait(v, "g")
        C.inc(v.tensor_scalar(Kb[:, :, 0:512], Kb[:, :, 0:512],
                              hzc[0:128, 0:1], None, ALU.mult), "v")
        C.inc(v.tensor_scalar(v_tm[:, 0:4, :, :], v_tm[:, 0:4, :, :],
                              hzc[0:128, 0:1], None, ALU.mult), "v")
        barrier(t, s, v)
        for hp in range(4):
            low = hp < 2
            cA, cB = 2 * hp, 2 * hp + 1
            if not low:
                cA, cB = 2 * hp - 4, 2 * hp - 3
            rA = slice(0, 64) if low else slice(64, 128)
            sets = [P_A[:, 0:1024], P_A[:, 1024:2048], P_B[:, 0:1024]]
            for w in range(NW):
                C.wait(t, "v")
                C.wait(t, "s")
                expmile, avmile = {}, {}
                for i in range(9):
                    if i < 8:
                        m = i % 3
                        sp_ = sets[m]
                        kcol = 512 * w + 128 * i
                        mtile = (maskslb[:, i * 512:(i + 1) * 512] if i < 4 else
                                 masksb[:, (i - 4) * 512:(i - 3) * 512])
                        C.wait(t, "s", expmile.get(i - 3, 0))
                        t.matmul(sp_[0:128, 0:512], Kb[0:64, hp, kcol:kcol + 128],
                                 Qb[0:64, hp, w * 512:(w + 1) * 512],
                                 start=True, stop=False, tile_position=(0, 0))
                        t.matmul(sp_[0:128, 512:1024],
                                 Kb[64:128, hp, kcol:kcol + 128],
                                 Qb[64:128, hp, w * 512:(w + 1) * 512],
                                 start=True, stop=False, tile_position=(64, 0))
                        t.matmul(sp_[0:128, 0:512], ident[:, :], mtile,
                                 start=False, stop=True)
                        mm = t.matmul(sp_[0:128, 512:1024], ident[:, :], mtile,
                                      start=False, stop=True)
                        C.inc(mm, "t")
                        qkm = C.cnt["t"]
                    if i >= 1:
                        j = i - 1
                        eb = expb[j % 2]
                        tc = 4 * w + j
                        st_, sp2 = (j == 0), (j == 7)
                        hA, hB = 2 * hp, 2 * hp + 1
                        C.wait(t, "s", expmile[j])
                        if low:
                            t.matmul(P_B[0:65, 1024:1536],
                                     v_tm[:, tc, hA, 0:65], eb[:, 0:512],
                                     start=st_, stop=sp2)
                            mm = t.matmul(P_C[0:65, 0:512],
                                          v_tm[:, tc, hB, 0:65],
                                          eb[:, 512:1024], start=st_, stop=sp2)
                        else:
                            t.matmul(P_B[64:128, 1024:1536],
                                     v_tm[:, tc, hA, 0:64], eb[:, 0:512],
                                     start=st_, stop=sp2, tile_position=(0, 64))
                            t.matmul(P_B[0:1, 1024:1536], onesb[:, :],
                                     eb[:, 0:512], start=st_, stop=sp2)
                            t.matmul(P_C[64:128, 0:512],
                                     v_tm[:, tc, hB, 0:64], eb[:, 512:1024],
                                     start=st_, stop=sp2, tile_position=(0, 64))
                            mm = t.matmul(P_C[0:1, 0:512], onesb[:, :],
                                          eb[:, 512:1024], start=st_, stop=sp2)
                        C.inc(mm, "t")
                        avmile[j] = C.cnt["t"]
                    if i < 8:
                        C.wait(s, "t", qkm)
                        if i >= 2:
                            C.wait(s, "t", avmile[i - 2])
                        bia = (macE[0:128, 0:1] if (w == 0 and i < 4) else EBIA)
                        C.inc(s.activation(expb[i % 2][:, 0:1024], sets[i % 3],
                                           AF.Exp, bias=bia, scale=8.0), "s")
                        expmile[i] = C.cnt["s"]
                qcols = slice(w * 512, (w + 1) * 512)
                if low:
                    attn_norm(True, P_B[0:64, 1024:1536], P_C[0:64, 0:512],
                              P_B[64:65, 1024:1536], P_C[64:65, 0:512],
                              o_fm[0:64, cA, qcols], o_fm[0:64, cB, qcols],
                              recip_sb, recbc)
                else:
                    attn_norm(False, P_B[64:128, 1024:1536], P_C[64:128, 0:512],
                              P_B[0:1, 1024:1536], P_C[0:1, 0:512],
                              o_fm[64:128, cA, qcols], o_fm[64:128, cB, qcols],
                              recip_sb, recbc)
        if stage == 11:
            return
        barrier(g)
        rearr_wo = wc_wo.ap()[l].rearrange("(kc p) o -> p kc o", p=128)
        C.wait(yq, "t")
        gdma(wos[:, 0:2, :], rearr_wo[:, 0:2, :])
        ydma(wos[:, 2:4, :], rearr_wo[:, 2:4, :], ch="q0")
        barrier(t)
        for oc in range(8):
            if oc >= 1:
                C.wait(t, "v")
            for nt in range(4):
                for kc in range(4):
                    mm = t.matmul(P_A[:, nt * 512:(nt + 1) * 512],
                                  wos[:, kc, oc * 128:(oc + 1) * 128],
                                  o_fm[:, kc, nt * 512:(nt + 1) * 512],
                                  start=(kc == 0), stop=(kc == 3))
            C.inc(mm, "t")
            C.wait(v, "t")
            C.inc(v.tensor_add(Y[:, oc, 1:1 + NT], Y[:, oc, 1:1 + NT], P_A[:, :]), "v")
        if stage == 12:
            return
        cc_tok()
        barrier(t, s, v)

        def ysrc(c):
            return Y[:, c, 1:1 + NT] if c < 4 else Y[:, c, 0:NT]
        ln_full(ysrc, "h", l=l)
        if stage == 13:
            return
        barrier(t, g, s, v)
        rearr1a = wc_w1a.ap()[l]
        rearr1g = wc_w1g.ap()[l]
        rearr2 = wc_w2.ap()[l].rearrange("(kc p) o -> p kc o", p=128)
        for tt in range(4):
            tcols = slice(tt * 512, (tt + 1) * 512)
            dmA, dmG = {}, {}

            def loadj(j):
                pch = "p0" if j % 2 == 0 else "p1"
                qch = "q0" if j % 2 == 0 else "q1"
                gdma(wt1a[j % 3][:, :, :], rearr1a[:, j, :, :], ch=pch)
                dmA[j] = (pch, C.cnt[pch])
                ydma(wt1g[j % 3][:, :, :], rearr1g[:, j, :, :], ch=qch)
                dmG[j] = (qch, C.cnt[qch])
            C.wait(g, "t")
            C.wait(yq, "t")
            loadj(0)
            loadj(1)
            jmile = {}
            vmile, smile = {}, {}
            for j in range(NJ):
                bi = j % 3
                for nm_, mi_ in (dmA[j], dmG[j]):
                    C.wait(t, nm_, mi_)
                if j >= 2:
                    C.wait(t, "v", vmile[j - 2])
                    C.wait(t, "s", smile[j - 2])
                pj = (j % 2) * 1024
                for kc in range(NCH):
                    t.matmul(P_A[:, pj:pj + 512], wt1a[bi][:, kc, :],
                             Hb[:, kc, tcols], start=(kc == 0), stop=(kc == 7))
                for kc in range(NCH):
                    mm = t.matmul(P_A[:, pj + 512:pj + 1024], wt1g[bi][:, kc, :],
                                  Hb[:, kc, tcols], start=(kc == 0), stop=(kc == 7))
                C.inc(mm, "t")
                if j + 2 < NJ:
                    C.wait(g, "t", jmile.get(j - 1, 0))
                    C.wait(yq, "t", jmile.get(j - 1, 0))
                    loadj(j + 2)
                jmile[j] = C.cnt["t"]
                tj = tnm[:, (j % 2) * 512:(j % 2) * 512 + 512]
                C.wait(s, "t")
                C.inc(s.activation(tj, P_A[:, pj + 512:pj + 1024],
                                   AF.Gelu, bias=b1g_c[0:128, l * NJ + j:l * NJ + j + 1]),
                      "s")
                smile[j] = C.cnt["s"]
                C.wait(v, "s")
                C.inc(v.scalar_tensor_tensor(
                    AGb[:, j, :], P_A[:, pj:pj + 512],
                    b1a_c[0:128, l * NJ + j:l * NJ + j + 1], tj,
                    ALU.add, ALU.mult), "v")
                vmile[j] = C.cnt["v"]
            dm2 = {}

            def loadk(kc):
                ch = "r0" if kc % 2 == 0 else "r1"
                gdma(wt2[kc % 3][:, :], rearr2[:, kc, :], ch=ch)
                dm2[kc] = (ch, C.cnt[ch])
            barrier(t, g)
            loadk(0)
            loadk(1)
            slots = [P_A[:, i * 512:(i + 1) * 512] for i in range(4)] + \
                    [P_B[:, i * 512:(i + 1) * 512] for i in range(3)] + [P_C[:, 0:512]]
            kmile = {}
            for kc in range(NJ):
                nm_, mi_ = dm2[kc]
                C.wait(t, nm_, mi_)
                for oc in range(8):
                    mm = t.matmul(slots[oc], wt2[kc % 3][:, oc * 128:(oc + 1) * 128],
                                  AGb[:, kc, :], start=(kc == 0), stop=(kc == NJ - 1))
                C.inc(mm, "t")
                if kc + 2 < NJ:
                    C.wait(g, "t", kmile.get(kc - 1, 0))
                    loadk(kc + 2)
                kmile[kc] = C.cnt["t"]
            C.wait(v, "t")
            for oc in range(8):
                C.inc(v.scalar_tensor_tensor(
                    Y[:, oc, 1 + tt * 512:1 + (tt + 1) * 512], slots[oc],
                    b2_c[0:128, l * 8 + oc:l * 8 + oc + 1],
                    Y[:, oc, 1 + tt * 512:1 + (tt + 1) * 512],
                    ALU.add, ALU.add), "v")
            barrier(t)
        if first_of_call:
            cc_tok()

    def dense_phase():
        def xsrc(c):
            return X[:, c, :]
        ln_full(xsrc, "xlb")
        barrier(g)
        gdma(ccD_in.ap(), XLb[:, :, :].rearrange("p c t -> p (c t)"))
        C.wait(g, "g")
        C.inc(g.collective_compute("AllGather", ALU.bypass, replica_groups=PAIRS,
                                   ins=[ccD_in.ap().opt()],
                                   outs=[ccD_out.ap().opt()]), "c")
        C.wait(g, "c")
        gdma(XLo[:, :, :].rearrange("p c t -> p (c t)"),
             ccD_out.ap()[bass.ds(oth_sv, 1), :, :].rearrange("o p e -> p (o e)"))
        C.inc(v.memset(Vd[:, :, 64:65], 1.0), "v")
        C.inc(v.memset(Vd[:, :, 129:130], 1.0), "v")
        barrier(t, s, v)
        import os as _os
        K_NP = int(_os.environ.get("K_NP", "8"))
        for c in range(K_NP):
            low = c < 4
            cA, cB = (2 * c, 2 * c + 1) if low else (2 * c - 8, 2 * c - 7)
            C.wait(t, "v")
            C.wait(t, "s")
            for nt in range(4):
                C.inc(t.matmul(P_A[:, nt * 512:(nt + 1) * 512], blkq[:, :],
                               XLb[:, c, nt * 512:(nt + 1) * 512],
                               start=True, stop=True), "t")
            C.wait(s, "t")
            C.inc(s.copy(Qd[:, :], P_A[:, :]), "s")
            C.wait(t, "s")
            for half, src in ((0, XLb), (1, XLo)):
                for nt in range(4):
                    C.inc(t.matmul(P_A[:, nt * 512:(nt + 1) * 512], blkk[:, :],
                                   src[:, c, nt * 512:(nt + 1) * 512],
                                   start=True, stop=True), "t")
                C.wait(s, "t")
                C.inc(s.copy(Kd[:, half * 2048:(half + 1) * 2048], P_A[:, :]), "s")
                C.wait(t, "s")
                lastv = {0: 0, 1: 0}
                for ttl in range(16):
                    ps = P_C[:, 0:512] if (ttl % 2 == 0) else P_B[:, 0:512]
                    psv = ps[0:128, 0:128]
                    if ttl >= 2:
                        C.wait(t, "v", lastv[ttl % 2])
                    C.inc(t.matmul(psv, src[:, c, ttl * 128:(ttl + 1) * 128],
                                   blkv[:, :], start=True, stop=True), "t")
                    C.wait(v, "t")
                    C.inc(v.tensor_copy(
                        Vd[:, half * 16 + ttl, 0:130].rearrange(
                            "p (h e) -> p h e", e=65)[:, :, 0:64],
                        psv.rearrange("p (h e) -> p h e", e=64)), "v")
                    lastv[ttl % 2] = C.cnt["v"]
            barrier(t, s, v)
            sets = [P_A[:, 0:1024], P_A[:, 1024:2048], P_B[:, 0:1024]]
            for qt in range(4):
                qcols = slice(qt * 512, (qt + 1) * 512)
                C.wait(t, "v")
                C.wait(t, "s")
                expmile, avmile = {}, {}
                for i in range(33):
                    if i < 32:
                        m = i % 3
                        sp_ = sets[m]
                        C.wait(t, "s", expmile.get(i - 3, 0))
                        t.matmul(sp_[0:128, 0:512],
                                 Kd[0:64, i * 128:(i + 1) * 128],
                                 Qd[0:64, qcols], start=True, stop=True,
                                 tile_position=(0, 0))
                        mm = t.matmul(sp_[0:128, 512:1024],
                                      Kd[64:128, i * 128:(i + 1) * 128],
                                      Qd[64:128, qcols], start=True, stop=True,
                                      tile_position=(64, 0))
                        C.inc(mm, "t")
                        qkm = C.cnt["t"]
                    if i >= 1:
                        j = i - 1
                        eb = expd[j % 2]
                        st_, sp2 = (j == 0), (j == 31)
                        C.wait(t, "s", expmile[j])
                        if low:
                            t.matmul(P_B[0:65, 1024:1536], Vd[:, j, 0:65],
                                     eb[:, 0:512], start=st_, stop=sp2)
                            mm = t.matmul(P_C[0:65, 0:512], Vd[:, j, 65:130],
                                          eb[:, 512:1024], start=st_, stop=sp2)
                        else:
                            t.matmul(P_B[64:128, 1024:1536], Vd[:, j, 0:64],
                                     eb[:, 0:512], start=st_, stop=sp2,
                                     tile_position=(0, 64))
                            t.matmul(P_B[0:1, 1024:1536], onesb[:, :],
                                     eb[:, 0:512], start=st_, stop=sp2)
                            t.matmul(P_C[64:128, 0:512], Vd[:, j, 65:129],
                                     eb[:, 512:1024], start=st_, stop=sp2,
                                     tile_position=(0, 64))
                            mm = t.matmul(P_C[0:1, 0:512], onesb[:, :],
                                          eb[:, 512:1024], start=st_, stop=sp2)
                        C.inc(mm, "t")
                        avmile[j] = C.cnt["t"]
                    if i < 32:
                        C.wait(s, "t", qkm)
                        if i >= 2:
                            C.wait(s, "t", avmile[i - 2])
                        C.inc(s.activation(expd[i % 2][:, 0:1024], sets[i % 3],
                                           AF.Exp, bias=EBIA,
                                           scale=1.0 / 32.0), "s")
                        expmile[i] = C.cnt["s"]
                if low:
                    attn_norm(True, P_B[0:64, 1024:1536], P_C[0:64, 0:512],
                              P_B[64:65, 1024:1536], P_C[64:65, 0:512],
                              o_dn[0:64, cA, qcols], o_dn[0:64, cB, qcols],
                              recip_d, recbc_d)
                else:
                    attn_norm(False, P_B[64:128, 1024:1536], P_C[64:128, 0:512],
                              P_B[0:1, 1024:1536], P_C[0:1, 0:512],
                              o_dn[64:128, cA, qcols], o_dn[64:128, cB, qcols],
                              recip_d, recbc_d)
        barrier(g, t, s, v, yq)
        rearr_wout = wc_woutT.ap().rearrange("(kc p) o -> p kc o", p=128)
        gdma(wouts[:, 0:4, :], rearr_wout[:, 0:4, :])
        ydma(wouts[:, 4:8, :], rearr_wout[:, 4:8, :], ch="q0")
        barrier(t)
        for oc in range(8):
            if oc >= 1:
                C.wait(t, "v")
            for nt in range(4):
                for kc in range(8):
                    mm = t.matmul(P_A[:, nt * 512:(nt + 1) * 512],
                                  wouts[:, kc, oc * 128:(oc + 1) * 128],
                                  o_dn[:, kc, nt * 512:(nt + 1) * 512],
                                  start=(kc == 0), stop=(kc == 7))
            C.inc(mm, "t")
            C.wait(v, "t")
            C.inc(v.scalar_tensor_tensor(
                X[:, oc, :], P_A[:, :], bout_c[0:128, oc:oc + 1], X[:, oc, :],
                ALU.add, ALU.add), "v")

    if stage <= 1:
        return finish()
    if stage == 85:
        barrier(t, s, v, g, yq)
        gdma(xout[:, 0:1536], wc_qkv.ap()[0][0:128, :])
        gdma(xout[:, 2048:2048 + 1536], wc_qkv.ap()[0][512:640, :])
        gdma(xout[:, 4096:4096 + 2816], wc_w1a.ap()[0][0:128, :])
        gdma(xout[:, 8192:8192 + 1024], wc_w2.ap()[0][0:128, :])
        barrier(g, t, s, v, yq)
        return nc, dbg_outs
    for call in range(2):
        def xsrc(c):
            return X[:, c, :]
        ln_full(xsrc, "y")
        if stage <= 2:
            return finish_y()
        cc_tok()
        if stage <= 3:
            return finish_y()
        barrier(yq)
        ydma(x_spill.ap(), X[:, :, :].rearrange("p c t -> p (c t)"))
        barrier(t, s, v, g)
        for li, l in enumerate((0, 1)):
            ffn_layer(l, first_of_call=(li == 0))
            if stage == 87:
                return nc, dbg_outs
            if stage in (8, 9, 10, 11, 12, 13):
                barrier(t, s, v, g, yq)
                if stage == 10:
                    import os as _os
                    if _os.environ.get("K_NODUMP"):
                        gdma(xout.rearrange("p (c t) -> p c t", t=NT),
                             Y[:, :, 1:1 + NT])
                    else:
                        gdma(xout[:, 0:8192].rearrange("p (c t) -> p c t", t=2048),
                             Qb[:, :, :])
                        for cc_ in range(4):
                            gdma(xout[:, 8192 + cc_ * 2048:8192 + (cc_ + 1) * 2048],
                                 Kb[:, cc_, 512:2560])
                elif stage == 11:
                    gdma(xout[:, 0:8192].rearrange("p (c t) -> p c t", t=2048),
                         o_fm[:, :, :])
                elif stage in (8, 9, 12):
                    gdma(xout.rearrange("p (c t) -> p c t", t=NT), Y[:, :, 1:1 + NT])
                elif stage == 13:
                    gdma(xout, Hb[:, :, :].rearrange("p c t -> p (c t)"))
                barrier(g, t, s, v, yq)
                return nc, dbg_outs
            if stage <= 4 and li == 0:
                barrier(t, s, v, g, yq)
                gdma(xout.rearrange("p (c t) -> p c t", t=NT), Y[:, :, 1:1 + NT])
                barrier(g, t, s, v, yq)
                return nc, dbg_outs
            if stage <= 5 and li == 1:
                barrier(t, s, v, g, yq)
                gdma(xout.rearrange("p (c t) -> p c t", t=NT), Y[:, :, 1:1 + NT])
                barrier(g, t, s, v, yq)
                return nc, dbg_outs
        barrier(g, yq)
        gdma(X[:, :, :], x_spill.ap().rearrange("p (c t) -> p c t", t=NT))
        C.wait(v, "g")
        for c in range(NCH):
            C.inc(v.scalar_tensor_tensor(
                X[:, c, :], Y[:, c, 1:1 + NT], 0.5, X[:, c, :],
                ALU.mult, ALU.add), "v")
        if stage <= 6:
            return finish()
        if call == 0:
            barrier(t, s, v, g, yq)
            dense_phase()
            if stage <= 7:
                return finish()
        barrier(t, s, v, g, yq)

    return finish()


_CACHE = {}


def _host_inputs(inputs):
    x = np.asarray(inputs["L0_attn_x"])
    Wq = np.asarray(inputs["L0_attn_Wq"])
    Wk = np.asarray(inputs["L0_attn_Wk"])
    Wv = np.asarray(inputs["L0_attn_Wv"])
    W_out = np.asarray(inputs["L0_attn_W_out"])
    b_out = np.asarray(inputs["L0_attn_b_out"])
    wqkv = np.asarray(inputs["L0_ffn_wqkv"])
    q_scale = np.asarray(inputs["L0_ffn_q_scale"])
    k_scale = np.asarray(inputs["L0_ffn_k_scale"])
    wo = np.asarray(inputs["L0_ffn_wo"])
    ln_g = np.asarray(inputs["L0_ffn_ln_g"])
    ln_b = np.asarray(inputs["L0_ffn_ln_b"])
    w1 = np.asarray(inputs["L0_ffn_w1"])
    b1 = np.asarray(inputs["L0_ffn_b1"])
    w2 = np.asarray(inputs["L0_ffn_w2"])
    b2 = np.asarray(inputs["L0_ffn_b2"])

    f32 = np.float32
    com = {}
    com["wqkv"] = np.ascontiguousarray(wqkv, dtype=f32)
    com["wo_p"] = np.ascontiguousarray(wo[:, PERM_WO, :], dtype=f32)
    w1a = np.zeros((2, D, HIDP), f32)
    w1g = np.zeros((2, D, HIDP), f32)
    w1a[:, :, :HID] = w1[:, :, :HID]
    w1g[:, :, :HID] = w1[:, :, HID:]
    com["w1a"], com["w1g"] = w1a, w1g
    w2pp = np.zeros((2, HIDP, D), f32)
    w2pp[:, :HID, :] = w2
    com["w2p"] = w2pp
    com["woutT"] = np.ascontiguousarray(W_out.T[PERM_WOUT, :], dtype=f32)
    com["wq_t"] = np.ascontiguousarray(Wq.T, dtype=f32)
    com["wk_t"] = np.ascontiguousarray(Wk.T, dtype=f32)
    com["wv_t"] = np.ascontiguousarray(Wv.T, dtype=f32)
    com["b_out_r"] = np.ascontiguousarray(b_out.reshape(8, 128).T, dtype=f32)
    com["ln_g_r"] = np.ascontiguousarray(ln_g.reshape(2, 8, 128).swapaxes(1, 2), dtype=f32)
    com["ln_b_r"] = np.ascontiguousarray(ln_b.reshape(2, 8, 128).swapaxes(1, 2), dtype=f32)
    com["b2_r"] = np.ascontiguousarray(b2.reshape(2, 8, 128).swapaxes(1, 2), dtype=f32)
    b1a = np.zeros((2, 128, NJ), f32)
    b1g = np.zeros((2, 128, NJ), f32)
    for l in range(2):
        pa = np.zeros(HIDP, f32)
        pg = np.zeros(HIDP, f32)
        pa[:HID] = b1[l, :HID]
        pg[:HID] = b1[l, HID:]
        b1a[l] = pa.reshape(NJ, 128).T
        b1g[l] = pg.reshape(NJ, 128).T
    com["b1a_r"], com["b1g_r"] = b1a, b1g
    perm64 = (np.arange(64) + 32) % 64
    com["qs_r"] = np.ascontiguousarray(
        np.tile(q_scale, (1, 2)).reshape(2, 128, 1), dtype=f32)
    com["qsp_r"] = np.ascontiguousarray(
        np.tile(q_scale[:, perm64], (1, 2)).reshape(2, 128, 1), dtype=f32)
    com["ks_r"] = np.ascontiguousarray(
        np.tile(k_scale, (1, 2)).reshape(2, 128, 1), dtype=f32)
    com["ksp_r"] = np.ascontiguousarray(
        np.tile(k_scale[:, perm64], (1, 2)).reshape(2, 128, 1), dtype=f32)

    in_maps = []
    for c in range(NCORE):
        b, h = c // 2, c % 2
        t0 = h * NT
        m = dict(com)
        m["xin"] = _pack_fm(np.asarray(x[b, t0:t0 + NT, :], dtype=f32))
        tc_, ts_, tic, tis = _rot_tables(t0)
        m["tab_cos"], m["tab_sin"] = tc_, ts_
        m["tab_icos"], m["tab_isin"] = tic, tis
        m["hz"] = np.full((128, 1), 0.0 if h == 0 else 1.0, f32)
        m["maskA"] = np.full((128, 1), NEG if h == 0 else 0.0, f32)
        in_maps.append(m)
    return in_maps


def kernel(**inputs):
    import os
    stg = int(os.environ.get("K_STAGE", "99"))
    key = ("prog", stg)
    if key not in _CACHE:
        _CACHE[key] = build_program(stage=stg)[0]
    nc = _CACHE[key]
    in_maps = _host_inputs(inputs)
    res = run_bass_kernel_spmd(nc, in_maps, core_ids=list(range(NCORE)))
    out = np.empty((B, S, D), dtype=np.float32)
    for c in range(NCORE):
        b, h = c // 2, c % 2
        fm = res.results[c]["xout"]
        out[b, h * NT:(h + 1) * NT, :] = (
            fm.reshape(128, NCH, NT).transpose(1, 0, 2).reshape(D, NT).T)
    return out


# version 06
# speedup vs baseline: 1.3884x; 1.0211x over previous
import numpy as np
import ml_dtypes
import concourse.bass as bass
import concourse.mybir as mybir
from concourse.bass_utils import run_bass_kernel_spmd

F32 = mybir.dt.float32
BF16 = mybir.dt.bfloat16
AF = mybir.ActivationFunctionType
ALU = mybir.AluOpType
BF = ml_dtypes.bfloat16

NCORE = 8
B, S, D = 4, 4096, 1024
NT = 2048
NCH = 8
YW = 2056
HL, DH, WIN, NW = 8, 64, 512, 4
HID, HIDP, NJ = 2730, 2816, 22
EXP_BIAS = -40.0
NEG = -30000.0
PAIRS = [[0, 1], [2, 3], [4, 5], [6, 7]]
PERM_WO = np.array([(c + 4 * (r // 64)) * 64 + (r % 64)
                    for c in range(4) for r in range(128)])
PERM_WOUT = np.array([(c + 8 * (r // 64)) * 64 + (r % 64)
                      for c in range(8) for r in range(128)])
PERM_ROT = np.array([(d // 64) * 64 + ((d % 64) + 32) % 64 for d in range(128)])

A_X = 0
A_Y = 66048
A_YB = 0
A_WT = 33024
A_WV = 37120
A_VT = 41216
A_QB = 131840
A_KB = 148224
A_TAB = 168704
A_RAW = 187136
A_RT = 192256
A_EXP = 0
A_OFM = 12288
A_RCP = 28672
A_RBC = 32768
A_WOS = 168704
A_HB = 0
A_AGB = 131840
A_W1T = 154368
A_W2T = 162560
A_LT = 166656
A_XLB = 66048
A_XLO = 98816
A_ODN = 131840
A_QD = 164608
A_KD = 168704
A_VD = 176896
A_EXD = 185216
A_RCD = 191360
A_RBD = 195456
A_WOUT = 66048
ARENA_B = 200448


def _rot_tables(t0):
    half_idx = np.arange(0, DH, 2, dtype=np.float64)
    inv_freq = 1.0 / 10000.0 ** (half_idx / DH)
    tpos = np.arange(t0 - 512, t0 + 2048, dtype=np.float64)
    fr = tpos[:, None] * inv_freq[None, :]
    freqs = np.concatenate([fr, fr], axis=1)
    base = (half_idx + 0.4 * DH) / (1.4 * DH)
    power = (tpos - S // 2) / float(WIN // 2)
    sc = base[None, :] ** power[:, None]
    scale = np.concatenate([sc, sc], axis=1)
    cos, sin = np.cos(freqs), np.sin(freqs)
    out = []
    for arr in (cos * scale, sin * scale, cos / scale, sin / scale):
        a = arr.T.astype(np.float32)
        out.append(np.ascontiguousarray(np.concatenate([a, a], axis=0)))
    return out


def _pack_fm(xs):
    T = xs.shape[0]
    return np.ascontiguousarray(
        xs.T.reshape(NCH, 128, T).transpose(1, 0, 2).reshape(128, NCH * T))


class Ctr:
    def __init__(self, nc):
        self.sem, self.cnt = {}, {}
        for n in ("g", "y", "t", "s", "v", "c", "p0", "p1", "q0", "q1", "r0", "r1"):
            self.sem[n] = nc.alloc_semaphore("sem_" + n)
            self.cnt[n] = 0

    def inc(self, inst, name, amt=1):
        inst.then_inc(self.sem[name], amt)
        self.cnt[name] += amt
        return inst

    def wait(self, eng, name, val=None):
        vv = self.cnt[name] if val is None else val
        if vv > 0:
            eng.wait_ge(self.sem[name], vv)


def build_program(stage=99, dbg=()):
    nc = bass.Bass("TRN2", target_bir_lowering=False, debug=False,
                   detect_race_conditions=False)

    def inp(name, shape, dt=F32):
        return nc.dram_tensor(name, list(shape), dt, kind="ExternalInput").ap()

    xin = inp("xin", [128, NCH * NT])
    wqkv = inp("wqkv", [2, D, 3 * 512])
    wo_p = inp("wo_p", [2, 512, D])
    w1a = inp("w1a", [2, D, HIDP])
    w1g = inp("w1g", [2, D, HIDP])
    w2p = inp("w2p", [2, HIDP, D])
    woutT = inp("woutT", [D, D])
    wq_t = inp("wq_t", [DH, DH])
    wk_t = inp("wk_t", [DH, DH])
    wv_t = inp("wv_t", [DH, DH])
    b_out_r = inp("b_out_r", [128, 8])
    ln_g_r = inp("ln_g_r", [2, 128, 8])
    ln_b_r = inp("ln_b_r", [2, 128, 8])
    b2_r = inp("b2_r", [2, 128, 8])
    b1a_r = inp("b1a_r", [2, 128, NJ])
    b1g_r = inp("b1g_r", [2, 128, NJ])
    qs_r = inp("qs_r", [2, 128, 1])
    qsp_r = inp("qsp_r", [2, 128, 1])
    ks_r = inp("ks_r", [2, 128, 1])
    ksp_r = inp("ksp_r", [2, 128, 1])
    tab_cos = inp("tab_cos", [128, 2560])
    tab_sin = inp("tab_sin", [128, 2560])
    tab_icos = inp("tab_icos", [128, 2560])
    tab_isin = inp("tab_isin", [128, 2560])
    hz = inp("hz", [128, 1])
    maskA = inp("maskA", [128, 1])

    xout = nc.dram_tensor("xout", [128, NCH * NT], F32, kind="ExternalOutput").ap()
    dbg_outs = {}
    for name, shape, dt in dbg:
        dbg_outs[name] = nc.dram_tensor("dbg_" + name, list(shape), dt,
                                        kind="ExternalOutput").ap()

    ident_d = nc.inline_tensor(np.eye(128, dtype=BF), name="c_ident")
    onesb_d = nc.inline_tensor(np.ones((128, 1), dtype=BF), name="c_onesb")
    onesrow_d = nc.inline_tensor(np.ones((1, 128), dtype=np.float32), name="c_onesrow")
    ones64_d = nc.inline_tensor(np.ones((128, 64), dtype=np.float32), name="c_ones64")
    ind2np = np.zeros((2, 128), dtype=np.float32)
    ind2np[0, :64] = 1.0
    ind2np[1, 64:] = 1.0
    ind2b_d = nc.inline_tensor(np.ascontiguousarray(ind2np.T).astype(BF), name="c_ind2b")
    ind2f_d = nc.inline_tensor(ind2np, name="c_ind2f")
    R64 = np.zeros((64, 64), dtype=np.float64)
    R64[np.arange(32), np.arange(32) + 32] = -1.0
    R64[np.arange(32, 64), np.arange(32)] = 1.0
    R128 = np.zeros((128, 128), dtype=np.float64)
    R128[0:64, 0:64] = R64
    R128[64:128, 64:128] = R64
    rotT_d = nc.inline_tensor(np.ascontiguousarray(R128.T).astype(BF), name="c_rotT")
    mk = np.zeros((128, 4, 512), dtype=np.float64)
    kk = np.arange(128)[:, None]
    qq = np.arange(512)[None, :]
    for jb in range(4):
        mk[:, jb, :] = np.where(qq < 128 * jb + kk, NEG, 0.0)
    masks_d = nc.inline_tensor(
        np.ascontiguousarray(mk.reshape(128, 2048)).astype(BF), name="c_masks")
    mlb = np.zeros((128, 4, 512), dtype=np.float64)
    for i_ in range(4):
        mlb[:, i_, :] = np.where(qq > 128 * i_ + kk, NEG, 0.0)
    masks_lb_d = nc.inline_tensor(
        np.ascontiguousarray(mlb.reshape(128, 2048)).astype(BF), name="c_masks_lb")
    cp = np.zeros((128, 8), dtype=np.float32)
    cp[:, 0] = 1e-5
    cp[:, 1] = EXP_BIAS
    cp[:, 2] = 1e-12
    cpool_d = nc.inline_tensor(cp, name="c_cpool")

    wc_qkv = nc.dram_tensor("wc_qkv", [2, D, 3 * 512], BF16, kind="Internal")
    wc_wo = nc.dram_tensor("wc_wo", [2, 512, D], BF16, kind="Internal")
    wc_w1a = nc.dram_tensor("wc_w1a", [2, 128, NJ, 8, 128], BF16, kind="Internal")
    wc_w1g = nc.dram_tensor("wc_w1g", [2, 128, NJ, 8, 128], BF16, kind="Internal")
    wc_w2 = nc.dram_tensor("wc_w2", [2, HIDP, D], BF16, kind="Internal")
    wc_woutT = nc.dram_tensor("wc_woutT", [D, D], BF16, kind="Internal")
    x_spill = nc.dram_tensor("x_spill", [128, NCH * NT], F32, kind="Internal")
    ccT_in = nc.dram_tensor("ccT_in", [128, 8], F32, kind="Internal")
    ccT_out = nc.dram_tensor("ccT_out", [2, 128, 8], F32, kind="Internal")
    VW = 4 * HL * 65
    ccA_in = nc.dram_tensor("ccA_in", [128, 2048 + VW], BF16, kind="Internal")
    ccA_out = nc.dram_tensor("ccA_out", [2, 128, 2048 + VW], BF16, kind="Internal")
    ccD_in = nc.dram_tensor("ccD_in", [128, NCH * NT], BF16, kind="Internal")
    ccD_out = nc.dram_tensor("ccD_out", [2, 128, NCH * NT], BF16, kind="Internal")

    ident = nc.alloc_sbuf_tensor("ident", [128, 128], BF16)
    onesb = nc.alloc_sbuf_tensor("onesb", [128, 1], BF16)
    onesrowf = nc.alloc_sbuf_tensor("onesrowf", [1, 128], F32)
    ones64f = nc.alloc_sbuf_tensor("ones64f", [128, 64], F32)
    ind2b = nc.alloc_sbuf_tensor("ind2b", [128, 2], BF16)
    ind2f = nc.alloc_sbuf_tensor("ind2f", [2, 128], F32)
    rotT = nc.alloc_sbuf_tensor("rotT", [128, 128], BF16)
    masksb = nc.alloc_sbuf_tensor("masksb", [128, 2048], BF16)
    maskslb = nc.alloc_sbuf_tensor("maskslb", [128, 2048], BF16)
    cpool = nc.alloc_sbuf_tensor("cpool", [128, 8], F32)
    blkq = nc.alloc_sbuf_tensor("blkq", [128, 128], BF16)
    blkk = nc.alloc_sbuf_tensor("blkk", [128, 128], BF16)
    blkv = nc.alloc_sbuf_tensor("blkv", [128, 128], BF16)
    bout_c = nc.alloc_sbuf_tensor("bout_c", [128, 8], F32)
    lng_c = nc.alloc_sbuf_tensor("lng_c", [128, 16], F32)
    lnb_c = nc.alloc_sbuf_tensor("lnb_c", [128, 16], F32)
    b2_c = nc.alloc_sbuf_tensor("b2_c", [128, 16], F32)
    b1a_c = nc.alloc_sbuf_tensor("b1a_c", [128, 2 * NJ], F32)
    b1g_c = nc.alloc_sbuf_tensor("b1g_c", [128, 2 * NJ], F32)
    qsc = nc.alloc_sbuf_tensor("qsc", [128, 8], F32)
    hzc = nc.alloc_sbuf_tensor("hzc", [128, 1], F32)
    mac = nc.alloc_sbuf_tensor("mac", [128, 1], F32)
    macE = nc.alloc_sbuf_tensor("macE", [128, 1], F32)

    arena = nc.alloc_sbuf_tensor("arena", [128, ARENA_B], mybir.dt.uint8)
    ab = nc.lookup_mloc(arena).addr
    rem = nc.sbuf_bytes_remaining
    assert rem >= 0, rem
    _n = [0]

    def place(off, shape, dtype):
        per = int(np.prod(shape[1:])) * mybir.dt.size(dtype)
        assert off % 32 == 0 and off + per <= ARENA_B, (off, per)
        _n[0] += 1
        h = nc.alloc_sbuf_tensor_at(f"ar{_n[0]}", list(shape), dtype,
                                    offset=ab + off)
        if not hasattr(nc, "_arena_map"):
            nc._arena_map = {}
        nc._arena_map.setdefault(off, []).append(h.name)
        return h

    X = place(A_X, [128, NCH, NT], F32)
    Y = place(A_Y, [128, NCH, YW], F32)
    Yb = place(A_YB, [128, NCH, YW], BF16)
    wtq = [place(A_WT + i * 2048, [128, 8, 128], BF16) for i in range(2)]
    wv_sb = place(A_WV, [128, 8, 256], BF16)
    v_tm = place(A_VT, [128, 20, HL, 65], BF16)
    Qb = place(A_QB, [128, 4, 2048], BF16)
    Kb = place(A_KB, [128, 4, 2560], BF16)
    tqc = place(A_TAB, [128, 2048], BF16)
    tqs = place(A_TAB + 4096, [128, 2048], BF16)
    tkc = place(A_TAB + 8192, [128, 2560], BF16)
    tks = place(A_TAB + 13312, [128, 2560], BF16)
    rawt = place(A_RAW, [128, 2560], BF16)
    rt_q2 = place(A_RT, [128, 512], BF16)
    rt_rb = place(A_RT + 1024, [128, 512], F32)
    rt_qn = place(A_RT + 3072, [128, 512], F32)
    rt_qnb = place(A_RT + 5120, [128, 512], BF16)
    rt_t2 = place(A_RT + 6144, [128, 512], F32)
    rt_ss = place(A_RAW, [2, 512], F32)
    expb = [place(A_EXP + i * 3072, [128, 1536], BF16) for i in range(4)]
    o_fm = place(A_OFM, [128, 4, 2048], BF16)
    recip_sb = place(A_RCP, [128, 1024], F32)
    recbc = place(A_RBC, [128, 1024], F32)
    wos = place(A_WOS, [128, 4, 1024], BF16)
    Hb = place(A_HB, [128, NCH, NT], BF16)
    AGb = place(A_AGB, [128, NJ, 512], BF16)
    wt1a = [place(A_W1T + i * 2048, [128, 8, 128], BF16) for i in range(3)]
    wt1g = [place(A_WT + i * 2048, [128, 8, 128], BF16) for i in range(3)]
    wt2 = [place(A_W2T, [128, 1024], BF16),
           place(A_W2T + 2048, [128, 1024], BF16),
           place(A_WT + 6144, [128, 1024], BF16)]
    tXb = place(A_LT, [128, NT], BF16)
    tSQb = place(A_LT + 4096, [128, NT], BF16)
    tXb_ = [tXb, place(A_RT, [128, NT], BF16)]
    tSQb_ = [tSQb, place(A_RT + 4096, [128, NT], BF16)]
    tnm = place(A_LT + 8192, [128, NT], F32)
    trs = place(A_LT + 16384, [128, NT], F32)
    r_m = place(A_LT + 24576, [1, 512], F32)
    r_q = place(A_LT + 26624, [1, 512], F32)
    r_v = place(A_LT + 28672, [1, 512], F32)
    r_s = place(A_LT + 30720, [1, 512], F32)
    XLb = place(A_XLB, [128, NCH, NT], BF16)
    XLo = place(A_XLO, [128, NCH, NT], BF16)
    o_dn = place(A_ODN, [128, NCH, NT], BF16)
    Qd = place(A_QD, [128, 2048], BF16)
    Kd = place(A_KD, [128, 4096], BF16)
    Vd = place(A_VD, [128, 32, 130], BF16)
    expd = [place(A_EXD + i * 3072, [128, 1536], BF16) for i in range(2)]
    recip_d = place(A_RCD, [128, 1024], F32)
    recbc_d = place(A_RBD, [128, 1024], F32)
    wouts = place(A_WOUT, [128, 8, 1024], BF16)

    P_A = nc.alloc_psum_tensor("P_A", [128, 2048], F32)
    P_B = nc.alloc_psum_tensor("P_B", [128, 1536], F32)
    P_C = nc.alloc_psum_tensor("P_C", [128, 512], F32)

    C = Ctr(nc)
    g, t, s, v, yq = nc.gpsimd, nc.tensor, nc.scalar, nc.vector, nc.sync

    def gdma(out, in_, slow=False, ch="g"):
        return C.inc(g.dma_start(out=out, in_=in_,
                                 allow_slow_non_contiguous=slow), ch, 16)

    def ydma(out, in_, slow=False, ch="y"):
        return C.inc(yq.dma_start(out=out, in_=in_,
                                  allow_slow_non_contiguous=slow), ch, 16)

    def barrier(*engs):
        for e in engs:
            for nm in ("g", "y", "t", "s", "v", "c", "p0", "p1", "q0", "q1", "r0", "r1"):
                C.wait(e, nm)

    EPS1 = cpool[0:1, 0:1]
    EBIA = cpool[0:128, 1:2]
    EPS12 = cpool[0:2, 2:3]

    def finish_y():
        barrier(t, s, v, g, yq)
        ydma(xout.rearrange("p (c t) -> p c t", t=NT), Y[:, :, 1:1 + NT])
        barrier(g, t, s, v, yq)
        return nc, dbg_outs

    def finish(ap=None, cast=False):
        barrier(t, s, v, g, yq)
        if ap is None:
            ap = X[:, :, :].rearrange("p c t -> p (c t)")
        if cast:
            gdma(xout, ap)
        else:
            ydma(xout, ap)
        barrier(g, t, s, v, yq)
        return nc, dbg_outs

    pid = g.partition_id()
    oth_reg = g.alloc_register("oth")
    g.reg_alu(oth_reg, pid, 2, ALU.mod)
    g.reg_alu(oth_reg, oth_reg, 1, ALU.bitwise_xor)
    oth_sv = g.snap(oth_reg, min_val=0, max_val=1)

    gdma(X[:, :, :], xin.rearrange("p (c t) -> p c t", t=NT))
    for dst, src in [(ident, ident_d), (onesb, onesb_d), (onesrowf, onesrow_d),
                     (ones64f, ones64_d), (ind2b, ind2b_d), (ind2f, ind2f_d),
                     (rotT, rotT_d), (masksb, masks_d), (maskslb, masks_lb_d), (cpool, cpool_d)]:
        gdma(dst[:, :], src.ap())
    gdma(bout_c[:, :], b_out_r)
    gdma(hzc[:, :], hz)
    gdma(mac[:, :], maskA)
    for l in range(2):
        gdma(lng_c[:, l * 8:(l + 1) * 8], ln_g_r[l])
        gdma(lnb_c[:, l * 8:(l + 1) * 8], ln_b_r[l])
        gdma(b2_c[:, l * 8:(l + 1) * 8], b2_r[l])
        gdma(b1a_c[:, l * NJ:(l + 1) * NJ], b1a_r[l])
        gdma(b1g_c[:, l * NJ:(l + 1) * NJ], b1g_r[l])
        gdma(qsc[:, l:l + 1], qs_r[l])
        gdma(qsc[:, 2 + l:3 + l], qsp_r[l])
        gdma(qsc[:, 4 + l:5 + l], ks_r[l])
        gdma(qsc[:, 6 + l:7 + l], ksp_r[l])
    C.inc(v.memset(blkq[:, :], 0.0), "v")
    C.inc(v.memset(blkk[:, :], 0.0), "v")
    C.inc(v.memset(blkv[:, :], 0.0), "v")
    C.wait(g, "v")
    for blk, w in [(blkq, wq_t), (blkk, wk_t), (blkv, wv_t)]:
        gdma(blk[0:64, 0:64], w)
        gdma(blk[64:128, 64:128], w)
    C.wait(v, "g")
    C.inc(v.tensor_scalar_add(macE[:, :], mac[:, :], EXP_BIAS), "v")
    stg = [place(A_QB, [128, HIDP], BF16), place(A_QB + 5664, [128, HIDP], BF16)]
    wjobs = []
    for l in range(2):
        wjobs += [(wqkv[l], wc_qkv.ap()[l], D, 1536, False),
                  (wo_p[l], wc_wo.ap()[l], 512, D, False),
                  (w1a[l], wc_w1a.ap()[l], D, HIDP, True),
                  (w1g[l], wc_w1g.ap()[l], D, HIDP, True),
                  (w2p[l], wc_w2.ap()[l], HIDP, D, False)]
    wjobs.append((woutT, wc_woutT.ap(), D, D, False))
    ci = 0
    for src, dst, rows, cols, tiled in wjobs:
        for r0 in range(0, rows, 128):
            bi = ci % 2
            pch, qch = ("p0", "q0") if bi == 0 else ("p1", "q1")
            if ci >= 2:
                C.wait(g, qch)
            gdma(stg[bi][:, 0:cols], src[r0:r0 + 128, :], ch=pch)
            C.wait(yq, pch)
            if tiled:
                ydma(dst[:, :, r0 // 128, :],
                     stg[bi][:, 0:cols].rearrange("p (j e) -> p j e", e=128),
                     ch=qch)
            else:
                ydma(dst[r0:r0 + 128, :], stg[bi][:, 0:cols], ch=qch)
            ci += 1

    def ln_full(src_fn, mode, l=None):
        def lcopies(c):
            b = c % 2
            C.inc(s.copy(tXb_[b][:, :], src_fn(c)), "s")
            C.inc(s.activation(tSQb_[b][:, :], src_fn(c), AF.Square), "s")
            return C.cnt["s"]
        C.wait(s, "v")
        cmile = {0: lcopies(0)}
        mmile = {}
        for c in range(NCH):
            b = c % 2
            C.wait(t, "s", cmile[c])
            for nt in range(4):
                t.matmul(P_A[0:1, nt * 512:(nt + 1) * 512], onesb[:, :],
                         tXb_[b][:, nt * 512:(nt + 1) * 512],
                         start=(c == 0), stop=(c == NCH - 1))
                dst = (P_B[0:1, nt * 512:(nt + 1) * 512] if nt < 3
                       else P_C[0:1, 0:512])
                mm = t.matmul(dst, onesb[:, :],
                              tSQb_[b][:, nt * 512:(nt + 1) * 512],
                              start=(c == 0), stop=(c == NCH - 1))
            C.inc(mm, "t")
            mmile[c] = C.cnt["t"]
            if c + 1 < NCH:
                C.wait(s, "t", mmile.get(c - 1, 0))
                cmile[c + 1] = lcopies(c + 1)
        for nt in range(4):
            mean_ap = P_A[0:1, nt * 512:(nt + 1) * 512]
            sq_ap = (P_B[0:1, nt * 512:(nt + 1) * 512] if nt < 3 else P_C[0:1, 0:512])
            C.wait(v, "t")
            C.inc(v.tensor_scalar(r_m[:, :], mean_ap, -1.0 / 1024.0, None, ALU.mult), "v")
            C.inc(v.tensor_scalar(r_q[:, :], sq_ap, 1.0 / 1024.0, None, ALU.mult), "v")
            C.inc(v.tensor_tensor(r_v[:, :], r_m[:, :], r_m[:, :], ALU.mult), "v")
            C.inc(v.tensor_sub(r_q[:, :], r_q[:, :], r_v[:, :]), "v")
            C.wait(s, "v")
            C.inc(s.activation(r_s[:, :], r_q[:, :], AF.Ln, bias=EPS1), "s")
            C.inc(s.activation(r_s[:, :], r_s[:, :], AF.Exp, scale=-0.5), "s")
            C.wait(t, "s")
            C.inc(t.matmul(P_A[:, nt * 512:(nt + 1) * 512], onesrowf[0:1, :],
                           r_m[0:1, :], start=True, stop=True), "t")
            C.wait(v, "t")
            C.inc(v.tensor_copy(tnm[:, nt * 512:(nt + 1) * 512],
                                P_A[:, nt * 512:(nt + 1) * 512]), "v")
            C.wait(t, "v")
            C.inc(t.matmul(P_A[:, nt * 512:(nt + 1) * 512], onesrowf[0:1, :],
                           r_s[0:1, :], start=True, stop=True), "t")
            C.wait(v, "t")
            C.inc(v.tensor_copy(trs[:, nt * 512:(nt + 1) * 512],
                                P_A[:, nt * 512:(nt + 1) * 512]), "v")
        for c in range(NCH):
            C.inc(v.tensor_add(P_A[:, :], src_fn(c), tnm[:, :]), "v")
            if mode == "y":
                C.inc(v.tensor_mul(Y[:, c, 1:1 + NT], P_A[:, :], trs[:, :]), "v")
            elif mode == "xlb":
                C.inc(v.tensor_mul(XLb[:, c, :], P_A[:, :], trs[:, :]), "v")
            else:
                C.inc(v.tensor_mul(P_A[:, :], P_A[:, :], trs[:, :]), "v")
                C.inc(v.tensor_scalar(Hb[:, c, :], P_A[:, :],
                                      lng_c[0:128, l * 8 + c:l * 8 + c + 1],
                                      lnb_c[0:128, l * 8 + c:l * 8 + c + 1],
                                      ALU.mult, ALU.add), "v")

    def cc_tok():
        C.wait(g, "v")
        C.wait(g, "t")
        C.wait(g, "s")
        gdma(ccT_in.ap(), Y[:, :, 2048:2049].rearrange("p c o -> p (c o)"), slow=True)
        C.wait(g, "g")
        C.inc(g.collective_compute("AllGather", ALU.bypass, replica_groups=PAIRS,
                                   ins=[ccT_in.ap().opt()],
                                   outs=[ccT_out.ap().opt()]), "c")
        C.wait(g, "c")
        gdma(Y[:, :, 0:1].rearrange("p c o -> p (c o)"),
             ccT_out.ap()[bass.ds(oth_sv, 1), :, :].rearrange("o p c -> p (o c)"),
             slow=True)
        C.wait(v, "g")
        C.inc(v.tensor_scalar(Y[:, :, 0:1], Y[:, :, 0:1], hzc[0:128, 0:1], None,
                              ALU.mult), "v")

    def attn_norm(low, oA_ap, oB_ap, sumA_ap, sumB_ap, dstA, dstB, rcp, rbc):
        rr = 64 if low else 0
        C.wait(s, "t")
        C.inc(s.activation(rcp[rr:rr + 1, 512:1024], sumA_ap, AF.Ln), "s")
        C.inc(s.activation(rcp[rr:rr + 1, 0:512], rcp[rr:rr + 1, 512:1024],
                           AF.Exp, scale=-1.0), "s")
        C.inc(s.activation(rbc[rr:rr + 1, 512:1024], sumB_ap, AF.Ln), "s")
        C.inc(s.activation(rbc[rr:rr + 1, 0:512], rbc[rr:rr + 1, 512:1024],
                           AF.Exp, scale=-1.0), "s")
        C.wait(t, "s")
        if low:
            t.matmul(P_B[0:64, 0:512], ones64f[64:65, :], rcp[64:65, 0:512],
                     start=True, stop=True)
            mm2 = t.matmul(P_B[0:64, 512:1024], ones64f[64:65, :],
                           rbc[64:65, 0:512], start=True, stop=True)
        else:
            t.matmul(P_B[64:128, 0:512], ones64f[0:1, :], rcp[0:1, 0:512],
                     start=True, stop=True, tile_position=(0, 64))
            mm2 = t.matmul(P_B[64:128, 512:1024], ones64f[0:1, :],
                           rbc[0:1, 0:512], start=True, stop=True,
                           tile_position=(0, 64))
        C.inc(mm2, "t")
        C.wait(v, "t")
        if low:
            C.inc(v.tensor_copy(rcp[0:64, 0:512], P_B[0:64, 0:512]), "v")
            C.inc(v.tensor_copy(rbc[0:64, 0:512], P_B[0:64, 512:1024]), "v")
            C.inc(v.tensor_mul(dstA, oA_ap, rcp[0:64, 0:512]), "v")
            C.inc(v.tensor_mul(dstB, oB_ap, rbc[0:64, 0:512]), "v")
        else:
            C.inc(v.tensor_copy(rcp[64:128, 0:512], P_B[64:128, 0:512]), "v")
            C.inc(v.tensor_copy(rbc[64:128, 0:512], P_B[64:128, 512:1024]), "v")
            C.inc(v.tensor_mul(dstA, oA_ap, rcp[64:128, 0:512]), "v")
            C.inc(v.tensor_mul(dstB, oB_ap, rbc[64:128, 0:512]), "v")

    LAYER_STOP = {}

    def ffn_layer(l, first_of_call):
        barrier(s)
        if first_of_call:
            C.wait(v, "y")
            C.inc(v.memset(v_tm[:, :, :, 64:65], 1.0), "v")
        for c in range(NCH):
            C.inc(s.copy(Yb[:, c, 0:2049], Y[:, c, 0:2049]), "s")
        for rawsrc, dstt, qcol, ncols in (
                (tab_cos, tqc, 0 + l, 2048), (tab_sin, tqs, 2 + l, 2048),
                (tab_icos, tkc, 4 + l, 2560), (tab_isin, tks, 6 + l, 2560)):
            C.wait(g, "v")
            gdma(rawt[:, :], rawsrc, ch="p0")
            C.wait(v, "p0")
            C.inc(v.tensor_scalar(dstt[:, 0:ncols], rawt[:, 2560 - ncols:2560],
                                  qsc[0:128, qcol:qcol + 1], None, ALU.mult), "v")
        if stage == 87:
            barrier(g, t, s, v, yq)
            gdma(xout[:, 0:2048], tqc[:, :])
            gdma(xout[:, 2048:4096], tqs[:, :])
            gdma(xout[:, 4096:6656], tkc[:, :])
            gdma(xout[:, 6656:9216], tks[:, :])
            barrier(g, t, s, v, yq)
            return
        if stage == 8:
            return
        barrier(t, g)
        for half in range(2):
            C.wait(g, "t")
            gdma(wv_sb[:, :, :],
                 wc_qkv.ap()[l].rearrange("(kc p) o -> p kc o", p=128)
                 [:, :, 1024 + half * 256:1024 + (half + 1) * 256], ch="p1")
            C.wait(t, "p1")
            lastv = {0: 0, 1: 0}
            for tt in range(16):
                ps = P_C[:, 0:512] if (tt % 2 == 0) else P_B[:, 0:512]
                psv = ps[0:128, 0:256]
                if tt >= 2:
                    C.wait(t, "v", lastv[tt % 2])
                for kc in range(NCH):
                    sh = 1 if kc < 4 else 0
                    mm = t.matmul(psv, Yb[:, kc, sh + tt * 128:sh + (tt + 1) * 128],
                                  wv_sb[:, kc, :], start=(kc == 0), stop=(kc == NCH - 1))
                C.inc(mm, "t")
                C.wait(v, "t")
                C.inc(v.tensor_copy(
                    v_tm[:, 4 + tt, half * 4:(half + 1) * 4, 0:64],
                    psv.rearrange("p (h e) -> p h e", e=64)), "v")
                lastv[tt % 2] = C.cnt["v"]
        if stage == 9:
            return
        import os as _os
        K_SUB = int(_os.environ.get("K_SUB", "99"))
        K_OC = int(_os.environ.get("K_OC", "8"))
        barrier(t, g)
        gdma(wtq[0][:, :, :],
             wc_qkv.ap()[l].rearrange("(kc p) o -> p kc o", p=128)[:, :, 0:128],
             ch="p0")
        pa_free = C.cnt["v"]
        tmile = {0: 0, 1: 0}
        for oc in range(K_OC):
            C.wait(t, "p0" if oc % 2 == 0 else "p1")
            C.wait(t, "v", pa_free)
            for nt in range(4):
                for kc in range(NCH):
                    sh = 1 if kc < 4 else 0
                    mm = t.matmul(P_A[:, nt * 512:(nt + 1) * 512],
                                  wtq[oc % 2][:, kc, :],
                                  Yb[:, kc, sh + nt * 512:sh + (nt + 1) * 512],
                                  start=(kc == 0), stop=(kc == NCH - 1))
            C.inc(mm, "t")
            tmile[oc % 2] = C.cnt["t"]
            if oc + 1 < K_OC:
                C.wait(g, "t", tmile[(oc + 1) % 2])
                gdma(wtq[(oc + 1) % 2][:, :, :],
                     wc_qkv.ap()[l].rearrange("(kc p) o -> p kc o", p=128)
                     [:, :, (oc + 1) * 128:(oc + 2) * 128],
                     ch="p0" if (oc + 1) % 2 == 0 else "p1")
            isq = oc < 4
            tc_, ts_ = (tqc, tqs) if isq else (tkc, tks)
            tofs = 0 if isq else 512
            if K_SUB == 0:
                C.wait(v, "t")
                for nt in range(4):
                    dst = (Qb[:, oc, nt * 512:(nt + 1) * 512] if isq else
                           Kb[:, oc - 4, 512 + nt * 512:512 + (nt + 1) * 512])
                    C.inc(v.tensor_copy(dst, P_A[:, nt * 512:(nt + 1) * 512]), "v")
                pa_free = C.cnt["v"]
                continue
            K_STEP = int(_os.environ.get("K_STEP", "99"))
            for nt in range(4):
                C.wait(s, "t")
                C.inc(s.activation(rt_q2[:, :], P_A[:, nt * 512:(nt + 1) * 512],
                                   AF.Square), "s")
                if K_STEP > 1:
                    C.wait(t, "s")
                    C.wait(t, "v")
                    C.inc(t.matmul(P_C[0:2, 0:512], ind2b[:, :], rt_q2[:, :],
                                   start=True, stop=True), "t")
                if K_STEP > 2:
                    C.wait(s, "t")
                    C.inc(s.activation(rt_ss[:, :], P_C[0:2, 0:512], AF.Ln,
                                       bias=EPS12), "s")
                    C.inc(s.activation(rt_ss[:, :], rt_ss[:, :], AF.Exp,
                                       scale=-0.5), "s")
                if K_STEP > 4:
                    C.wait(t, "s")
                    C.inc(t.matmul(P_C[:, 0:512], ind2f[:, :], rt_ss[0:2, :],
                                   start=True, stop=True), "t")
                if K_STEP > 5:
                    C.wait(v, "t")
                    C.inc(v.tensor_copy(rt_rb[:, :], P_C[:, 0:512]), "v")
                    C.inc(v.tensor_mul(rt_qn[:, :], P_A[:, nt * 512:(nt + 1) * 512],
                                       rt_rb[:, :]), "v")
                if K_STEP > 6:
                    C.wait(s, "v")
                    C.inc(s.copy(rt_qnb[:, :], rt_qn[:, :]), "s")
                if K_STEP > 7:
                    C.wait(t, "s")
                    C.inc(t.matmul(P_C[:, 0:512], rotT[:, :], rt_qnb[:, :],
                                   start=True, stop=True), "t")
                if K_STEP > 8:
                    C.wait(v, "t")
                    tci = tc_[:, tofs + nt * 512:tofs + (nt + 1) * 512]
                    tsi = ts_[:, tofs + nt * 512:tofs + (nt + 1) * 512]
                    C.inc(v.tensor_mul(rt_rb[:, :], rt_qn[:, :], tci), "v")
                if K_STEP > 9:
                    C.inc(v.tensor_copy(rt_t2[:, :], P_C[:, 0:512]), "v")
                    C.inc(v.tensor_mul(rt_qn[:, :], rt_t2[:, :], tsi), "v")
                if K_STEP > 10:
                    if K_SUB >= 2:
                        dst = (Qb[:, oc, nt * 512:(nt + 1) * 512] if isq else
                               Kb[:, oc - 4, 512 + nt * 512:512 + (nt + 1) * 512])
                        C.inc(v.tensor_add(dst, rt_rb[:, :], rt_qn[:, :]), "v")
            pa_free = C.cnt["v"]
        LAYER_STOP["qkv"] = True
        if stage == 10:
            return
        C.wait(g, "v")
        gdma(ccA_in.ap()[:, 0:2048].rearrange("p (c w) -> p c w", w=512),
             Kb[:, :, 2048:2560])
        gdma(ccA_in.ap()[:, 2048:2048 + VW],
             v_tm[:, 16:20, :, :].rearrange("p c h e -> p (c h e)"))
        C.wait(g, "g")
        C.inc(g.collective_compute("AllGather", ALU.bypass, replica_groups=PAIRS,
                                   ins=[ccA_in.ap().opt()],
                                   outs=[ccA_out.ap().opt()]), "c")
        C.wait(g, "c")
        oslc = ccA_out.ap()[bass.ds(oth_sv, 1), :, :]
        gdma(Kb[:, :, 0:512],
             oslc[:, :, 0:2048].rearrange("o p (c w) -> p (o c) w", w=512))
        gdma(v_tm[:, 0:4, :, :].rearrange("p c h e -> p (c h e)"),
             oslc[:, :, 2048:2048 + VW].rearrange("o p e -> p (o e)"))
        C.wait(v, "g")
        C.inc(v.tensor_scalar(Kb[:, :, 0:512], Kb[:, :, 0:512],
                              hzc[0:128, 0:1], None, ALU.mult), "v")
        C.inc(v.tensor_scalar(v_tm[:, 0:4, :, :], v_tm[:, 0:4, :, :],
                              hzc[0:128, 0:1], None, ALU.mult), "v")
        barrier(t, s, v)
        for hp in range(4):
            low = hp < 2
            cA, cB = 2 * hp, 2 * hp + 1
            if not low:
                cA, cB = 2 * hp - 4, 2 * hp - 3
            rA = slice(0, 64) if low else slice(64, 128)
            sets = [P_A[:, 0:1024], P_A[:, 1024:2048], P_B[:, 0:1024]]
            for w in range(NW):
                normv = C.cnt["v"]
                expmile, avmile = {}, {}
                for i in range(9):
                    if i < 8:
                        m = i % 3
                        sp_ = sets[m]
                        kcol = 512 * w + 128 * i
                        mtile = (maskslb[:, i * 512:(i + 1) * 512] if i < 4 else
                                 masksb[:, (i - 4) * 512:(i - 3) * 512])
                        if i == 2:
                            C.wait(t, "v", normv)
                        C.wait(t, "s", expmile.get(i - 3, 0))
                        t.matmul(sp_[0:128, 0:512], Kb[0:64, hp, kcol:kcol + 128],
                                 Qb[0:64, hp, w * 512:(w + 1) * 512],
                                 start=True, stop=False, tile_position=(0, 0))
                        t.matmul(sp_[0:128, 512:1024],
                                 Kb[64:128, hp, kcol:kcol + 128],
                                 Qb[64:128, hp, w * 512:(w + 1) * 512],
                                 start=True, stop=False, tile_position=(64, 0))
                        t.matmul(sp_[0:128, 0:512], ident[:, :], mtile,
                                 start=False, stop=True)
                        mm = t.matmul(sp_[0:128, 512:1024], ident[:, :], mtile,
                                      start=False, stop=True)
                        C.inc(mm, "t")
                        qkm = C.cnt["t"]
                    if i >= 1:
                        j = i - 1
                        eb = expb[j % 2]
                        tc = 4 * w + j
                        st_, sp2 = (j == 0), (j == 7)
                        hA, hB = 2 * hp, 2 * hp + 1
                        if j == 0:
                            C.wait(t, "v", normv)
                        C.wait(t, "s", expmile[j])
                        if low:
                            t.matmul(P_B[0:65, 1024:1536],
                                     v_tm[:, tc, hA, 0:65], eb[:, 0:512],
                                     start=st_, stop=sp2)
                            mm = t.matmul(P_C[0:65, 0:512],
                                          v_tm[:, tc, hB, 0:65],
                                          eb[:, 512:1024], start=st_, stop=sp2)
                        else:
                            t.matmul(P_B[64:128, 1024:1536],
                                     v_tm[:, tc, hA, 0:64], eb[:, 0:512],
                                     start=st_, stop=sp2, tile_position=(0, 64))
                            t.matmul(P_B[0:1, 1024:1536], onesb[:, :],
                                     eb[:, 0:512], start=st_, stop=sp2)
                            t.matmul(P_C[64:128, 0:512],
                                     v_tm[:, tc, hB, 0:64], eb[:, 512:1024],
                                     start=st_, stop=sp2, tile_position=(0, 64))
                            mm = t.matmul(P_C[0:1, 0:512], onesb[:, :],
                                          eb[:, 512:1024], start=st_, stop=sp2)
                        C.inc(mm, "t")
                        avmile[j] = C.cnt["t"]
                    if i < 8:
                        C.wait(s, "t", qkm)
                        if i >= 2:
                            C.wait(s, "t", avmile[i - 2])
                        bia = (macE[0:128, 0:1] if (w == 0 and i < 4) else EBIA)
                        C.inc(s.activation(expb[i % 2][:, 0:1024], sets[i % 3],
                                           AF.Exp, bias=bia, scale=8.0), "s")
                        expmile[i] = C.cnt["s"]
                qcols = slice(w * 512, (w + 1) * 512)
                if low:
                    attn_norm(True, P_B[0:64, 1024:1536], P_C[0:64, 0:512],
                              P_B[64:65, 1024:1536], P_C[64:65, 0:512],
                              o_fm[0:64, cA, qcols], o_fm[0:64, cB, qcols],
                              recip_sb, recbc)
                else:
                    attn_norm(False, P_B[64:128, 1024:1536], P_C[64:128, 0:512],
                              P_B[0:1, 1024:1536], P_C[0:1, 0:512],
                              o_fm[64:128, cA, qcols], o_fm[64:128, cB, qcols],
                              recip_sb, recbc)
        if stage == 11:
            return
        barrier(g)
        rearr_wo = wc_wo.ap()[l].rearrange("(kc p) o -> p kc o", p=128)
        C.wait(yq, "t")
        gdma(wos[:, 0:2, :], rearr_wo[:, 0:2, :])
        ydma(wos[:, 2:4, :], rearr_wo[:, 2:4, :], ch="q0")
        barrier(t)
        for oc in range(8):
            if oc >= 1:
                C.wait(t, "v")
            for nt in range(4):
                for kc in range(4):
                    mm = t.matmul(P_A[:, nt * 512:(nt + 1) * 512],
                                  wos[:, kc, oc * 128:(oc + 1) * 128],
                                  o_fm[:, kc, nt * 512:(nt + 1) * 512],
                                  start=(kc == 0), stop=(kc == 3))
            C.inc(mm, "t")
            C.wait(v, "t")
            C.inc(v.tensor_add(Y[:, oc, 1:1 + NT], Y[:, oc, 1:1 + NT], P_A[:, :]), "v")
        if stage == 12:
            return
        cc_tok()
        barrier(t, s, v)

        def ysrc(c):
            return Y[:, c, 1:1 + NT] if c < 4 else Y[:, c, 0:NT]
        ln_full(ysrc, "h", l=l)
        if stage == 13:
            return
        barrier(t, g, s, v)
        rearr1a = wc_w1a.ap()[l]
        rearr1g = wc_w1g.ap()[l]
        rearr2 = wc_w2.ap()[l].rearrange("(kc p) o -> p kc o", p=128)
        for tt in range(4):
            tcols = slice(tt * 512, (tt + 1) * 512)
            dmA, dmG = {}, {}

            def loadj(j):
                pch = "p0" if j % 2 == 0 else "p1"
                qch = "q0" if j % 2 == 0 else "q1"
                gdma(wt1a[j % 3][:, :, :], rearr1a[:, j, :, :], ch=pch)
                dmA[j] = (pch, C.cnt[pch])
                ydma(wt1g[j % 3][:, :, :], rearr1g[:, j, :, :], ch=qch)
                dmG[j] = (qch, C.cnt[qch])
            C.wait(g, "t")
            C.wait(yq, "t")
            loadj(0)
            loadj(1)
            jmile = {}
            vmile, smile = {}, {}
            for j in range(NJ):
                bi = j % 3
                for nm_, mi_ in (dmA[j], dmG[j]):
                    C.wait(t, nm_, mi_)
                if j >= 2:
                    C.wait(t, "v", vmile[j - 2])
                    C.wait(t, "s", smile[j - 2])
                pj = (j % 2) * 1024
                for kc in range(NCH):
                    t.matmul(P_A[:, pj:pj + 512], wt1a[bi][:, kc, :],
                             Hb[:, kc, tcols], start=(kc == 0), stop=(kc == 7))
                for kc in range(NCH):
                    mm = t.matmul(P_A[:, pj + 512:pj + 1024], wt1g[bi][:, kc, :],
                                  Hb[:, kc, tcols], start=(kc == 0), stop=(kc == 7))
                C.inc(mm, "t")
                if j + 2 < NJ:
                    C.wait(g, "t", jmile.get(j - 1, 0))
                    C.wait(yq, "t", jmile.get(j - 1, 0))
                    loadj(j + 2)
                jmile[j] = C.cnt["t"]
                tj = tnm[:, (j % 2) * 512:(j % 2) * 512 + 512]
                C.wait(s, "t")
                C.inc(s.activation(tj, P_A[:, pj + 512:pj + 1024],
                                   AF.Gelu, bias=b1g_c[0:128, l * NJ + j:l * NJ + j + 1]),
                      "s")
                smile[j] = C.cnt["s"]
                C.wait(v, "s")
                C.inc(v.scalar_tensor_tensor(
                    AGb[:, j, :], P_A[:, pj:pj + 512],
                    b1a_c[0:128, l * NJ + j:l * NJ + j + 1], tj,
                    ALU.add, ALU.mult), "v")
                vmile[j] = C.cnt["v"]
            dm2 = {}

            def loadk(kc):
                ch = "r0" if kc % 2 == 0 else "r1"
                gdma(wt2[kc % 3][:, :], rearr2[:, kc, :], ch=ch)
                dm2[kc] = (ch, C.cnt[ch])
            barrier(t, g)
            loadk(0)
            loadk(1)
            slots = [P_A[:, i * 512:(i + 1) * 512] for i in range(4)] + \
                    [P_B[:, i * 512:(i + 1) * 512] for i in range(3)] + [P_C[:, 0:512]]
            kmile = {}
            for kc in range(NJ):
                nm_, mi_ = dm2[kc]
                C.wait(t, nm_, mi_)
                for oc in range(8):
                    mm = t.matmul(slots[oc], wt2[kc % 3][:, oc * 128:(oc + 1) * 128],
                                  AGb[:, kc, :], start=(kc == 0), stop=(kc == NJ - 1))
                C.inc(mm, "t")
                if kc + 2 < NJ:
                    C.wait(g, "t", kmile.get(kc - 1, 0))
                    loadk(kc + 2)
                kmile[kc] = C.cnt["t"]
            C.wait(v, "t")
            for oc in range(8):
                C.inc(v.scalar_tensor_tensor(
                    Y[:, oc, 1 + tt * 512:1 + (tt + 1) * 512], slots[oc],
                    b2_c[0:128, l * 8 + oc:l * 8 + oc + 1],
                    Y[:, oc, 1 + tt * 512:1 + (tt + 1) * 512],
                    ALU.add, ALU.add), "v")
            barrier(t)
        if first_of_call:
            cc_tok()

    def dense_phase():
        def xsrc(c):
            return X[:, c, :]
        ln_full(xsrc, "xlb")
        barrier(g)
        gdma(ccD_in.ap(), XLb[:, :, :].rearrange("p c t -> p (c t)"))
        C.wait(g, "g")
        C.inc(g.collective_compute("AllGather", ALU.bypass, replica_groups=PAIRS,
                                   ins=[ccD_in.ap().opt()],
                                   outs=[ccD_out.ap().opt()]), "c")
        C.wait(g, "c")
        gdma(XLo[:, :, :].rearrange("p c t -> p (c t)"),
             ccD_out.ap()[bass.ds(oth_sv, 1), :, :].rearrange("o p e -> p (o e)"))
        C.inc(v.memset(Vd[:, :, 64:65], 1.0), "v")
        C.inc(v.memset(Vd[:, :, 129:130], 1.0), "v")
        barrier(t, s, v)
        import os as _os
        K_NP = int(_os.environ.get("K_NP", "8"))
        for c in range(K_NP):
            low = c < 4
            cA, cB = (2 * c, 2 * c + 1) if low else (2 * c - 8, 2 * c - 7)
            C.wait(t, "v")
            C.wait(t, "s")
            for nt in range(4):
                C.inc(t.matmul(P_A[:, nt * 512:(nt + 1) * 512], blkq[:, :],
                               XLb[:, c, nt * 512:(nt + 1) * 512],
                               start=True, stop=True), "t")
            C.wait(s, "t")
            C.inc(s.copy(Qd[:, :], P_A[:, :]), "s")
            C.wait(t, "s")
            for half, src in ((0, XLb), (1, XLo)):
                for nt in range(4):
                    C.inc(t.matmul(P_A[:, nt * 512:(nt + 1) * 512], blkk[:, :],
                                   src[:, c, nt * 512:(nt + 1) * 512],
                                   start=True, stop=True), "t")
                C.wait(s, "t")
                C.inc(s.copy(Kd[:, half * 2048:(half + 1) * 2048], P_A[:, :]), "s")
                C.wait(t, "s")
                lastv = {0: 0, 1: 0}
                for ttl in range(16):
                    ps = P_C[:, 0:512] if (ttl % 2 == 0) else P_B[:, 0:512]
                    psv = ps[0:128, 0:128]
                    if ttl >= 2:
                        C.wait(t, "v", lastv[ttl % 2])
                    C.inc(t.matmul(psv, src[:, c, ttl * 128:(ttl + 1) * 128],
                                   blkv[:, :], start=True, stop=True), "t")
                    C.wait(v, "t")
                    C.inc(v.tensor_copy(
                        Vd[:, half * 16 + ttl, 0:130].rearrange(
                            "p (h e) -> p h e", e=65)[:, :, 0:64],
                        psv.rearrange("p (h e) -> p h e", e=64)), "v")
                    lastv[ttl % 2] = C.cnt["v"]
            barrier(t, s, v)
            sets = [P_A[:, 0:1024], P_A[:, 1024:2048], P_B[:, 0:1024]]
            for qt in range(4):
                qcols = slice(qt * 512, (qt + 1) * 512)
                normv = C.cnt["v"]
                expmile, avmile = {}, {}
                for i in range(33):
                    if i < 32:
                        m = i % 3
                        sp_ = sets[m]
                        if i == 2:
                            C.wait(t, "v", normv)
                        C.wait(t, "s", expmile.get(i - 3, 0))
                        t.matmul(sp_[0:128, 0:512],
                                 Kd[0:64, i * 128:(i + 1) * 128],
                                 Qd[0:64, qcols], start=True, stop=True,
                                 tile_position=(0, 0))
                        mm = t.matmul(sp_[0:128, 512:1024],
                                      Kd[64:128, i * 128:(i + 1) * 128],
                                      Qd[64:128, qcols], start=True, stop=True,
                                      tile_position=(64, 0))
                        C.inc(mm, "t")
                        qkm = C.cnt["t"]
                    if i >= 1:
                        j = i - 1
                        eb = expd[j % 2]
                        st_, sp2 = (j == 0), (j == 31)
                        if j == 0:
                            C.wait(t, "v", normv)
                        C.wait(t, "s", expmile[j])
                        if low:
                            t.matmul(P_B[0:65, 1024:1536], Vd[:, j, 0:65],
                                     eb[:, 0:512], start=st_, stop=sp2)
                            mm = t.matmul(P_C[0:65, 0:512], Vd[:, j, 65:130],
                                          eb[:, 512:1024], start=st_, stop=sp2)
                        else:
                            t.matmul(P_B[64:128, 1024:1536], Vd[:, j, 0:64],
                                     eb[:, 0:512], start=st_, stop=sp2,
                                     tile_position=(0, 64))
                            t.matmul(P_B[0:1, 1024:1536], onesb[:, :],
                                     eb[:, 0:512], start=st_, stop=sp2)
                            t.matmul(P_C[64:128, 0:512], Vd[:, j, 65:129],
                                     eb[:, 512:1024], start=st_, stop=sp2,
                                     tile_position=(0, 64))
                            mm = t.matmul(P_C[0:1, 0:512], onesb[:, :],
                                          eb[:, 512:1024], start=st_, stop=sp2)
                        C.inc(mm, "t")
                        avmile[j] = C.cnt["t"]
                    if i < 32:
                        C.wait(s, "t", qkm)
                        if i >= 2:
                            C.wait(s, "t", avmile[i - 2])
                        C.inc(s.activation(expd[i % 2][:, 0:1024], sets[i % 3],
                                           AF.Exp, bias=EBIA,
                                           scale=1.0 / 32.0), "s")
                        expmile[i] = C.cnt["s"]
                if low:
                    attn_norm(True, P_B[0:64, 1024:1536], P_C[0:64, 0:512],
                              P_B[64:65, 1024:1536], P_C[64:65, 0:512],
                              o_dn[0:64, cA, qcols], o_dn[0:64, cB, qcols],
                              recip_d, recbc_d)
                else:
                    attn_norm(False, P_B[64:128, 1024:1536], P_C[64:128, 0:512],
                              P_B[0:1, 1024:1536], P_C[0:1, 0:512],
                              o_dn[64:128, cA, qcols], o_dn[64:128, cB, qcols],
                              recip_d, recbc_d)
        barrier(g, t, s, v, yq)
        rearr_wout = wc_woutT.ap().rearrange("(kc p) o -> p kc o", p=128)
        gdma(wouts[:, 0:4, :], rearr_wout[:, 0:4, :])
        ydma(wouts[:, 4:8, :], rearr_wout[:, 4:8, :], ch="q0")
        barrier(t)
        for oc in range(8):
            if oc >= 1:
                C.wait(t, "v")
            for nt in range(4):
                for kc in range(8):
                    mm = t.matmul(P_A[:, nt * 512:(nt + 1) * 512],
                                  wouts[:, kc, oc * 128:(oc + 1) * 128],
                                  o_dn[:, kc, nt * 512:(nt + 1) * 512],
                                  start=(kc == 0), stop=(kc == 7))
            C.inc(mm, "t")
            C.wait(v, "t")
            C.inc(v.scalar_tensor_tensor(
                X[:, oc, :], P_A[:, :], bout_c[0:128, oc:oc + 1], X[:, oc, :],
                ALU.add, ALU.add), "v")

    if stage <= 1:
        return finish()
    if stage == 85:
        barrier(t, s, v, g, yq)
        gdma(xout[:, 0:1536], wc_qkv.ap()[0][0:128, :])
        gdma(xout[:, 2048:2048 + 1536], wc_qkv.ap()[0][512:640, :])
        gdma(xout[:, 4096:4096 + 2816], wc_w1a.ap()[0][0:128, :])
        gdma(xout[:, 8192:8192 + 1024], wc_w2.ap()[0][0:128, :])
        barrier(g, t, s, v, yq)
        return nc, dbg_outs
    for call in range(2):
        def xsrc(c):
            return X[:, c, :]
        ln_full(xsrc, "y")
        if stage <= 2:
            return finish_y()
        cc_tok()
        if stage <= 3:
            return finish_y()
        barrier(yq)
        ydma(x_spill.ap(), X[:, :, :].rearrange("p c t -> p (c t)"))
        barrier(t, s, v, g)
        for li, l in enumerate((0, 1)):
            ffn_layer(l, first_of_call=(li == 0))
            if stage == 87:
                return nc, dbg_outs
            if stage in (8, 9, 10, 11, 12, 13):
                barrier(t, s, v, g, yq)
                if stage == 10:
                    import os as _os
                    if _os.environ.get("K_NODUMP"):
                        gdma(xout.rearrange("p (c t) -> p c t", t=NT),
                             Y[:, :, 1:1 + NT])
                    else:
                        gdma(xout[:, 0:8192].rearrange("p (c t) -> p c t", t=2048),
                             Qb[:, :, :])
                        for cc_ in range(4):
                            gdma(xout[:, 8192 + cc_ * 2048:8192 + (cc_ + 1) * 2048],
                                 Kb[:, cc_, 512:2560])
                elif stage == 11:
                    gdma(xout[:, 0:8192].rearrange("p (c t) -> p c t", t=2048),
                         o_fm[:, :, :])
                elif stage in (8, 9, 12):
                    gdma(xout.rearrange("p (c t) -> p c t", t=NT), Y[:, :, 1:1 + NT])
                elif stage == 13:
                    gdma(xout, Hb[:, :, :].rearrange("p c t -> p (c t)"))
                barrier(g, t, s, v, yq)
                return nc, dbg_outs
            if stage <= 4 and li == 0:
                barrier(t, s, v, g, yq)
                gdma(xout.rearrange("p (c t) -> p c t", t=NT), Y[:, :, 1:1 + NT])
                barrier(g, t, s, v, yq)
                return nc, dbg_outs
            if stage <= 5 and li == 1:
                barrier(t, s, v, g, yq)
                gdma(xout.rearrange("p (c t) -> p c t", t=NT), Y[:, :, 1:1 + NT])
                barrier(g, t, s, v, yq)
                return nc, dbg_outs
        barrier(g, yq)
        gdma(X[:, :, :], x_spill.ap().rearrange("p (c t) -> p c t", t=NT))
        C.wait(v, "g")
        for c in range(NCH):
            C.inc(v.scalar_tensor_tensor(
                X[:, c, :], Y[:, c, 1:1 + NT], 0.5, X[:, c, :],
                ALU.mult, ALU.add), "v")
        if stage <= 6:
            return finish()
        if call == 0:
            barrier(t, s, v, g, yq)
            dense_phase()
            if stage <= 7:
                return finish()
        barrier(t, s, v, g, yq)

    return finish()


_CACHE = {}


def _host_inputs(inputs):
    x = np.asarray(inputs["L0_attn_x"])
    Wq = np.asarray(inputs["L0_attn_Wq"])
    Wk = np.asarray(inputs["L0_attn_Wk"])
    Wv = np.asarray(inputs["L0_attn_Wv"])
    W_out = np.asarray(inputs["L0_attn_W_out"])
    b_out = np.asarray(inputs["L0_attn_b_out"])
    wqkv = np.asarray(inputs["L0_ffn_wqkv"])
    q_scale = np.asarray(inputs["L0_ffn_q_scale"])
    k_scale = np.asarray(inputs["L0_ffn_k_scale"])
    wo = np.asarray(inputs["L0_ffn_wo"])
    ln_g = np.asarray(inputs["L0_ffn_ln_g"])
    ln_b = np.asarray(inputs["L0_ffn_ln_b"])
    w1 = np.asarray(inputs["L0_ffn_w1"])
    b1 = np.asarray(inputs["L0_ffn_b1"])
    w2 = np.asarray(inputs["L0_ffn_w2"])
    b2 = np.asarray(inputs["L0_ffn_b2"])

    f32 = np.float32
    com = {}
    com["wqkv"] = np.ascontiguousarray(wqkv, dtype=f32)
    com["wo_p"] = np.ascontiguousarray(wo[:, PERM_WO, :], dtype=f32)
    w1a = np.zeros((2, D, HIDP), f32)
    w1g = np.zeros((2, D, HIDP), f32)
    w1a[:, :, :HID] = w1[:, :, :HID]
    w1g[:, :, :HID] = w1[:, :, HID:]
    com["w1a"], com["w1g"] = w1a, w1g
    w2pp = np.zeros((2, HIDP, D), f32)
    w2pp[:, :HID, :] = w2
    com["w2p"] = w2pp
    com["woutT"] = np.ascontiguousarray(W_out.T[PERM_WOUT, :], dtype=f32)
    com["wq_t"] = np.ascontiguousarray(Wq.T, dtype=f32)
    com["wk_t"] = np.ascontiguousarray(Wk.T, dtype=f32)
    com["wv_t"] = np.ascontiguousarray(Wv.T, dtype=f32)
    com["b_out_r"] = np.ascontiguousarray(b_out.reshape(8, 128).T, dtype=f32)
    com["ln_g_r"] = np.ascontiguousarray(ln_g.reshape(2, 8, 128).swapaxes(1, 2), dtype=f32)
    com["ln_b_r"] = np.ascontiguousarray(ln_b.reshape(2, 8, 128).swapaxes(1, 2), dtype=f32)
    com["b2_r"] = np.ascontiguousarray(b2.reshape(2, 8, 128).swapaxes(1, 2), dtype=f32)
    b1a = np.zeros((2, 128, NJ), f32)
    b1g = np.zeros((2, 128, NJ), f32)
    for l in range(2):
        pa = np.zeros(HIDP, f32)
        pg = np.zeros(HIDP, f32)
        pa[:HID] = b1[l, :HID]
        pg[:HID] = b1[l, HID:]
        b1a[l] = pa.reshape(NJ, 128).T
        b1g[l] = pg.reshape(NJ, 128).T
    com["b1a_r"], com["b1g_r"] = b1a, b1g
    perm64 = (np.arange(64) + 32) % 64
    com["qs_r"] = np.ascontiguousarray(
        np.tile(q_scale, (1, 2)).reshape(2, 128, 1), dtype=f32)
    com["qsp_r"] = np.ascontiguousarray(
        np.tile(q_scale[:, perm64], (1, 2)).reshape(2, 128, 1), dtype=f32)
    com["ks_r"] = np.ascontiguousarray(
        np.tile(k_scale, (1, 2)).reshape(2, 128, 1), dtype=f32)
    com["ksp_r"] = np.ascontiguousarray(
        np.tile(k_scale[:, perm64], (1, 2)).reshape(2, 128, 1), dtype=f32)

    in_maps = []
    for c in range(NCORE):
        b, h = c // 2, c % 2
        t0 = h * NT
        m = dict(com)
        m["xin"] = _pack_fm(np.asarray(x[b, t0:t0 + NT, :], dtype=f32))
        tc_, ts_, tic, tis = _rot_tables(t0)
        m["tab_cos"], m["tab_sin"] = tc_, ts_
        m["tab_icos"], m["tab_isin"] = tic, tis
        m["hz"] = np.full((128, 1), 0.0 if h == 0 else 1.0, f32)
        m["maskA"] = np.full((128, 1), NEG if h == 0 else 0.0, f32)
        in_maps.append(m)
    return in_maps


def kernel(**inputs):
    import os
    stg = int(os.environ.get("K_STAGE", "99"))
    key = ("prog", stg)
    if key not in _CACHE:
        _CACHE[key] = build_program(stage=stg)[0]
    nc = _CACHE[key]
    in_maps = _host_inputs(inputs)
    res = run_bass_kernel_spmd(nc, in_maps, core_ids=list(range(NCORE)))
    out = np.empty((B, S, D), dtype=np.float32)
    for c in range(NCORE):
        b, h = c // 2, c % 2
        fm = res.results[c]["xout"]
        out[b, h * NT:(h + 1) * NT, :] = (
            fm.reshape(128, NCH, NT).transpose(1, 0, 2).reshape(D, NT).T)
    return out


# version 07 (final)
# speedup vs baseline: 1.4071x; 1.0135x over previous
import numpy as np
import ml_dtypes
import concourse.bass as bass
import concourse.mybir as mybir
from concourse.bass_utils import run_bass_kernel_spmd

F32 = mybir.dt.float32
BF16 = mybir.dt.bfloat16
AF = mybir.ActivationFunctionType
ALU = mybir.AluOpType
BF = ml_dtypes.bfloat16

NCORE = 8
B, S, D = 4, 4096, 1024
NT = 2048
NCH = 8
YW = 2056
HL, DH, WIN, NW = 8, 64, 512, 4
HID, HIDP, NJ = 2730, 2816, 22
EXP_BIAS = -40.0
NEG = -30000.0
PAIRS = [[0, 1], [2, 3], [4, 5], [6, 7]]
PERM_WO = np.array([(c + 4 * (r // 64)) * 64 + (r % 64)
                    for c in range(4) for r in range(128)])
PERM_WOUT = np.array([(c + 8 * (r // 64)) * 64 + (r % 64)
                      for c in range(8) for r in range(128)])
PERM_ROT = np.array([(d // 64) * 64 + ((d % 64) + 32) % 64 for d in range(128)])

A_X = 0
A_Y = 66048
A_YB = 0
A_WT = 33024
A_WV = 37120
A_VT = 41216
A_QB = 131840
A_KB = 148224
A_TAB = 168704
A_RAW = 187136
A_RT = 192256
A_EXP = 0
A_OFM = 12288
A_RCP = 28672
A_RBC = 32768
A_WOS = 168704
A_HB = 0
A_AGB = 131840
A_W1T = 154368
A_W2T = 162560
A_LT = 166656
A_XLB = 66048
A_XLO = 98816
A_ODN = 131840
A_QD = 164608
A_KD = 168704
A_VD = 176896
A_EXD = 185216
A_RCD = 191360
A_RBD = 195456
A_WOUT = 66048
ARENA_B = 200448


def _rot_tables(t0):
    half_idx = np.arange(0, DH, 2, dtype=np.float64)
    inv_freq = 1.0 / 10000.0 ** (half_idx / DH)
    tpos = np.arange(t0 - 512, t0 + 2048, dtype=np.float64)
    fr = tpos[:, None] * inv_freq[None, :]
    freqs = np.concatenate([fr, fr], axis=1)
    base = (half_idx + 0.4 * DH) / (1.4 * DH)
    power = (tpos - S // 2) / float(WIN // 2)
    sc = base[None, :] ** power[:, None]
    scale = np.concatenate([sc, sc], axis=1)
    cos, sin = np.cos(freqs), np.sin(freqs)
    out = []
    for arr in (cos * scale, sin * scale, cos / scale, sin / scale):
        a = arr.T.astype(np.float32)
        out.append(np.ascontiguousarray(np.concatenate([a, a], axis=0)))
    return out


def _pack_fm(xs):
    T = xs.shape[0]
    return np.ascontiguousarray(
        xs.T.reshape(NCH, 128, T).transpose(1, 0, 2).reshape(128, NCH * T))


class Ctr:
    def __init__(self, nc):
        self.sem, self.cnt = {}, {}
        for n in ("g", "y", "t", "s", "v", "c", "p0", "p1", "q0", "q1", "r0", "r1"):
            self.sem[n] = nc.alloc_semaphore("sem_" + n)
            self.cnt[n] = 0

    def inc(self, inst, name, amt=1):
        inst.then_inc(self.sem[name], amt)
        self.cnt[name] += amt
        return inst

    def wait(self, eng, name, val=None):
        vv = self.cnt[name] if val is None else val
        if vv > 0:
            eng.wait_ge(self.sem[name], vv)


def build_program(stage=99, dbg=()):
    nc = bass.Bass("TRN2", target_bir_lowering=False, debug=False,
                   detect_race_conditions=False)

    def inp(name, shape, dt=F32):
        return nc.dram_tensor(name, list(shape), dt, kind="ExternalInput").ap()

    xin = inp("xin", [128, NCH * NT])
    wqkv = inp("wqkv", [2, D, 3 * 512])
    wo_p = inp("wo_p", [2, 512, D])
    w1a = inp("w1a", [2, D, HIDP])
    w1g = inp("w1g", [2, D, HIDP])
    w2p = inp("w2p", [2, HIDP, D])
    woutT = inp("woutT", [D, D])
    wq_t = inp("wq_t", [DH, DH])
    wk_t = inp("wk_t", [DH, DH])
    wv_t = inp("wv_t", [DH, DH])
    b_out_r = inp("b_out_r", [128, 8])
    ln_g_r = inp("ln_g_r", [2, 128, 8])
    ln_b_r = inp("ln_b_r", [2, 128, 8])
    b2_r = inp("b2_r", [2, 128, 8])
    b1a_r = inp("b1a_r", [2, 128, NJ])
    b1g_r = inp("b1g_r", [2, 128, NJ])
    qs_r = inp("qs_r", [2, 128, 1])
    qsp_r = inp("qsp_r", [2, 128, 1])
    ks_r = inp("ks_r", [2, 128, 1])
    ksp_r = inp("ksp_r", [2, 128, 1])
    tab_cos = inp("tab_cos", [128, 2560])
    tab_sin = inp("tab_sin", [128, 2560])
    tab_icos = inp("tab_icos", [128, 2560])
    tab_isin = inp("tab_isin", [128, 2560])
    hz = inp("hz", [128, 1])
    maskA = inp("maskA", [128, 1])

    xout = nc.dram_tensor("xout", [128, NCH * NT], F32, kind="ExternalOutput").ap()
    dbg_outs = {}
    for name, shape, dt in dbg:
        dbg_outs[name] = nc.dram_tensor("dbg_" + name, list(shape), dt,
                                        kind="ExternalOutput").ap()

    ident_d = nc.inline_tensor(np.eye(128, dtype=BF), name="c_ident")
    onesb_d = nc.inline_tensor(np.ones((128, 1), dtype=BF), name="c_onesb")
    onesrow_d = nc.inline_tensor(np.ones((1, 128), dtype=np.float32), name="c_onesrow")
    ones64_d = nc.inline_tensor(np.ones((128, 64), dtype=np.float32), name="c_ones64")
    ind2np = np.zeros((2, 128), dtype=np.float32)
    ind2np[0, :64] = 1.0
    ind2np[1, 64:] = 1.0
    ind2b_d = nc.inline_tensor(np.ascontiguousarray(ind2np.T).astype(BF), name="c_ind2b")
    ind2f_d = nc.inline_tensor(ind2np, name="c_ind2f")
    R64 = np.zeros((64, 64), dtype=np.float64)
    R64[np.arange(32), np.arange(32) + 32] = -1.0
    R64[np.arange(32, 64), np.arange(32)] = 1.0
    R128 = np.zeros((128, 128), dtype=np.float64)
    R128[0:64, 0:64] = R64
    R128[64:128, 64:128] = R64
    rotT_d = nc.inline_tensor(np.ascontiguousarray(R128.T).astype(BF), name="c_rotT")
    mk = np.zeros((128, 4, 512), dtype=np.float64)
    kk = np.arange(128)[:, None]
    qq = np.arange(512)[None, :]
    for jb in range(4):
        mk[:, jb, :] = np.where(qq < 128 * jb + kk, NEG, 0.0)
    masks_d = nc.inline_tensor(
        np.ascontiguousarray(mk.reshape(128, 2048)).astype(BF), name="c_masks")
    mlb = np.zeros((128, 4, 512), dtype=np.float64)
    for i_ in range(4):
        mlb[:, i_, :] = np.where(qq > 128 * i_ + kk, NEG, 0.0)
    masks_lb_d = nc.inline_tensor(
        np.ascontiguousarray(mlb.reshape(128, 2048)).astype(BF), name="c_masks_lb")
    cp = np.zeros((128, 8), dtype=np.float32)
    cp[:, 0] = 1e-5
    cp[:, 1] = EXP_BIAS
    cp[:, 2] = 1e-12
    cpool_d = nc.inline_tensor(cp, name="c_cpool")

    wc_qkv = nc.dram_tensor("wc_qkv", [2, D, 3 * 512], BF16, kind="Internal")
    wc_wo = nc.dram_tensor("wc_wo", [2, 512, D], BF16, kind="Internal")
    wc_w1a = nc.dram_tensor("wc_w1a", [2, 128, NJ, 8, 128], BF16, kind="Internal")
    wc_w1g = nc.dram_tensor("wc_w1g", [2, 128, NJ, 8, 128], BF16, kind="Internal")
    wc_w2 = nc.dram_tensor("wc_w2", [2, HIDP, D], BF16, kind="Internal")
    wc_woutT = nc.dram_tensor("wc_woutT", [D, D], BF16, kind="Internal")
    x_spill = nc.dram_tensor("x_spill", [128, NCH * NT], F32, kind="Internal")
    ccT_in = nc.dram_tensor("ccT_in", [128, 8], F32, kind="Internal")
    ccT_out = nc.dram_tensor("ccT_out", [2, 128, 8], F32, kind="Internal")
    VW = 4 * HL * 65
    ccA_in = nc.dram_tensor("ccA_in", [128, 2048 + VW], BF16, kind="Internal")
    ccA_out = nc.dram_tensor("ccA_out", [2, 128, 2048 + VW], BF16, kind="Internal")
    ccD_in = nc.dram_tensor("ccD_in", [128, NCH * NT], BF16, kind="Internal")
    ccD_out = nc.dram_tensor("ccD_out", [2, 128, NCH * NT], BF16, kind="Internal")

    ident = nc.alloc_sbuf_tensor("ident", [128, 128], BF16)
    onesb = nc.alloc_sbuf_tensor("onesb", [128, 1], BF16)
    onesrowf = nc.alloc_sbuf_tensor("onesrowf", [1, 128], F32)
    ones64f = nc.alloc_sbuf_tensor("ones64f", [128, 64], F32)
    ind2b = nc.alloc_sbuf_tensor("ind2b", [128, 2], BF16)
    ind2f = nc.alloc_sbuf_tensor("ind2f", [2, 128], F32)
    rotT = nc.alloc_sbuf_tensor("rotT", [128, 128], BF16)
    masksb = nc.alloc_sbuf_tensor("masksb", [128, 2048], BF16)
    maskslb = nc.alloc_sbuf_tensor("maskslb", [128, 2048], BF16)
    cpool = nc.alloc_sbuf_tensor("cpool", [128, 8], F32)
    blkq = nc.alloc_sbuf_tensor("blkq", [128, 128], BF16)
    blkk = nc.alloc_sbuf_tensor("blkk", [128, 128], BF16)
    blkv = nc.alloc_sbuf_tensor("blkv", [128, 128], BF16)
    bout_c = nc.alloc_sbuf_tensor("bout_c", [128, 8], F32)
    lng_c = nc.alloc_sbuf_tensor("lng_c", [128, 16], F32)
    lnb_c = nc.alloc_sbuf_tensor("lnb_c", [128, 16], F32)
    b2_c = nc.alloc_sbuf_tensor("b2_c", [128, 16], F32)
    b1a_c = nc.alloc_sbuf_tensor("b1a_c", [128, 2 * NJ], F32)
    b1g_c = nc.alloc_sbuf_tensor("b1g_c", [128, 2 * NJ], F32)
    qsc = nc.alloc_sbuf_tensor("qsc", [128, 8], F32)
    hzc = nc.alloc_sbuf_tensor("hzc", [128, 1], F32)
    mac = nc.alloc_sbuf_tensor("mac", [128, 1], F32)
    macE = nc.alloc_sbuf_tensor("macE", [128, 1], F32)

    arena = nc.alloc_sbuf_tensor("arena", [128, ARENA_B], mybir.dt.uint8)
    ab = nc.lookup_mloc(arena).addr
    rem = nc.sbuf_bytes_remaining
    assert rem >= 0, rem
    _n = [0]

    def place(off, shape, dtype):
        per = int(np.prod(shape[1:])) * mybir.dt.size(dtype)
        assert off % 32 == 0 and off + per <= ARENA_B, (off, per)
        _n[0] += 1
        h = nc.alloc_sbuf_tensor_at(f"ar{_n[0]}", list(shape), dtype,
                                    offset=ab + off)
        if not hasattr(nc, "_arena_map"):
            nc._arena_map = {}
        nc._arena_map.setdefault(off, []).append(h.name)
        return h

    X = place(A_X, [128, NCH, NT], F32)
    Y = place(A_Y, [128, NCH, YW], F32)
    Yb = place(A_YB, [128, NCH, YW], BF16)
    wtq = [place(A_WT + i * 2048, [128, 8, 128], BF16) for i in range(2)]
    wv_sb = place(A_WV, [128, 8, 256], BF16)
    v_tm = place(A_VT, [128, 20, HL, 65], BF16)
    Qb = place(A_QB, [128, 4, 2048], BF16)
    Kb = place(A_KB, [128, 4, 2560], BF16)
    tqc = place(A_TAB, [128, 2048], BF16)
    tqs = place(A_TAB + 4096, [128, 2048], BF16)
    tkc = place(A_TAB + 8192, [128, 2560], BF16)
    tks = place(A_TAB + 13312, [128, 2560], BF16)
    rawt = place(A_RAW, [128, 2560], BF16)
    rt_q2 = place(A_RT, [128, 512], BF16)
    rt_rb = place(A_RT + 1024, [128, 512], F32)
    rt_qn = place(A_RT + 3072, [128, 512], F32)
    rt_qnb = place(A_RT + 5120, [128, 512], BF16)
    rt_t2 = place(A_RT + 6144, [128, 512], F32)
    rt_ss = place(A_RAW, [2, 512], F32)
    expb = [place(A_EXP + i * 3072, [128, 1536], BF16) for i in range(4)]
    o_fm = place(A_OFM, [128, 4, 2048], BF16)
    recip_sb = place(A_RCP, [128, 1024], F32)
    recbc = place(A_RBC, [128, 1024], F32)
    wos = place(A_WOS, [128, 4, 1024], BF16)
    Hb = place(A_HB, [128, NCH, NT], BF16)
    AGb = place(A_AGB, [128, NJ, 512], BF16)
    wt1a = [place(A_W1T + i * 2048, [128, 8, 128], BF16) for i in range(3)]
    wt1g = [place(A_WT + i * 2048, [128, 8, 128], BF16) for i in range(3)]
    wt2 = [place(A_W2T, [128, 1024], BF16),
           place(A_W2T + 2048, [128, 1024], BF16),
           place(A_WT + 6144, [128, 1024], BF16)]
    tXb = place(A_LT, [128, NT], BF16)
    tSQb = place(A_LT + 4096, [128, NT], BF16)
    tXb_ = [tXb, place(A_RT, [128, NT], BF16)]
    tSQb_ = [tSQb, place(A_RT + 4096, [128, NT], BF16)]
    tnm = place(A_LT + 8192, [128, NT], F32)
    trs = place(A_LT + 16384, [128, NT], F32)
    r_m = place(A_LT + 24576, [1, 512], F32)
    r_q = place(A_LT + 26624, [1, 512], F32)
    r_v = place(A_LT + 28672, [1, 512], F32)
    r_s = place(A_LT + 30720, [1, 512], F32)
    XLb = place(A_XLB, [128, NCH, NT], BF16)
    XLo = place(A_XLO, [128, NCH, NT], BF16)
    o_dn = place(A_ODN, [128, NCH, NT], BF16)
    Qd = place(A_QD, [128, 2048], BF16)
    Kd = place(A_KD, [128, 4096], BF16)
    Vd = place(A_VD, [128, 32, 130], BF16)
    expd = [place(A_EXD + i * 3072, [128, 1536], BF16) for i in range(2)]
    recip_d = place(A_RCD, [128, 1024], F32)
    recbc_d = place(A_RBD, [128, 1024], F32)
    wouts = place(A_WOUT, [128, 8, 1024], BF16)

    P_A = nc.alloc_psum_tensor("P_A", [128, 2048], F32)
    P_B = nc.alloc_psum_tensor("P_B", [128, 1536], F32)
    P_C = nc.alloc_psum_tensor("P_C", [128, 512], F32)

    C = Ctr(nc)
    g, t, s, v, yq = nc.gpsimd, nc.tensor, nc.scalar, nc.vector, nc.sync

    def gdma(out, in_, slow=False, ch="g"):
        return C.inc(g.dma_start(out=out, in_=in_,
                                 allow_slow_non_contiguous=slow), ch, 16)

    def ydma(out, in_, slow=False, ch="y"):
        return C.inc(yq.dma_start(out=out, in_=in_,
                                  allow_slow_non_contiguous=slow), ch, 16)

    def barrier(*engs):
        for e in engs:
            for nm in ("g", "y", "t", "s", "v", "c", "p0", "p1", "q0", "q1", "r0", "r1"):
                C.wait(e, nm)

    EPS1 = cpool[0:1, 0:1]
    EBIA = cpool[0:128, 1:2]
    EPS12 = cpool[0:2, 2:3]

    def finish_y():
        barrier(t, s, v, g, yq)
        ydma(xout.rearrange("p (c t) -> p c t", t=NT), Y[:, :, 1:1 + NT])
        barrier(g, t, s, v, yq)
        return nc, dbg_outs

    def finish(ap=None, cast=False):
        barrier(t, s, v, g, yq)
        if ap is None:
            ap = X[:, :, :].rearrange("p c t -> p (c t)")
        if cast:
            gdma(xout, ap)
        else:
            ydma(xout, ap)
        barrier(g, t, s, v, yq)
        return nc, dbg_outs

    pid = g.partition_id()
    oth_reg = g.alloc_register("oth")
    g.reg_alu(oth_reg, pid, 2, ALU.mod)
    g.reg_alu(oth_reg, oth_reg, 1, ALU.bitwise_xor)
    oth_sv = g.snap(oth_reg, min_val=0, max_val=1)

    gdma(X[:, :, :], xin.rearrange("p (c t) -> p c t", t=NT))
    for dst, src in [(ident, ident_d), (onesb, onesb_d), (onesrowf, onesrow_d),
                     (ones64f, ones64_d), (ind2b, ind2b_d), (ind2f, ind2f_d),
                     (rotT, rotT_d), (masksb, masks_d), (maskslb, masks_lb_d), (cpool, cpool_d)]:
        gdma(dst[:, :], src.ap())
    gdma(bout_c[:, :], b_out_r)
    gdma(hzc[:, :], hz)
    gdma(mac[:, :], maskA)
    for l in range(2):
        gdma(lng_c[:, l * 8:(l + 1) * 8], ln_g_r[l])
        gdma(lnb_c[:, l * 8:(l + 1) * 8], ln_b_r[l])
        gdma(b2_c[:, l * 8:(l + 1) * 8], b2_r[l])
        gdma(b1a_c[:, l * NJ:(l + 1) * NJ], b1a_r[l])
        gdma(b1g_c[:, l * NJ:(l + 1) * NJ], b1g_r[l])
        gdma(qsc[:, l:l + 1], qs_r[l])
        gdma(qsc[:, 2 + l:3 + l], qsp_r[l])
        gdma(qsc[:, 4 + l:5 + l], ks_r[l])
        gdma(qsc[:, 6 + l:7 + l], ksp_r[l])
    C.inc(v.memset(blkq[:, :], 0.0), "v")
    C.inc(v.memset(blkk[:, :], 0.0), "v")
    C.inc(v.memset(blkv[:, :], 0.0), "v")
    C.wait(g, "v")
    for blk, w in [(blkq, wq_t), (blkk, wk_t), (blkv, wv_t)]:
        gdma(blk[0:64, 0:64], w)
        gdma(blk[64:128, 64:128], w)
    C.wait(v, "g")
    C.inc(v.tensor_scalar_add(macE[:, :], mac[:, :], EXP_BIAS), "v")
    stg = [place(A_QB, [128, HIDP], BF16), place(A_QB + 5664, [128, HIDP], BF16)]
    wjobs = []
    for l in range(2):
        wjobs += [(wqkv[l], wc_qkv.ap()[l], D, 1536, False),
                  (wo_p[l], wc_wo.ap()[l], 512, D, False),
                  (w1a[l], wc_w1a.ap()[l], D, HIDP, True),
                  (w1g[l], wc_w1g.ap()[l], D, HIDP, True),
                  (w2p[l], wc_w2.ap()[l], HIDP, D, False)]
    wjobs.append((woutT, wc_woutT.ap(), D, D, False))
    ci = 0
    for src, dst, rows, cols, tiled in wjobs:
        for r0 in range(0, rows, 128):
            bi = ci % 2
            pch, qch = ("p0", "q0") if bi == 0 else ("p1", "q1")
            if ci >= 2:
                C.wait(g, qch)
            gdma(stg[bi][:, 0:cols], src[r0:r0 + 128, :], ch=pch)
            C.wait(yq, pch)
            if tiled:
                ydma(dst[:, :, r0 // 128, :],
                     stg[bi][:, 0:cols].rearrange("p (j e) -> p j e", e=128),
                     ch=qch)
            else:
                ydma(dst[r0:r0 + 128, :], stg[bi][:, 0:cols], ch=qch)
            ci += 1

    def ln_full(src_fn, mode, l=None):
        def lcopies(c):
            b = c % 2
            C.inc(v.tensor_copy(tXb_[b][:, :], src_fn(c)), "v")
            C.inc(s.activation(tSQb_[b][:, :], src_fn(c), AF.Square), "s")
            return (C.cnt["s"], C.cnt["v"])
        C.wait(s, "v")
        cmile = {0: lcopies(0)}
        mmile = {}
        for c in range(NCH):
            b = c % 2
            C.wait(t, "s", cmile[c][0])
            C.wait(t, "v", cmile[c][1])
            for nt in range(4):
                t.matmul(P_A[0:1, nt * 512:(nt + 1) * 512], onesb[:, :],
                         tXb_[b][:, nt * 512:(nt + 1) * 512],
                         start=(c == 0), stop=(c == NCH - 1))
                dst = (P_B[0:1, nt * 512:(nt + 1) * 512] if nt < 3
                       else P_C[0:1, 0:512])
                mm = t.matmul(dst, onesb[:, :],
                              tSQb_[b][:, nt * 512:(nt + 1) * 512],
                              start=(c == 0), stop=(c == NCH - 1))
            C.inc(mm, "t")
            mmile[c] = C.cnt["t"]
            if c + 1 < NCH:
                C.wait(s, "t", mmile.get(c - 1, 0))
                C.wait(v, "t", mmile.get(c - 1, 0))
                cmile[c + 1] = lcopies(c + 1)
        for nt in range(4):
            mean_ap = P_A[0:1, nt * 512:(nt + 1) * 512]
            sq_ap = (P_B[0:1, nt * 512:(nt + 1) * 512] if nt < 3 else P_C[0:1, 0:512])
            C.wait(v, "t")
            C.inc(v.tensor_scalar(r_m[:, :], mean_ap, -1.0 / 1024.0, None, ALU.mult), "v")
            C.inc(v.tensor_scalar(r_q[:, :], sq_ap, 1.0 / 1024.0, None, ALU.mult), "v")
            C.inc(v.tensor_tensor(r_v[:, :], r_m[:, :], r_m[:, :], ALU.mult), "v")
            C.inc(v.tensor_sub(r_q[:, :], r_q[:, :], r_v[:, :]), "v")
            C.wait(s, "v")
            C.inc(s.activation(r_s[:, :], r_q[:, :], AF.Ln, bias=EPS1), "s")
            C.inc(s.activation(r_s[:, :], r_s[:, :], AF.Exp, scale=-0.5), "s")
            C.wait(t, "s")
            C.inc(t.matmul(P_A[:, nt * 512:(nt + 1) * 512], onesrowf[0:1, :],
                           r_m[0:1, :], start=True, stop=True), "t")
            C.wait(v, "t")
            C.inc(v.tensor_copy(tnm[:, nt * 512:(nt + 1) * 512],
                                P_A[:, nt * 512:(nt + 1) * 512]), "v")
            C.wait(t, "v")
            C.inc(t.matmul(P_A[:, nt * 512:(nt + 1) * 512], onesrowf[0:1, :],
                           r_s[0:1, :], start=True, stop=True), "t")
            C.wait(v, "t")
            C.inc(v.tensor_copy(trs[:, nt * 512:(nt + 1) * 512],
                                P_A[:, nt * 512:(nt + 1) * 512]), "v")
        for c in range(NCH):
            C.inc(v.tensor_add(P_A[:, :], src_fn(c), tnm[:, :]), "v")
            if mode == "y":
                C.inc(v.tensor_mul(Y[:, c, 1:1 + NT], P_A[:, :], trs[:, :]), "v")
            elif mode == "xlb":
                C.inc(v.tensor_mul(XLb[:, c, :], P_A[:, :], trs[:, :]), "v")
            else:
                C.inc(v.tensor_mul(P_A[:, :], P_A[:, :], trs[:, :]), "v")
                C.inc(v.tensor_scalar(Hb[:, c, :], P_A[:, :],
                                      lng_c[0:128, l * 8 + c:l * 8 + c + 1],
                                      lnb_c[0:128, l * 8 + c:l * 8 + c + 1],
                                      ALU.mult, ALU.add), "v")

    def cc_tok():
        C.wait(g, "v")
        C.wait(g, "t")
        C.wait(g, "s")
        gdma(ccT_in.ap(), Y[:, :, 2048:2049].rearrange("p c o -> p (c o)"), slow=True)
        C.wait(g, "g")
        C.inc(g.collective_compute("AllGather", ALU.bypass, replica_groups=PAIRS,
                                   ins=[ccT_in.ap().opt()],
                                   outs=[ccT_out.ap().opt()]), "c")
        C.wait(g, "c")
        gdma(Y[:, :, 0:1].rearrange("p c o -> p (c o)"),
             ccT_out.ap()[bass.ds(oth_sv, 1), :, :].rearrange("o p c -> p (o c)"),
             slow=True)
        C.wait(v, "g")
        C.inc(v.tensor_scalar(Y[:, :, 0:1], Y[:, :, 0:1], hzc[0:128, 0:1], None,
                              ALU.mult), "v")

    def attn_norm(low, oA_ap, oB_ap, sumA_ap, sumB_ap, dstA, dstB, rcp, rbc):
        rr = 64 if low else 0
        C.wait(s, "t")
        C.inc(s.activation(rcp[rr:rr + 1, 512:1024], sumA_ap, AF.Ln), "s")
        C.inc(s.activation(rcp[rr:rr + 1, 0:512], rcp[rr:rr + 1, 512:1024],
                           AF.Exp, scale=-1.0), "s")
        C.inc(s.activation(rbc[rr:rr + 1, 512:1024], sumB_ap, AF.Ln), "s")
        C.inc(s.activation(rbc[rr:rr + 1, 0:512], rbc[rr:rr + 1, 512:1024],
                           AF.Exp, scale=-1.0), "s")
        C.wait(t, "s")
        if low:
            t.matmul(P_B[0:64, 0:512], ones64f[64:65, :], rcp[64:65, 0:512],
                     start=True, stop=True)
            mm2 = t.matmul(P_B[0:64, 512:1024], ones64f[64:65, :],
                           rbc[64:65, 0:512], start=True, stop=True)
        else:
            t.matmul(P_B[64:128, 0:512], ones64f[0:1, :], rcp[0:1, 0:512],
                     start=True, stop=True, tile_position=(0, 64))
            mm2 = t.matmul(P_B[64:128, 512:1024], ones64f[0:1, :],
                           rbc[0:1, 0:512], start=True, stop=True,
                           tile_position=(0, 64))
        C.inc(mm2, "t")
        C.wait(v, "t")
        if low:
            C.inc(v.tensor_copy(rcp[0:64, 0:512], P_B[0:64, 0:512]), "v")
            C.inc(v.tensor_copy(rbc[0:64, 0:512], P_B[0:64, 512:1024]), "v")
            C.inc(v.tensor_mul(dstA, oA_ap, rcp[0:64, 0:512]), "v")
            C.inc(v.tensor_mul(dstB, oB_ap, rbc[0:64, 0:512]), "v")
        else:
            C.inc(v.tensor_copy(rcp[64:128, 0:512], P_B[64:128, 0:512]), "v")
            C.inc(v.tensor_copy(rbc[64:128, 0:512], P_B[64:128, 512:1024]), "v")
            C.inc(v.tensor_mul(dstA, oA_ap, rcp[64:128, 0:512]), "v")
            C.inc(v.tensor_mul(dstB, oB_ap, rbc[64:128, 0:512]), "v")

    LAYER_STOP = {}

    def ffn_layer(l, first_of_call):
        barrier(s)
        if first_of_call:
            C.wait(v, "y")
            C.inc(v.memset(v_tm[:, :, :, 64:65], 1.0), "v")
        for c in range(NCH):
            C.inc(s.copy(Yb[:, c, 0:2049], Y[:, c, 0:2049]), "s")
        cast_s = C.cnt["s"]
        pre_v = C.cnt["v"]
        for rawsrc, dstt, qcol, ncols in (
                (tab_cos, tqc, 0 + l, 2048), (tab_sin, tqs, 2 + l, 2048),
                (tab_icos, tkc, 4 + l, 2560), (tab_isin, tks, 6 + l, 2560)):
            C.wait(g, "v")
            gdma(rawt[:, :], rawsrc, ch="p0")
            C.wait(v, "p0")
            C.inc(v.tensor_scalar(dstt[:, 0:ncols], rawt[:, 2560 - ncols:2560],
                                  qsc[0:128, qcol:qcol + 1], None, ALU.mult), "v")
        if stage == 87:
            barrier(g, t, s, v, yq)
            gdma(xout[:, 0:2048], tqc[:, :])
            gdma(xout[:, 2048:4096], tqs[:, :])
            gdma(xout[:, 4096:6656], tkc[:, :])
            gdma(xout[:, 6656:9216], tks[:, :])
            barrier(g, t, s, v, yq)
            return
        if stage == 8:
            return
        C.wait(t, "s", cast_s)
        C.wait(t, "v", pre_v)
        for half in range(2):
            C.wait(g, "t")
            gdma(wv_sb[:, :, :],
                 wc_qkv.ap()[l].rearrange("(kc p) o -> p kc o", p=128)
                 [:, :, 1024 + half * 256:1024 + (half + 1) * 256], ch="p1")
            C.wait(t, "p1")
            lastv = {0: 0, 1: 0}
            for tt in range(16):
                ps = P_C[:, 0:512] if (tt % 2 == 0) else P_B[:, 0:512]
                psv = ps[0:128, 0:256]
                if tt >= 2:
                    C.wait(t, "v", lastv[tt % 2])
                for kc in range(NCH):
                    sh = 1 if kc < 4 else 0
                    mm = t.matmul(psv, Yb[:, kc, sh + tt * 128:sh + (tt + 1) * 128],
                                  wv_sb[:, kc, :], start=(kc == 0), stop=(kc == NCH - 1))
                C.inc(mm, "t")
                C.wait(v, "t")
                C.inc(v.tensor_copy(
                    v_tm[:, 4 + tt, half * 4:(half + 1) * 4, 0:64],
                    psv.rearrange("p (h e) -> p h e", e=64)), "v")
                lastv[tt % 2] = C.cnt["v"]
        if stage == 9:
            return
        import os as _os
        K_SUB = int(_os.environ.get("K_SUB", "99"))
        K_OC = int(_os.environ.get("K_OC", "8"))
        barrier(t, g)
        gdma(wtq[0][:, :, :],
             wc_qkv.ap()[l].rearrange("(kc p) o -> p kc o", p=128)[:, :, 0:128],
             ch="p0")
        pa_free = C.cnt["v"]
        tmile = {0: 0, 1: 0}
        for oc in range(K_OC):
            C.wait(t, "p0" if oc % 2 == 0 else "p1")
            C.wait(t, "v", pa_free)
            for nt in range(4):
                for kc in range(NCH):
                    sh = 1 if kc < 4 else 0
                    mm = t.matmul(P_A[:, nt * 512:(nt + 1) * 512],
                                  wtq[oc % 2][:, kc, :],
                                  Yb[:, kc, sh + nt * 512:sh + (nt + 1) * 512],
                                  start=(kc == 0), stop=(kc == NCH - 1))
            C.inc(mm, "t")
            tmile[oc % 2] = C.cnt["t"]
            if oc + 1 < K_OC:
                C.wait(g, "t", tmile[(oc + 1) % 2])
                gdma(wtq[(oc + 1) % 2][:, :, :],
                     wc_qkv.ap()[l].rearrange("(kc p) o -> p kc o", p=128)
                     [:, :, (oc + 1) * 128:(oc + 2) * 128],
                     ch="p0" if (oc + 1) % 2 == 0 else "p1")
            isq = oc < 4
            tc_, ts_ = (tqc, tqs) if isq else (tkc, tks)
            tofs = 0 if isq else 512
            if K_SUB == 0:
                C.wait(v, "t")
                for nt in range(4):
                    dst = (Qb[:, oc, nt * 512:(nt + 1) * 512] if isq else
                           Kb[:, oc - 4, 512 + nt * 512:512 + (nt + 1) * 512])
                    C.inc(v.tensor_copy(dst, P_A[:, nt * 512:(nt + 1) * 512]), "v")
                pa_free = C.cnt["v"]
                continue
            K_STEP = int(_os.environ.get("K_STEP", "99"))
            for nt in range(4):
                C.wait(s, "t")
                C.inc(s.activation(rt_q2[:, :], P_A[:, nt * 512:(nt + 1) * 512],
                                   AF.Square), "s")
                if K_STEP > 1:
                    C.wait(t, "s")
                    C.wait(t, "v")
                    C.inc(t.matmul(P_C[0:2, 0:512], ind2b[:, :], rt_q2[:, :],
                                   start=True, stop=True), "t")
                if K_STEP > 2:
                    C.wait(s, "t")
                    C.inc(s.activation(rt_ss[:, :], P_C[0:2, 0:512], AF.Ln,
                                       bias=EPS12), "s")
                    C.inc(s.activation(rt_ss[:, :], rt_ss[:, :], AF.Exp,
                                       scale=-0.5), "s")
                if K_STEP > 4:
                    C.wait(t, "s")
                    C.inc(t.matmul(P_C[:, 0:512], ind2f[:, :], rt_ss[0:2, :],
                                   start=True, stop=True), "t")
                if K_STEP > 5:
                    C.wait(v, "t")
                    C.inc(v.tensor_copy(rt_rb[:, :], P_C[:, 0:512]), "v")
                    C.inc(v.tensor_mul(rt_qn[:, :], P_A[:, nt * 512:(nt + 1) * 512],
                                       rt_rb[:, :]), "v")
                if K_STEP > 6:
                    C.wait(s, "v")
                    C.inc(s.copy(rt_qnb[:, :], rt_qn[:, :]), "s")
                if K_STEP > 7:
                    C.wait(t, "s")
                    C.inc(t.matmul(P_C[:, 0:512], rotT[:, :], rt_qnb[:, :],
                                   start=True, stop=True), "t")
                if K_STEP > 8:
                    C.wait(v, "t")
                    tci = tc_[:, tofs + nt * 512:tofs + (nt + 1) * 512]
                    tsi = ts_[:, tofs + nt * 512:tofs + (nt + 1) * 512]
                    C.inc(v.tensor_mul(rt_rb[:, :], rt_qn[:, :], tci), "v")
                if K_STEP > 9:
                    C.inc(v.tensor_copy(rt_t2[:, :], P_C[:, 0:512]), "v")
                    C.inc(v.tensor_mul(rt_qn[:, :], rt_t2[:, :], tsi), "v")
                if K_STEP > 10:
                    if K_SUB >= 2:
                        dst = (Qb[:, oc, nt * 512:(nt + 1) * 512] if isq else
                               Kb[:, oc - 4, 512 + nt * 512:512 + (nt + 1) * 512])
                        C.inc(v.tensor_add(dst, rt_rb[:, :], rt_qn[:, :]), "v")
            pa_free = C.cnt["v"]
        LAYER_STOP["qkv"] = True
        if stage == 10:
            return
        C.wait(g, "v")
        gdma(ccA_in.ap()[:, 0:2048].rearrange("p (c w) -> p c w", w=512),
             Kb[:, :, 2048:2560])
        gdma(ccA_in.ap()[:, 2048:2048 + VW],
             v_tm[:, 16:20, :, :].rearrange("p c h e -> p (c h e)"))
        C.wait(g, "g")
        C.inc(g.collective_compute("AllGather", ALU.bypass, replica_groups=PAIRS,
                                   ins=[ccA_in.ap().opt()],
                                   outs=[ccA_out.ap().opt()]), "c")
        C.wait(g, "c")
        oslc = ccA_out.ap()[bass.ds(oth_sv, 1), :, :]
        gdma(Kb[:, :, 0:512],
             oslc[:, :, 0:2048].rearrange("o p (c w) -> p (o c) w", w=512))
        gdma(v_tm[:, 0:4, :, :].rearrange("p c h e -> p (c h e)"),
             oslc[:, :, 2048:2048 + VW].rearrange("o p e -> p (o e)"))
        C.wait(v, "g")
        C.inc(v.tensor_scalar(Kb[:, :, 0:512], Kb[:, :, 0:512],
                              hzc[0:128, 0:1], None, ALU.mult), "v")
        C.inc(v.tensor_scalar(v_tm[:, 0:4, :, :], v_tm[:, 0:4, :, :],
                              hzc[0:128, 0:1], None, ALU.mult), "v")
        barrier(t, s, v)
        for hp in range(4):
            low = hp < 2
            cA, cB = 2 * hp, 2 * hp + 1
            if not low:
                cA, cB = 2 * hp - 4, 2 * hp - 3
            rA = slice(0, 64) if low else slice(64, 128)
            sets = [P_A[:, 0:1024], P_A[:, 1024:2048], P_B[:, 0:1024]]
            for w in range(NW):
                normv = C.cnt["v"]
                expmile, avmile = {}, {}
                for i in range(9):
                    if i < 8:
                        m = i % 3
                        sp_ = sets[m]
                        kcol = 512 * w + 128 * i
                        mtile = (maskslb[:, i * 512:(i + 1) * 512] if i < 4 else
                                 masksb[:, (i - 4) * 512:(i - 3) * 512])
                        if i == 2:
                            C.wait(t, "v", normv)
                        C.wait(t, "s", expmile.get(i - 3, 0))
                        t.matmul(sp_[0:128, 0:512], Kb[0:64, hp, kcol:kcol + 128],
                                 Qb[0:64, hp, w * 512:(w + 1) * 512],
                                 start=True, stop=False, tile_position=(0, 0))
                        t.matmul(sp_[0:128, 512:1024],
                                 Kb[64:128, hp, kcol:kcol + 128],
                                 Qb[64:128, hp, w * 512:(w + 1) * 512],
                                 start=True, stop=False, tile_position=(64, 0))
                        t.matmul(sp_[0:128, 0:512], ident[:, :], mtile,
                                 start=False, stop=True)
                        mm = t.matmul(sp_[0:128, 512:1024], ident[:, :], mtile,
                                      start=False, stop=True)
                        C.inc(mm, "t")
                        qkm = C.cnt["t"]
                    if i >= 1:
                        j = i - 1
                        eb = expb[j % 2]
                        tc = 4 * w + j
                        st_, sp2 = (j == 0), (j == 7)
                        hA, hB = 2 * hp, 2 * hp + 1
                        if j == 0:
                            C.wait(t, "v", normv)
                        C.wait(t, "s", expmile[j])
                        if low:
                            t.matmul(P_B[0:65, 1024:1536],
                                     v_tm[:, tc, hA, 0:65], eb[:, 0:512],
                                     start=st_, stop=sp2)
                            mm = t.matmul(P_C[0:65, 0:512],
                                          v_tm[:, tc, hB, 0:65],
                                          eb[:, 512:1024], start=st_, stop=sp2)
                        else:
                            t.matmul(P_B[64:128, 1024:1536],
                                     v_tm[:, tc, hA, 0:64], eb[:, 0:512],
                                     start=st_, stop=sp2, tile_position=(0, 64))
                            t.matmul(P_B[0:1, 1024:1536], onesb[:, :],
                                     eb[:, 0:512], start=st_, stop=sp2)
                            t.matmul(P_C[64:128, 0:512],
                                     v_tm[:, tc, hB, 0:64], eb[:, 512:1024],
                                     start=st_, stop=sp2, tile_position=(0, 64))
                            mm = t.matmul(P_C[0:1, 0:512], onesb[:, :],
                                          eb[:, 512:1024], start=st_, stop=sp2)
                        C.inc(mm, "t")
                        avmile[j] = C.cnt["t"]
                    if i < 8:
                        C.wait(s, "t", qkm)
                        if i >= 2:
                            C.wait(s, "t", avmile[i - 2])
                        bia = (macE[0:128, 0:1] if (w == 0 and i < 4) else EBIA)
                        C.inc(s.activation(expb[i % 2][:, 0:1024], sets[i % 3],
                                           AF.Exp, bias=bia, scale=8.0), "s")
                        expmile[i] = C.cnt["s"]
                qcols = slice(w * 512, (w + 1) * 512)
                if low:
                    attn_norm(True, P_B[0:64, 1024:1536], P_C[0:64, 0:512],
                              P_B[64:65, 1024:1536], P_C[64:65, 0:512],
                              o_fm[0:64, cA, qcols], o_fm[0:64, cB, qcols],
                              recip_sb, recbc)
                else:
                    attn_norm(False, P_B[64:128, 1024:1536], P_C[64:128, 0:512],
                              P_B[0:1, 1024:1536], P_C[0:1, 0:512],
                              o_fm[64:128, cA, qcols], o_fm[64:128, cB, qcols],
                              recip_sb, recbc)
        if stage == 11:
            return
        barrier(g)
        rearr_wo = wc_wo.ap()[l].rearrange("(kc p) o -> p kc o", p=128)
        C.wait(yq, "t")
        gdma(wos[:, 0:2, :], rearr_wo[:, 0:2, :])
        ydma(wos[:, 2:4, :], rearr_wo[:, 2:4, :], ch="q0")
        barrier(t)
        for oc in range(8):
            if oc >= 1:
                C.wait(t, "v")
            for nt in range(4):
                for kc in range(4):
                    mm = t.matmul(P_A[:, nt * 512:(nt + 1) * 512],
                                  wos[:, kc, oc * 128:(oc + 1) * 128],
                                  o_fm[:, kc, nt * 512:(nt + 1) * 512],
                                  start=(kc == 0), stop=(kc == 3))
            C.inc(mm, "t")
            C.wait(v, "t")
            C.inc(v.tensor_add(Y[:, oc, 1:1 + NT], Y[:, oc, 1:1 + NT], P_A[:, :]), "v")
        if stage == 12:
            return
        cc_tok()
        barrier(t, s, v)

        def ysrc(c):
            return Y[:, c, 1:1 + NT] if c < 4 else Y[:, c, 0:NT]
        ln_full(ysrc, "h", l=l)
        if stage == 13:
            return
        barrier(t, g, s, v)
        rearr1a = wc_w1a.ap()[l]
        rearr1g = wc_w1g.ap()[l]
        rearr2 = wc_w2.ap()[l].rearrange("(kc p) o -> p kc o", p=128)
        for tt in range(4):
            tcols = slice(tt * 512, (tt + 1) * 512)
            dmA, dmG = {}, {}

            def loadj(j):
                pch = "p0" if j % 2 == 0 else "p1"
                qch = "q0" if j % 2 == 0 else "q1"
                gdma(wt1a[j % 3][:, :, :], rearr1a[:, j, :, :], ch=pch)
                dmA[j] = (pch, C.cnt[pch])
                ydma(wt1g[j % 3][:, :, :], rearr1g[:, j, :, :], ch=qch)
                dmG[j] = (qch, C.cnt[qch])
            C.wait(g, "t")
            C.wait(yq, "t")
            loadj(0)
            loadj(1)
            jmile = {}
            vmile, smile = {}, {}
            for j in range(NJ):
                bi = j % 3
                for nm_, mi_ in (dmA[j], dmG[j]):
                    C.wait(t, nm_, mi_)
                if j >= 2:
                    C.wait(t, "v", vmile[j - 2])
                    C.wait(t, "s", smile[j - 2])
                pj = (j % 2) * 1024
                for kc in range(NCH):
                    t.matmul(P_A[:, pj:pj + 512], wt1a[bi][:, kc, :],
                             Hb[:, kc, tcols], start=(kc == 0), stop=(kc == 7))
                for kc in range(NCH):
                    mm = t.matmul(P_A[:, pj + 512:pj + 1024], wt1g[bi][:, kc, :],
                                  Hb[:, kc, tcols], start=(kc == 0), stop=(kc == 7))
                C.inc(mm, "t")
                if j + 2 < NJ:
                    C.wait(g, "t", jmile.get(j - 1, 0))
                    C.wait(yq, "t", jmile.get(j - 1, 0))
                    loadj(j + 2)
                jmile[j] = C.cnt["t"]
                tj = tnm[:, (j % 2) * 512:(j % 2) * 512 + 512]
                C.wait(s, "t")
                C.inc(s.activation(tj, P_A[:, pj + 512:pj + 1024],
                                   AF.Gelu, bias=b1g_c[0:128, l * NJ + j:l * NJ + j + 1]),
                      "s")
                smile[j] = C.cnt["s"]
                C.wait(v, "s")
                C.inc(v.scalar_tensor_tensor(
                    AGb[:, j, :], P_A[:, pj:pj + 512],
                    b1a_c[0:128, l * NJ + j:l * NJ + j + 1], tj,
                    ALU.add, ALU.mult), "v")
                vmile[j] = C.cnt["v"]
            dm2 = {}

            def loadk(kc):
                ch = "r0" if kc % 2 == 0 else "r1"
                gdma(wt2[kc % 3][:, :], rearr2[:, kc, :], ch=ch)
                dm2[kc] = (ch, C.cnt[ch])
            barrier(t, g)
            loadk(0)
            loadk(1)
            slots = [P_A[:, i * 512:(i + 1) * 512] for i in range(4)] + \
                    [P_B[:, i * 512:(i + 1) * 512] for i in range(3)] + [P_C[:, 0:512]]
            kmile = {}
            for kc in range(NJ):
                nm_, mi_ = dm2[kc]
                C.wait(t, nm_, mi_)
                for oc in range(8):
                    mm = t.matmul(slots[oc], wt2[kc % 3][:, oc * 128:(oc + 1) * 128],
                                  AGb[:, kc, :], start=(kc == 0), stop=(kc == NJ - 1))
                C.inc(mm, "t")
                if kc + 2 < NJ:
                    C.wait(g, "t", kmile.get(kc - 1, 0))
                    loadk(kc + 2)
                kmile[kc] = C.cnt["t"]
            C.wait(v, "t")
            for oc in range(8):
                C.inc(v.scalar_tensor_tensor(
                    Y[:, oc, 1 + tt * 512:1 + (tt + 1) * 512], slots[oc],
                    b2_c[0:128, l * 8 + oc:l * 8 + oc + 1],
                    Y[:, oc, 1 + tt * 512:1 + (tt + 1) * 512],
                    ALU.add, ALU.add), "v")
            barrier(t)
        if first_of_call:
            cc_tok()

    def dense_phase():
        def xsrc(c):
            return X[:, c, :]
        ln_full(xsrc, "xlb")
        barrier(g)
        gdma(ccD_in.ap(), XLb[:, :, :].rearrange("p c t -> p (c t)"))
        C.wait(g, "g")
        C.inc(g.collective_compute("AllGather", ALU.bypass, replica_groups=PAIRS,
                                   ins=[ccD_in.ap().opt()],
                                   outs=[ccD_out.ap().opt()]), "c")
        C.wait(g, "c")
        gdma(XLo[:, :, :].rearrange("p c t -> p (c t)"),
             ccD_out.ap()[bass.ds(oth_sv, 1), :, :].rearrange("o p e -> p (o e)"))
        C.inc(v.memset(Vd[:, :, 64:65], 1.0), "v")
        C.inc(v.memset(Vd[:, :, 129:130], 1.0), "v")
        barrier(t, s, v)
        import os as _os
        K_NP = int(_os.environ.get("K_NP", "8"))
        for c in range(K_NP):
            low = c < 4
            cA, cB = (2 * c, 2 * c + 1) if low else (2 * c - 8, 2 * c - 7)
            C.wait(t, "v")
            C.wait(t, "s")
            for nt in range(4):
                C.inc(t.matmul(P_A[:, nt * 512:(nt + 1) * 512], blkq[:, :],
                               XLb[:, c, nt * 512:(nt + 1) * 512],
                               start=True, stop=True), "t")
            C.wait(s, "t")
            C.inc(s.copy(Qd[:, :], P_A[:, :]), "s")
            C.wait(t, "s")
            for half, src in ((0, XLb), (1, XLo)):
                for nt in range(4):
                    C.inc(t.matmul(P_A[:, nt * 512:(nt + 1) * 512], blkk[:, :],
                                   src[:, c, nt * 512:(nt + 1) * 512],
                                   start=True, stop=True), "t")
                C.wait(s, "t")
                C.inc(s.copy(Kd[:, half * 2048:(half + 1) * 2048], P_A[:, :]), "s")
                C.wait(t, "s")
                lastv = {0: 0, 1: 0}
                for ttl in range(16):
                    ps = P_C[:, 0:512] if (ttl % 2 == 0) else P_B[:, 0:512]
                    psv = ps[0:128, 0:128]
                    if ttl >= 2:
                        C.wait(t, "v", lastv[ttl % 2])
                    C.inc(t.matmul(psv, src[:, c, ttl * 128:(ttl + 1) * 128],
                                   blkv[:, :], start=True, stop=True), "t")
                    C.wait(v, "t")
                    C.inc(v.tensor_copy(
                        Vd[:, half * 16 + ttl, 0:130].rearrange(
                            "p (h e) -> p h e", e=65)[:, :, 0:64],
                        psv.rearrange("p (h e) -> p h e", e=64)), "v")
                    lastv[ttl % 2] = C.cnt["v"]
            barrier(t, s, v)
            sets = [P_A[:, 0:1024], P_A[:, 1024:2048], P_B[:, 0:1024]]
            for qt in range(4):
                qcols = slice(qt * 512, (qt + 1) * 512)
                normv = C.cnt["v"]
                expmile, avmile = {}, {}
                for i in range(33):
                    if i < 32:
                        m = i % 3
                        sp_ = sets[m]
                        if i == 2:
                            C.wait(t, "v", normv)
                        C.wait(t, "s", expmile.get(i - 3, 0))
                        t.matmul(sp_[0:128, 0:512],
                                 Kd[0:64, i * 128:(i + 1) * 128],
                                 Qd[0:64, qcols], start=True, stop=True,
                                 tile_position=(0, 0))
                        mm = t.matmul(sp_[0:128, 512:1024],
                                      Kd[64:128, i * 128:(i + 1) * 128],
                                      Qd[64:128, qcols], start=True, stop=True,
                                      tile_position=(64, 0))
                        C.inc(mm, "t")
                        qkm = C.cnt["t"]
                    if i >= 1:
                        j = i - 1
                        eb = expd[j % 2]
                        st_, sp2 = (j == 0), (j == 31)
                        if j == 0:
                            C.wait(t, "v", normv)
                        C.wait(t, "s", expmile[j])
                        if low:
                            t.matmul(P_B[0:65, 1024:1536], Vd[:, j, 0:65],
                                     eb[:, 0:512], start=st_, stop=sp2)
                            mm = t.matmul(P_C[0:65, 0:512], Vd[:, j, 65:130],
                                          eb[:, 512:1024], start=st_, stop=sp2)
                        else:
                            t.matmul(P_B[64:128, 1024:1536], Vd[:, j, 0:64],
                                     eb[:, 0:512], start=st_, stop=sp2,
                                     tile_position=(0, 64))
                            t.matmul(P_B[0:1, 1024:1536], onesb[:, :],
                                     eb[:, 0:512], start=st_, stop=sp2)
                            t.matmul(P_C[64:128, 0:512], Vd[:, j, 65:129],
                                     eb[:, 512:1024], start=st_, stop=sp2,
                                     tile_position=(0, 64))
                            mm = t.matmul(P_C[0:1, 0:512], onesb[:, :],
                                          eb[:, 512:1024], start=st_, stop=sp2)
                        C.inc(mm, "t")
                        avmile[j] = C.cnt["t"]
                    if i < 32:
                        C.wait(s, "t", qkm)
                        if i >= 2:
                            C.wait(s, "t", avmile[i - 2])
                        C.inc(s.activation(expd[i % 2][:, 0:1024], sets[i % 3],
                                           AF.Exp, bias=EBIA,
                                           scale=1.0 / 32.0), "s")
                        expmile[i] = C.cnt["s"]
                if low:
                    attn_norm(True, P_B[0:64, 1024:1536], P_C[0:64, 0:512],
                              P_B[64:65, 1024:1536], P_C[64:65, 0:512],
                              o_dn[0:64, cA, qcols], o_dn[0:64, cB, qcols],
                              recip_d, recbc_d)
                else:
                    attn_norm(False, P_B[64:128, 1024:1536], P_C[64:128, 0:512],
                              P_B[0:1, 1024:1536], P_C[0:1, 0:512],
                              o_dn[64:128, cA, qcols], o_dn[64:128, cB, qcols],
                              recip_d, recbc_d)
        barrier(g, t, s, v, yq)
        rearr_wout = wc_woutT.ap().rearrange("(kc p) o -> p kc o", p=128)
        gdma(wouts[:, 0:4, :], rearr_wout[:, 0:4, :])
        ydma(wouts[:, 4:8, :], rearr_wout[:, 4:8, :], ch="q0")
        barrier(t)
        for oc in range(8):
            if oc >= 1:
                C.wait(t, "v")
            for nt in range(4):
                for kc in range(8):
                    mm = t.matmul(P_A[:, nt * 512:(nt + 1) * 512],
                                  wouts[:, kc, oc * 128:(oc + 1) * 128],
                                  o_dn[:, kc, nt * 512:(nt + 1) * 512],
                                  start=(kc == 0), stop=(kc == 7))
            C.inc(mm, "t")
            C.wait(v, "t")
            C.inc(v.scalar_tensor_tensor(
                X[:, oc, :], P_A[:, :], bout_c[0:128, oc:oc + 1], X[:, oc, :],
                ALU.add, ALU.add), "v")

    if stage <= 1:
        return finish()
    if stage == 85:
        barrier(t, s, v, g, yq)
        gdma(xout[:, 0:1536], wc_qkv.ap()[0][0:128, :])
        gdma(xout[:, 2048:2048 + 1536], wc_qkv.ap()[0][512:640, :])
        gdma(xout[:, 4096:4096 + 2816], wc_w1a.ap()[0][0:128, :])
        gdma(xout[:, 8192:8192 + 1024], wc_w2.ap()[0][0:128, :])
        barrier(g, t, s, v, yq)
        return nc, dbg_outs
    for call in range(2):
        def xsrc(c):
            return X[:, c, :]
        ln_full(xsrc, "y")
        if stage <= 2:
            return finish_y()
        cc_tok()
        if stage <= 3:
            return finish_y()
        barrier(yq)
        ydma(x_spill.ap(), X[:, :, :].rearrange("p c t -> p (c t)"))
        barrier(t, s, v, g)
        for li, l in enumerate((0, 1)):
            ffn_layer(l, first_of_call=(li == 0))
            if stage == 87:
                return nc, dbg_outs
            if stage in (8, 9, 10, 11, 12, 13):
                barrier(t, s, v, g, yq)
                if stage == 10:
                    import os as _os
                    if _os.environ.get("K_NODUMP"):
                        gdma(xout.rearrange("p (c t) -> p c t", t=NT),
                             Y[:, :, 1:1 + NT])
                    else:
                        gdma(xout[:, 0:8192].rearrange("p (c t) -> p c t", t=2048),
                             Qb[:, :, :])
                        for cc_ in range(4):
                            gdma(xout[:, 8192 + cc_ * 2048:8192 + (cc_ + 1) * 2048],
                                 Kb[:, cc_, 512:2560])
                elif stage == 11:
                    gdma(xout[:, 0:8192].rearrange("p (c t) -> p c t", t=2048),
                         o_fm[:, :, :])
                elif stage in (8, 9, 12):
                    gdma(xout.rearrange("p (c t) -> p c t", t=NT), Y[:, :, 1:1 + NT])
                elif stage == 13:
                    gdma(xout, Hb[:, :, :].rearrange("p c t -> p (c t)"))
                barrier(g, t, s, v, yq)
                return nc, dbg_outs
            if stage <= 4 and li == 0:
                barrier(t, s, v, g, yq)
                gdma(xout.rearrange("p (c t) -> p c t", t=NT), Y[:, :, 1:1 + NT])
                barrier(g, t, s, v, yq)
                return nc, dbg_outs
            if stage <= 5 and li == 1:
                barrier(t, s, v, g, yq)
                gdma(xout.rearrange("p (c t) -> p c t", t=NT), Y[:, :, 1:1 + NT])
                barrier(g, t, s, v, yq)
                return nc, dbg_outs
        barrier(g, yq)
        gdma(X[:, :, :], x_spill.ap().rearrange("p (c t) -> p c t", t=NT))
        C.wait(v, "g")
        for c in range(NCH):
            C.inc(v.scalar_tensor_tensor(
                X[:, c, :], Y[:, c, 1:1 + NT], 0.5, X[:, c, :],
                ALU.mult, ALU.add), "v")
        if stage <= 6:
            return finish()
        if call == 0:
            barrier(t, s, v, g, yq)
            dense_phase()
            if stage <= 7:
                return finish()
        barrier(t, s, v, g, yq)

    return finish()


_CACHE = {}


def _host_inputs(inputs):
    x = np.asarray(inputs["L0_attn_x"])
    Wq = np.asarray(inputs["L0_attn_Wq"])
    Wk = np.asarray(inputs["L0_attn_Wk"])
    Wv = np.asarray(inputs["L0_attn_Wv"])
    W_out = np.asarray(inputs["L0_attn_W_out"])
    b_out = np.asarray(inputs["L0_attn_b_out"])
    wqkv = np.asarray(inputs["L0_ffn_wqkv"])
    q_scale = np.asarray(inputs["L0_ffn_q_scale"])
    k_scale = np.asarray(inputs["L0_ffn_k_scale"])
    wo = np.asarray(inputs["L0_ffn_wo"])
    ln_g = np.asarray(inputs["L0_ffn_ln_g"])
    ln_b = np.asarray(inputs["L0_ffn_ln_b"])
    w1 = np.asarray(inputs["L0_ffn_w1"])
    b1 = np.asarray(inputs["L0_ffn_b1"])
    w2 = np.asarray(inputs["L0_ffn_w2"])
    b2 = np.asarray(inputs["L0_ffn_b2"])

    f32 = np.float32
    com = {}
    com["wqkv"] = np.ascontiguousarray(wqkv, dtype=f32)
    com["wo_p"] = np.ascontiguousarray(wo[:, PERM_WO, :], dtype=f32)
    w1a = np.zeros((2, D, HIDP), f32)
    w1g = np.zeros((2, D, HIDP), f32)
    w1a[:, :, :HID] = w1[:, :, :HID]
    w1g[:, :, :HID] = w1[:, :, HID:]
    com["w1a"], com["w1g"] = w1a, w1g
    w2pp = np.zeros((2, HIDP, D), f32)
    w2pp[:, :HID, :] = w2
    com["w2p"] = w2pp
    com["woutT"] = np.ascontiguousarray(W_out.T[PERM_WOUT, :], dtype=f32)
    com["wq_t"] = np.ascontiguousarray(Wq.T, dtype=f32)
    com["wk_t"] = np.ascontiguousarray(Wk.T, dtype=f32)
    com["wv_t"] = np.ascontiguousarray(Wv.T, dtype=f32)
    com["b_out_r"] = np.ascontiguousarray(b_out.reshape(8, 128).T, dtype=f32)
    com["ln_g_r"] = np.ascontiguousarray(ln_g.reshape(2, 8, 128).swapaxes(1, 2), dtype=f32)
    com["ln_b_r"] = np.ascontiguousarray(ln_b.reshape(2, 8, 128).swapaxes(1, 2), dtype=f32)
    com["b2_r"] = np.ascontiguousarray(b2.reshape(2, 8, 128).swapaxes(1, 2), dtype=f32)
    b1a = np.zeros((2, 128, NJ), f32)
    b1g = np.zeros((2, 128, NJ), f32)
    for l in range(2):
        pa = np.zeros(HIDP, f32)
        pg = np.zeros(HIDP, f32)
        pa[:HID] = b1[l, :HID]
        pg[:HID] = b1[l, HID:]
        b1a[l] = pa.reshape(NJ, 128).T
        b1g[l] = pg.reshape(NJ, 128).T
    com["b1a_r"], com["b1g_r"] = b1a, b1g
    perm64 = (np.arange(64) + 32) % 64
    com["qs_r"] = np.ascontiguousarray(
        np.tile(q_scale, (1, 2)).reshape(2, 128, 1), dtype=f32)
    com["qsp_r"] = np.ascontiguousarray(
        np.tile(q_scale[:, perm64], (1, 2)).reshape(2, 128, 1), dtype=f32)
    com["ks_r"] = np.ascontiguousarray(
        np.tile(k_scale, (1, 2)).reshape(2, 128, 1), dtype=f32)
    com["ksp_r"] = np.ascontiguousarray(
        np.tile(k_scale[:, perm64], (1, 2)).reshape(2, 128, 1), dtype=f32)

    in_maps = []
    for c in range(NCORE):
        b, h = c // 2, c % 2
        t0 = h * NT
        m = dict(com)
        m["xin"] = _pack_fm(np.asarray(x[b, t0:t0 + NT, :], dtype=f32))
        tc_, ts_, tic, tis = _rot_tables(t0)
        m["tab_cos"], m["tab_sin"] = tc_, ts_
        m["tab_icos"], m["tab_isin"] = tic, tis
        m["hz"] = np.full((128, 1), 0.0 if h == 0 else 1.0, f32)
        m["maskA"] = np.full((128, 1), NEG if h == 0 else 0.0, f32)
        in_maps.append(m)
    return in_maps


def kernel(**inputs):
    import os
    stg = int(os.environ.get("K_STAGE", "99"))
    key = ("prog", stg)
    if key not in _CACHE:
        _CACHE[key] = build_program(stage=stg)[0]
    nc = _CACHE[key]
    in_maps = _host_inputs(inputs)
    res = run_bass_kernel_spmd(nc, in_maps, core_ids=list(range(NCORE)))
    out = np.empty((B, S, D), dtype=np.float32)
    for c in range(NCORE):
        b, h = c // 2, c % 2
        fm = res.results[c]["xout"]
        out[b, h * NT:(h + 1) * NT, :] = (
            fm.reshape(128, NCH, NT).transpose(1, 0, 2).reshape(D, NT).T)
    return out
